# version 05
# baseline (speedup 1.0000x reference)
import numpy as np
import ml_dtypes
import concourse.bass as bass
import concourse.mybir as mybir
from concourse.bass_utils import run_bass_kernel_spmd

F32 = mybir.dt.float32
BF16 = mybir.dt.bfloat16
I32 = mybir.dt.int32
AF = mybir.ActivationFunctionType
ALU = mybir.AluOpType
AX = mybir.AxisListType
IOX = bass.IndirectOffsetOnAxis

D = 2048
NCORE = 8
TOWN = 1024
WU = 64
TEXT = TOWN + WU
H, DK, DV, M = 16, 128, 128, 128
SCALE = float(DK ** -0.5)
E = 8
CAP = 352
NER = E * CAP
TRASH = NER
FI = 1408
FJ = 5632
CSHIFT = -45.0
ECONST = float(np.exp(np.float64(CSHIFT)))
EPS = 1e-5
KB = 1024


class Ctr:
    def __init__(self, nc, name):
        self.h = nc.alloc_semaphore(name)
        self.n = 0

    def bump(self, k=1):
        self.n += k
        return self.n


class MK:
    def __init__(self, do_attn=True, do_ffn2=True, dbg=()):
        self.do_attn = do_attn
        self.do_ffn2 = do_ffn2
        nc = bass.Bass("TRN2", target_bir_lowering=False)
        self.nc = nc
        self.ops = {"t": [], "v": [], "s": [], "g": [], "y": []}

        def din(name, shape, dt=F32):
            return nc.dram_tensor(name, shape, dt, kind="ExternalInput").ap()

        self.xin = din("xin", [TEXT, D])
        self.WqA = din("Wq", [4, 128, 16, 512]); self.WkA = din("Wk", [4, 128, 16, 512])
        self.WvA = din("Wv", [4, 128, 16, 512]); self.WwA = din("Ww", [4, 128, 16, 512])
        self.cqA = din("cq", [D, 4]); self.ckA = din("ck", [D, 4]); self.cvA = din("cv", [D, 4])
        self.Wf1A = din("Wf1", [D, DV]); self.Wf2A = din("Wf2", [DV, H * M])
        self.WbA = din("Wb", [D, H])
        self.Wg1A = din("Wg1", [D, DV]); self.Wg2A = din("Wg2", [DV, H * DV])
        self.bg2A = din("bg2", [H * DV])
        self.normwA = din("norm_w", [DV])
        self.WoA = din("Wo", [4, 128, 16, 512])
        self.RtA = din("router", [D, E])
        self.WugA = din("w_up_gate", [E, 11, 128, 16, 256])
        self.WdnA = din("w_down", [E, 4, 128, 11, 512])
        self.SugA = din("shared_w_up_gate", [2, 2, 11, 128, 16, 256])
        self.SdnA = din("shared_w_down", [2, 4, 128, 22, 512])
        self.wvalidA = din("wvalid", [1, 1])
        self.xoutA = nc.dram_tensor("xout", [TOWN, D], F32, kind="ExternalOutput").ap()

        def scr(name, shape, dt):
            return nc.dram_tensor(name, shape, dt, kind="Internal").ap()

        self.y_dram = scr("y_dram", [TEXT, D], BF16)
        self.x1_dram = scr("x1_dram", [TEXT, D], F32)
        self.x2_dram = scr("x2_dram", [TOWN, D], F32)
        self.xbuf = scr("xbuf", [NER + 1, D], BF16)
        self.outbuf = scr("outbuf", [NER + 1, D], F32)
        self.qfm_d = scr("qfm_d", [D, TEXT], BF16)
        self.kfm_d = scr("kfm_d", [D, TEXT], BF16)
        self.vfm_d = scr("vfm_d", [D, TEXT], BF16)
        self.wfm_d = scr("wfm_d", [D, TEXT], BF16)
        self.obuf = scr("obuf", [TOWN, D], BF16)
        self.SugB = scr("SugB", [2, 2, 11, 128, 16, 256], BF16)
        self.SdnB = scr("SdnB", [2, 4, 128, 22, 512], BF16)
        self.WugB = scr("WugB", [E, 11, 128, 16, 256], BF16)
        self.WdnB = scr("WdnB", [E, 4, 128, 11, 512], BF16)

        it = nc.inline_tensor
        self.C_UINC = it(np.triu(np.ones((128, 128), np.float32)).copy(), "c_UINC").ap()
        self.C_USTR = it(np.triu(np.ones((128, 128), np.float32), 1).copy(), "c_USTR").ap()
        self.C_ONES1 = it(np.ones((1, 128), np.float32), "c_ONES1").ap()
        self.C_ONESC = it(np.ones((128, 1), np.float32), "c_ONESC").ap()
        self.C_IDENT = it(np.eye(128, dtype=ml_dtypes.bfloat16), "c_IDENT").ap()
        self.C_MASKR = it(np.tile(np.triu(np.ones((128, 128), np.float32)), (1, H))
                          .astype(ml_dtypes.bfloat16), "c_MASKR").ap()
        self.C_IOTAP = it((np.arange(8, dtype=np.float32)[None, :] + 1000.0).repeat(128, 0).copy(),
                          "c_IOTAP").ap()
        self.C_IOTA8 = it(np.arange(8, dtype=np.float32)[None, :].repeat(128, 0).copy(),
                          "c_IOTA8").ap()

        sb = lambda n, s, t: nc.alloc_sbuf_tensor(n, s, t)
        self.identb = sb("identb", [128, 128], BF16)
        self.uinc = sb("uinc", [128, 128], F32)
        self.ustrict = sb("ustrict", [128, 128], F32)
        self.ones1x = sb("ones1x", [1, 128], F32)
        self.onesc = sb("onesc", [128, 1], F32)
        self.maskrep = sb("maskrep", [128, H * 128], BF16)
        self.iota8p = sb("iota8p", [128, 8], F32)
        self.iota8 = sb("iota8", [128, 8], F32)
        self.rt_sb = sb("rt_sb", [128, 16, E], BF16)
        self.wv_sb = sb("wv_sb", [1, 1], F32)
        self.wv_col = sb("wv_col", [128, 1], F32)
        self.tcol = sb("tcol", [128, 1], F32)
        self.convw = sb("convw", [128, 3, 16, 4], F32)
        self.wf2_sb = sb("wf2_sb", [128, H * M], BF16)
        self.wg2_sb = sb("wg2_sb", [128, H * DV], BF16)
        self.wb_sb = sb("wb_sb", [128, 16, H], BF16)
        self.wf1_sb = sb("wf1_sb", [128, 16, 128], BF16)
        self.wg1_sb = sb("wg1_sb", [128, 16, 128], BF16)
        self.nwrow = sb("nwrow", [1, DV], F32)
        self.nw128 = sb("nw128", [128, DV], F32)
        self.lgt = sb("lgt", [128, 9, 8], F32)
        self.off_i = [sb(f"off{k}", [128, 9], I32) for k in range(2)]
        self.cwvt = [sb(f"cwv{k}", [128, 9], F32) for k in range(2)]
        self.r_top8 = sb("r_top8", [128, 8], F32)
        self.r_t8 = sb("r_t8", [128, 8], F32)
        self.r_ohA = sb("r_ohA", [128, 8], F32)
        self.r_ohB = sb("r_ohB", [128, 8], F32)
        self.r_ohs = sb("r_ohs", [128, 8], F32)
        self.t_sel0 = sb("t_sel0", [128, 1], F32)
        self.t_sel1 = sb("t_sel1", [128, 1], F32)
        self.t_pos0 = sb("t_pos0", [128, 1], F32)
        self.t_pos1 = sb("t_pos1", [128, 1], F32)
        self.t_v0 = sb("t_v0", [128, 1], F32)
        self.t_v1 = sb("t_v1", [128, 1], F32)
        self.t_cwA = sb("t_cwA", [128, 1], F32)
        self.t_cwB = sb("t_cwB", [128, 1], F32)
        self.t_a = sb("t_a", [128, 1], F32)
        self.t_b = sb("t_b", [128, 1], F32)
        self.t_mr = sb("t_mr", [128, 1], F32)
        self.t_rs = sb("t_rs", [128, 1], F32)
        self.carry = sb("carry", [1, 8], F32)
        self.CVALS = [0.0, 1.0, -1.0, float(EPS), 45.0, -45.0, -0.5, 1.0 / 128.0,
                      float(SCALE), 0.5, float(ECONST), 1e-6]
        self.cpool = sb("cpool", [128, len(self.CVALS)], F32)
        for ci, val in enumerate(self.CVALS):
            nc.const_aps.register(F32, val, self.cpool[:, ci:ci + 1])

        ARENA = 180 * KB
        base, _ = nc.bump_sbuf(ARENA)
        self._ab = base

        def at(name, shape, dt, off):
            return nc.alloc_sbuf_tensor_at(name, shape, dt, offset=base + off)

        self.yfm = at("yfm", [128, 16, TEXT], BF16, 0)
        self.hsh = at("hsh", [128, 22, TEXT], BF16, 34816)
        self.xt = [at(f"xt{i}", [128, D], F32, 82688 + i * 8192) for i in range(2)]
        self.estg = [at("estg0", [128, 512], F32, 86784)]
        self.wslab = [at(f"wslab{i}", [128, 16, 256], BF16, 99072 + i * 8192) for i in range(2)]
        self.dslab = [at(f"dslab{i}", [128, 4, 512], BF16, 115456 + i * 4096) for i in range(2)]
        self.sqbuf = at("sqbuf", [128, D], F32, 123648)
        self.eslab = [at("eslabB", [128, 16, 256], BF16, 123648),
                      at("eslabA", [128, 16, 256], BF16, 131840)]
        self.xg = [at("xg0", [128, 16, CAP], BF16, 140032)]
        self.he = at("he", [128, 11, CAP], BF16, 151296)
        self.edslab = [at(f"edslab{i}", [128, 11, 512], BF16, 159040 + i * 11264) for i in range(2)]
        self.ytm_t = [at(f"ytm{i}", [128, D], BF16, 159040 + i * 4096) for i in range(2)]
        self.g0t = at("g0t", [128, D], F32, 131840)
        self.g1t = at("g1t", [128, D], F32, 140032)
        self.bg2row = at("bg2row", [1, H * DV], F32, 140032)
        self.conv_z = [at(f"conv_z{i}", [128, TEXT + 3], F32, 107264 + i * 4608) for i in range(2)]
        self.conv_a = at("conv_a", [128, TEXT], F32, 116480)
        self.conv_o = [at(f"conv_o{i}", [128, TEXT], BF16, 120832 + i * 2176) for i in range(2)]
        self.wslab4 = [at(f"wslab4_{i}", [128, 16, 512], BF16, 34816 + i * 16384) for i in range(2)]
        so = 0
        self.qfm_c = [at(f"qfmc{i}", [128, H, 128], BF16, so + i * 4096) for i in range(2)]
        self.kfm_c = [at(f"kfmc{i}", [128, H, 128], BF16, so + 8192 + i * 4096) for i in range(2)]
        self.wfm_c = [at(f"wfmc{i}", [128, H, 128], BF16, so + 16384 + i * 4096) for i in range(2)]
        self.ktm_c = [at(f"ktmc{i}", [128, H * DK], BF16, so + 24576 + i * 4096) for i in range(2)]
        self.vtm_c = [at(f"vtmc{i}", [128, H * DV], BF16, so + 32768 + i * 4096) for i in range(2)]
        self.wtm_c = [at(f"wtmc{i}", [128, H * M], BF16, so + 40960 + i * 4096) for i in range(2)]
        self.Sk = at("Sk", [128, H * DK], F32, 49152)
        self.Sv = at("Sv", [128, H * DV], F32, 57344)
        self.Skp = at("Skp", [128, H * M], BF16, 65536)
        self.Svp = at("Svp", [128, H * DV], BF16, 69632)
        self.f1_sb = at("f1_sb", [128, TEXT], BF16, 73728)
        self.g1_sb = at("g1_sb", [128, TEXT], BF16, 76032)
        self.beta = at("beta", [128, 9, H], F32, 78336)
        self.fcol = at("fcol", [128, H], F32, 78976)
        self.mrow = at("mrow", [128, H], F32, 79104)
        self.srow = at("srow", [128, H], F32, 79232)
        self.rrow = at("rrow", [128, H], F32, 79360)
        self.wrow = at("wrow", [128, H], F32, 79488)
        self.brs = at("brs", [128, H], F32, 79616)
        self.bg2f = at("bg2f", [128, H * DV], F32, 79872)
        self.nwfull = at("nwfull", [128, H, DV], F32, 88064)
        self.sgf = at("sgf", [128, H * DV], F32, 96256)
        self.gpre = at("gpre", [128, H * M], F32, 104448)
        self.s_sb = at("s_sb", [128, H * M], F32, 104448)
        self.sq2 = at("sq2", [128, H * DV], F32, 104448)
        self.tmpA = at("tmpA", [128, H * M], F32, 112640)
        self.lfm = at("lfm", [128, H * M], F32, 112640)
        self.at_sb = at("at_sb", [128, H * 128], BF16, 112640)
        self.m_sb = at("m_sb", [128, H * 128], BF16, 116736)
        self.onrm = at("onrm", [128, H * DV], F32, 112640)
        self.tmpB = at("tmpB", [128, H * M], F32, 120832)
        self.p1f = at("p1f", [128, H * M], F32, 120832)
        self.ogat = at("ogat", [128, H * DV], BF16, 120832)
        self.gtm = at("gtm", [128, H * M], F32, 129024)
        self.p2b = at("p2b", [128, H * M], BF16, 129024)
        self.p2fb = at("p2fb", [128, H * M], BF16, 133120)
        self.ltm = at("ltm", [128, H * M], F32, 137216)
        self.sfold = at("sfold", [128, H * DK], F32, 137216)
        self.kpb = at("kpb", [128, H * M], BF16, 145408)
        self.wtf = at("wtf", [128, H * M], BF16, 149504)
        self.a_sb = at("a_sb", [128, H * 128], BF16, 153600)
        self.skb16 = at("skb16", [128, H * DK], BF16, 157696)
        self.ofm = at("ofm", [128, 16, TOWN], BF16, 0)
        self.oslab = [at(f"oslab{i}", [128, 16, 512], BF16, 32768 + i * 16384) for i in range(2)]

        self.PA = nc.alloc_psum_tensor("PA", [128, 2048], F32)
        self.PB = nc.alloc_psum_tensor("PB", [128, 2048], F32)

        self.cT = Ctr(nc, "cT")
        self.cV = Ctr(nc, "cV")
        self.cS = Ctr(nc, "cS")
        self.qS = Ctr(nc, "qS")
        self.qXT = [Ctr(nc, f"qXT{i}") for i in range(2)]
        self.qYS = [Ctr(nc, f"qYS{i}") for i in range(2)]
        self.qF = Ctr(nc, "qF")
        self.qYT = [Ctr(nc, f"qYT{i}") for i in range(2)]
        self.qSCp = [Ctr(nc, f"qSCp{i}") for i in range(2)]
        self.qWS = [Ctr(nc, f"qWS{i}") for i in range(2)]
        self.qDS = [Ctr(nc, f"qDS{i}") for i in range(2)]
        self.qES = [Ctr(nc, f"qES{i}") for i in range(2)]
        self.qED = [Ctr(nc, f"qED{i}") for i in range(2)]
        self.qXG = [Ctr(nc, f"qXG{i}") for i in range(2)]
        self.qCW = [Ctr(nc, f"qCW{i}") for i in range(2)]
        self.qCE = [Ctr(nc, f"qCE{i}") for i in range(2)]
        self.qCDW = [Ctr(nc, f"qCDW{i}") for i in range(2)]
        self.qCED = [Ctr(nc, f"qCED{i}") for i in range(2)]
        self.cG = Ctr(nc, "cG")
        self.gdummy = nc.alloc_sbuf_tensor("gdummy", [1, 8], F32)
        self.qOBs = [Ctr(nc, f"qOBs{i}") for i in range(3)]
        self.qG = Ctr(nc, "qG")
        self.qXS = [Ctr(nc, f"qXS{i}") for i in range(2)]
        self.qKV = [Ctr(nc, f"qKV{i}") for i in range(2)]
        self.qCH = [Ctr(nc, f"qCH{i}") for i in range(2)]
        self.qOS = [Ctr(nc, f"qOS{i}") for i in range(2)]
        self.qOBU = Ctr(nc, "qOBU")
        self.qOF = Ctr(nc, "qOF")

        self.dbg_aps = {}
        shp_map = {"y_dram": ([TEXT, D], BF16), "x1_dram": ([TEXT, D], F32),
                   "xbuf": ([NER + 1, D], BF16), "outbuf": ([NER + 1, D], F32),
                   "x2_dram": ([TOWN, D], F32), "obuf": ([TOWN, D], BF16),
                   "qfm_d": ([D, TEXT], BF16), "kfm_d": ([D, TEXT], BF16),
                   "vfm_d": ([D, TEXT], BF16), "wfm_d": ([D, TEXT], BF16)}
        for name in dbg:
            shp, dt = shp_map[name]
            self.dbg_aps[name] = nc.dram_tensor("dbg_" + name, shp, dt, kind="ExternalOutput").ap()

    def t(self, fn): self.ops["t"].append(fn)
    def v(self, fn): self.ops["v"].append(fn)
    def s(self, fn): self.ops["s"].append(fn)
    def g(self, fn): self.ops["g"].append(fn)
    def y(self, fn): self.ops["y"].append(fn)

    def emit_setup(self):
        S = self
        for dst, src in [(S.identb, S.C_IDENT), (S.uinc, S.C_UINC), (S.ustrict, S.C_USTR),
                         (S.ones1x, S.C_ONES1), (S.onesc, S.C_ONESC), (S.maskrep, S.C_MASKR),
                         (S.iota8p, S.C_IOTAP), (S.iota8, S.C_IOTA8)]:
            def f(g, dst=dst, src=src):
                g.dma_start(out=dst[:], in_=src[:, :]).then_inc(S.qS.h, 16)
            S.g(f); S.qS.bump(16)

        def f(g):
            g.dma_start(out=S.rt_sb[:], in_=S.RtA.rearrange("(c p) e -> p c e", p=128)).then_inc(S.qS.h, 16)
            g.dma_start(out=S.convw[:, 0], in_=S.cqA.rearrange("(c p) k -> p c k", p=128)).then_inc(S.qS.h, 16)
            g.dma_start(out=S.convw[:, 1], in_=S.ckA.rearrange("(c p) k -> p c k", p=128)).then_inc(S.qS.h, 16)
            g.dma_start(out=S.convw[:, 2], in_=S.cvA.rearrange("(c p) k -> p c k", p=128)).then_inc(S.qS.h, 16)
            g.dma_start(out=S.wf2_sb[:], in_=S.Wf2A[:, :]).then_inc(S.qS.h, 16)
            g.dma_start(out=S.wg2_sb[:], in_=S.Wg2A[:, :]).then_inc(S.qS.h, 16)
            g.dma_start(out=S.wb_sb[:], in_=S.WbA.rearrange("(c p) e -> p c e", p=128)).then_inc(S.qS.h, 16)
            g.dma_start(out=S.wf1_sb[:], in_=S.Wf1A.rearrange("(c p) m -> p c m", p=128)).then_inc(S.qS.h, 16)
            g.dma_start(out=S.wg1_sb[:], in_=S.Wg1A.rearrange("(c p) m -> p c m", p=128)).then_inc(S.qS.h, 16)
            g.dma_start(out=S.nwrow[:], in_=S.normwA.rearrange("(o f) -> o f", o=1)).then_inc(S.qS.h, 16)
            g.dma_start(out=S.wv_sb[:], in_=S.wvalidA[:, :]).then_inc(S.qS.h, 16)
        S.g(f); S.qS.bump(16 * 11)

        def f(v):
            for ci, val in enumerate(S.CVALS):
                v.memset(S.cpool[:, ci:ci + 1], val)
            v.memset(S.sqbuf[0:1, :], 0.0).then_inc(S.cV.h, 1)
        S.v(f); zv = S.cV.bump()

        def f(g, zv=zv):
            g.wait_ge(S.cV.h, zv)
            g.dma_start(out=S.outbuf[TRASH:TRASH + 1, :], in_=S.sqbuf[0:1, :]).then_inc(S.qS.h, 16)
        S.g(f); S.qS.bump(16)
        self.setup_q = S.qS.n

        def f(t):
            t.wait_ge(S.qS.h, S.setup_q)
            t.matmul(S.PB[:, 0:128], S.ones1x[:, :], S.nwrow[:, :], start=True, stop=True)
            t.matmul(S.PB[:, 512:513], S.ones1x[:, :], S.wv_sb[:, :],
                     start=True, stop=True).then_inc(S.cT.h, 1)
        S.t(f); tv = S.cT.bump()

        def f(v, tv=tv):
            v.wait_ge(S.cT.h, tv)
            v.tensor_copy(S.nw128[:], S.PB[:, 0:128])
            v.tensor_copy(S.wv_col[:], S.PB[:, 512:513])
            v.drain()
            v.tensor_scalar(S.tcol[:], S.wv_col[:], -float(TRASH), float(TRASH), ALU.mult, ALU.add)
            v.engine_nop().then_inc(S.cV.h, 1)
        S.v(f); S.cV.bump()
        self.setup_v = S.cV.n
        self.setup_t = tv

    def emit_barrier(self, ctrs):
        S = self
        waits = [(c.h, c.n) for c in ctrs if c.n]
        gprev = S.cG.n

        def f(g, waits=waits, gprev=gprev):
            g.wait_ge(S.cG.h, gprev)
            for h, n in waits:
                g.wait_ge(h, n)
            g.memset(S.gdummy[:], 0.0).then_inc(S.cG.h, 1)
        S.g(f)
        return S.cG.bump()

    def emit_ln(self, x_src, T, src_wait=None):
        S = self
        nt = (T + 127) // 128
        ytm_store = [S.qYS[0].n, S.qYS[1].n]
        for i in range(nt):
            r0 = i * 128
            P = min(128, T - r0)
            b = i % 2
            xtile = S.xt[b]
            vprev = S.cV.n

            def f(y, xtile=xtile, r0=r0, P=P, b=b, vprev=vprev, src_wait=src_wait,
                  xsw=S.qXS[b].n):
                y.wait_ge(S.cV.h, vprev)
                y.wait_ge(S.qXS[b].h, xsw)
                for hh, nn in (src_wait or []):
                    y.wait_ge(hh, nn)
                y.dma_start(out=xtile[0:P, :], in_=x_src[r0:r0 + P, :]).then_inc(S.qXT[b].h, 16)
            S.y(f); xv = S.qXT[b].bump(16)

            def f(s, xtile=xtile, P=P, b=b, xv=xv, vprev=vprev):
                s.wait_ge(S.qXT[b].h, xv)
                s.wait_ge(S.cV.h, vprev)
                s.wait_ge(S.qS.h, S.setup_q)
                s.activation(S.sqbuf[0:P, :], xtile[0:P, :], AF.Square).then_inc(S.cS.h, 1)
            S.s(f); sv = S.cS.bump()

            def f(v, xtile=xtile, P=P, b=b, xv=xv, sv=sv):
                v.wait_ge(S.qXT[b].h, xv)
                v.reduce_sum(S.t_a[0:P, :], xtile[0:P, :], axis=AX.X)
                v.wait_ge(S.cS.h, sv)
                v.reduce_sum(S.t_b[0:P, :], S.sqbuf[0:P, :], axis=AX.X)
                v.drain()
                v.tensor_scalar(S.t_a[0:P, :], S.t_a[0:P, :], 1.0 / D, None, ALU.mult)
                v.tensor_scalar(S.t_b[0:P, :], S.t_b[0:P, :], 1.0 / D, None, ALU.mult)
                v.drain()
                v.tensor_tensor(S.t_mr[0:P, :], S.t_a[0:P, :], S.t_a[0:P, :], ALU.mult)
                v.drain()
                v.tensor_tensor(S.t_b[0:P, :], S.t_b[0:P, :], S.t_mr[0:P, :], ALU.subtract)
                v.engine_nop().then_inc(S.cV.h, 1)
            S.v(f); vv = S.cV.bump()

            def f(s, P=P, vv=vv):
                s.wait_ge(S.cV.h, vv)
                s.activation(S.t_rs[0:P, :], S.t_b[0:P, :], AF.Ln, bias=float(EPS))
                s.drain()
                s.activation(S.t_rs[0:P, :], S.t_rs[0:P, :], AF.Exp, scale=-0.5).then_inc(S.cS.h, 1)
            S.s(f); sv2 = S.cS.bump()

            def f(v, xtile=xtile, P=P, b=b, sv2=sv2, ytw=S.qSCp[b].n, ysw=ytm_store[b]):
                v.wait_ge(S.cS.h, sv2)
                v.wait_ge(S.qSCp[b].h, ytw)
                v.wait_ge(S.qYS[b].h, ysw)
                v.tensor_tensor(S.t_mr[0:P, :], S.t_a[0:P, :], S.t_rs[0:P, :], ALU.mult)
                v.drain()
                v.tensor_scalar(S.t_mr[0:P, :], S.t_mr[0:P, :], -1.0, None, ALU.mult)
                v.drain()
                v.tensor_scalar(S.ytm_t[b][0:P, :], xtile[0:P, :], S.t_rs[0:P, :], S.t_mr[0:P, :],
                                ALU.mult, ALU.add).then_inc(S.cV.h, 1)
            S.v(f); yv = S.cV.bump()

            def f(y, b=b, r0=r0, P=P, yv=yv):
                y.wait_ge(S.cV.h, yv)
                y.dma_start(out=S.y_dram[r0:r0 + P, :], in_=S.ytm_t[b][0:P, :]).then_inc(S.qYS[b].h, 16)
            S.y(f); ytm_store[b] = S.qYS[b].bump(16)
        return (S.qYS[0].n, S.qYS[1].n)

    def emit_yfm(self, T, ydone):
        S = self
        tprev = S.cT.n
        for c in range(16):
            def f(y, c=c, ydone=ydone, tprev=tprev):
                y.wait_ge(S.qYS[0].h, ydone[0])
                y.wait_ge(S.qYS[1].h, ydone[1])
                y.wait_ge(S.cT.h, tprev)
                y.dma_start(out=S.yfm[:, c, 0:T], in_=S.y_dram[0:T, c * 128:(c + 1) * 128],
                            transpose=True).then_inc(S.qF.h, 16)
            S.y(f); S.qF.bump(16)
        return S.qF.n

    def emit_routing(self, T, yfm_done, wv_apply):
        S = self
        nt = (T + 127) // 128
        for i in range(nt):
            r0 = i * 128
            P = min(128, T - r0)
            vprev = S.cV.n

            def f(t, i=i, r0=r0, P=P, yfm_done=yfm_done, vprev=vprev):
                t.wait_ge(S.qF.h, yfm_done)
                t.wait_ge(S.cV.h, vprev)
                for c in range(15):
                    t.matmul(S.PB[0:P, 1024:1032], S.yfm[:, c, r0:r0 + P], S.rt_sb[:, c, :],
                             start=(c == 0), stop=False)
                t.matmul(S.PB[0:P, 1024:1032], S.yfm[:, 15, r0:r0 + P], S.rt_sb[:, 15, :],
                         start=False, stop=True).then_inc(S.cT.h, 1)
            S.t(f); tv = S.cT.bump()

            def f(v, i=i, P=P, tv=tv):
                v.wait_ge(S.cT.h, tv)
                v.tensor_copy(S.lgt[0:P, i, :], S.PB[0:P, 1024:1032]).then_inc(S.cV.h, 1)
            S.v(f); S.cV.bump()

        def f(v):
            v.memset(S.carry[:], 0.0).then_inc(S.cV.h, 1)
        S.v(f); S.cV.bump()

        for i in range(nt):
            r0 = i * 128
            P = min(128, T - r0)
            lgdone = S.cV.n

            def f(v, i=i, P=P, lgdone=lgdone):
                L = S.lgt[0:P, i, :]
                v.wait_ge(S.cV.h, lgdone)
                v.max(S.r_top8[0:P, :], L)
                v.drain()
                v.tensor_tensor(S.r_t8[0:P, :], L, S.r_top8[0:P, 0:1].broadcast_to([P, 8]), ALU.is_equal)
                v.drain()
                v.scalar_tensor_tensor(S.r_t8[0:P, :], S.r_t8[0:P, :], -1000.0, S.iota8p[0:P, :],
                                       ALU.mult, ALU.add)
                v.drain()
                v.tensor_reduce(S.t_sel0[0:P, :], S.r_t8[0:P, :], axis=AX.X, op=ALU.min)
                v.drain()
                v.tensor_tensor(S.r_ohA[0:P, :], S.iota8[0:P, :],
                                S.t_sel0[0:P, :].broadcast_to([P, 8]), ALU.is_equal)
                v.drain()
                v.scalar_tensor_tensor(S.r_t8[0:P, :], S.r_ohA[0:P, :], -1e30, L, ALU.mult, ALU.add)
                v.drain()
                v.max(S.r_ohs[0:P, :], S.r_t8[0:P, :])
                v.drain()
                v.tensor_tensor(S.r_ohB[0:P, :], S.r_t8[0:P, :],
                                S.r_ohs[0:P, 0:1].broadcast_to([P, 8]), ALU.is_equal)
                v.drain()
                v.scalar_tensor_tensor(S.r_ohB[0:P, :], S.r_ohB[0:P, :], -1000.0, S.iota8p[0:P, :],
                                       ALU.mult, ALU.add)
                v.drain()
                v.tensor_reduce(S.t_sel1[0:P, :], S.r_ohB[0:P, :], axis=AX.X, op=ALU.min)
                v.drain()
                v.tensor_tensor(S.r_ohB[0:P, :], S.iota8[0:P, :],
                                S.t_sel1[0:P, :].broadcast_to([P, 8]), ALU.is_equal)
                v.tensor_tensor(S.t_a[0:P, :], S.r_ohs[0:P, 0:1], S.r_top8[0:P, 0:1], ALU.subtract)
                v.drain()
                v.tensor_tensor(S.r_ohs[0:P, :], S.r_ohA[0:P, :], S.r_ohB[0:P, :], ALU.add)
                v.engine_nop().then_inc(S.cV.h, 1)
            S.v(f); vv = S.cV.bump()

            def f(s, P=P, vv=vv):
                s.wait_ge(S.cV.h, vv)
                s.activation(S.t_a[0:P, :], S.t_a[0:P, :], AF.Exp).then_inc(S.cS.h, 1)
            S.s(f); sv = S.cS.bump()

            def f(v, P=P, sv=sv):
                v.wait_ge(S.cS.h, sv)
                v.tensor_scalar(S.t_a[0:P, :], S.t_a[0:P, :], 1.0, 1.0, ALU.mult, ALU.add)
                v.drain()
                v.reciprocal(S.t_cwA[0:P, :], S.t_a[0:P, :])
                v.drain()
                v.tensor_scalar(S.t_cwB[0:P, :], S.t_cwA[0:P, :], -1.0, 1.0, ALU.mult, ALU.add)
                v.engine_nop().then_inc(S.cV.h, 1)
            S.v(f); cwv_v = S.cV.bump()

            def f(t, i=i, P=P, cwv_v=cwv_v):
                t.wait_ge(S.cV.h, cwv_v)
                t.matmul(S.PB[0:P, 1536:1544], S.ustrict[0:P, 0:P], S.r_ohs[0:P, :], start=True, stop=False)
                t.matmul(S.PB[0:P, 1536:1544], S.ones1x[0:1, 0:P], S.carry[:, :], start=False, stop=True)
                t.matmul(S.PB[0:1, 1544:1552], S.onesc[0:P, 0:1], S.r_ohs[0:P, :],
                         start=True, stop=True).then_inc(S.cT.h, 1)
            S.t(f); tv2 = S.cT.bump()

            def f(v, i=i, P=P, tv2=tv2, wv_apply=wv_apply):
                v.wait_ge(S.cT.h, tv2)
                v.tensor_copy(S.r_t8[0:P, :], S.PB[0:P, 1536:1544])
                v.drain()
                v.tensor_tensor(S.carry[:, :], S.carry[:, :], S.PB[0:1, 1544:1552], ALU.add)
                v.drain()
                v.drain()
                v.tensor_tensor(S.r_top8[0:P, :], S.r_t8[0:P, :], S.r_ohA[0:P, :], ALU.mult)
                v.drain()
                v.tensor_reduce(S.t_pos0[0:P, :], S.r_top8[0:P, :], axis=AX.X, op=ALU.add)
                v.drain()
                v.tensor_tensor(S.r_top8[0:P, :], S.r_t8[0:P, :], S.r_ohB[0:P, :], ALU.mult)
                v.drain()
                v.tensor_reduce(S.t_pos1[0:P, :], S.r_top8[0:P, :], axis=AX.X, op=ALU.add)
                v.drain()
                for k, (sel, pos, vld, cw) in enumerate(
                        [(S.t_sel0, S.t_pos0, S.t_v0, S.t_cwA), (S.t_sel1, S.t_pos1, S.t_v1, S.t_cwB)]):
                    v.tensor_single_scalar(vld[0:P, :], pos[0:P, :], float(CAP), ALU.is_lt)
                    v.scalar_tensor_tensor(S.t_b[0:P, :], sel[0:P, :], float(CAP), pos[0:P, :],
                                           ALU.mult, ALU.add)
                    v.drain()
                    v.tensor_tensor(S.t_b[0:P, :], S.t_b[0:P, :], vld[0:P, :], ALU.mult)
                    v.tensor_scalar(S.t_mr[0:P, :], vld[0:P, :], -float(TRASH), float(TRASH),
                                    ALU.mult, ALU.add)
                    v.drain()
                    v.tensor_tensor(S.t_b[0:P, :], S.t_b[0:P, :], S.t_mr[0:P, :], ALU.add)
                    v.drain()
                    if wv_apply and i == 0:
                        v.tensor_tensor(S.t_b[0:WU, :], S.t_b[0:WU, :], S.wv_col[0:WU, :], ALU.mult)
                        v.drain()
                        v.tensor_tensor(S.t_b[0:WU, :], S.t_b[0:WU, :], S.tcol[0:WU, :], ALU.add)
                        v.drain()
                    v.tensor_copy(S.off_i[k][0:P, i:i + 1], S.t_b[0:P, :])
                    v.tensor_tensor(S.t_mr[0:P, :], cw[0:P, :], vld[0:P, :], ALU.mult)
                    v.drain()
                    v.tensor_scalar(S.cwvt[k][0:P, i:i + 1], S.t_mr[0:P, :], 0.5, None, ALU.mult)
                    v.drain()
                v.engine_nop().then_inc(S.cV.h, 1)
            S.v(f); S.cV.bump()
        return S.cV.n

    def emit_scatter(self, T, ln_done, rout_done):
        S = self
        nt = (T + 127) // 128
        for i in range(nt):
            r0 = i * 128
            P = min(128, T - r0)
            b = i % 2
            scprev = S.qSC.n

            def f(g, r0=r0, P=P, b=b, ln_done=ln_done, scprev=scprev):
                g.wait_ge(S.qYS[0].h, ln_done[0])
                g.wait_ge(S.qYS[1].h, ln_done[1])
                g.wait_ge(S.qSC.h, scprev)
                g.dma_start(out=S.ytm_t[b][0:P, :], in_=S.y_dram[r0:r0 + P, :]).then_inc(S.qYT[b].h, 16)
            S.g(f); ldv = S.qYT[b].bump(16)

            def f(g, i=i, P=P, b=b, ldv=ldv, rout_done=rout_done):
                g.wait_ge(S.qYT[b].h, ldv)
                g.wait_ge(S.cV.h, rout_done)
                g.indirect_dma_start(
                    out=S.xbuf[:, :], out_offset=IOX(S.off_i[0][0:P, i:i + 1], 0),
                    in_=S.ytm_t[b][0:P, :], in_offset=None).then_inc(S.qSC.h, 16)
                g.indirect_dma_start(
                    out=S.xbuf[:, :], out_offset=IOX(S.off_i[1][0:P, i:i + 1], 0),
                    in_=S.ytm_t[b][0:P, :], in_offset=None).then_inc(S.qSC.h, 16)
            S.g(f); S.qSC.bump(32)
        return S.qSC.n

    def emit_ffn(self, T, x_src, x_dst, use_cache, wv_apply, src_wait=None):
        S = self
        nt = (T + 127) // 128
        ntf = min(nt, 8)
        has64 = (nt == 9)

        ln_done = self.emit_ln(x_src, T, src_wait)
        yfm_done = self.emit_yfm(T, ln_done)

        cache_bar = 0
        if use_cache:
            cws = [(q.h, q.n) for q in (S.qCW + S.qCE + S.qCDW + S.qCED)]

            def f(g, cws=cws, cgp=S.cG.n):
                g.wait_ge(S.cG.h, cgp)
                for h, n in cws:
                    if n:
                        g.wait_ge(h, n)
                g.memset(S.gdummy[:], 0.0).then_inc(S.cG.h, 1)
            S.g(f); cache_bar = S.cG.bump()

        state = {}

        def gen_routing():
            for i in range(nt):
                r0 = i * 128
                P = min(128, T - r0)
                vprev = S.cV.n

                def f(t, i=i, r0=r0, P=P, yfm_done=yfm_done, vprev=vprev):
                    t.wait_ge(S.qF.h, yfm_done)
                    t.wait_ge(S.cV.h, vprev)
                    for c in range(15):
                        t.matmul(S.PB[0:P, 1024:1032], S.yfm[:, c, r0:r0 + P], S.rt_sb[:, c, :],
                                 start=(c == 0), stop=False)
                    t.matmul(S.PB[0:P, 1024:1032], S.yfm[:, 15, r0:r0 + P], S.rt_sb[:, 15, :],
                             start=False, stop=True).then_inc(S.cT.h, 1)
                S.t(f); tv = S.cT.bump()

                def f(v, i=i, P=P, tv=tv):
                    v.wait_ge(S.cT.h, tv)
                    v.tensor_copy(S.lgt[0:P, i, :], S.PB[0:P, 1024:1032]).then_inc(S.cV.h, 1)
                S.v(f); S.cV.bump()
                yield

            def f(v):
                v.memset(S.carry[:], 0.0).then_inc(S.cV.h, 1)
            S.v(f); S.cV.bump()

            ytm_rel = [S.qSCp[0].n, S.qSCp[1].n]
            for i in range(nt):
                r0 = i * 128
                P = min(128, T - r0)
                lgdone = S.cV.n

                def f(v, i=i, P=P, lgdone=lgdone):
                    L = S.lgt[0:P, i, :]
                    v.wait_ge(S.cV.h, lgdone)
                    v.max(S.r_top8[0:P, :], L)
                    v.drain()
                    v.tensor_tensor(S.r_t8[0:P, :], L, S.r_top8[0:P, 0:1].broadcast_to([P, 8]), ALU.is_equal)
                    v.drain()
                    v.scalar_tensor_tensor(S.r_t8[0:P, :], S.r_t8[0:P, :], -1000.0, S.iota8p[0:P, :],
                                           ALU.mult, ALU.add)
                    v.drain()
                    v.tensor_reduce(S.t_sel0[0:P, :], S.r_t8[0:P, :], axis=AX.X, op=ALU.min)
                    v.drain()
                    v.tensor_tensor(S.r_ohA[0:P, :], S.iota8[0:P, :],
                                    S.t_sel0[0:P, :].broadcast_to([P, 8]), ALU.is_equal)
                    v.drain()
                    v.scalar_tensor_tensor(S.r_t8[0:P, :], S.r_ohA[0:P, :], -1e30, L, ALU.mult, ALU.add)
                    v.drain()
                    v.max(S.r_ohs[0:P, :], S.r_t8[0:P, :])
                    v.drain()
                    v.tensor_tensor(S.r_ohB[0:P, :], S.r_t8[0:P, :],
                                    S.r_ohs[0:P, 0:1].broadcast_to([P, 8]), ALU.is_equal)
                    v.drain()
                    v.scalar_tensor_tensor(S.r_ohB[0:P, :], S.r_ohB[0:P, :], -1000.0, S.iota8p[0:P, :],
                                           ALU.mult, ALU.add)
                    v.drain()
                    v.tensor_reduce(S.t_sel1[0:P, :], S.r_ohB[0:P, :], axis=AX.X, op=ALU.min)
                    v.drain()
                    v.tensor_tensor(S.r_ohB[0:P, :], S.iota8[0:P, :],
                                    S.t_sel1[0:P, :].broadcast_to([P, 8]), ALU.is_equal)
                    v.tensor_tensor(S.t_a[0:P, :], S.r_ohs[0:P, 0:1], S.r_top8[0:P, 0:1], ALU.subtract)
                    v.drain()
                    v.tensor_tensor(S.r_ohs[0:P, :], S.r_ohA[0:P, :], S.r_ohB[0:P, :], ALU.add)
                    v.engine_nop().then_inc(S.cV.h, 1)
                S.v(f); vv = S.cV.bump()

                def f(s, P=P, vv=vv):
                    s.wait_ge(S.cV.h, vv)
                    s.activation(S.t_a[0:P, :], S.t_a[0:P, :], AF.Exp).then_inc(S.cS.h, 1)
                S.s(f); sv = S.cS.bump()

                def f(v, P=P, sv=sv):
                    v.wait_ge(S.cS.h, sv)
                    v.tensor_scalar(S.t_a[0:P, :], S.t_a[0:P, :], 1.0, 1.0, ALU.mult, ALU.add)
                    v.drain()
                    v.reciprocal(S.t_cwA[0:P, :], S.t_a[0:P, :])
                    v.drain()
                    v.tensor_scalar(S.t_cwB[0:P, :], S.t_cwA[0:P, :], -1.0, 1.0, ALU.mult, ALU.add)
                    v.engine_nop().then_inc(S.cV.h, 1)
                S.v(f); cwv_v = S.cV.bump()

                def f(t, i=i, P=P, cwv_v=cwv_v):
                    t.wait_ge(S.cV.h, cwv_v)
                    t.matmul(S.PB[0:P, 1536:1544], S.ustrict[0:P, 0:P], S.r_ohs[0:P, :],
                             start=True, stop=False)
                    t.matmul(S.PB[0:P, 1536:1544], S.ones1x[0:1, 0:P], S.carry[:, :],
                             start=False, stop=True)
                    t.matmul(S.PB[0:1, 1544:1552], S.onesc[0:P, 0:1], S.r_ohs[0:P, :],
                             start=True, stop=True).then_inc(S.cT.h, 1)
                S.t(f); tv2 = S.cT.bump()

                def f(v, i=i, P=P, tv2=tv2, wv_apply=wv_apply):
                    v.wait_ge(S.cT.h, tv2)
                    v.tensor_copy(S.r_t8[0:P, :], S.PB[0:P, 1536:1544])
                    v.drain()
                    v.tensor_tensor(S.carry[:, :], S.carry[:, :], S.PB[0:1, 1544:1552], ALU.add)
                    v.drain()
                    v.tensor_tensor(S.r_top8[0:P, :], S.r_t8[0:P, :], S.r_ohA[0:P, :], ALU.mult)
                    v.drain()
                    v.tensor_reduce(S.t_pos0[0:P, :], S.r_top8[0:P, :], axis=AX.X, op=ALU.add)
                    v.drain()
                    v.tensor_tensor(S.r_top8[0:P, :], S.r_t8[0:P, :], S.r_ohB[0:P, :], ALU.mult)
                    v.drain()
                    v.tensor_reduce(S.t_pos1[0:P, :], S.r_top8[0:P, :], axis=AX.X, op=ALU.add)
                    v.drain()
                    for k, (sel, pos, vld, cw) in enumerate(
                            [(S.t_sel0, S.t_pos0, S.t_v0, S.t_cwA), (S.t_sel1, S.t_pos1, S.t_v1, S.t_cwB)]):
                        v.tensor_single_scalar(vld[0:P, :], pos[0:P, :], float(CAP), ALU.is_lt)
                        v.scalar_tensor_tensor(S.t_b[0:P, :], sel[0:P, :], float(CAP), pos[0:P, :],
                                               ALU.mult, ALU.add)
                        v.drain()
                        v.tensor_tensor(S.t_b[0:P, :], S.t_b[0:P, :], vld[0:P, :], ALU.mult)
                        v.tensor_scalar(S.t_mr[0:P, :], vld[0:P, :], -float(TRASH), float(TRASH),
                                        ALU.mult, ALU.add)
                        v.drain()
                        v.tensor_tensor(S.t_b[0:P, :], S.t_b[0:P, :], S.t_mr[0:P, :], ALU.add)
                        v.drain()
                        if wv_apply and i == 0:
                            v.tensor_tensor(S.t_b[0:WU, :], S.t_b[0:WU, :], S.wv_col[0:WU, :], ALU.mult)
                            v.drain()
                            v.tensor_tensor(S.t_b[0:WU, :], S.t_b[0:WU, :], S.tcol[0:WU, :], ALU.add)
                            v.drain()
                        v.tensor_copy(S.off_i[k][0:P, i:i + 1], S.t_b[0:P, :])
                        v.tensor_tensor(S.t_mr[0:P, :], cw[0:P, :], vld[0:P, :], ALU.mult)
                        v.drain()
                        v.tensor_scalar(S.cwvt[k][0:P, i:i + 1], S.t_mr[0:P, :], 0.5, None, ALU.mult)
                        v.drain()
                    v.engine_nop().then_inc(S.cV.h, 1)
                S.v(f); rd = S.cV.bump()

                b = i % 2

                def f(g, r0=r0, P=P, b=b, ln_done=ln_done, yrel=ytm_rel[b]):
                    g.wait_ge(S.qYS[0].h, ln_done[0])
                    g.wait_ge(S.qYS[1].h, ln_done[1])
                    g.wait_ge(S.qSCp[b].h, yrel)
                    g.dma_start(out=S.ytm_t[b][0:P, :], in_=S.y_dram[r0:r0 + P, :]).then_inc(S.qYT[b].h, 16)
                S.g(f); ldv = S.qYT[b].bump(16)

                def f(g, i=i, P=P, b=b, ldv=ldv, rd=rd):
                    g.wait_ge(S.qYT[b].h, ldv)
                    g.wait_ge(S.cV.h, rd)
                    g.indirect_dma_start(
                        out=S.xbuf[:, :], out_offset=IOX(S.off_i[0][0:P, i:i + 1], 0),
                        in_=S.ytm_t[b][0:P, :], in_offset=None).then_inc(S.qSCp[b].h, 16)
                    g.indirect_dma_start(
                        out=S.xbuf[:, :], out_offset=IOX(S.off_i[1][0:P, i:i + 1], 0),
                        in_=S.ytm_t[b][0:P, :], in_offset=None).then_inc(S.qSCp[b].h, 16)
                S.g(f); ytm_rel[b] = S.qSCp[b].bump(32)
                yield
            state["scat_done"] = [(S.qSCp[0].h, S.qSCp[0].n), (S.qSCp[1].h, S.qSCp[1].n)]

        def gen_shared():
            HB = FJ // 2
            xs_store = state["xs_store"]
            slab_mm = [S.cT.n, S.cT.n]
            slab_cw = [S.qCW[0].n, S.qCW[1].n]
            t8_ev = 0
            for hf in range(2):
                ev_par = [[(S.cV.h, S.cV.n), (S.cS.h, S.cS.n)] for _ in range(2)]
                ev64 = [(S.cV.h, t8_ev)] if t8_ev else []
                for phase in range(2):
                    for k2 in range(11):
                        b = k2 % 2

                        def f(g, b=b, phase=phase, hf=hf, k2=k2, sfm=slab_mm[b], sfc=slab_cw[b],
                              cb=cache_bar):
                            g.wait_ge(S.cT.h, sfm)
                            g.wait_ge(S.qCW[b].h, sfc)
                            if cb:
                                g.wait_ge(S.cG.h, cb)
                            if use_cache:
                                g.dma_start(out=S.wslab[b][:],
                                            in_=S.SugB[phase, hf, k2]).then_inc(S.qWS[b].h, 16)
                            else:
                                g.dma_start(out=S.wslab[b][:],
                                            in_=S.SugA[phase, hf, k2]).then_inc(S.qWS[b].h, 16)
                        S.g(f); wld = S.qWS[b].bump(16)
                        if not use_cache:
                            def f(y, b=b, phase=phase, hf=hf, k2=k2, wld=wld):
                                y.wait_ge(S.qWS[b].h, wld)
                                y.dma_start(out=S.SugB[phase, hf, k2],
                                            in_=S.wslab[b][:]).then_inc(S.qCW[b].h, 16)
                            S.y(f); slab_cw[b] = S.qCW[b].bump(16)

                        for sub in range(2):
                            j = k2 * 2 + sub
                            par = j % 2

                            def f(t, b=b, sub=sub, par=par, wld=wld, epw=list(ev_par[par]),
                                  e64=list(ev64), has64=has64):
                                t.wait_ge(S.qWS[b].h, wld)
                                for hh, nn in epw:
                                    t.wait_ge(hh, nn)
                                for c in range(16):
                                    last = (c == 15)
                                    wsl = S.wslab[b][:, c, sub * 128:(sub + 1) * 128]
                                    t.matmul(S.PA[:, par * 1024: par * 1024 + 512], wsl,
                                             S.yfm[:, c, 0:512], start=(c == 0), stop=last)
                                    mm2 = t.matmul(S.PA[:, par * 1024 + 512: par * 1024 + 1024], wsl,
                                                   S.yfm[:, c, 512:1024], start=(c == 0), stop=last)
                                if has64:
                                    for hh, nn in e64:
                                        t.wait_ge(hh, nn)
                                    for c in range(16):
                                        mm2 = t.matmul(S.PB[:, 0:64], S.wslab[b][:, c, sub * 128:(sub + 1) * 128],
                                                       S.yfm[:, c, 1024:1088], start=(c == 0), stop=(c == 15))
                                mm2.then_inc(S.cT.h, 1)
                            S.t(f); mmv = S.cT.bump()
                            slab_mm[b] = mmv

                            if phase == 0:
                                def f(s, j=j, par=par, mmv=mmv, has64=has64):
                                    s.wait_ge(S.cT.h, mmv)
                                    s.activation(S.hsh[:, j, 0:512], S.PA[:, par * 1024: par * 1024 + 512], AF.Silu)
                                    act2 = s.activation(S.hsh[:, j, 512:1024],
                                                        S.PA[:, par * 1024 + 512: par * 1024 + 1024], AF.Silu)
                                    if has64:
                                        act2 = s.activation(S.hsh[:, j, 1024:1088], S.PB[:, 0:64], AF.Silu)
                                    act2.then_inc(S.cS.h, 1)
                                S.s(f); ev = (S.cS.h, S.cS.bump())
                            else:
                                def f(v, j=j, par=par, mmv=mmv, has64=has64):
                                    v.wait_ge(S.cT.h, mmv)
                                    v.tensor_tensor(S.hsh[:, j, 0:512], S.PA[:, par * 1024: par * 1024 + 512],
                                                    S.hsh[:, j, 0:512], ALU.mult)
                                    tt2 = v.tensor_tensor(S.hsh[:, j, 512:1024],
                                                          S.PA[:, par * 1024 + 512: par * 1024 + 1024],
                                                          S.hsh[:, j, 512:1024], ALU.mult)
                                    if has64:
                                        tt2 = v.tensor_tensor(S.hsh[:, j, 1024:1088], S.PB[:, 0:64],
                                                              S.hsh[:, j, 1024:1088], ALU.mult)
                                    tt2.then_inc(S.cV.h, 1)
                                S.v(f); ev = (S.cV.h, S.cV.bump())
                            ev_par[par] = [ev]
                            ev64 = [ev]
                        yield "up"
                up_done = S.cV.n
                up_done_s = S.cS.n

                dslab_mm = [S.cT.n, S.cT.n]
                dslab_cw = [S.qCDW[0].n, S.qCDW[1].n]
                ev_pa = [(S.cV.h, S.cV.n), (S.cS.h, S.cS.n)]
                batches = [list(range(0, min(4, ntf)))] + ([list(range(4, ntf))] if ntf > 4 else [])
                chunk_sets = [list(range(k, min(k + 4, 22))) for k in range(0, 22, 4)]
                for dg in range(4):
                    for bi, tts in enumerate(batches):
                        with_t8 = has64 and bi == len(batches) - 1
                        set_mm = []
                        for si, cs in enumerate(chunk_sets):
                            b = si % 2
                            use_blk = use_cache or bi > 0

                            def f(g, b=b, cs=cs, dg=dg, hf=hf, sfm=dslab_mm[b], sfc=dslab_cw[b],
                                  use_blk=use_blk, up_done=up_done, cb=cache_bar):
                                g.wait_ge(S.cT.h, sfm)
                                g.wait_ge(S.qCDW[b].h, sfc)
                                g.wait_ge(S.cV.h, up_done)
                                if cb:
                                    g.wait_ge(S.cG.h, cb)
                                if use_blk:
                                    g.dma_start(out=S.dslab[b][:, 0:len(cs), :],
                                                in_=S.SdnB[hf, dg][:, cs[0]:cs[0] + len(cs), :]
                                                ).then_inc(S.qDS[b].h, 16)
                                else:
                                    g.dma_start(out=S.dslab[b][:, 0:len(cs), :],
                                                in_=S.SdnA[hf, dg][:, cs[0]:cs[0] + len(cs), :]
                                                ).then_inc(S.qDS[b].h, 16)
                            S.g(f); wld = S.qDS[b].bump(16)
                            if not use_cache and bi == 0:
                                def f(y, b=b, cs=cs, dg=dg, hf=hf, wld=wld):
                                    y.wait_ge(S.qDS[b].h, wld)
                                    y.dma_start(out=S.SdnB[hf, dg][:, cs[0]:cs[0] + len(cs), :],
                                                in_=S.dslab[b][:, 0:len(cs), :]).then_inc(S.qCDW[b].h, 16)
                                S.y(f); dslab_cw[b] = S.qCDW[b].bump(16)

                            def f(t, b=b, cs=cs, si=si, wld=wld, tts=tts, epa=list(ev_pa),
                                  ups=up_done_s, nsets=len(chunk_sets), with_t8=with_t8):
                                t.wait_ge(S.qDS[b].h, wld)
                                if si == 0:
                                    t.wait_ge(S.cS.h, ups)
                                    for hh, nn in epa:
                                        t.wait_ge(hh, nn)
                                for ci, fc in enumerate(cs):
                                    for ti, tt in enumerate(tts):
                                        ps = S.PA[:, ti * 512:(ti + 1) * 512]
                                        mm = t.matmul(ps, S.hsh[:, fc, tt * 128:(tt + 1) * 128],
                                                      S.dslab[b][:, ci, :],
                                                      start=(si == 0 and ci == 0),
                                                      stop=(si == nsets - 1 and ci == len(cs) - 1),
                                                      skip_group_check=True)
                                    if with_t8:
                                        mm = t.matmul(S.PB[0:64, 0:512], S.hsh[:, fc, 1024:1088],
                                                      S.dslab[b][:, ci, :],
                                                      start=(si == 0 and ci == 0),
                                                      stop=(si == nsets - 1 and ci == len(cs) - 1),
                                                      skip_group_check=True)
                                mm.then_inc(S.cT.h, 1)
                            S.t(f); setv = S.cT.bump()
                            set_mm.append(setv)
                            dslab_mm[b] = setv
                        mmv = set_mm[-1]

                        base_src = x_src if hf == 0 else x_dst
                        bsw = (src_wait or []) if hf == 0 else state["x_written"]
                        ev_pa = []
                        rtl = list(tts) + ([8] if with_t8 else [])
                        for ti, tt in enumerate(rtl):
                            is8 = tt == 8
                            P = 64 if is8 else 128
                            xb = tt % 2
                            vprev = S.cV.n

                            def f(y, tt=tt, P=P, xb=xb, dg=dg, base_src=base_src, vprev=vprev,
                                  bsw=bsw, xsw=xs_store[xb]):
                                y.wait_ge(S.cV.h, vprev)
                                y.wait_ge(S.qXS[xb].h, xsw)
                                for hh, nn in bsw:
                                    y.wait_ge(hh, nn)
                                y.dma_start(out=S.xt[xb][0:P, 0:512],
                                            in_=base_src[tt * 128:tt * 128 + P, dg * 512:(dg + 1) * 512]
                                            ).then_inc(S.qXT[xb].h, 16)
                            S.y(f); xld = S.qXT[xb].bump(16)

                            def f(v, ti=ti, P=P, is8=is8, xb=xb, mmv=mmv, xld=xld):
                                v.wait_ge(S.cT.h, mmv)
                                v.wait_ge(S.qXT[xb].h, xld)
                                ps = S.PB[0:64, 0:512] if is8 else S.PA[0:P, ti * 512:(ti + 1) * 512]
                                v.scalar_tensor_tensor(S.xt[xb][0:P, 0:512], ps,
                                                       0.5, S.xt[xb][0:P, 0:512],
                                                       ALU.mult, ALU.add).then_inc(S.cV.h, 1)
                            S.v(f); vv = S.cV.bump()
                            ev_pa.append((S.cV.h, vv))
                            if is8:
                                t8_ev = vv

                            def f(y, tt=tt, P=P, xb=xb, dg=dg, vv=vv):
                                y.wait_ge(S.cV.h, vv)
                                y.dma_start(out=x_dst[tt * 128:tt * 128 + P, dg * 512:(dg + 1) * 512],
                                            in_=S.xt[xb][0:P, 0:512]).then_inc(S.qXS[xb].h, 16)
                            S.y(f); xs_store[xb] = S.qXS[xb].bump(16)
                        ev_pa = [ev_pa[-1]]
                        yield "down"
                state["x_written"] = [(S.qXS[0].h, S.qXS[0].n), (S.qXS[1].h, S.qXS[1].n)]

        def gen_experts():
            xg_mm = S.cT.n
            eslab_mm = [S.cT.n, S.cT.n]
            edslab_mm = [S.cT.n, S.cT.n]
            estg_free = S.qOBs[0].n
            ev_sub = [[], []]
            ev_d = []
            scd = state["scat_done"]
            xgld = S.qXG[0].n
            for e in range(E):
                def f(y, e=e, scd=list(scd), xf=xg_mm):
                    for hh, nn in scd:
                        y.wait_ge(hh, nn)
                    y.wait_ge(S.cT.h, xf)
                    for c in range(16):
                        y.dma_start(out=S.xg[0][:, c, :],
                                    in_=S.xbuf[e * CAP:(e + 1) * CAP, c * 128:(c + 1) * 128],
                                    transpose=True).then_inc(S.qXG[0].h, 16)
                S.y(f); xgld = S.qXG[0].bump(16 * 16)
                yield

                for sl in range(11):
                    eb = sl % 2

                    def f(g, e=e, sl=sl, eb=eb, sfm=eslab_mm[eb], sfc=S.qCE[eb].n,
                          scw=(list(scd) if e == 0 and sl < 2 else [])):
                        g.wait_ge(S.cT.h, sfm)
                        if sfc:
                            g.wait_ge(S.qCE[eb].h, sfc)
                        for hh, nn in scw:
                            g.wait_ge(hh, nn)
                        if use_cache:
                            g.dma_start(out=S.eslab[eb][:],
                                        in_=S.WugB[e, sl]).then_inc(S.qES[eb].h, 16)
                        else:
                            g.dma_start(out=S.eslab[eb][:],
                                        in_=S.WugA[e, sl]).then_inc(S.qES[eb].h, 16)
                    S.g(f); wld = S.qES[eb].bump(16)
                    if not use_cache:
                        def f(y, e=e, sl=sl, eb=eb, wld=wld):
                            y.wait_ge(S.qES[eb].h, wld)
                            y.dma_start(out=S.WugB[e, sl],
                                        in_=S.eslab[eb][:]).then_inc(S.qCE[eb].h, 16)
                        S.y(f); S.qCE[eb].bump(16)

                    for jj in range(2):
                        j = sl * 2 + jj
                        is_gate = j < 11
                        pcol = 512 if j % 2 == 0 else 1024

                        def f(t, eb=eb, jj=jj, pcol=pcol, wld=wld, xgld=xgld,
                              epw=list(ev_sub[j % 2])):
                            t.wait_ge(S.qES[eb].h, wld)
                            t.wait_ge(S.qXG[0].h, xgld)
                            for hh, nn in epw:
                                t.wait_ge(hh, nn)
                            for c in range(16):
                                mm = t.matmul(S.PB[:, pcol:pcol + CAP],
                                              S.eslab[eb][:, c, jj * 128:(jj + 1) * 128],
                                              S.xg[0][:, c, :], start=(c == 0), stop=(c == 15))
                            mm.then_inc(S.cT.h, 1)
                        S.t(f); mmv = S.cT.bump()
                        xg_mm = mmv
                        eslab_mm[eb] = mmv

                        if is_gate:
                            def f(s, j=j, pcol=pcol, mmv=mmv):
                                s.wait_ge(S.cT.h, mmv)
                                s.activation(S.he[:, j, :], S.PB[:, pcol:pcol + CAP],
                                             AF.Silu).then_inc(S.cS.h, 1)
                            S.s(f); ev_sub[j % 2] = [(S.cS.h, S.cS.bump())]
                        else:
                            def f(v, j=j, pcol=pcol, mmv=mmv):
                                v.wait_ge(S.cT.h, mmv)
                                v.tensor_tensor(S.he[:, j - 11, :], S.PB[:, pcol:pcol + CAP],
                                                S.he[:, j - 11, :], ALU.mult).then_inc(S.cV.h, 1)
                            S.v(f); ev_sub[j % 2] = [(S.cV.h, S.cV.bump())]
                    if sl % 2 == 1 or sl == 10:
                        yield
                he_done = S.cV.n
                he_done_s = S.cS.n

                for dg in range(4):
                    db = dg % 2

                    def f(g, e=e, dg=dg, db=db, sfm=edslab_mm[db], sfc=S.qCED[db].n,
                          scw=(list(scd) if e == 0 and dg < 2 else [])):
                        g.wait_ge(S.cT.h, sfm)
                        if sfc:
                            g.wait_ge(S.qCED[db].h, sfc)
                        for hh, nn in scw:
                            g.wait_ge(hh, nn)
                        if use_cache:
                            g.dma_start(out=S.edslab[db][:],
                                        in_=S.WdnB[e, dg]).then_inc(S.qED[db].h, 16)
                        else:
                            g.dma_start(out=S.edslab[db][:],
                                        in_=S.WdnA[e, dg]).then_inc(S.qED[db].h, 16)
                    S.g(f); wld = S.qED[db].bump(16)
                    if not use_cache:
                        def f(y, e=e, dg=dg, db=db, wld=wld):
                            y.wait_ge(S.qED[db].h, wld)
                            y.dma_start(out=S.WdnB[e, dg],
                                        in_=S.edslab[db][:]).then_inc(S.qCED[db].h, 16)
                        S.y(f); S.qCED[db].bump(16)

                    for st in range(3):
                        SS = 128 if st < 2 else 96

                        def f(t, st=st, SS=SS, db=db, wld=wld, epw=list(ev_d), hs=he_done_s,
                              hv=he_done):
                            t.wait_ge(S.qED[db].h, wld)
                            t.wait_ge(S.cS.h, hs)
                            t.wait_ge(S.cV.h, hv)
                            for hh, nn in epw:
                                t.wait_ge(hh, nn)
                            for fc in range(11):
                                mm = t.matmul(S.PB[0:SS, 1536:2048],
                                              S.he[:, fc, st * 128: st * 128 + SS],
                                              S.edslab[db][:, fc, :], start=(fc == 0), stop=(fc == 10))
                            mm.then_inc(S.cT.h, 1)
                        S.t(f); mmv = S.cT.bump()
                        edslab_mm[db] = mmv

                        def f(v, st=st, SS=SS, mmv=mmv, ef=estg_free):
                            v.wait_ge(S.cT.h, mmv)
                            v.wait_ge(S.qOBs[0].h, ef)
                            v.tensor_copy(S.estg[0][0:SS, :],
                                          S.PB[0:SS, 1536:2048]).then_inc(S.cV.h, 1)
                        S.v(f); ev_d = [(S.cV.h, S.cV.bump())]
                        evv = ev_d[0][1]

                        def f(y, e=e, st=st, SS=SS, dg=dg, evv=evv):
                            y.wait_ge(S.cV.h, evv)
                            y.dma_start(out=S.outbuf[e * CAP + st * 128: e * CAP + st * 128 + SS,
                                                     dg * 512:(dg + 1) * 512],
                                        in_=S.estg[0][0:SS, :]).then_inc(S.qOBs[0].h, 16)
                        S.y(f); estg_free = S.qOBs[0].bump(16)
                    yield
            state["experts_done"] = [(S.qOBs[0].h, S.qOBs[0].n)]

        state["xs_store"] = [S.qXS[0].n, S.qXS[1].n]
        gr = gen_routing()
        gs = gen_shared()
        ge = gen_experts()
        alive_r, alive_s, alive_e = True, True, False
        started_e = False
        while alive_r or alive_s or started_e:
            if alive_r:
                try:
                    next(gr); 
                    try:
                        next(gr)
                    except StopIteration:
                        alive_r = False
                except StopIteration:
                    alive_r = False
                if not alive_r:
                    started_e = True
            if alive_s:
                try:
                    next(gs)
                except StopIteration:
                    alive_s = False
            if started_e:
                try:
                    next(ge)
                except StopIteration:
                    started_e = False
            if not (alive_r or alive_s or started_e):
                break

        xs_store = state["xs_store"]
        x_written = state["x_written"]
        experts_done = state["experts_done"]
        for i in range(nt):
            r0 = i * 128
            P = min(128, T - r0)
            xb = i % 2
            gprev = S.cV.n

            def f(g, i=i, P=P, experts_done=experts_done, gprev=gprev,
                  cwf=[(q.h, q.n) for q in (S.qCE + S.qCED)]):
                for hh, nn in experts_done:
                    g.wait_ge(hh, nn)
                for hh, nn in cwf:
                    if nn:
                        g.wait_ge(hh, nn)
                g.wait_ge(S.cV.h, gprev)
                g.indirect_dma_start(
                    out=S.g0t[0:P, :], out_offset=None,
                    in_=S.outbuf[:, :], in_offset=IOX(S.off_i[0][0:P, i:i + 1], 0)).then_inc(S.qG.h, 16)
                g.indirect_dma_start(
                    out=S.g1t[0:P, :], out_offset=None,
                    in_=S.outbuf[:, :], in_offset=IOX(S.off_i[1][0:P, i:i + 1], 0)).then_inc(S.qG.h, 16)
            S.g(f); gld = S.qG.bump(32)

            def f(y, r0=r0, P=P, xb=xb, gprev=gprev, xww=list(x_written) + list(experts_done),
                  xsw=xs_store[xb]):
                y.wait_ge(S.cV.h, gprev)
                y.wait_ge(S.qXS[xb].h, xsw)
                for hh, nn in xww:
                    y.wait_ge(hh, nn)
                y.dma_start(out=S.xt[xb][0:P, :], in_=x_dst[r0:r0 + P, :]).then_inc(S.qXT[xb].h, 16)
            S.y(f); xld = S.qXT[xb].bump(16)

            def f(v, i=i, P=P, xb=xb, gld=gld, xld=xld):
                v.wait_ge(S.qG.h, gld)
                v.wait_ge(S.qXT[xb].h, xld)
                v.scalar_tensor_tensor(S.xt[xb][0:P, :], S.g0t[0:P, :], S.cwvt[0][0:P, i:i + 1],
                                       S.xt[xb][0:P, :], ALU.mult, ALU.add)
                v.drain()
                v.scalar_tensor_tensor(S.xt[xb][0:P, :], S.g1t[0:P, :], S.cwvt[1][0:P, i:i + 1],
                                       S.xt[xb][0:P, :], ALU.mult, ALU.add).then_inc(S.cV.h, 1)
            S.v(f); vv = S.cV.bump()

            def f(y, r0=r0, P=P, xb=xb, vv=vv):
                y.wait_ge(S.cV.h, vv)
                y.dma_start(out=x_dst[r0:r0 + P, :], in_=S.xt[xb][0:P, :]).then_inc(S.qXS[xb].h, 16)
            S.y(f); xs_store[xb] = S.qXS[xb].bump(16)
        return [(S.qXS[0].h, S.qXS[0].n), (S.qXS[1].h, S.qXS[1].n)]

    def emit_attn(self, x1_wait):
        S = self
        T = TEXT
        bar0 = self.emit_barrier(self.all_sems())
        x1_wait = list(x1_wait) + [(S.cG.h, bar0)]
        ln_done = self.emit_ln(S.x1_dram, T, x1_wait)
        yfm_done = self.emit_yfm(T, ln_done)

        b3bar = self.emit_barrier([S.qCW[0], S.qCW[1], S.qCDW[0], S.qCDW[1],
                                   S.qCE[0], S.qCE[1], S.qCED[0], S.qCED[1],
                                   S.qDS[0], S.qDS[1], S.qYT[0], S.qYT[1]])
        def f(v, b3bar=b3bar):
            v.wait_ge(S.cG.h, b3bar)
            v.memset(S.conv_z[0][:, 0:3], 0.0)
            v.memset(S.conv_z[1][:, 0:3], 0.0).then_inc(S.cV.h, 1)
        S.v(f); padz = S.cV.bump()

        projs = [(S.WqA, S.qfm_d, 0), (S.WkA, S.kfm_d, 1), (S.WvA, S.vfm_d, 2), (S.WwA, S.wfm_d, 2)]
        slab_mm = [S.cT.n, S.cT.n]
        ev_par = [[], []]
        for wi, (Wsrc, dst_dram, cslot) in enumerate(projs):
            for k4 in range(4):
                gi4 = wi * 4 + k4
                b = gi4 % 2

                def f(g, Wsrc=Wsrc, k4=k4, b=b, sfm=slab_mm[b]):
                    g.wait_ge(S.cT.h, sfm)
                    g.dma_start(out=S.wslab4[b][:],
                                in_=Wsrc[k4]).then_inc(S.qWS[b].h, 16)
                S.g(f); wld = S.qWS[b].bump(16)

                for sub in range(4):
                    mt = k4 * 4 + sub
                    gi = wi * 16 + mt
                    par = gi % 2
                    ob = gi % 2

                    def f(t, b=b, sub=sub, par=par, wld=wld, epw=list(ev_par[par]), yfm_done=yfm_done):
                        t.wait_ge(S.qWS[b].h, wld)
                        t.wait_ge(S.qF.h, yfm_done)
                        for hh, nn in epw:
                            t.wait_ge(hh, nn)
                        for c in range(16):
                            last = (c == 15)
                            wsl = S.wslab4[b][:, c, sub * 128:(sub + 1) * 128]
                            t.matmul(S.PA[:, par * 1024: par * 1024 + 512], wsl,
                                     S.yfm[:, c, 0:512], start=(c == 0), stop=last)
                            t.matmul(S.PA[:, par * 1024 + 512: par * 1024 + 1024], wsl,
                                     S.yfm[:, c, 512:1024], start=(c == 0), stop=last)
                            mm2 = t.matmul(S.PB[:, par * 512:par * 512 + 64], wsl,
                                           S.yfm[:, c, 1024:1088], start=(c == 0), stop=last)
                            if last:
                                mm2.then_inc(S.cT.h, 1)
                    S.t(f); mmv = S.cT.bump()
                    slab_mm[b] = mmv

                    def f(s, par=par, ob=ob, mmv=mmv, padz=padz, cvd=S.cV.n):
                        s.wait_ge(S.cT.h, mmv)
                        s.wait_ge(S.cV.h, cvd)
                        s.activation(S.conv_z[ob][:, 3:515], S.PA[:, par * 1024: par * 1024 + 512], AF.Copy)
                        s.activation(S.conv_z[ob][:, 515:1027], S.PA[:, par * 1024 + 512: par * 1024 + 1024],
                                     AF.Copy)
                        s.activation(S.conv_z[ob][:, 1027:1091], S.PB[:, par * 512:par * 512 + 64],
                                     AF.Copy).then_inc(S.cS.h, 1)
                    S.s(f); zev = S.cS.bump()
                    ev_par[par] = [(S.cS.h, zev)]

                    def f(v, mt=mt, cslot=cslot, ob=ob, zev=zev, padz=padz, kvw=S.qKV[ob].n):
                        v.wait_ge(S.cS.h, zev)
                        v.wait_ge(S.cV.h, padz)
                        z = S.conv_z[ob]
                        cw = S.convw
                        v.tensor_scalar(S.conv_a[:, 0:TEXT], z[:, 0:TEXT], cw[:, cslot, mt, 0:1], None, ALU.mult)
                        v.drain()
                        v.scalar_tensor_tensor(S.conv_a[:, 0:TEXT], z[:, 1:TEXT + 1], cw[:, cslot, mt, 1:2],
                                               S.conv_a[:, 0:TEXT], ALU.mult, ALU.add)
                        v.drain()
                        v.scalar_tensor_tensor(S.conv_a[:, 0:TEXT], z[:, 2:TEXT + 2], cw[:, cslot, mt, 2:3],
                                               S.conv_a[:, 0:TEXT], ALU.mult, ALU.add)
                        v.drain()
                        v.scalar_tensor_tensor(S.conv_a[:, 0:TEXT], z[:, 3:TEXT + 3], cw[:, cslot, mt, 3:4],
                                               S.conv_a[:, 0:TEXT], ALU.mult, ALU.add)
                        v.wait_ge(S.qKV[ob].h, kvw)
                        v.engine_nop().then_inc(S.cV.h, 1)
                    S.v(f); cvv = S.cV.bump()

                    def f(s, ob=ob, cvv=cvv):
                        s.wait_ge(S.cV.h, cvv)
                        s.activation(S.conv_o[ob][:, :], S.conv_a[:, 0:TEXT], AF.Silu).then_inc(S.cS.h, 1)
                    S.s(f); sil = S.cS.bump()

                    def f(y, dst_dram=dst_dram, mt=mt, ob=ob, sil=sil):
                        y.wait_ge(S.cS.h, sil)
                        y.dma_start(out=dst_dram[mt * 128:(mt + 1) * 128, :],
                                    in_=S.conv_o[ob][:, :]).then_inc(S.qKV[ob].h, 16)
                    S.y(f); S.qKV[ob].bump(16)
        kv_done = (S.qKV[0].n, S.qKV[1].n)

        evs, evv = S.cS.n, S.cV.n

        def f(t, evs=evs, evv=evv, yfm_done=yfm_done):
            t.wait_ge(S.cS.h, evs)
            t.wait_ge(S.cV.h, evv)
            t.wait_ge(S.qF.h, yfm_done)
            t.wait_ge(S.qS.h, S.setup_q)
            for c in range(16):
                last = (c == 15)
                t.matmul(S.PA[:, 0:512], S.wf1_sb[:, c, :], S.yfm[:, c, 0:512], start=(c == 0), stop=last)
                t.matmul(S.PA[:, 512:1024], S.wf1_sb[:, c, :], S.yfm[:, c, 512:1024], start=(c == 0), stop=last)
                t.matmul(S.PB[:, 0:64], S.wf1_sb[:, c, :], S.yfm[:, c, 1024:1088], start=(c == 0), stop=last)
                t.matmul(S.PA[:, 1024:1536], S.wg1_sb[:, c, :], S.yfm[:, c, 0:512], start=(c == 0), stop=last)
                t.matmul(S.PA[:, 1536:2048], S.wg1_sb[:, c, :], S.yfm[:, c, 512:1024], start=(c == 0), stop=last)
                mm = t.matmul(S.PB[:, 512:576], S.wg1_sb[:, c, :], S.yfm[:, c, 1024:1088],
                              start=(c == 0), stop=last)
            mm.then_inc(S.cT.h, 1)
        S.t(f); fgv = S.cT.bump()

        def f(v, fgv=fgv):
            v.wait_ge(S.cT.h, fgv)
            v.tensor_copy(S.f1_sb[:, 0:512], S.PA[:, 0:512])
            v.tensor_copy(S.f1_sb[:, 512:1024], S.PA[:, 512:1024])
            v.tensor_copy(S.f1_sb[:, 1024:1088], S.PB[:, 0:64])
            v.tensor_copy(S.g1_sb[:, 0:512], S.PA[:, 1024:1536])
            v.tensor_copy(S.g1_sb[:, 512:1024], S.PA[:, 1536:2048])
            v.tensor_copy(S.g1_sb[:, 1024:1088], S.PB[:, 512:576]).then_inc(S.cV.h, 1)
        S.v(f); fgev = S.cV.bump()

        for ci in range(9):
            t0 = 0 if ci == 0 else 64 + (ci - 1) * 128
            Tc = 64 if ci == 0 else 128

            def f(t, ci=ci, t0=t0, Tc=Tc, fgev=fgev, prev=S.cS.n, yfm_done=yfm_done):
                t.wait_ge(S.cV.h, fgev)
                t.wait_ge(S.cS.h, prev)
                t.wait_ge(S.qF.h, yfm_done)
                for c in range(15):
                    t.matmul(S.PB[0:Tc, 1024:1040], S.yfm[:, c, t0:t0 + Tc], S.wb_sb[:, c, :],
                             start=(c == 0), stop=False)
                t.matmul(S.PB[0:Tc, 1024:1040], S.yfm[:, 15, t0:t0 + Tc], S.wb_sb[:, 15, :],
                         start=False, stop=True).then_inc(S.cT.h, 1)
            S.t(f); bv = S.cT.bump()

            def f(s, ci=ci, Tc=Tc, bv=bv):
                s.wait_ge(S.cT.h, bv)
                s.activation(S.beta[0:Tc, ci, :], S.PB[0:Tc, 1024:1040], AF.Sigmoid).then_inc(S.cS.h, 1)
            S.s(f); S.cS.bump()

        sbar = self.emit_barrier(self.all_sems())

        def f(g, sbar=sbar):
            g.wait_ge(S.cG.h, sbar)
            g.dma_start(out=S.bg2row[:],
                        in_=S.bg2A.rearrange("(o f) -> o f", o=1)).then_inc(S.qS.h, 16)
        S.g(f); bg2ld = S.qS.bump(16)

        def f(v, sbar=sbar):
            v.wait_ge(S.cG.h, sbar)
            v.memset(S.Sk[:], 0.0)
            v.memset(S.Sv[:], 0.0)
            v.memset(S.Skp[:], 0.0)
            v.memset(S.Svp[:], 0.0)
            v.tensor_copy(S.nwfull[:], S.nw128[:, :].unsqueeze(1).broadcast_to([128, H, DV]))
            v.engine_nop().then_inc(S.cV.h, 1)
        S.v(f); sinit = S.cV.bump()

        def f(t, sinit=sinit, bg2ld=bg2ld):
            t.wait_ge(S.cV.h, sinit)
            t.wait_ge(S.qS.h, bg2ld)
            for i in range(4):
                mm = t.matmul(S.PA[:, i * 512:(i + 1) * 512], S.ones1x[:, :],
                              S.bg2row[:, i * 512:(i + 1) * 512], start=True, stop=True)
            mm.then_inc(S.cT.h, 1)
        S.t(f); bgt = S.cT.bump()

        def f(v, bgt=bgt):
            v.wait_ge(S.cT.h, bgt)
            v.tensor_copy(S.bg2f[:], S.PA[:, :]).then_inc(S.cV.h, 1)
        S.v(f); S.cV.bump()

        ch_consumed = [S.cT.n, S.cT.n]
        for ci in range(9):
            t0 = 0 if ci == 0 else 64 + (ci - 1) * 128
            Tc = 64 if ci == 0 else 128
            par = ci % 2
            state_only = (ci == 0)
            self._scan_chunk(ci, t0, Tc, par, state_only, kv_done, ch_consumed, sinit)
        obuf_done = S.qOBU.n

        b6bar = self.emit_barrier(self.all_sems())
        for c in range(16):
            def f(y, c=c, b6bar=b6bar, obuf_done=obuf_done):
                y.wait_ge(S.cG.h, b6bar)
                y.wait_ge(S.qOBU.h, obuf_done)
                y.dma_start(out=S.ofm[:, c, :], in_=S.obuf[0:TOWN, c * 128:(c + 1) * 128],
                            transpose=True).then_inc(S.qOF.h, 16)
            S.y(f); S.qOF.bump(16)
        ofm_done = S.qOF.n

        xs_store = [S.qXS[0].n, S.qXS[1].n]
        for dg in range(4):
            b = dg % 2

            def f(g, dg=dg, b=b, sf=S.cT.n, b6bar=b6bar):
                g.wait_ge(S.cG.h, b6bar)
                g.wait_ge(S.cT.h, sf)
                g.dma_start(out=S.oslab[b][:],
                            in_=S.WoA[dg]).then_inc(S.qOS[b].h, 16)
            S.g(f); wld = S.qOS[b].bump(16)

            for tt in range(8):
                xb = tt % 2
                pab = (tt % 4)
                ps = lambda: (S.PA[:, pab * 512:(pab + 1) * 512] if tt % 8 < 4 else
                              S.PB[:, pab * 512:(pab + 1) * 512])
                evv = S.cV.n

                def f(t, tt=tt, b=b, wld=wld, ofm_done=ofm_done, evv=evv, pab=pab):
                    t.wait_ge(S.qOS[b].h, wld)
                    t.wait_ge(S.qOF.h, ofm_done)
                    t.wait_ge(S.cV.h, evv)
                    pp = S.PA[:, pab * 512:(pab + 1) * 512] if tt < 4 else \
                         S.PB[:, pab * 512:(pab + 1) * 512]
                    for c in range(16):
                        mm = t.matmul(pp, S.ofm[:, c, tt * 128:(tt + 1) * 128],
                                      S.oslab[b][:, c, :], start=(c == 0), stop=(c == 15))
                    mm.then_inc(S.cT.h, 1)
                S.t(f); mmv = S.cT.bump()

                def f(y, tt=tt, xb=xb, dg=dg, vprev=S.cV.n, xsw=xs_store[xb], x1w=list(x1_wait or [])):
                    y.wait_ge(S.cV.h, vprev)
                    y.wait_ge(S.qXS[xb].h, xsw)
                    y.dma_start(out=S.xt[xb][:, 0:512],
                                in_=S.x1_dram[WU + tt * 128: WU + (tt + 1) * 128, dg * 512:(dg + 1) * 512]
                                ).then_inc(S.qXT[xb].h, 16)
                S.y(f); xld = S.qXT[xb].bump(16)

                def f(v, tt=tt, xb=xb, mmv=mmv, xld=xld, pab=pab):
                    v.wait_ge(S.cT.h, mmv)
                    v.wait_ge(S.qXT[xb].h, xld)
                    pp = S.PA[:, pab * 512:(pab + 1) * 512] if tt < 4 else \
                         S.PB[:, pab * 512:(pab + 1) * 512]
                    v.tensor_tensor(S.xt[xb][:, 0:512], pp, S.xt[xb][:, 0:512],
                                    ALU.add).then_inc(S.cV.h, 1)
                S.v(f); vv = S.cV.bump()

                def f(y, tt=tt, xb=xb, dg=dg, vv=vv):
                    y.wait_ge(S.cV.h, vv)
                    y.dma_start(out=S.x2_dram[tt * 128:(tt + 1) * 128, dg * 512:(dg + 1) * 512],
                                in_=S.xt[xb][:, 0:512]).then_inc(S.qXS[xb].h, 16)
                S.y(f); xs_store[xb] = S.qXS[xb].bump(16)
        return [(S.qXS[0].h, S.qXS[0].n), (S.qXS[1].h, S.qXS[1].n)]


    def _scan_chunk(self, ci, t0, Tc, par, state_only, kv_done, ch_consumed, sinit):
        S = self
        HS = lambda h: slice(h * 128, h * 128 + Tc)
        HF = lambda h: slice(h * 128, (h + 1) * 128)

        def f(y, ci=ci, t0=t0, Tc=Tc, par=par, state_only=state_only,
              kv_done=kv_done, chc=ch_consumed[par], sinit=sinit):
            y.wait_ge(S.cT.h, chc)
            y.wait_ge(S.cV.h, sinit)
            y.wait_ge(S.qKV[0].h, kv_done[0])
            y.wait_ge(S.qKV[1].h, kv_done[1])
            if not state_only:
                y.dma_start(out=S.qfm_c[par][:, :, 0:Tc],
                            in_=S.qfm_d.rearrange("(c p) t -> p c t", p=128)[:, :, t0:t0 + Tc]
                            ).then_inc(S.qCH[par].h, 16)
                y.dma_start(out=S.kfm_c[par][:, :, 0:Tc],
                            in_=S.kfm_d.rearrange("(c p) t -> p c t", p=128)[:, :, t0:t0 + Tc]
                            ).then_inc(S.qCH[par].h, 16)
            y.dma_start(out=S.wfm_c[par][:, :, 0:Tc],
                        in_=S.wfm_d.rearrange("(c p) t -> p c t", p=128)[:, :, t0:t0 + Tc]
                        ).then_inc(S.qCH[par].h, 16)
            y.dma_start(out=S.ktm_c[par][0:128, :], in_=S.kfm_d[:, t0:t0 + 128],
                        transpose=True).then_inc(S.qCH[par].h, 16)
            y.dma_start(out=S.vtm_c[par][0:128, :], in_=S.vfm_d[:, t0:t0 + 128],
                        transpose=True).then_inc(S.qCH[par].h, 16)
            y.dma_start(out=S.wtm_c[par][0:128, :], in_=S.wfm_d[:, t0:t0 + 128],
                        transpose=True).then_inc(S.qCH[par].h, 16)
        S.y(f)
        chld = S.qCH[par].bump(16 * (4 if state_only else 6))

        if not state_only:
            def f(t, t0=t0, Tc=Tc, evs=S.cS.n, evv=S.cV.n, sinit=sinit, obw=S.qOBU.n):
                t.wait_ge(S.cS.h, evs)
                t.wait_ge(S.cV.h, evv)
                t.wait_ge(S.cV.h, sinit)
                t.wait_ge(S.qOBU.h, obw)
                for i in range(4):
                    mm = t.matmul(S.PB[0:Tc, i * 512:(i + 1) * 512], S.g1_sb[:, t0:t0 + Tc],
                                  S.wg2_sb[:, i * 512:(i + 1) * 512], start=True, stop=True)
                mm.then_inc(S.cT.h, 1)
            S.t(f); gv = S.cT.bump()

            def f(v, Tc=Tc, gv=gv):
                v.wait_ge(S.cT.h, gv)
                v.tensor_tensor(S.sgf[0:Tc, :], S.PB[0:Tc, 0:2048], S.bg2f[0:Tc, :],
                                ALU.add).then_inc(S.cV.h, 1)
            S.v(f); gev = S.cV.bump()

            def f(s, Tc=Tc, gev=gev):
                s.wait_ge(S.cV.h, gev)
                s.activation(S.sgf[0:Tc, :], S.sgf[0:Tc, :], AF.Sigmoid).then_inc(S.cS.h, 1)
            S.s(f); gsg = S.cS.bump()

            def f(v, Tc=Tc, gsg=gsg):
                v.wait_ge(S.cS.h, gsg)
                v.tensor_tensor(S.sgf[0:Tc, :], S.sgf[0:Tc, :],
                                S.nwfull[0:Tc, :, :].rearrange("p a b -> p (a b)"),
                                ALU.mult).then_inc(S.cV.h, 1)
            S.v(f); S.cV.bump()

        def f(t, t0=t0, Tc=Tc, evs=S.cS.n, evv=S.cV.n, sinit=sinit, obw=S.qOBU.n):
            t.wait_ge(S.cS.h, evs)
            t.wait_ge(S.cV.h, evv)
            t.wait_ge(S.cV.h, sinit)
            t.wait_ge(S.qOBU.h, obw)
            for i in range(4):
                mm = t.matmul(S.PA[0:Tc, i * 512:(i + 1) * 512], S.f1_sb[:, t0:t0 + Tc],
                              S.wf2_sb[:, i * 512:(i + 1) * 512], start=True, stop=True)
            mm.then_inc(S.cT.h, 1)
        S.t(f); gpv = S.cT.bump()

        def f(v, Tc=Tc, gpv=gpv):
            v.wait_ge(S.cT.h, gpv)
            v.tensor_copy(S.gpre[0:Tc, :], S.PA[0:Tc, 0:2048]).then_inc(S.cV.h, 1)
        S.v(f); gpe = S.cV.bump()

        def f(s, Tc=Tc, gpe=gpe):
            s.wait_ge(S.cV.h, gpe)
            s.activation(S.tmpA[0:Tc, :], S.gpre[0:Tc, :], AF.Abs)
            s.drain()
            s.activation(S.tmpB[0:Tc, :], S.tmpA[0:Tc, :], AF.Exp, scale=-1.0)
            s.drain()
            s.activation(S.gtm[0:Tc, :], S.tmpB[0:Tc, :], AF.Ln, bias=1.0).then_inc(S.cS.h, 1)
        S.s(f); lgs = S.cS.bump()

        def f(v, Tc=Tc, lgs=lgs):
            v.wait_ge(S.cS.h, lgs)
            v.tensor_tensor(S.tmpB[0:Tc, :], S.gpre[0:Tc, :], S.tmpA[0:Tc, :], ALU.subtract)
            v.drain()
            v.scalar_tensor_tensor(S.gtm[0:Tc, :], S.tmpB[0:Tc, :], 0.5, S.gtm[0:Tc, :],
                                   ALU.mult, ALU.subtract).then_inc(S.cV.h, 1)
        S.v(f); gtv = S.cV.bump()

        def f(t, Tc=Tc, gtv=gtv, gpe=gpe):
            t.wait_ge(S.cV.h, gtv)
            for i in range(4):
                t.matmul(S.PA[0:Tc, i * 512:(i + 1) * 512], S.uinc[0:Tc, 0:Tc],
                         S.gtm[0:Tc, i * 512:(i + 1) * 512], start=True, stop=True)
            for h in range(H):
                mm = t.matmul(S.PB[0:128, h * 128:h * 128 + Tc], S.gtm[0:Tc, HF(h)],
                              S.uinc[0:Tc, 0:Tc], start=True, stop=True)
            mm.then_inc(S.cT.h, 1)
        S.t(f); lamv = S.cT.bump()

        def f(v, Tc=Tc, lamv=lamv):
            v.wait_ge(S.cT.h, lamv)
            v.tensor_copy(S.ltm[0:Tc, :], S.PA[0:Tc, 0:2048])
            v.tensor_copy(S.lfm.rearrange("p (a b) -> p a b", b=128)[:, :, 0:Tc],
                          S.PB[:, 0:2048].rearrange("p (a b) -> p a b", b=128)[:, :, 0:Tc]
                          ).then_inc(S.cV.h, 1)
        S.v(f); lev = S.cV.bump()

        def f(s, Tc=Tc, lev=lev):
            s.wait_ge(S.cV.h, lev)
            s.activation(S.p1f[0:Tc, :], S.ltm[0:Tc, :], AF.Exp, bias=45.0)
            s.activation(S.p2b[0:Tc, :], S.ltm[0:Tc, :], AF.Exp, scale=-1.0, bias=-45.0)
            s.activation(S.p2fb.rearrange("p (a b) -> p a b", b=128)[:, :, 0:Tc],
                         S.lfm.rearrange("p (a b) -> p a b", b=128)[:, :, 0:Tc],
                         AF.Exp, scale=-1.0, bias=-45.0)
            s.activation(S.fcol[:, :],
                         S.lfm.rearrange("p (a b) -> p a b", b=128)[:, :, Tc - 1:Tc]
                         .rearrange("p a b -> p (a b)"), AF.Exp, bias=45.0).then_inc(S.cS.h, 1)
        S.s(f); exv = S.cS.bump()

        def f(s, Tc=Tc, par=par, chld=chld, gtv2=S.cV.n):
            s.wait_ge(S.qCH[par].h, chld)
            s.wait_ge(S.cV.h, gtv2)
            s.activation(S.sq2[0:Tc, :], S.wtm_c[par][0:Tc, :], AF.Square).then_inc(S.cS.h, 1)
        S.s(f); wsq = S.cS.bump()

        def f(v, Tc=Tc, wsq=wsq):
            v.wait_ge(S.cS.h, wsq)
            v.tensor_reduce(S.wrow[0:Tc, :],
                            S.sq2[0:Tc, :].rearrange("p (a b) -> p a b", b=128),
                            axis=AX.X, op=ALU.add).then_inc(S.cV.h, 1)
        S.v(f); wsv = S.cV.bump()

        def f(s, Tc=Tc, wsv=wsv):
            s.wait_ge(S.cV.h, wsv)
            s.activation(S.wrow[0:Tc, :], S.wrow[0:Tc, :], AF.Ln, bias=1e-6)
            s.drain()
            s.activation(S.wrow[0:Tc, :], S.wrow[0:Tc, :], AF.Exp, scale=-0.5).then_inc(S.cS.h, 1)
        S.s(f); wrv = S.cS.bump()

        def f(v, ci=ci, Tc=Tc, wrv=wrv):
            v.wait_ge(S.cS.h, wrv)
            v.tensor_tensor(S.brs[0:Tc, :], S.beta[0:Tc, ci, :], S.wrow[0:Tc, :],
                            ALU.mult).then_inc(S.cV.h, 1)
        S.v(f); brsv = S.cV.bump()

        def f(v, ci=ci, Tc=Tc, exv=exv, chld=chld, par=par, state_only=state_only, brsv=brsv):
            v.wait_ge(S.cS.h, exv)
            v.wait_ge(S.cV.h, brsv)
            v.wait_ge(S.qCH[par].h, chld)
            for h in range(H):
                v.scalar_tensor_tensor(S.kpb[0:Tc, HF(h)], S.wtm_c[par][0:Tc, HF(h)],
                                       S.brs[0:Tc, h:h + 1], S.p2b[0:Tc, HF(h)],
                                       ALU.mult, ALU.mult)
            if not state_only:
                v.tensor_tensor(S.wtf[:, :],
                                S.wfm_c[par].rearrange("p a b -> p (a b)")[:, 0:H * 128],
                                S.p2fb[:, :], ALU.mult)
            v.engine_nop().then_inc(S.cV.h, 1)
        S.v(f); kpv = S.cV.bump()

        if not state_only:
            def f(t, Tc=Tc, par=par, chld=chld, evv=S.cV.n, lev=lev):
                t.wait_ge(S.qCH[par].h, chld)
                t.wait_ge(S.cV.h, lev)
                for h in range(H):
                    mm = t.matmul(S.PA[0:Tc, h * 128:h * 128 + Tc], S.kfm_c[par][:, h, 0:Tc],
                                  S.qfm_c[par][:, h, 0:Tc], start=True, stop=True)
                mm.then_inc(S.cT.h, 1)
            S.t(f); atv = S.cT.bump()

            def f(v, Tc=Tc, atv=atv):
                v.wait_ge(S.cT.h, atv)
                v.tensor_tensor(S.a_sb.rearrange("p (a b) -> p a b", b=128)[0:Tc, :, 0:Tc],
                                S.PA[0:Tc, 0:2048].rearrange("p (a b) -> p a b", b=128)[0:Tc, :, 0:Tc],
                                S.maskrep.rearrange("p (a b) -> p a b", b=128)[0:Tc, :, 0:Tc],
                                ALU.mult).then_inc(S.cV.h, 1)
            S.v(f); aev = S.cV.bump()

            def f(t, Tc=Tc, par=par, aev=aev, kpv=kpv, gev2=S.cV.n):
                t.wait_ge(S.cV.h, aev)
                t.wait_ge(S.cV.h, kpv)
                for h in range(H):
                    t.matmul(S.PB[0:Tc, h * 128:(h + 1) * 128], S.qfm_c[par][:, h, 0:Tc],
                             S.Skp[:, HF(h)], start=True, stop=False)
                    mm = t.matmul(S.PB[0:Tc, h * 128:(h + 1) * 128],
                                  S.a_sb[0:Tc, h * 128:h * 128 + Tc],
                                  S.kpb[0:Tc, HF(h)], start=False, stop=True)
                mm.then_inc(S.cT.h, 1)
            S.t(f); sv = S.cT.bump()

            def f(v, Tc=Tc, sv=sv):
                v.wait_ge(S.cT.h, sv)
                v.scalar_tensor_tensor(S.s_sb[0:Tc, :], S.PB[0:Tc, 0:2048], SCALE,
                                       S.p1f[0:Tc, :], ALU.mult, ALU.mult).then_inc(S.cV.h, 1)
            S.v(f); sev = S.cV.bump()

            def f(v, Tc=Tc, sev=sev):
                v.wait_ge(S.cV.h, sev)
                v.tensor_reduce(S.mrow[0:Tc, :],
                                S.s_sb[0:Tc, :].rearrange("p (a b) -> p a b", b=128),
                                axis=AX.X, op=ALU.max)
                v.drain()
                v.tensor_tensor(S.s_sb[0:Tc, :].rearrange("p (a b) -> p a b", b=128),
                                S.s_sb[0:Tc, :].rearrange("p (a b) -> p a b", b=128),
                                S.mrow[0:Tc, :].unsqueeze(2).broadcast_to([Tc, H, 128]),
                                ALU.subtract)
                v.engine_nop().then_inc(S.cV.h, 1)
            S.v(f); sm1 = S.cV.bump()

            def f(s, Tc=Tc, sm1=sm1):
                s.wait_ge(S.cV.h, sm1)
                s.activation(S.s_sb[0:Tc, :], S.s_sb[0:Tc, :], AF.Exp).then_inc(S.cS.h, 1)
            S.s(f); sm2 = S.cS.bump()

            def f(v, Tc=Tc, sm2=sm2):
                v.wait_ge(S.cS.h, sm2)
                v.tensor_reduce(S.srow[0:Tc, :],
                                S.s_sb[0:Tc, :].rearrange("p (a b) -> p a b", b=128),
                                axis=AX.X, op=ALU.add)
                v.drain()
                v.reciprocal(S.rrow[0:Tc, :], S.srow[0:Tc, :])
                v.drain()
                v.tensor_tensor(S.s_sb[0:Tc, :].rearrange("p (a b) -> p a b", b=128),
                                S.s_sb[0:Tc, :].rearrange("p (a b) -> p a b", b=128),
                                S.rrow[0:Tc, :].unsqueeze(2).broadcast_to([Tc, H, 128]),
                                ALU.mult)
                v.drain()
                v.tensor_tensor(S.a_sb[0:Tc, 0:2048], S.s_sb[0:Tc, :], S.p1f[0:Tc, :],
                                ALU.mult).then_inc(S.cV.h, 1)
            S.v(f); apv = S.cV.bump()

            PT = S.PA[:, 0:1024].bitcast(BF16)

            def f(t, Tc=Tc, apv=apv, PT=PT):
                t.wait_ge(S.cV.h, apv)
                for h in range(H):
                    mm = t.transpose(PT[:, h * 128:h * 128 + Tc], S.a_sb[0:Tc, HF(h)],
                                     S.identb[0:Tc, 0:Tc])
                mm.then_inc(S.cT.h, 1)
            S.t(f); aptv = S.cT.bump()

            def f(v, Tc=Tc, aptv=aptv, PT=PT):
                v.wait_ge(S.cT.h, aptv)
                v.tensor_copy(S.at_sb.rearrange("p (a b) -> p a b", b=128)[:, :, 0:Tc],
                              PT.rearrange("p (a b) -> p a b", b=128)[:, :, 0:Tc]
                              ).then_inc(S.cV.h, 1)
            S.v(f); apte = S.cV.bump()

            def f(t, Tc=Tc, apte=apte, kpv=kpv):
                t.wait_ge(S.cV.h, apte)
                for h in range(H):
                    mm = t.matmul(S.PA[0:Tc, h * 128:h * 128 + Tc], S.wtf[:, HS(h)],
                                  S.at_sb[:, h * 128:h * 128 + Tc], start=True, stop=True)
                mm.then_inc(S.cT.h, 1)
            S.t(f); mtv = S.cT.bump()

            def f(v, ci=ci, Tc=Tc, mtv=mtv):
                v.wait_ge(S.cT.h, mtv)
                for h in range(H):
                    v.scalar_tensor_tensor(S.m_sb[0:Tc, h * 128:h * 128 + Tc],
                                           S.PA[0:Tc, h * 128:h * 128 + Tc],
                                           S.brs[0:Tc, h:h + 1],
                                           S.maskrep[0:Tc, h * 128:h * 128 + Tc],
                                           ALU.mult, ALU.mult)
                v.engine_nop().then_inc(S.cV.h, 1)
            S.v(f); mev = S.cV.bump()

            def f(t, Tc=Tc, par=par, mev=mev, sev=sev):
                t.wait_ge(S.cV.h, mev)
                t.wait_ge(S.cV.h, sev)
                for h in range(H):
                    t.matmul(S.PB[0:Tc, HF(h)], S.m_sb[0:Tc, h * 128:h * 128 + Tc],
                             S.vtm_c[par][0:Tc, HF(h)], start=True, stop=False)
                    mm = t.matmul(S.PB[0:Tc, HF(h)], S.at_sb[:, h * 128:h * 128 + Tc],
                                  S.Svp[:, HF(h)], start=False, stop=True)
                mm.then_inc(S.cT.h, 1)
            S.t(f); ov = S.cT.bump()

            def f(s, Tc=Tc, ov=ov):
                s.wait_ge(S.cT.h, ov)
                s.activation(S.sq2[0:Tc, :], S.PB[0:Tc, 0:2048], AF.Square).then_inc(S.cS.h, 1)
            S.s(f); sqv = S.cS.bump()

            def f(v, Tc=Tc, sqv=sqv):
                v.wait_ge(S.cS.h, sqv)
                v.tensor_reduce(S.mrow[0:Tc, :],
                                S.sq2[0:Tc, :].rearrange("p (a b) -> p a b", b=128),
                                axis=AX.X, op=ALU.add).then_inc(S.cV.h, 1)
            S.v(f); msv = S.cV.bump()

            def f(s, Tc=Tc, msv=msv):
                s.wait_ge(S.cV.h, msv)
                s.activation(S.srow[0:Tc, :], S.mrow[0:Tc, :], AF.Ln, scale=1.0 / 128.0, bias=float(EPS))
                s.drain()
                s.activation(S.srow[0:Tc, :], S.srow[0:Tc, :], AF.Exp, scale=-0.5).then_inc(S.cS.h, 1)
            S.s(f); rsv = S.cS.bump()

            def f(v, ci=ci, t0=t0, Tc=Tc, rsv=rsv):
                v.wait_ge(S.cS.h, rsv)
                v.tensor_tensor(S.onrm[0:Tc, :].rearrange("p (a b) -> p a b", b=128),
                                S.PB[0:Tc, 0:2048].rearrange("p (a b) -> p a b", b=128),
                                S.srow[0:Tc, :].unsqueeze(2).broadcast_to([Tc, H, 128]),
                                ALU.mult)
                v.drain()
                v.tensor_tensor(S.ogat[0:Tc, :], S.onrm[0:Tc, :], S.sgf[0:Tc, :],
                                ALU.mult).then_inc(S.cV.h, 1)
            S.v(f); oev = S.cV.bump()

            def f(y, t0=t0, Tc=Tc, oev=oev):
                y.wait_ge(S.cV.h, oev)
                y.dma_start(out=S.obuf[t0 - WU:t0 - WU + Tc, :], in_=S.ogat[0:Tc, :]
                            ).then_inc(S.qOBU.h, 16)
            S.y(f); S.qOBU.bump(16)
        else:
            oev = S.cV.n

        evv = S.cV.n

        def f(t, Tc=Tc, par=par, kpv=kpv, evv=evv, chld=chld):
            t.wait_ge(S.cV.h, kpv)
            t.wait_ge(S.cV.h, evv)
            t.wait_ge(S.qCH[par].h, chld)
            for h in range(H):
                t.matmul(S.PA[:, HF(h)], S.kpb[0:Tc, HF(h)], S.ktm_c[par][0:Tc, HF(h)],
                         start=True, stop=True)
                mm = t.matmul(S.PB[:, HF(h)], S.kpb[0:Tc, HF(h)], S.vtm_c[par][0:Tc, HF(h)],
                              start=True, stop=True)
            mm.then_inc(S.cT.h, 1)
        S.t(f); dsv = S.cT.bump()
        ch_consumed[par] = dsv

        def f(v, Tc=Tc, dsv=dsv):
            v.wait_ge(S.cT.h, dsv)
            v.scalar_tensor_tensor(S.sfold[:, :], S.Sk[:, :], float(ECONST), S.PA[:, 0:2048],
                                   ALU.mult, ALU.add)
            v.drain()
            for h in range(H):
                v.tensor_scalar(S.Sk[:, HF(h)], S.sfold[:, HF(h)], S.fcol[:, h:h + 1], None, ALU.mult)
            v.drain()
            v.scalar_tensor_tensor(S.sfold[:, :], S.Sv[:, :], float(ECONST), S.PB[:, 0:2048],
                                   ALU.mult, ALU.add)
            v.drain()
            for h in range(H):
                v.tensor_scalar(S.Sv[:, HF(h)], S.sfold[:, HF(h)], S.fcol[:, h:h + 1], None, ALU.mult)
            v.drain()
            v.tensor_scalar(S.skb16[:, :], S.Sk[:, :], float(ECONST), None, ALU.mult)
            v.tensor_scalar(S.Svp[:, :], S.Sv[:, :], float(ECONST), None, ALU.mult)
            v.engine_nop().then_inc(S.cV.h, 1)
        S.v(f); fold = S.cV.bump()

        PT2 = S.PA[:, 0:1024].bitcast(BF16)

        def f(t, fold=fold, PT2=PT2):
            t.wait_ge(S.cV.h, fold)
            for h in range(H):
                mm = t.transpose(PT2[:, HF(h)], S.skb16[:, HF(h)], S.identb[:, :])
            mm.then_inc(S.cT.h, 1)
        S.t(f); skt = S.cT.bump()

        def f(v, skt=skt, PT2=PT2):
            v.wait_ge(S.cT.h, skt)
            v.tensor_copy(S.Skp[:, :], PT2[:, 0:2048]).then_inc(S.cV.h, 1)
        S.v(f); S.cV.bump()

    def all_sems(self):
        S = self
        out = [S.cT, S.cV, S.cS, S.cG, S.qS, S.qF, S.qG,
               S.qOBU, S.qOF]
        for lst in [S.qXT, S.qYT, S.qWS, S.qDS, S.qES, S.qED, S.qXG, S.qCH, S.qOS, S.qYS, S.qCW, S.qCE, S.qCDW, S.qCED, S.qXS, S.qOBs, S.qKV, S.qSCp]:
            out.extend(lst)
        return out

    def finalize(self):
        S = self
        nc = S.nc
        src_map = {"y_dram": S.y_dram, "x1_dram": S.x1_dram, "xbuf": S.xbuf,
                   "outbuf": S.outbuf, "x2_dram": S.x2_dram, "obuf": S.obuf,
                   "qfm_d": S.qfm_d, "kfm_d": S.kfm_d, "vfm_d": S.vfm_d, "wfm_d": S.wfm_d}
        sems = self.all_sems()
        allw = [(c.h, c.n) for c in sems]
        for name, dst in S.dbg_aps.items():

            def f(g, name=name, dst=dst, allw=allw):
                for h, n in allw:
                    if n:
                        g.wait_ge(h, n)
                g.dma_start(out=dst[:, :], in_=src_map[name][:, :]).then_inc(S.qS.h, 16)
            S.g(f); S.qS.bump(16)

        final = [(c.h, c.n) for c in sems]

        def f(y, final=final):
            for h, n in final:
                if n:
                    y.wait_ge(h, n)
        S.y(f)

        with nc.Block() as block:
            @block.tensor
            def _(eng):
                for fn in S.ops["t"]:
                    fn(eng)

            @block.vector
            def _(eng):
                for fn in S.ops["v"]:
                    fn(eng)

            @block.scalar
            def _(eng):
                for fn in S.ops["s"]:
                    fn(eng)

            @block.gpsimd
            def _(eng):
                for fn in S.ops["g"]:
                    fn(eng)

            @block.sync
            def _(eng):
                for fn in S.ops["y"]:
                    fn(eng)
        return nc


def shard_inputs(inputs):
    x = np.ascontiguousarray(inputs["L0_attn_hidden_states"][0])

    def slab4(w):
        return np.asarray(w).reshape(16, 128, 4, 512).transpose(2, 1, 0, 3)

    shared = {
        "Wq": slab4(inputs["L0_attn_Wq"]), "Wk": slab4(inputs["L0_attn_Wk"]),
        "Wv": slab4(inputs["L0_attn_Wv"]),
        "Ww": slab4(inputs["L0_attn_Ww"]), "cq": inputs["L0_attn_cq"], "ck": inputs["L0_attn_ck"],
        "cv": inputs["L0_attn_cv"], "Wf1": inputs["L0_attn_Wf1"], "Wf2": inputs["L0_attn_Wf2"],
        "Wb": inputs["L0_attn_Wb"], "Wg1": inputs["L0_attn_Wg1"], "Wg2": inputs["L0_attn_Wg2"],
        "bg2": inputs["L0_attn_bg2"], "norm_w": inputs["L0_attn_norm_w"],
        "Wo": slab4(inputs["L0_attn_Wo"]),
        "router": inputs["L0_ffn_router"],
        "w_up_gate": np.asarray(inputs["L0_ffn_w_up_gate"])
            .reshape(8, 16, 128, 11, 256).transpose(0, 3, 2, 1, 4),
        "w_down": np.asarray(inputs["L0_ffn_w_down"])
            .reshape(8, 11, 128, 4, 512).transpose(0, 3, 2, 1, 4),
        "shared_w_up_gate": np.asarray(inputs["L0_ffn_shared_w_up_gate"])
            .reshape(16, 128, 2, 2, 11, 256).transpose(2, 3, 4, 1, 0, 5),
        "shared_w_down": np.asarray(inputs["L0_ffn_shared_w_down"])
            .reshape(2, 22, 128, 4, 512).transpose(0, 3, 2, 1, 4),
    }
    shared = {k: np.ascontiguousarray(v) for k, v in shared.items()}
    in_maps = []
    for c in range(NCORE):
        lo = c * TOWN - WU
        if c == 0:
            ext = np.concatenate([np.zeros((WU, D), np.float32), x[0:TOWN]], axis=0)
        else:
            ext = x[lo:(c + 1) * TOWN]
        m = dict(shared)
        m["xin"] = np.ascontiguousarray(ext)
        m["wvalid"] = np.array([[0.0 if c == 0 else 1.0]], np.float32)
        in_maps.append(m)
    return in_maps


_CACHED = {}


def _build():
    if "nc" not in _CACHED:
        m = MK()
        m.emit_setup()
        w1 = m.emit_ffn(TEXT, m.xin, m.x1_dram, False, True)
        w2 = m.emit_attn(w1)
        bar = m.emit_barrier(m.all_sems())
        m.emit_ffn(TOWN, m.x2_dram, m.xoutA, True, False, src_wait=list(w2) + [(m.cG.h, bar)])
        _CACHED["nc"] = m.finalize()
    return _CACHED["nc"]


def kernel(**inputs):
    nc = _build()
    in_maps = shard_inputs(inputs)
    res = run_bass_kernel_spmd(nc, in_maps, core_ids=list(range(NCORE)))
    out = np.concatenate([res.results[c]["xout"] for c in range(NCORE)], axis=0)
    return out.reshape(1, NCORE * TOWN, D).astype(np.float32)


# version 06
# speedup vs baseline: 1.0336x; 1.0336x over previous
import numpy as np
import ml_dtypes
import concourse.bass as bass
import concourse.mybir as mybir
from concourse.bass_utils import run_bass_kernel_spmd

F32 = mybir.dt.float32
BF16 = mybir.dt.bfloat16
I32 = mybir.dt.int32
AF = mybir.ActivationFunctionType
ALU = mybir.AluOpType
AX = mybir.AxisListType
IOX = bass.IndirectOffsetOnAxis

D = 2048
NCORE = 8
TOWN = 1024
WU = 64
TEXT = TOWN + WU
H, DK, DV, M = 16, 128, 128, 128
SCALE = float(DK ** -0.5)
E = 8
CAP = 352
NER = E * CAP
TRASH = NER
FI = 1408
FJ = 5632
CSHIFT = -45.0
ECONST = float(np.exp(np.float64(CSHIFT)))
EPS = 1e-5
KB = 1024


class Ctr:
    def __init__(self, nc, name):
        self.h = nc.alloc_semaphore(name)
        self.n = 0

    def bump(self, k=1):
        self.n += k
        return self.n


class MK:
    def __init__(self, do_attn=True, do_ffn2=True, dbg=()):
        self.do_attn = do_attn
        self.do_ffn2 = do_ffn2
        nc = bass.Bass("TRN2", target_bir_lowering=False)
        self.nc = nc
        self.ops = {"t": [], "v": [], "s": [], "g": [], "y": []}

        def din(name, shape, dt=F32):
            return nc.dram_tensor(name, shape, dt, kind="ExternalInput").ap()

        self.xin = din("xin", [TEXT, D])
        self.WqA = din("Wq", [4, 128, 16, 512]); self.WkA = din("Wk", [4, 128, 16, 512])
        self.WvA = din("Wv", [4, 128, 16, 512]); self.WwA = din("Ww", [4, 128, 16, 512])
        self.cqA = din("cq", [D, 4]); self.ckA = din("ck", [D, 4]); self.cvA = din("cv", [D, 4])
        self.Wf1A = din("Wf1", [D, DV]); self.Wf2A = din("Wf2", [DV, H * M])
        self.WbA = din("Wb", [D, H])
        self.Wg1A = din("Wg1", [D, DV]); self.Wg2A = din("Wg2", [DV, H * DV])
        self.bg2A = din("bg2", [H * DV])
        self.normwA = din("norm_w", [DV])
        self.WoA = din("Wo", [4, 128, 16, 512])
        self.RtA = din("router", [D, E])
        self.WugA = din("w_up_gate", [E, 11, 128, 16, 256])
        self.WdnA = din("w_down", [E, 4, 128, 11, 512])
        self.SugA = din("shared_w_up_gate", [2, 2, 11, 128, 16, 256])
        self.SdnA = din("shared_w_down", [2, 4, 128, 22, 512])
        self.wvalidA = din("wvalid", [1, 1])
        self.xoutA = nc.dram_tensor("xout", [TOWN, D], F32, kind="ExternalOutput").ap()

        def scr(name, shape, dt):
            return nc.dram_tensor(name, shape, dt, kind="Internal").ap()

        self.y_dram = scr("y_dram", [TEXT, D], BF16)
        self.x1_dram = scr("x1_dram", [TEXT, D], F32)
        self.x2_dram = scr("x2_dram", [TOWN, D], F32)
        self.xbuf = scr("xbuf", [NER + 1, D], BF16)
        self.outbuf = scr("outbuf", [NER + 1, D], F32)
        self.qfm_d = scr("qfm_d", [D, TEXT], BF16)
        self.kfm_d = scr("kfm_d", [D, TEXT], BF16)
        self.vfm_d = scr("vfm_d", [D, TEXT], BF16)
        self.wfm_d = scr("wfm_d", [D, TEXT], BF16)
        self.obuf = scr("obuf", [TOWN, D], BF16)
        self.SugB = scr("SugB", [2, 2, 11, 128, 16, 256], BF16)
        self.SdnB = scr("SdnB", [2, 4, 128, 22, 512], BF16)
        self.WugB = scr("WugB", [E, 11, 128, 16, 256], BF16)
        self.WdnB = scr("WdnB", [E, 4, 128, 11, 512], BF16)

        it = nc.inline_tensor
        self.C_UINC = it(np.triu(np.ones((128, 128), np.float32)).copy(), "c_UINC").ap()
        self.C_USTR = it(np.triu(np.ones((128, 128), np.float32), 1).copy(), "c_USTR").ap()
        self.C_ONES1 = it(np.ones((1, 128), np.float32), "c_ONES1").ap()
        self.C_ONESC = it(np.ones((128, 1), np.float32), "c_ONESC").ap()
        self.C_IDENT = it(np.eye(128, dtype=ml_dtypes.bfloat16), "c_IDENT").ap()
        self.C_MASKR = it(np.tile(np.triu(np.ones((128, 128), np.float32)), (1, H))
                          .astype(ml_dtypes.bfloat16), "c_MASKR").ap()
        self.C_IOTAP = it((np.arange(8, dtype=np.float32)[None, :] + 1000.0).repeat(128, 0).copy(),
                          "c_IOTAP").ap()
        self.C_IOTA8 = it(np.arange(8, dtype=np.float32)[None, :].repeat(128, 0).copy(),
                          "c_IOTA8").ap()

        sb = lambda n, s, t: nc.alloc_sbuf_tensor(n, s, t)
        self.identb = sb("identb", [128, 128], BF16)
        self.uinc = sb("uinc", [128, 128], F32)
        self.ustrict = sb("ustrict", [128, 128], F32)
        self.ones1x = sb("ones1x", [1, 128], F32)
        self.onesc = sb("onesc", [128, 1], F32)
        self.maskrep = sb("maskrep", [128, H * 128], BF16)
        self.iota8p = sb("iota8p", [128, 8], F32)
        self.iota8 = sb("iota8", [128, 8], F32)
        self.rt_sb = sb("rt_sb", [128, 16, E], BF16)
        self.wv_sb = sb("wv_sb", [1, 1], F32)
        self.wv_col = sb("wv_col", [128, 1], F32)
        self.tcol = sb("tcol", [128, 1], F32)
        self.convw = sb("convw", [128, 3, 16, 4], F32)
        self.wf2_sb = sb("wf2_sb", [128, H * M], BF16)
        self.wg2_sb = sb("wg2_sb", [128, H * DV], BF16)
        self.wb_sb = sb("wb_sb", [128, 16, H], BF16)
        self.wf1_sb = sb("wf1_sb", [128, 16, 128], BF16)
        self.wg1_sb = sb("wg1_sb", [128, 16, 128], BF16)
        self.nwrow = sb("nwrow", [1, DV], F32)
        self.nw128 = sb("nw128", [128, DV], F32)
        self.lgt = sb("lgt", [128, 9, 8], F32)
        self.off_i = [sb(f"off{k}", [128, 9], I32) for k in range(2)]
        self.cwvt = [sb(f"cwv{k}", [128, 9], F32) for k in range(2)]
        self.r_top8 = sb("r_top8", [128, 8], F32)
        self.r_t8 = sb("r_t8", [128, 8], F32)
        self.r_ohA = sb("r_ohA", [128, 8], F32)
        self.r_ohB = sb("r_ohB", [128, 8], F32)
        self.r_ohs = sb("r_ohs", [128, 8], F32)
        self.t_sel0 = sb("t_sel0", [128, 1], F32)
        self.t_sel1 = sb("t_sel1", [128, 1], F32)
        self.t_pos0 = sb("t_pos0", [128, 1], F32)
        self.t_pos1 = sb("t_pos1", [128, 1], F32)
        self.t_v0 = sb("t_v0", [128, 1], F32)
        self.t_v1 = sb("t_v1", [128, 1], F32)
        self.t_cwA = sb("t_cwA", [128, 1], F32)
        self.t_cwB = sb("t_cwB", [128, 1], F32)
        self.t_a = sb("t_a", [128, 1], F32)
        self.t_b = sb("t_b", [128, 1], F32)
        self.t_mr = sb("t_mr", [128, 1], F32)
        self.t_rs = sb("t_rs", [128, 1], F32)
        self.carry = sb("carry", [1, 8], F32)
        self.CVALS = [0.0, 1.0, -1.0, float(EPS), 45.0, -45.0, -0.5, 1.0 / 128.0,
                      float(SCALE), 0.5, float(ECONST), 1e-6]
        self.cpool = sb("cpool", [128, len(self.CVALS)], F32)
        for ci, val in enumerate(self.CVALS):
            nc.const_aps.register(F32, val, self.cpool[:, ci:ci + 1])

        ARENA = 180 * KB
        base, _ = nc.bump_sbuf(ARENA)
        self._ab = base

        def at(name, shape, dt, off):
            return nc.alloc_sbuf_tensor_at(name, shape, dt, offset=base + off)

        self.yfm = at("yfm", [128, 16, TEXT], BF16, 0)
        self.hsh = at("hsh", [128, 22, TEXT], BF16, 34816)
        self.xt = [at(f"xt{i}", [128, D], F32, 82688 + i * 8192) for i in range(2)]
        self.estg = [at("estg0", [128, 512], F32, 86784),
                     at("estg1", [128, 512], F32, 88832)]
        self.wslab = [at(f"wslab{i}", [128, 16, 256], BF16, 99072 + i * 8192) for i in range(2)]
        self.dslab = [at(f"dslab{i}", [128, 4, 512], BF16, 115456 + i * 4096) for i in range(2)]
        self.sqbuf = at("sqbuf", [128, D], F32, 123648)
        self.eslab = [at("eslabB", [128, 16, 256], BF16, 123648),
                      at("eslabA", [128, 16, 256], BF16, 131840)]
        self.xg = [at("xg0", [128, 16, CAP], BF16, 140032)]
        self.he = at("he", [128, 11, CAP], BF16, 151296)
        self.edslab = [at(f"edslab{i}", [128, 11, 512], BF16, 159040 + i * 11264) for i in range(2)]
        self.ytm_t = [at(f"ytm{i}", [128, D], BF16, 159040 + i * 4096) for i in range(2)]
        self.g0t = at("g0t", [128, D], F32, 131840)
        self.g1t = at("g1t", [128, D], F32, 140032)
        self.bg2row = at("bg2row", [1, H * DV], F32, 140032)
        self.conv_z = [at(f"conv_z{i}", [128, TEXT + 3], F32, 107264 + i * 4608) for i in range(2)]
        self.conv_a = at("conv_a", [128, TEXT], F32, 116480)
        self.conv_o = [at(f"conv_o{i}", [128, TEXT], BF16, 120832 + i * 2176) for i in range(2)]
        self.wslab4 = [at(f"wslab4_{i}", [128, 16, 512], BF16, 34816 + i * 16384) for i in range(2)]
        so = 0
        self.qfm_c = [at(f"qfmc{i}", [128, H, 128], BF16, so + i * 4096) for i in range(2)]
        self.kfm_c = [at(f"kfmc{i}", [128, H, 128], BF16, so + 8192 + i * 4096) for i in range(2)]
        self.wfm_c = [at(f"wfmc{i}", [128, H, 128], BF16, so + 16384 + i * 4096) for i in range(2)]
        self.ktm_c = [at(f"ktmc{i}", [128, H * DK], BF16, so + 24576 + i * 4096) for i in range(2)]
        self.vtm_c = [at(f"vtmc{i}", [128, H * DV], BF16, so + 32768 + i * 4096) for i in range(2)]
        self.wtm_c = [at(f"wtmc{i}", [128, H * M], BF16, so + 40960 + i * 4096) for i in range(2)]
        self.Sk = at("Sk", [128, H * DK], F32, 49152)
        self.Sv = at("Sv", [128, H * DV], F32, 57344)
        self.Skp = at("Skp", [128, H * M], BF16, 65536)
        self.Svp = at("Svp", [128, H * DV], BF16, 69632)
        self.f1_sb = at("f1_sb", [128, TEXT], BF16, 73728)
        self.g1_sb = at("g1_sb", [128, TEXT], BF16, 76032)
        self.beta = at("beta", [128, 9, H], F32, 78336)
        self.fcol = at("fcol", [128, H], F32, 78976)
        self.mrow = at("mrow", [128, H], F32, 79104)
        self.srow = at("srow", [128, H], F32, 79232)
        self.rrow = at("rrow", [128, H], F32, 79360)
        self.wrow = at("wrow", [128, H], F32, 79488)
        self.brs = at("brs", [128, H], F32, 79616)
        self.bg2f = at("bg2f", [128, H * DV], F32, 79872)
        self.nwfull = at("nwfull", [128, H, DV], F32, 88064)
        self.sgf = at("sgf", [128, H * DV], F32, 96256)
        self.gpre = at("gpre", [128, H * M], F32, 104448)
        self.s_sb = at("s_sb", [128, H * M], F32, 104448)
        self.sq2 = at("sq2", [128, H * DV], F32, 104448)
        self.tmpA = at("tmpA", [128, H * M], F32, 112640)
        self.lfm = at("lfm", [128, H * M], F32, 112640)
        self.at_sb = at("at_sb", [128, H * 128], BF16, 112640)
        self.m_sb = at("m_sb", [128, H * 128], BF16, 116736)
        self.onrm = at("onrm", [128, H * DV], F32, 112640)
        self.tmpB = at("tmpB", [128, H * M], F32, 120832)
        self.p1f = at("p1f", [128, H * M], F32, 120832)
        self.ogat = at("ogat", [128, H * DV], BF16, 120832)
        self.gtm = at("gtm", [128, H * M], F32, 129024)
        self.p2b = at("p2b", [128, H * M], BF16, 129024)
        self.p2fb = at("p2fb", [128, H * M], BF16, 133120)
        self.ltm = at("ltm", [128, H * M], F32, 137216)
        self.sfold = at("sfold", [128, H * DK], F32, 137216)
        self.kpb = at("kpb", [128, H * M], BF16, 145408)
        self.wtf = at("wtf", [128, H * M], BF16, 149504)
        self.a_sb = at("a_sb", [128, H * 128], BF16, 153600)
        self.skb16 = at("skb16", [128, H * DK], BF16, 157696)
        self.ofm = at("ofm", [128, 16, TOWN], BF16, 0)
        self.oslab = [at(f"oslab{i}", [128, 16, 512], BF16, 32768 + i * 16384) for i in range(2)]

        self.PA = nc.alloc_psum_tensor("PA", [128, 2048], F32)
        self.PB = nc.alloc_psum_tensor("PB", [128, 2048], F32)

        self.cT = Ctr(nc, "cT")
        self.cV = Ctr(nc, "cV")
        self.cS = Ctr(nc, "cS")
        self.qS = Ctr(nc, "qS")
        self.qXT = [Ctr(nc, f"qXT{i}") for i in range(2)]
        self.qYS = [Ctr(nc, f"qYS{i}") for i in range(2)]
        self.qF = Ctr(nc, "qF")
        self.qYT = [Ctr(nc, f"qYT{i}") for i in range(2)]
        self.qSCp = [Ctr(nc, f"qSCp{i}") for i in range(2)]
        self.qWS = [Ctr(nc, f"qWS{i}") for i in range(2)]
        self.qDS = [Ctr(nc, f"qDS{i}") for i in range(2)]
        self.qES = [Ctr(nc, f"qES{i}") for i in range(2)]
        self.qED = [Ctr(nc, f"qED{i}") for i in range(2)]
        self.qXG = [Ctr(nc, f"qXG{i}") for i in range(2)]
        self.qCW = [Ctr(nc, f"qCW{i}") for i in range(2)]
        self.qCE = [Ctr(nc, f"qCE{i}") for i in range(2)]
        self.qCDW = [Ctr(nc, f"qCDW{i}") for i in range(2)]
        self.qCED = [Ctr(nc, f"qCED{i}") for i in range(2)]
        self.cG = Ctr(nc, "cG")
        self.gdummy = nc.alloc_sbuf_tensor("gdummy", [1, 8], F32)
        self.qOBs = [Ctr(nc, f"qOBs{i}") for i in range(3)]
        self.qG = Ctr(nc, "qG")
        self.qXS = [Ctr(nc, f"qXS{i}") for i in range(2)]
        self.qKV = [Ctr(nc, f"qKV{i}") for i in range(2)]
        self.qCH = [Ctr(nc, f"qCH{i}") for i in range(2)]
        self.qOS = [Ctr(nc, f"qOS{i}") for i in range(2)]
        self.qOBU = Ctr(nc, "qOBU")
        self.qOF = Ctr(nc, "qOF")

        self.dbg_aps = {}
        shp_map = {"y_dram": ([TEXT, D], BF16), "x1_dram": ([TEXT, D], F32),
                   "xbuf": ([NER + 1, D], BF16), "outbuf": ([NER + 1, D], F32),
                   "x2_dram": ([TOWN, D], F32), "obuf": ([TOWN, D], BF16),
                   "qfm_d": ([D, TEXT], BF16), "kfm_d": ([D, TEXT], BF16),
                   "vfm_d": ([D, TEXT], BF16), "wfm_d": ([D, TEXT], BF16)}
        for name in dbg:
            shp, dt = shp_map[name]
            self.dbg_aps[name] = nc.dram_tensor("dbg_" + name, shp, dt, kind="ExternalOutput").ap()

    def t(self, fn): self.ops["t"].append(fn)
    def v(self, fn): self.ops["v"].append(fn)
    def s(self, fn): self.ops["s"].append(fn)
    def g(self, fn): self.ops["g"].append(fn)
    def y(self, fn): self.ops["y"].append(fn)

    def emit_setup(self):
        S = self
        for dst, src in [(S.identb, S.C_IDENT), (S.uinc, S.C_UINC), (S.ustrict, S.C_USTR),
                         (S.ones1x, S.C_ONES1), (S.onesc, S.C_ONESC), (S.maskrep, S.C_MASKR),
                         (S.iota8p, S.C_IOTAP), (S.iota8, S.C_IOTA8)]:
            def f(g, dst=dst, src=src):
                g.dma_start(out=dst[:], in_=src[:, :]).then_inc(S.qS.h, 16)
            S.g(f); S.qS.bump(16)

        def f(g):
            g.dma_start(out=S.rt_sb[:], in_=S.RtA.rearrange("(c p) e -> p c e", p=128)).then_inc(S.qS.h, 16)
            g.dma_start(out=S.convw[:, 0], in_=S.cqA.rearrange("(c p) k -> p c k", p=128)).then_inc(S.qS.h, 16)
            g.dma_start(out=S.convw[:, 1], in_=S.ckA.rearrange("(c p) k -> p c k", p=128)).then_inc(S.qS.h, 16)
            g.dma_start(out=S.convw[:, 2], in_=S.cvA.rearrange("(c p) k -> p c k", p=128)).then_inc(S.qS.h, 16)
            g.dma_start(out=S.wf2_sb[:], in_=S.Wf2A[:, :]).then_inc(S.qS.h, 16)
            g.dma_start(out=S.wg2_sb[:], in_=S.Wg2A[:, :]).then_inc(S.qS.h, 16)
            g.dma_start(out=S.wb_sb[:], in_=S.WbA.rearrange("(c p) e -> p c e", p=128)).then_inc(S.qS.h, 16)
            g.dma_start(out=S.wf1_sb[:], in_=S.Wf1A.rearrange("(c p) m -> p c m", p=128)).then_inc(S.qS.h, 16)
            g.dma_start(out=S.wg1_sb[:], in_=S.Wg1A.rearrange("(c p) m -> p c m", p=128)).then_inc(S.qS.h, 16)
            g.dma_start(out=S.nwrow[:], in_=S.normwA.rearrange("(o f) -> o f", o=1)).then_inc(S.qS.h, 16)
            g.dma_start(out=S.wv_sb[:], in_=S.wvalidA[:, :]).then_inc(S.qS.h, 16)
        S.g(f); S.qS.bump(16 * 11)

        def f(v):
            for ci, val in enumerate(S.CVALS):
                v.memset(S.cpool[:, ci:ci + 1], val)
            v.memset(S.sqbuf[0:1, :], 0.0).then_inc(S.cV.h, 1)
        S.v(f); zv = S.cV.bump()

        def f(g, zv=zv):
            g.wait_ge(S.cV.h, zv)
            g.dma_start(out=S.outbuf[TRASH:TRASH + 1, :], in_=S.sqbuf[0:1, :]).then_inc(S.qS.h, 16)
        S.g(f); S.qS.bump(16)
        self.setup_q = S.qS.n

        def f(t):
            t.wait_ge(S.qS.h, S.setup_q)
            t.matmul(S.PB[:, 0:128], S.ones1x[:, :], S.nwrow[:, :], start=True, stop=True)
            t.matmul(S.PB[:, 512:513], S.ones1x[:, :], S.wv_sb[:, :],
                     start=True, stop=True).then_inc(S.cT.h, 1)
        S.t(f); tv = S.cT.bump()

        def f(v, tv=tv):
            v.wait_ge(S.cT.h, tv)
            v.tensor_copy(S.nw128[:], S.PB[:, 0:128])
            v.tensor_copy(S.wv_col[:], S.PB[:, 512:513])
            v.drain()
            v.tensor_scalar(S.tcol[:], S.wv_col[:], -float(TRASH), float(TRASH), ALU.mult, ALU.add)
            v.engine_nop().then_inc(S.cV.h, 1)
        S.v(f); S.cV.bump()
        self.setup_v = S.cV.n
        self.setup_t = tv

    def emit_barrier(self, ctrs):
        S = self
        waits = [(c.h, c.n) for c in ctrs if c.n]
        gprev = S.cG.n

        def f(g, waits=waits, gprev=gprev):
            g.wait_ge(S.cG.h, gprev)
            for h, n in waits:
                g.wait_ge(h, n)
            g.memset(S.gdummy[:], 0.0).then_inc(S.cG.h, 1)
        S.g(f)
        return S.cG.bump()

    def emit_ln(self, x_src, T, src_wait=None):
        S = self
        nt = (T + 127) // 128
        ytm_store = [S.qYS[0].n, S.qYS[1].n]
        for i in range(nt):
            r0 = i * 128
            P = min(128, T - r0)
            b = i % 2
            xtile = S.xt[b]
            vprev = S.cV.n

            def f(y, xtile=xtile, r0=r0, P=P, b=b, vprev=vprev, src_wait=src_wait,
                  xsw=S.qXS[b].n):
                y.wait_ge(S.cV.h, vprev)
                y.wait_ge(S.qXS[b].h, xsw)
                for hh, nn in (src_wait or []):
                    y.wait_ge(hh, nn)
                y.dma_start(out=xtile[0:P, :], in_=x_src[r0:r0 + P, :]).then_inc(S.qXT[b].h, 16)
            S.y(f); xv = S.qXT[b].bump(16)

            def f(s, xtile=xtile, P=P, b=b, xv=xv, vprev=vprev):
                s.wait_ge(S.qXT[b].h, xv)
                s.wait_ge(S.cV.h, vprev)
                s.wait_ge(S.qS.h, S.setup_q)
                s.activation(S.sqbuf[0:P, :], xtile[0:P, :], AF.Square).then_inc(S.cS.h, 1)
            S.s(f); sv = S.cS.bump()

            def f(v, xtile=xtile, P=P, b=b, xv=xv, sv=sv):
                v.wait_ge(S.qXT[b].h, xv)
                v.reduce_sum(S.t_a[0:P, :], xtile[0:P, :], axis=AX.X)
                v.wait_ge(S.cS.h, sv)
                v.reduce_sum(S.t_b[0:P, :], S.sqbuf[0:P, :], axis=AX.X)
                v.drain()
                v.tensor_scalar(S.t_a[0:P, :], S.t_a[0:P, :], 1.0 / D, None, ALU.mult)
                v.tensor_scalar(S.t_b[0:P, :], S.t_b[0:P, :], 1.0 / D, None, ALU.mult)
                v.drain()
                v.tensor_tensor(S.t_mr[0:P, :], S.t_a[0:P, :], S.t_a[0:P, :], ALU.mult)
                v.drain()
                v.tensor_tensor(S.t_b[0:P, :], S.t_b[0:P, :], S.t_mr[0:P, :], ALU.subtract)
                v.engine_nop().then_inc(S.cV.h, 1)
            S.v(f); vv = S.cV.bump()

            def f(s, P=P, vv=vv):
                s.wait_ge(S.cV.h, vv)
                s.activation(S.t_rs[0:P, :], S.t_b[0:P, :], AF.Ln, bias=float(EPS))
                s.drain()
                s.activation(S.t_rs[0:P, :], S.t_rs[0:P, :], AF.Exp, scale=-0.5).then_inc(S.cS.h, 1)
            S.s(f); sv2 = S.cS.bump()

            def f(v, xtile=xtile, P=P, b=b, sv2=sv2, ytw=S.qSCp[b].n, ysw=ytm_store[b]):
                v.wait_ge(S.cS.h, sv2)
                v.wait_ge(S.qSCp[b].h, ytw)
                v.wait_ge(S.qYS[b].h, ysw)
                v.tensor_tensor(S.t_mr[0:P, :], S.t_a[0:P, :], S.t_rs[0:P, :], ALU.mult)
                v.drain()
                v.tensor_scalar(S.t_mr[0:P, :], S.t_mr[0:P, :], -1.0, None, ALU.mult)
                v.drain()
                v.tensor_scalar(S.ytm_t[b][0:P, :], xtile[0:P, :], S.t_rs[0:P, :], S.t_mr[0:P, :],
                                ALU.mult, ALU.add).then_inc(S.cV.h, 1)
            S.v(f); yv = S.cV.bump()

            def f(y, b=b, r0=r0, P=P, yv=yv):
                y.wait_ge(S.cV.h, yv)
                y.dma_start(out=S.y_dram[r0:r0 + P, :], in_=S.ytm_t[b][0:P, :]).then_inc(S.qYS[b].h, 16)
            S.y(f); ytm_store[b] = S.qYS[b].bump(16)
        return (S.qYS[0].n, S.qYS[1].n)

    def emit_yfm(self, T, ydone):
        S = self
        tprev = S.cT.n
        for c in range(16):
            def f(y, c=c, ydone=ydone, tprev=tprev):
                y.wait_ge(S.qYS[0].h, ydone[0])
                y.wait_ge(S.qYS[1].h, ydone[1])
                y.wait_ge(S.cT.h, tprev)
                y.dma_start(out=S.yfm[:, c, 0:T], in_=S.y_dram[0:T, c * 128:(c + 1) * 128],
                            transpose=True).then_inc(S.qF.h, 16)
            S.y(f); S.qF.bump(16)
        return S.qF.n

    def emit_routing(self, T, yfm_done, wv_apply):
        S = self
        nt = (T + 127) // 128
        for i in range(nt):
            r0 = i * 128
            P = min(128, T - r0)
            vprev = S.cV.n

            def f(t, i=i, r0=r0, P=P, yfm_done=yfm_done, vprev=vprev):
                t.wait_ge(S.qF.h, yfm_done)
                t.wait_ge(S.cV.h, vprev)
                for c in range(15):
                    t.matmul(S.PB[0:P, 1024:1032], S.yfm[:, c, r0:r0 + P], S.rt_sb[:, c, :],
                             start=(c == 0), stop=False)
                t.matmul(S.PB[0:P, 1024:1032], S.yfm[:, 15, r0:r0 + P], S.rt_sb[:, 15, :],
                         start=False, stop=True).then_inc(S.cT.h, 1)
            S.t(f); tv = S.cT.bump()

            def f(v, i=i, P=P, tv=tv):
                v.wait_ge(S.cT.h, tv)
                v.tensor_copy(S.lgt[0:P, i, :], S.PB[0:P, 1024:1032]).then_inc(S.cV.h, 1)
            S.v(f); S.cV.bump()

        def f(v):
            v.memset(S.carry[:], 0.0).then_inc(S.cV.h, 1)
        S.v(f); S.cV.bump()

        for i in range(nt):
            r0 = i * 128
            P = min(128, T - r0)
            lgdone = S.cV.n

            def f(v, i=i, P=P, lgdone=lgdone):
                L = S.lgt[0:P, i, :]
                v.wait_ge(S.cV.h, lgdone)
                v.max(S.r_top8[0:P, :], L)
                v.drain()
                v.tensor_tensor(S.r_t8[0:P, :], L, S.r_top8[0:P, 0:1].broadcast_to([P, 8]), ALU.is_equal)
                v.drain()
                v.scalar_tensor_tensor(S.r_t8[0:P, :], S.r_t8[0:P, :], -1000.0, S.iota8p[0:P, :],
                                       ALU.mult, ALU.add)
                v.drain()
                v.tensor_reduce(S.t_sel0[0:P, :], S.r_t8[0:P, :], axis=AX.X, op=ALU.min)
                v.drain()
                v.tensor_tensor(S.r_ohA[0:P, :], S.iota8[0:P, :],
                                S.t_sel0[0:P, :].broadcast_to([P, 8]), ALU.is_equal)
                v.drain()
                v.scalar_tensor_tensor(S.r_t8[0:P, :], S.r_ohA[0:P, :], -1e30, L, ALU.mult, ALU.add)
                v.drain()
                v.max(S.r_ohs[0:P, :], S.r_t8[0:P, :])
                v.drain()
                v.tensor_tensor(S.r_ohB[0:P, :], S.r_t8[0:P, :],
                                S.r_ohs[0:P, 0:1].broadcast_to([P, 8]), ALU.is_equal)
                v.drain()
                v.scalar_tensor_tensor(S.r_ohB[0:P, :], S.r_ohB[0:P, :], -1000.0, S.iota8p[0:P, :],
                                       ALU.mult, ALU.add)
                v.drain()
                v.tensor_reduce(S.t_sel1[0:P, :], S.r_ohB[0:P, :], axis=AX.X, op=ALU.min)
                v.drain()
                v.tensor_tensor(S.r_ohB[0:P, :], S.iota8[0:P, :],
                                S.t_sel1[0:P, :].broadcast_to([P, 8]), ALU.is_equal)
                v.tensor_tensor(S.t_a[0:P, :], S.r_ohs[0:P, 0:1], S.r_top8[0:P, 0:1], ALU.subtract)
                v.drain()
                v.tensor_tensor(S.r_ohs[0:P, :], S.r_ohA[0:P, :], S.r_ohB[0:P, :], ALU.add)
                v.engine_nop().then_inc(S.cV.h, 1)
            S.v(f); vv = S.cV.bump()

            def f(s, P=P, vv=vv):
                s.wait_ge(S.cV.h, vv)
                s.activation(S.t_a[0:P, :], S.t_a[0:P, :], AF.Exp).then_inc(S.cS.h, 1)
            S.s(f); sv = S.cS.bump()

            def f(v, P=P, sv=sv):
                v.wait_ge(S.cS.h, sv)
                v.tensor_scalar(S.t_a[0:P, :], S.t_a[0:P, :], 1.0, 1.0, ALU.mult, ALU.add)
                v.drain()
                v.reciprocal(S.t_cwA[0:P, :], S.t_a[0:P, :])
                v.drain()
                v.tensor_scalar(S.t_cwB[0:P, :], S.t_cwA[0:P, :], -1.0, 1.0, ALU.mult, ALU.add)
                v.engine_nop().then_inc(S.cV.h, 1)
            S.v(f); cwv_v = S.cV.bump()

            def f(t, i=i, P=P, cwv_v=cwv_v):
                t.wait_ge(S.cV.h, cwv_v)
                t.matmul(S.PB[0:P, 1536:1544], S.ustrict[0:P, 0:P], S.r_ohs[0:P, :], start=True, stop=False)
                t.matmul(S.PB[0:P, 1536:1544], S.ones1x[0:1, 0:P], S.carry[:, :], start=False, stop=True)
                t.matmul(S.PB[0:1, 1544:1552], S.onesc[0:P, 0:1], S.r_ohs[0:P, :],
                         start=True, stop=True).then_inc(S.cT.h, 1)
            S.t(f); tv2 = S.cT.bump()

            def f(v, i=i, P=P, tv2=tv2, wv_apply=wv_apply):
                v.wait_ge(S.cT.h, tv2)
                v.tensor_copy(S.r_t8[0:P, :], S.PB[0:P, 1536:1544])
                v.drain()
                v.tensor_tensor(S.carry[:, :], S.carry[:, :], S.PB[0:1, 1544:1552], ALU.add)
                v.drain()
                v.drain()
                v.tensor_tensor(S.r_top8[0:P, :], S.r_t8[0:P, :], S.r_ohA[0:P, :], ALU.mult)
                v.drain()
                v.tensor_reduce(S.t_pos0[0:P, :], S.r_top8[0:P, :], axis=AX.X, op=ALU.add)
                v.drain()
                v.tensor_tensor(S.r_top8[0:P, :], S.r_t8[0:P, :], S.r_ohB[0:P, :], ALU.mult)
                v.drain()
                v.tensor_reduce(S.t_pos1[0:P, :], S.r_top8[0:P, :], axis=AX.X, op=ALU.add)
                v.drain()
                for k, (sel, pos, vld, cw) in enumerate(
                        [(S.t_sel0, S.t_pos0, S.t_v0, S.t_cwA), (S.t_sel1, S.t_pos1, S.t_v1, S.t_cwB)]):
                    v.tensor_single_scalar(vld[0:P, :], pos[0:P, :], float(CAP), ALU.is_lt)
                    v.scalar_tensor_tensor(S.t_b[0:P, :], sel[0:P, :], float(CAP), pos[0:P, :],
                                           ALU.mult, ALU.add)
                    v.drain()
                    v.tensor_tensor(S.t_b[0:P, :], S.t_b[0:P, :], vld[0:P, :], ALU.mult)
                    v.tensor_scalar(S.t_mr[0:P, :], vld[0:P, :], -float(TRASH), float(TRASH),
                                    ALU.mult, ALU.add)
                    v.drain()
                    v.tensor_tensor(S.t_b[0:P, :], S.t_b[0:P, :], S.t_mr[0:P, :], ALU.add)
                    v.drain()
                    if wv_apply and i == 0:
                        v.tensor_tensor(S.t_b[0:WU, :], S.t_b[0:WU, :], S.wv_col[0:WU, :], ALU.mult)
                        v.drain()
                        v.tensor_tensor(S.t_b[0:WU, :], S.t_b[0:WU, :], S.tcol[0:WU, :], ALU.add)
                        v.drain()
                    v.tensor_copy(S.off_i[k][0:P, i:i + 1], S.t_b[0:P, :])
                    v.tensor_tensor(S.t_mr[0:P, :], cw[0:P, :], vld[0:P, :], ALU.mult)
                    v.drain()
                    v.tensor_scalar(S.cwvt[k][0:P, i:i + 1], S.t_mr[0:P, :], 0.5, None, ALU.mult)
                    v.drain()
                v.engine_nop().then_inc(S.cV.h, 1)
            S.v(f); S.cV.bump()
        return S.cV.n

    def emit_scatter(self, T, ln_done, rout_done):
        S = self
        nt = (T + 127) // 128
        for i in range(nt):
            r0 = i * 128
            P = min(128, T - r0)
            b = i % 2
            scprev = S.qSC.n

            def f(g, r0=r0, P=P, b=b, ln_done=ln_done, scprev=scprev):
                g.wait_ge(S.qYS[0].h, ln_done[0])
                g.wait_ge(S.qYS[1].h, ln_done[1])
                g.wait_ge(S.qSC.h, scprev)
                g.dma_start(out=S.ytm_t[b][0:P, :], in_=S.y_dram[r0:r0 + P, :]).then_inc(S.qYT[b].h, 16)
            S.g(f); ldv = S.qYT[b].bump(16)

            def f(g, i=i, P=P, b=b, ldv=ldv, rout_done=rout_done):
                g.wait_ge(S.qYT[b].h, ldv)
                g.wait_ge(S.cV.h, rout_done)
                g.indirect_dma_start(
                    out=S.xbuf[:, :], out_offset=IOX(S.off_i[0][0:P, i:i + 1], 0),
                    in_=S.ytm_t[b][0:P, :], in_offset=None).then_inc(S.qSC.h, 16)
                g.indirect_dma_start(
                    out=S.xbuf[:, :], out_offset=IOX(S.off_i[1][0:P, i:i + 1], 0),
                    in_=S.ytm_t[b][0:P, :], in_offset=None).then_inc(S.qSC.h, 16)
            S.g(f); S.qSC.bump(32)
        return S.qSC.n

    def emit_ffn(self, T, x_src, x_dst, use_cache, wv_apply, src_wait=None):
        S = self
        nt = (T + 127) // 128
        ntf = min(nt, 8)
        has64 = (nt == 9)

        ln_done = self.emit_ln(x_src, T, src_wait)
        yfm_done = self.emit_yfm(T, ln_done)

        cache_bar = 0
        if use_cache:
            cws = [(q.h, q.n) for q in (S.qCW + S.qCE + S.qCDW + S.qCED)]

            def f(g, cws=cws, cgp=S.cG.n):
                g.wait_ge(S.cG.h, cgp)
                for h, n in cws:
                    if n:
                        g.wait_ge(h, n)
                g.memset(S.gdummy[:], 0.0).then_inc(S.cG.h, 1)
            S.g(f); cache_bar = S.cG.bump()

        state = {}

        def gen_routing():
            for i in range(nt):
                r0 = i * 128
                P = min(128, T - r0)
                vprev = S.cV.n

                def f(t, i=i, r0=r0, P=P, yfm_done=yfm_done, vprev=vprev):
                    t.wait_ge(S.qF.h, yfm_done)
                    t.wait_ge(S.cV.h, vprev)
                    for c in range(15):
                        t.matmul(S.PB[0:P, 1024:1032], S.yfm[:, c, r0:r0 + P], S.rt_sb[:, c, :],
                                 start=(c == 0), stop=False)
                    t.matmul(S.PB[0:P, 1024:1032], S.yfm[:, 15, r0:r0 + P], S.rt_sb[:, 15, :],
                             start=False, stop=True).then_inc(S.cT.h, 1)
                S.t(f); tv = S.cT.bump()

                def f(v, i=i, P=P, tv=tv):
                    v.wait_ge(S.cT.h, tv)
                    v.tensor_copy(S.lgt[0:P, i, :], S.PB[0:P, 1024:1032]).then_inc(S.cV.h, 1)
                S.v(f); S.cV.bump()
                yield

            def f(v):
                v.memset(S.carry[:], 0.0).then_inc(S.cV.h, 1)
            S.v(f); S.cV.bump()

            ytm_rel = [S.qSCp[0].n, S.qSCp[1].n]
            for i in range(nt):
                r0 = i * 128
                P = min(128, T - r0)
                lgdone = S.cV.n

                def f(v, i=i, P=P, lgdone=lgdone):
                    L = S.lgt[0:P, i, :]
                    v.wait_ge(S.cV.h, lgdone)
                    v.max(S.r_top8[0:P, :], L)
                    v.drain()
                    v.tensor_tensor(S.r_t8[0:P, :], L, S.r_top8[0:P, 0:1].broadcast_to([P, 8]), ALU.is_equal)
                    v.drain()
                    v.scalar_tensor_tensor(S.r_t8[0:P, :], S.r_t8[0:P, :], -1000.0, S.iota8p[0:P, :],
                                           ALU.mult, ALU.add)
                    v.drain()
                    v.tensor_reduce(S.t_sel0[0:P, :], S.r_t8[0:P, :], axis=AX.X, op=ALU.min)
                    v.drain()
                    v.tensor_tensor(S.r_ohA[0:P, :], S.iota8[0:P, :],
                                    S.t_sel0[0:P, :].broadcast_to([P, 8]), ALU.is_equal)
                    v.drain()
                    v.scalar_tensor_tensor(S.r_t8[0:P, :], S.r_ohA[0:P, :], -1e30, L, ALU.mult, ALU.add)
                    v.drain()
                    v.max(S.r_ohs[0:P, :], S.r_t8[0:P, :])
                    v.drain()
                    v.tensor_tensor(S.r_ohB[0:P, :], S.r_t8[0:P, :],
                                    S.r_ohs[0:P, 0:1].broadcast_to([P, 8]), ALU.is_equal)
                    v.drain()
                    v.scalar_tensor_tensor(S.r_ohB[0:P, :], S.r_ohB[0:P, :], -1000.0, S.iota8p[0:P, :],
                                           ALU.mult, ALU.add)
                    v.drain()
                    v.tensor_reduce(S.t_sel1[0:P, :], S.r_ohB[0:P, :], axis=AX.X, op=ALU.min)
                    v.drain()
                    v.tensor_tensor(S.r_ohB[0:P, :], S.iota8[0:P, :],
                                    S.t_sel1[0:P, :].broadcast_to([P, 8]), ALU.is_equal)
                    v.tensor_tensor(S.t_a[0:P, :], S.r_ohs[0:P, 0:1], S.r_top8[0:P, 0:1], ALU.subtract)
                    v.drain()
                    v.tensor_tensor(S.r_ohs[0:P, :], S.r_ohA[0:P, :], S.r_ohB[0:P, :], ALU.add)
                    v.engine_nop().then_inc(S.cV.h, 1)
                S.v(f); vv = S.cV.bump()

                def f(s, P=P, vv=vv):
                    s.wait_ge(S.cV.h, vv)
                    s.activation(S.t_a[0:P, :], S.t_a[0:P, :], AF.Exp).then_inc(S.cS.h, 1)
                S.s(f); sv = S.cS.bump()

                def f(v, P=P, sv=sv):
                    v.wait_ge(S.cS.h, sv)
                    v.tensor_scalar(S.t_a[0:P, :], S.t_a[0:P, :], 1.0, 1.0, ALU.mult, ALU.add)
                    v.drain()
                    v.reciprocal(S.t_cwA[0:P, :], S.t_a[0:P, :])
                    v.drain()
                    v.tensor_scalar(S.t_cwB[0:P, :], S.t_cwA[0:P, :], -1.0, 1.0, ALU.mult, ALU.add)
                    v.engine_nop().then_inc(S.cV.h, 1)
                S.v(f); cwv_v = S.cV.bump()

                def f(t, i=i, P=P, cwv_v=cwv_v):
                    t.wait_ge(S.cV.h, cwv_v)
                    t.matmul(S.PB[0:P, 1536:1544], S.ustrict[0:P, 0:P], S.r_ohs[0:P, :],
                             start=True, stop=False)
                    t.matmul(S.PB[0:P, 1536:1544], S.ones1x[0:1, 0:P], S.carry[:, :],
                             start=False, stop=True)
                    t.matmul(S.PB[0:1, 1544:1552], S.onesc[0:P, 0:1], S.r_ohs[0:P, :],
                             start=True, stop=True).then_inc(S.cT.h, 1)
                S.t(f); tv2 = S.cT.bump()

                def f(v, i=i, P=P, tv2=tv2, wv_apply=wv_apply):
                    v.wait_ge(S.cT.h, tv2)
                    v.tensor_copy(S.r_t8[0:P, :], S.PB[0:P, 1536:1544])
                    v.drain()
                    v.tensor_tensor(S.carry[:, :], S.carry[:, :], S.PB[0:1, 1544:1552], ALU.add)
                    v.drain()
                    v.tensor_tensor(S.r_top8[0:P, :], S.r_t8[0:P, :], S.r_ohA[0:P, :], ALU.mult)
                    v.drain()
                    v.tensor_reduce(S.t_pos0[0:P, :], S.r_top8[0:P, :], axis=AX.X, op=ALU.add)
                    v.drain()
                    v.tensor_tensor(S.r_top8[0:P, :], S.r_t8[0:P, :], S.r_ohB[0:P, :], ALU.mult)
                    v.drain()
                    v.tensor_reduce(S.t_pos1[0:P, :], S.r_top8[0:P, :], axis=AX.X, op=ALU.add)
                    v.drain()
                    for k, (sel, pos, vld, cw) in enumerate(
                            [(S.t_sel0, S.t_pos0, S.t_v0, S.t_cwA), (S.t_sel1, S.t_pos1, S.t_v1, S.t_cwB)]):
                        v.tensor_single_scalar(vld[0:P, :], pos[0:P, :], float(CAP), ALU.is_lt)
                        v.scalar_tensor_tensor(S.t_b[0:P, :], sel[0:P, :], float(CAP), pos[0:P, :],
                                               ALU.mult, ALU.add)
                        v.drain()
                        v.tensor_tensor(S.t_b[0:P, :], S.t_b[0:P, :], vld[0:P, :], ALU.mult)
                        v.tensor_scalar(S.t_mr[0:P, :], vld[0:P, :], -float(TRASH), float(TRASH),
                                        ALU.mult, ALU.add)
                        v.drain()
                        v.tensor_tensor(S.t_b[0:P, :], S.t_b[0:P, :], S.t_mr[0:P, :], ALU.add)
                        v.drain()
                        if wv_apply and i == 0:
                            v.tensor_tensor(S.t_b[0:WU, :], S.t_b[0:WU, :], S.wv_col[0:WU, :], ALU.mult)
                            v.drain()
                            v.tensor_tensor(S.t_b[0:WU, :], S.t_b[0:WU, :], S.tcol[0:WU, :], ALU.add)
                            v.drain()
                        v.tensor_copy(S.off_i[k][0:P, i:i + 1], S.t_b[0:P, :])
                        v.tensor_tensor(S.t_mr[0:P, :], cw[0:P, :], vld[0:P, :], ALU.mult)
                        v.drain()
                        v.tensor_scalar(S.cwvt[k][0:P, i:i + 1], S.t_mr[0:P, :], 0.5, None, ALU.mult)
                        v.drain()
                    v.engine_nop().then_inc(S.cV.h, 1)
                S.v(f); rd = S.cV.bump()

                b = i % 2

                def f(g, r0=r0, P=P, b=b, ln_done=ln_done, yrel=ytm_rel[b]):
                    g.wait_ge(S.qYS[0].h, ln_done[0])
                    g.wait_ge(S.qYS[1].h, ln_done[1])
                    g.wait_ge(S.qSCp[b].h, yrel)
                    g.dma_start(out=S.ytm_t[b][0:P, :], in_=S.y_dram[r0:r0 + P, :]).then_inc(S.qYT[b].h, 16)
                S.g(f); ldv = S.qYT[b].bump(16)

                def f(g, i=i, P=P, b=b, ldv=ldv, rd=rd):
                    g.wait_ge(S.qYT[b].h, ldv)
                    g.wait_ge(S.cV.h, rd)
                    g.indirect_dma_start(
                        out=S.xbuf[:, :], out_offset=IOX(S.off_i[0][0:P, i:i + 1], 0),
                        in_=S.ytm_t[b][0:P, :], in_offset=None).then_inc(S.qSCp[b].h, 16)
                    g.indirect_dma_start(
                        out=S.xbuf[:, :], out_offset=IOX(S.off_i[1][0:P, i:i + 1], 0),
                        in_=S.ytm_t[b][0:P, :], in_offset=None).then_inc(S.qSCp[b].h, 16)
                S.g(f); ytm_rel[b] = S.qSCp[b].bump(32)
                yield
            state["scat_done"] = [(S.qSCp[0].h, S.qSCp[0].n), (S.qSCp[1].h, S.qSCp[1].n)]

        def gen_shared():
            HB = FJ // 2
            xs_store = state["xs_store"]
            slab_mm = [S.cT.n, S.cT.n]
            slab_cw = [S.qCW[0].n, S.qCW[1].n]
            t8_ev = 0
            for hf in range(2):
                ev_par = [[(S.cV.h, S.cV.n), (S.cS.h, S.cS.n)] for _ in range(2)]
                ev64 = [(S.cV.h, t8_ev)] if t8_ev else []
                for phase in range(2):
                    for k2 in range(11):
                        b = k2 % 2

                        def f(g, b=b, phase=phase, hf=hf, k2=k2, sfm=slab_mm[b], sfc=slab_cw[b],
                              cb=cache_bar):
                            g.wait_ge(S.cT.h, sfm)
                            g.wait_ge(S.qCW[b].h, sfc)
                            if cb:
                                g.wait_ge(S.cG.h, cb)
                            g.dma_start(out=S.wslab[b][:],
                                        in_=S.SugA[phase, hf, k2]).then_inc(S.qWS[b].h, 16)
                        S.g(f); wld = S.qWS[b].bump(16)

                        for sub in range(2):
                            j = k2 * 2 + sub
                            par = j % 2

                            def f(t, b=b, sub=sub, par=par, wld=wld, epw=list(ev_par[par]),
                                  e64=list(ev64), has64=has64):
                                t.wait_ge(S.qWS[b].h, wld)
                                for hh, nn in epw:
                                    t.wait_ge(hh, nn)
                                for c in range(16):
                                    last = (c == 15)
                                    wsl = S.wslab[b][:, c, sub * 128:(sub + 1) * 128]
                                    t.matmul(S.PA[:, par * 1024: par * 1024 + 512], wsl,
                                             S.yfm[:, c, 0:512], start=(c == 0), stop=last)
                                    mm2 = t.matmul(S.PA[:, par * 1024 + 512: par * 1024 + 1024], wsl,
                                                   S.yfm[:, c, 512:1024], start=(c == 0), stop=last)
                                if has64:
                                    for hh, nn in e64:
                                        t.wait_ge(hh, nn)
                                    for c in range(16):
                                        mm2 = t.matmul(S.PB[:, 0:64], S.wslab[b][:, c, sub * 128:(sub + 1) * 128],
                                                       S.yfm[:, c, 1024:1088], start=(c == 0), stop=(c == 15))
                                mm2.then_inc(S.cT.h, 1)
                            S.t(f); mmv = S.cT.bump()
                            slab_mm[b] = mmv

                            if phase == 0:
                                def f(s, j=j, par=par, mmv=mmv, has64=has64):
                                    s.wait_ge(S.cT.h, mmv)
                                    s.activation(S.hsh[:, j, 0:512], S.PA[:, par * 1024: par * 1024 + 512], AF.Silu)
                                    act2 = s.activation(S.hsh[:, j, 512:1024],
                                                        S.PA[:, par * 1024 + 512: par * 1024 + 1024], AF.Silu)
                                    if has64:
                                        act2 = s.activation(S.hsh[:, j, 1024:1088], S.PB[:, 0:64], AF.Silu)
                                    act2.then_inc(S.cS.h, 1)
                                S.s(f); ev = (S.cS.h, S.cS.bump())
                            else:
                                def f(v, j=j, par=par, mmv=mmv, has64=has64):
                                    v.wait_ge(S.cT.h, mmv)
                                    v.tensor_tensor(S.hsh[:, j, 0:512], S.PA[:, par * 1024: par * 1024 + 512],
                                                    S.hsh[:, j, 0:512], ALU.mult)
                                    tt2 = v.tensor_tensor(S.hsh[:, j, 512:1024],
                                                          S.PA[:, par * 1024 + 512: par * 1024 + 1024],
                                                          S.hsh[:, j, 512:1024], ALU.mult)
                                    if has64:
                                        tt2 = v.tensor_tensor(S.hsh[:, j, 1024:1088], S.PB[:, 0:64],
                                                              S.hsh[:, j, 1024:1088], ALU.mult)
                                    tt2.then_inc(S.cV.h, 1)
                                S.v(f); ev = (S.cV.h, S.cV.bump())
                            ev_par[par] = [ev]
                            ev64 = [ev]
                        yield "up"
                up_done = S.cV.n
                up_done_s = S.cS.n

                dslab_mm = [S.cT.n, S.cT.n]
                dslab_cw = [S.qCDW[0].n, S.qCDW[1].n]
                ev_pa = [(S.cV.h, S.cV.n), (S.cS.h, S.cS.n)]
                batches = [list(range(0, min(4, ntf)))] + ([list(range(4, ntf))] if ntf > 4 else [])
                chunk_sets = [list(range(k, min(k + 4, 22))) for k in range(0, 22, 4)]
                for dg in range(4):
                    for bi, tts in enumerate(batches):
                        with_t8 = has64 and bi == len(batches) - 1
                        set_mm = []
                        for si, cs in enumerate(chunk_sets):
                            b = si % 2
                            use_blk = use_cache or bi > 0

                            def f(g, b=b, cs=cs, dg=dg, hf=hf, sfm=dslab_mm[b], sfc=dslab_cw[b],
                                  use_blk=use_blk, up_done=up_done, cb=cache_bar):
                                g.wait_ge(S.cT.h, sfm)
                                g.wait_ge(S.qCDW[b].h, sfc)
                                g.wait_ge(S.cV.h, up_done)
                                if cb:
                                    g.wait_ge(S.cG.h, cb)
                                g.dma_start(out=S.dslab[b][:, 0:len(cs), :],
                                            in_=S.SdnA[hf, dg][:, cs[0]:cs[0] + len(cs), :]
                                            ).then_inc(S.qDS[b].h, 16)
                            S.g(f); wld = S.qDS[b].bump(16)

                            def f(t, b=b, cs=cs, si=si, wld=wld, tts=tts, epa=list(ev_pa),
                                  ups=up_done_s, nsets=len(chunk_sets), with_t8=with_t8):
                                t.wait_ge(S.qDS[b].h, wld)
                                if si == 0:
                                    t.wait_ge(S.cS.h, ups)
                                    for hh, nn in epa:
                                        t.wait_ge(hh, nn)
                                for ci, fc in enumerate(cs):
                                    for ti, tt in enumerate(tts):
                                        ps = S.PA[:, ti * 512:(ti + 1) * 512]
                                        mm = t.matmul(ps, S.hsh[:, fc, tt * 128:(tt + 1) * 128],
                                                      S.dslab[b][:, ci, :],
                                                      start=(si == 0 and ci == 0),
                                                      stop=(si == nsets - 1 and ci == len(cs) - 1),
                                                      skip_group_check=True)
                                    if with_t8:
                                        mm = t.matmul(S.PB[0:64, 0:512], S.hsh[:, fc, 1024:1088],
                                                      S.dslab[b][:, ci, :],
                                                      start=(si == 0 and ci == 0),
                                                      stop=(si == nsets - 1 and ci == len(cs) - 1),
                                                      skip_group_check=True)
                                mm.then_inc(S.cT.h, 1)
                            S.t(f); setv = S.cT.bump()
                            set_mm.append(setv)
                            dslab_mm[b] = setv
                        mmv = set_mm[-1]

                        base_src = x_src if hf == 0 else x_dst
                        bsw = (src_wait or []) if hf == 0 else state["x_written"]
                        ev_pa = []
                        rtl = list(tts) + ([8] if with_t8 else [])
                        for ti, tt in enumerate(rtl):
                            is8 = tt == 8
                            P = 64 if is8 else 128
                            xb = tt % 2
                            vprev = S.cV.n

                            def f(y, tt=tt, P=P, xb=xb, dg=dg, base_src=base_src, vprev=vprev,
                                  bsw=bsw, xsw=xs_store[xb]):
                                y.wait_ge(S.cV.h, vprev)
                                y.wait_ge(S.qXS[xb].h, xsw)
                                for hh, nn in bsw:
                                    y.wait_ge(hh, nn)
                                y.dma_start(out=S.xt[xb][0:P, 0:512],
                                            in_=base_src[tt * 128:tt * 128 + P, dg * 512:(dg + 1) * 512]
                                            ).then_inc(S.qXT[xb].h, 16)
                            S.y(f); xld = S.qXT[xb].bump(16)

                            def f(v, ti=ti, P=P, is8=is8, xb=xb, mmv=mmv, xld=xld):
                                v.wait_ge(S.cT.h, mmv)
                                v.wait_ge(S.qXT[xb].h, xld)
                                ps = S.PB[0:64, 0:512] if is8 else S.PA[0:P, ti * 512:(ti + 1) * 512]
                                v.scalar_tensor_tensor(S.xt[xb][0:P, 0:512], ps,
                                                       0.5, S.xt[xb][0:P, 0:512],
                                                       ALU.mult, ALU.add).then_inc(S.cV.h, 1)
                            S.v(f); vv = S.cV.bump()
                            ev_pa.append((S.cV.h, vv))
                            if is8:
                                t8_ev = vv

                            def f(y, tt=tt, P=P, xb=xb, dg=dg, vv=vv):
                                y.wait_ge(S.cV.h, vv)
                                y.dma_start(out=x_dst[tt * 128:tt * 128 + P, dg * 512:(dg + 1) * 512],
                                            in_=S.xt[xb][0:P, 0:512]).then_inc(S.qXS[xb].h, 16)
                            S.y(f); xs_store[xb] = S.qXS[xb].bump(16)
                        ev_pa = [ev_pa[-1]]
                        yield "down"
                state["x_written"] = [(S.qXS[0].h, S.qXS[0].n), (S.qXS[1].h, S.qXS[1].n)]

        def gen_experts():
            xg_mm = S.cT.n
            eslab_mm = [S.cT.n, S.cT.n]
            edslab_mm = [S.cT.n, S.cT.n]
            estg_free = [S.qOBs[0].n, S.qOBs[1].n]
            estg_ix = 0
            ev_sub = [[], []]
            ev_d = []
            scd = state["scat_done"]
            xgld = S.qXG[0].n
            for e in range(E):
                def f(y, e=e, scd=list(scd), xf=xg_mm):
                    for hh, nn in scd:
                        y.wait_ge(hh, nn)
                    y.wait_ge(S.cT.h, xf)
                    for c in range(16):
                        y.dma_start(out=S.xg[0][:, c, :],
                                    in_=S.xbuf[e * CAP:(e + 1) * CAP, c * 128:(c + 1) * 128],
                                    transpose=True).then_inc(S.qXG[0].h, 16)
                S.y(f); xgld = S.qXG[0].bump(16 * 16)
                yield

                for sl in range(11):
                    eb = sl % 2

                    def f(g, e=e, sl=sl, eb=eb, sfm=eslab_mm[eb],
                          scw=(list(scd) if e == 0 and sl < 2 else [])):
                        g.wait_ge(S.cT.h, sfm)
                        for hh, nn in scw:
                            g.wait_ge(hh, nn)
                        g.dma_start(out=S.eslab[eb][:],
                                    in_=S.WugA[e, sl]).then_inc(S.qES[eb].h, 16)
                    S.g(f); wld = S.qES[eb].bump(16)

                    for jj in range(2):
                        j = sl * 2 + jj
                        is_gate = j < 11
                        pcol = 512 if j % 2 == 0 else 1024

                        def f(t, eb=eb, jj=jj, pcol=pcol, wld=wld, xgld=xgld,
                              epw=list(ev_sub[j % 2])):
                            t.wait_ge(S.qES[eb].h, wld)
                            t.wait_ge(S.qXG[0].h, xgld)
                            for hh, nn in epw:
                                t.wait_ge(hh, nn)
                            for c in range(16):
                                mm = t.matmul(S.PB[:, pcol:pcol + CAP],
                                              S.eslab[eb][:, c, jj * 128:(jj + 1) * 128],
                                              S.xg[0][:, c, :], start=(c == 0), stop=(c == 15))
                            mm.then_inc(S.cT.h, 1)
                        S.t(f); mmv = S.cT.bump()
                        xg_mm = mmv
                        eslab_mm[eb] = mmv

                        if is_gate:
                            def f(s, j=j, pcol=pcol, mmv=mmv):
                                s.wait_ge(S.cT.h, mmv)
                                s.activation(S.he[:, j, :], S.PB[:, pcol:pcol + CAP],
                                             AF.Silu).then_inc(S.cS.h, 1)
                            S.s(f); ev_sub[j % 2] = [(S.cS.h, S.cS.bump())]
                        else:
                            def f(v, j=j, pcol=pcol, mmv=mmv):
                                v.wait_ge(S.cT.h, mmv)
                                v.tensor_tensor(S.he[:, j - 11, :], S.PB[:, pcol:pcol + CAP],
                                                S.he[:, j - 11, :], ALU.mult).then_inc(S.cV.h, 1)
                            S.v(f); ev_sub[j % 2] = [(S.cV.h, S.cV.bump())]
                    if sl % 2 == 1 or sl == 10:
                        yield
                he_done = S.cV.n
                he_done_s = S.cS.n

                for dg in range(4):
                    db = dg % 2

                    def f(g, e=e, dg=dg, db=db, sfm=edslab_mm[db],
                          scw=(list(scd) if e == 0 and dg < 2 else [])):
                        g.wait_ge(S.cT.h, sfm)
                        for hh, nn in scw:
                            g.wait_ge(hh, nn)
                        g.dma_start(out=S.edslab[db][:],
                                    in_=S.WdnA[e, dg]).then_inc(S.qED[db].h, 16)
                    S.g(f); wld = S.qED[db].bump(16)

                    for st in range(3):
                        SS = 128 if st < 2 else 96

                        def f(t, st=st, SS=SS, db=db, wld=wld, epw=list(ev_d), hs=he_done_s,
                              hv=he_done):
                            t.wait_ge(S.qED[db].h, wld)
                            t.wait_ge(S.cS.h, hs)
                            t.wait_ge(S.cV.h, hv)
                            for hh, nn in epw:
                                t.wait_ge(hh, nn)
                            for fc in range(11):
                                mm = t.matmul(S.PB[0:SS, 1536:2048],
                                              S.he[:, fc, st * 128: st * 128 + SS],
                                              S.edslab[db][:, fc, :], start=(fc == 0), stop=(fc == 10))
                            mm.then_inc(S.cT.h, 1)
                        S.t(f); mmv = S.cT.bump()
                        edslab_mm[db] = mmv

                        eix = estg_ix % 2
                        estg_ix += 1
                        if eix == 0:
                            def f(v, SS=SS, mmv=mmv, ef=estg_free[0]):
                                v.wait_ge(S.cT.h, mmv)
                                v.wait_ge(S.qOBs[0].h, ef)
                                v.tensor_copy(S.estg[0][0:SS, :],
                                              S.PB[0:SS, 1536:2048]).then_inc(S.cV.h, 1)
                            S.v(f); ev_d = [(S.cV.h, S.cV.bump())]
                        else:
                            def f(s, SS=SS, mmv=mmv, ef=estg_free[1]):
                                s.wait_ge(S.cT.h, mmv)
                                s.wait_ge(S.qOBs[1].h, ef)
                                s.activation(S.estg[1][0:SS, :],
                                             S.PB[0:SS, 1536:2048], AF.Copy).then_inc(S.cS.h, 1)
                            S.s(f); ev_d = [(S.cS.h, S.cS.bump())]
                        evh, evv = ev_d[0]

                        def f(y, e=e, st=st, SS=SS, dg=dg, eix=eix, evh=evh, evv=evv):
                            y.wait_ge(evh, evv)
                            y.dma_start(out=S.outbuf[e * CAP + st * 128: e * CAP + st * 128 + SS,
                                                     dg * 512:(dg + 1) * 512],
                                        in_=S.estg[eix][0:SS, :]).then_inc(S.qOBs[eix].h, 16)
                        S.y(f); estg_free[eix] = S.qOBs[eix].bump(16)
                    yield
            state["experts_done"] = [(S.qOBs[0].h, S.qOBs[0].n), (S.qOBs[1].h, S.qOBs[1].n)]

        state["xs_store"] = [S.qXS[0].n, S.qXS[1].n]
        gr = gen_routing()
        gs = gen_shared()
        ge = gen_experts()
        alive_r, alive_s, alive_e = True, True, False
        started_e = False
        while alive_r or alive_s or started_e:
            if alive_r:
                try:
                    next(gr); 
                    try:
                        next(gr)
                    except StopIteration:
                        alive_r = False
                except StopIteration:
                    alive_r = False
                if not alive_r:
                    started_e = True
            if alive_s:
                try:
                    next(gs)
                except StopIteration:
                    alive_s = False
            if started_e:
                try:
                    next(ge)
                except StopIteration:
                    started_e = False
            if not (alive_r or alive_s or started_e):
                break

        xs_store = state["xs_store"]
        x_written = state["x_written"]
        experts_done = state["experts_done"]
        for i in range(nt):
            r0 = i * 128
            P = min(128, T - r0)
            xb = i % 2
            gprev = S.cV.n

            def f(g, i=i, P=P, experts_done=experts_done, gprev=gprev,
                  cwf=[(q.h, q.n) for q in (S.qCE + S.qCED)]):
                for hh, nn in experts_done:
                    g.wait_ge(hh, nn)
                for hh, nn in cwf:
                    if nn:
                        g.wait_ge(hh, nn)
                g.wait_ge(S.cV.h, gprev)
                g.indirect_dma_start(
                    out=S.g0t[0:P, :], out_offset=None,
                    in_=S.outbuf[:, :], in_offset=IOX(S.off_i[0][0:P, i:i + 1], 0)).then_inc(S.qG.h, 16)
                g.indirect_dma_start(
                    out=S.g1t[0:P, :], out_offset=None,
                    in_=S.outbuf[:, :], in_offset=IOX(S.off_i[1][0:P, i:i + 1], 0)).then_inc(S.qG.h, 16)
            S.g(f); gld = S.qG.bump(32)

            def f(y, r0=r0, P=P, xb=xb, gprev=gprev, xww=list(x_written) + list(experts_done),
                  xsw=xs_store[xb]):
                y.wait_ge(S.cV.h, gprev)
                y.wait_ge(S.qXS[xb].h, xsw)
                for hh, nn in xww:
                    y.wait_ge(hh, nn)
                y.dma_start(out=S.xt[xb][0:P, :], in_=x_dst[r0:r0 + P, :]).then_inc(S.qXT[xb].h, 16)
            S.y(f); xld = S.qXT[xb].bump(16)

            def f(v, i=i, P=P, xb=xb, gld=gld, xld=xld):
                v.wait_ge(S.qG.h, gld)
                v.wait_ge(S.qXT[xb].h, xld)
                v.scalar_tensor_tensor(S.xt[xb][0:P, :], S.g0t[0:P, :], S.cwvt[0][0:P, i:i + 1],
                                       S.xt[xb][0:P, :], ALU.mult, ALU.add)
                v.drain()
                v.scalar_tensor_tensor(S.xt[xb][0:P, :], S.g1t[0:P, :], S.cwvt[1][0:P, i:i + 1],
                                       S.xt[xb][0:P, :], ALU.mult, ALU.add).then_inc(S.cV.h, 1)
            S.v(f); vv = S.cV.bump()

            def f(y, r0=r0, P=P, xb=xb, vv=vv):
                y.wait_ge(S.cV.h, vv)
                y.dma_start(out=x_dst[r0:r0 + P, :], in_=S.xt[xb][0:P, :]).then_inc(S.qXS[xb].h, 16)
            S.y(f); xs_store[xb] = S.qXS[xb].bump(16)
        return [(S.qXS[0].h, S.qXS[0].n), (S.qXS[1].h, S.qXS[1].n)]

    def emit_attn(self, x1_wait):
        S = self
        T = TEXT
        bar0 = self.emit_barrier(self.all_sems())
        x1_wait = list(x1_wait) + [(S.cG.h, bar0)]
        ln_done = self.emit_ln(S.x1_dram, T, x1_wait)
        yfm_done = self.emit_yfm(T, ln_done)

        b3bar = self.emit_barrier([S.qCW[0], S.qCW[1], S.qCDW[0], S.qCDW[1],
                                   S.qCE[0], S.qCE[1], S.qCED[0], S.qCED[1],
                                   S.qDS[0], S.qDS[1], S.qYT[0], S.qYT[1]])
        def f(v, b3bar=b3bar):
            v.wait_ge(S.cG.h, b3bar)
            v.memset(S.conv_z[0][:, 0:3], 0.0)
            v.memset(S.conv_z[1][:, 0:3], 0.0).then_inc(S.cV.h, 1)
        S.v(f); padz = S.cV.bump()

        projs = [(S.WqA, S.qfm_d, 0), (S.WkA, S.kfm_d, 1), (S.WvA, S.vfm_d, 2), (S.WwA, S.wfm_d, 2)]
        slab_mm = [S.cT.n, S.cT.n]
        ev_par = [[], []]
        for wi, (Wsrc, dst_dram, cslot) in enumerate(projs):
            for k4 in range(4):
                gi4 = wi * 4 + k4
                b = gi4 % 2

                def f(g, Wsrc=Wsrc, k4=k4, b=b, sfm=slab_mm[b]):
                    g.wait_ge(S.cT.h, sfm)
                    g.dma_start(out=S.wslab4[b][:],
                                in_=Wsrc[k4]).then_inc(S.qWS[b].h, 16)
                S.g(f); wld = S.qWS[b].bump(16)

                for sub in range(4):
                    mt = k4 * 4 + sub
                    gi = wi * 16 + mt
                    par = gi % 2
                    ob = gi % 2

                    def f(t, b=b, sub=sub, par=par, wld=wld, epw=list(ev_par[par]), yfm_done=yfm_done):
                        t.wait_ge(S.qWS[b].h, wld)
                        t.wait_ge(S.qF.h, yfm_done)
                        for hh, nn in epw:
                            t.wait_ge(hh, nn)
                        for c in range(16):
                            last = (c == 15)
                            wsl = S.wslab4[b][:, c, sub * 128:(sub + 1) * 128]
                            t.matmul(S.PA[:, par * 1024: par * 1024 + 512], wsl,
                                     S.yfm[:, c, 0:512], start=(c == 0), stop=last)
                            t.matmul(S.PA[:, par * 1024 + 512: par * 1024 + 1024], wsl,
                                     S.yfm[:, c, 512:1024], start=(c == 0), stop=last)
                            mm2 = t.matmul(S.PB[:, par * 512:par * 512 + 64], wsl,
                                           S.yfm[:, c, 1024:1088], start=(c == 0), stop=last)
                            if last:
                                mm2.then_inc(S.cT.h, 1)
                    S.t(f); mmv = S.cT.bump()
                    slab_mm[b] = mmv

                    def f(s, par=par, ob=ob, mmv=mmv, padz=padz, cvd=S.cV.n):
                        s.wait_ge(S.cT.h, mmv)
                        s.wait_ge(S.cV.h, cvd)
                        s.activation(S.conv_z[ob][:, 3:515], S.PA[:, par * 1024: par * 1024 + 512], AF.Copy)
                        s.activation(S.conv_z[ob][:, 515:1027], S.PA[:, par * 1024 + 512: par * 1024 + 1024],
                                     AF.Copy)
                        s.activation(S.conv_z[ob][:, 1027:1091], S.PB[:, par * 512:par * 512 + 64],
                                     AF.Copy).then_inc(S.cS.h, 1)
                    S.s(f); zev = S.cS.bump()
                    ev_par[par] = [(S.cS.h, zev)]

                    def f(v, mt=mt, cslot=cslot, ob=ob, zev=zev, padz=padz, kvw=S.qKV[ob].n):
                        v.wait_ge(S.cS.h, zev)
                        v.wait_ge(S.cV.h, padz)
                        z = S.conv_z[ob]
                        cw = S.convw
                        v.tensor_scalar(S.conv_a[:, 0:TEXT], z[:, 0:TEXT], cw[:, cslot, mt, 0:1], None, ALU.mult)
                        v.drain()
                        v.scalar_tensor_tensor(S.conv_a[:, 0:TEXT], z[:, 1:TEXT + 1], cw[:, cslot, mt, 1:2],
                                               S.conv_a[:, 0:TEXT], ALU.mult, ALU.add)
                        v.drain()
                        v.scalar_tensor_tensor(S.conv_a[:, 0:TEXT], z[:, 2:TEXT + 2], cw[:, cslot, mt, 2:3],
                                               S.conv_a[:, 0:TEXT], ALU.mult, ALU.add)
                        v.drain()
                        v.scalar_tensor_tensor(S.conv_a[:, 0:TEXT], z[:, 3:TEXT + 3], cw[:, cslot, mt, 3:4],
                                               S.conv_a[:, 0:TEXT], ALU.mult, ALU.add)
                        v.wait_ge(S.qKV[ob].h, kvw)
                        v.engine_nop().then_inc(S.cV.h, 1)
                    S.v(f); cvv = S.cV.bump()

                    def f(s, ob=ob, cvv=cvv):
                        s.wait_ge(S.cV.h, cvv)
                        s.activation(S.conv_o[ob][:, :], S.conv_a[:, 0:TEXT], AF.Silu).then_inc(S.cS.h, 1)
                    S.s(f); sil = S.cS.bump()

                    def f(y, dst_dram=dst_dram, mt=mt, ob=ob, sil=sil):
                        y.wait_ge(S.cS.h, sil)
                        y.dma_start(out=dst_dram[mt * 128:(mt + 1) * 128, :],
                                    in_=S.conv_o[ob][:, :]).then_inc(S.qKV[ob].h, 16)
                    S.y(f); S.qKV[ob].bump(16)
        kv_done = (S.qKV[0].n, S.qKV[1].n)

        evs, evv = S.cS.n, S.cV.n

        def f(t, evs=evs, evv=evv, yfm_done=yfm_done):
            t.wait_ge(S.cS.h, evs)
            t.wait_ge(S.cV.h, evv)
            t.wait_ge(S.qF.h, yfm_done)
            t.wait_ge(S.qS.h, S.setup_q)
            for c in range(16):
                last = (c == 15)
                t.matmul(S.PA[:, 0:512], S.wf1_sb[:, c, :], S.yfm[:, c, 0:512], start=(c == 0), stop=last)
                t.matmul(S.PA[:, 512:1024], S.wf1_sb[:, c, :], S.yfm[:, c, 512:1024], start=(c == 0), stop=last)
                t.matmul(S.PB[:, 0:64], S.wf1_sb[:, c, :], S.yfm[:, c, 1024:1088], start=(c == 0), stop=last)
                t.matmul(S.PA[:, 1024:1536], S.wg1_sb[:, c, :], S.yfm[:, c, 0:512], start=(c == 0), stop=last)
                t.matmul(S.PA[:, 1536:2048], S.wg1_sb[:, c, :], S.yfm[:, c, 512:1024], start=(c == 0), stop=last)
                mm = t.matmul(S.PB[:, 512:576], S.wg1_sb[:, c, :], S.yfm[:, c, 1024:1088],
                              start=(c == 0), stop=last)
            mm.then_inc(S.cT.h, 1)
        S.t(f); fgv = S.cT.bump()

        def f(v, fgv=fgv):
            v.wait_ge(S.cT.h, fgv)
            v.tensor_copy(S.f1_sb[:, 0:512], S.PA[:, 0:512])
            v.tensor_copy(S.f1_sb[:, 512:1024], S.PA[:, 512:1024])
            v.tensor_copy(S.f1_sb[:, 1024:1088], S.PB[:, 0:64])
            v.tensor_copy(S.g1_sb[:, 0:512], S.PA[:, 1024:1536])
            v.tensor_copy(S.g1_sb[:, 512:1024], S.PA[:, 1536:2048])
            v.tensor_copy(S.g1_sb[:, 1024:1088], S.PB[:, 512:576]).then_inc(S.cV.h, 1)
        S.v(f); fgev = S.cV.bump()

        for ci in range(9):
            t0 = 0 if ci == 0 else 64 + (ci - 1) * 128
            Tc = 64 if ci == 0 else 128

            def f(t, ci=ci, t0=t0, Tc=Tc, fgev=fgev, prev=S.cS.n, yfm_done=yfm_done):
                t.wait_ge(S.cV.h, fgev)
                t.wait_ge(S.cS.h, prev)
                t.wait_ge(S.qF.h, yfm_done)
                for c in range(15):
                    t.matmul(S.PB[0:Tc, 1024:1040], S.yfm[:, c, t0:t0 + Tc], S.wb_sb[:, c, :],
                             start=(c == 0), stop=False)
                t.matmul(S.PB[0:Tc, 1024:1040], S.yfm[:, 15, t0:t0 + Tc], S.wb_sb[:, 15, :],
                         start=False, stop=True).then_inc(S.cT.h, 1)
            S.t(f); bv = S.cT.bump()

            def f(s, ci=ci, Tc=Tc, bv=bv):
                s.wait_ge(S.cT.h, bv)
                s.activation(S.beta[0:Tc, ci, :], S.PB[0:Tc, 1024:1040], AF.Sigmoid).then_inc(S.cS.h, 1)
            S.s(f); S.cS.bump()

        sbar = self.emit_barrier(self.all_sems())

        def f(g, sbar=sbar):
            g.wait_ge(S.cG.h, sbar)
            g.dma_start(out=S.bg2row[:],
                        in_=S.bg2A.rearrange("(o f) -> o f", o=1)).then_inc(S.qS.h, 16)
        S.g(f); bg2ld = S.qS.bump(16)

        def f(v, sbar=sbar):
            v.wait_ge(S.cG.h, sbar)
            v.memset(S.Sk[:], 0.0)
            v.memset(S.Sv[:], 0.0)
            v.memset(S.Skp[:], 0.0)
            v.memset(S.Svp[:], 0.0)
            v.tensor_copy(S.nwfull[:], S.nw128[:, :].unsqueeze(1).broadcast_to([128, H, DV]))
            v.engine_nop().then_inc(S.cV.h, 1)
        S.v(f); sinit = S.cV.bump()

        def f(t, sinit=sinit, bg2ld=bg2ld):
            t.wait_ge(S.cV.h, sinit)
            t.wait_ge(S.qS.h, bg2ld)
            for i in range(4):
                mm = t.matmul(S.PA[:, i * 512:(i + 1) * 512], S.ones1x[:, :],
                              S.bg2row[:, i * 512:(i + 1) * 512], start=True, stop=True)
            mm.then_inc(S.cT.h, 1)
        S.t(f); bgt = S.cT.bump()

        def f(v, bgt=bgt):
            v.wait_ge(S.cT.h, bgt)
            v.tensor_copy(S.bg2f[:], S.PA[:, :]).then_inc(S.cV.h, 1)
        S.v(f); S.cV.bump()

        ch_consumed = [S.cT.n, S.cT.n]
        for ci in range(9):
            t0 = 0 if ci == 0 else 64 + (ci - 1) * 128
            Tc = 64 if ci == 0 else 128
            par = ci % 2
            state_only = (ci == 0)
            self._scan_chunk(ci, t0, Tc, par, state_only, kv_done, ch_consumed, sinit)
        obuf_done = S.qOBU.n

        b6bar = self.emit_barrier(self.all_sems())
        for c in range(16):
            def f(y, c=c, b6bar=b6bar, obuf_done=obuf_done):
                y.wait_ge(S.cG.h, b6bar)
                y.wait_ge(S.qOBU.h, obuf_done)
                y.dma_start(out=S.ofm[:, c, :], in_=S.obuf[0:TOWN, c * 128:(c + 1) * 128],
                            transpose=True).then_inc(S.qOF.h, 16)
            S.y(f); S.qOF.bump(16)
        ofm_done = S.qOF.n

        xs_store = [S.qXS[0].n, S.qXS[1].n]
        for dg in range(4):
            b = dg % 2

            def f(g, dg=dg, b=b, sf=S.cT.n, b6bar=b6bar):
                g.wait_ge(S.cG.h, b6bar)
                g.wait_ge(S.cT.h, sf)
                g.dma_start(out=S.oslab[b][:],
                            in_=S.WoA[dg]).then_inc(S.qOS[b].h, 16)
            S.g(f); wld = S.qOS[b].bump(16)

            for tt in range(8):
                xb = tt % 2
                pab = (tt % 4)
                ps = lambda: (S.PA[:, pab * 512:(pab + 1) * 512] if tt % 8 < 4 else
                              S.PB[:, pab * 512:(pab + 1) * 512])
                evv = S.cV.n

                def f(t, tt=tt, b=b, wld=wld, ofm_done=ofm_done, evv=evv, pab=pab):
                    t.wait_ge(S.qOS[b].h, wld)
                    t.wait_ge(S.qOF.h, ofm_done)
                    t.wait_ge(S.cV.h, evv)
                    pp = S.PA[:, pab * 512:(pab + 1) * 512] if tt < 4 else \
                         S.PB[:, pab * 512:(pab + 1) * 512]
                    for c in range(16):
                        mm = t.matmul(pp, S.ofm[:, c, tt * 128:(tt + 1) * 128],
                                      S.oslab[b][:, c, :], start=(c == 0), stop=(c == 15))
                    mm.then_inc(S.cT.h, 1)
                S.t(f); mmv = S.cT.bump()

                def f(y, tt=tt, xb=xb, dg=dg, vprev=S.cV.n, xsw=xs_store[xb], x1w=list(x1_wait or [])):
                    y.wait_ge(S.cV.h, vprev)
                    y.wait_ge(S.qXS[xb].h, xsw)
                    y.dma_start(out=S.xt[xb][:, 0:512],
                                in_=S.x1_dram[WU + tt * 128: WU + (tt + 1) * 128, dg * 512:(dg + 1) * 512]
                                ).then_inc(S.qXT[xb].h, 16)
                S.y(f); xld = S.qXT[xb].bump(16)

                def f(v, tt=tt, xb=xb, mmv=mmv, xld=xld, pab=pab):
                    v.wait_ge(S.cT.h, mmv)
                    v.wait_ge(S.qXT[xb].h, xld)
                    pp = S.PA[:, pab * 512:(pab + 1) * 512] if tt < 4 else \
                         S.PB[:, pab * 512:(pab + 1) * 512]
                    v.tensor_tensor(S.xt[xb][:, 0:512], pp, S.xt[xb][:, 0:512],
                                    ALU.add).then_inc(S.cV.h, 1)
                S.v(f); vv = S.cV.bump()

                def f(y, tt=tt, xb=xb, dg=dg, vv=vv):
                    y.wait_ge(S.cV.h, vv)
                    y.dma_start(out=S.x2_dram[tt * 128:(tt + 1) * 128, dg * 512:(dg + 1) * 512],
                                in_=S.xt[xb][:, 0:512]).then_inc(S.qXS[xb].h, 16)
                S.y(f); xs_store[xb] = S.qXS[xb].bump(16)
        return [(S.qXS[0].h, S.qXS[0].n), (S.qXS[1].h, S.qXS[1].n)]


    def _scan_chunk(self, ci, t0, Tc, par, state_only, kv_done, ch_consumed, sinit):
        S = self
        HS = lambda h: slice(h * 128, h * 128 + Tc)
        HF = lambda h: slice(h * 128, (h + 1) * 128)

        def f(y, ci=ci, t0=t0, Tc=Tc, par=par, state_only=state_only,
              kv_done=kv_done, chc=ch_consumed[par], sinit=sinit):
            y.wait_ge(S.cT.h, chc)
            y.wait_ge(S.cV.h, sinit)
            y.wait_ge(S.qKV[0].h, kv_done[0])
            y.wait_ge(S.qKV[1].h, kv_done[1])
            if not state_only:
                y.dma_start(out=S.qfm_c[par][:, :, 0:Tc],
                            in_=S.qfm_d.rearrange("(c p) t -> p c t", p=128)[:, :, t0:t0 + Tc]
                            ).then_inc(S.qCH[par].h, 16)
                y.dma_start(out=S.kfm_c[par][:, :, 0:Tc],
                            in_=S.kfm_d.rearrange("(c p) t -> p c t", p=128)[:, :, t0:t0 + Tc]
                            ).then_inc(S.qCH[par].h, 16)
            y.dma_start(out=S.wfm_c[par][:, :, 0:Tc],
                        in_=S.wfm_d.rearrange("(c p) t -> p c t", p=128)[:, :, t0:t0 + Tc]
                        ).then_inc(S.qCH[par].h, 16)
            y.dma_start(out=S.ktm_c[par][0:128, :], in_=S.kfm_d[:, t0:t0 + 128],
                        transpose=True).then_inc(S.qCH[par].h, 16)
            y.dma_start(out=S.vtm_c[par][0:128, :], in_=S.vfm_d[:, t0:t0 + 128],
                        transpose=True).then_inc(S.qCH[par].h, 16)
            y.dma_start(out=S.wtm_c[par][0:128, :], in_=S.wfm_d[:, t0:t0 + 128],
                        transpose=True).then_inc(S.qCH[par].h, 16)
        S.y(f)
        chld = S.qCH[par].bump(16 * (4 if state_only else 6))

        if not state_only:
            def f(t, t0=t0, Tc=Tc, evs=S.cS.n, evv=S.cV.n, sinit=sinit, obw=S.qOBU.n):
                t.wait_ge(S.cS.h, evs)
                t.wait_ge(S.cV.h, evv)
                t.wait_ge(S.cV.h, sinit)
                t.wait_ge(S.qOBU.h, obw)
                for i in range(4):
                    mm = t.matmul(S.PB[0:Tc, i * 512:(i + 1) * 512], S.g1_sb[:, t0:t0 + Tc],
                                  S.wg2_sb[:, i * 512:(i + 1) * 512], start=True, stop=True)
                mm.then_inc(S.cT.h, 1)
            S.t(f); gv = S.cT.bump()

            def f(v, Tc=Tc, gv=gv):
                v.wait_ge(S.cT.h, gv)
                v.tensor_tensor(S.sgf[0:Tc, :], S.PB[0:Tc, 0:2048], S.bg2f[0:Tc, :],
                                ALU.add).then_inc(S.cV.h, 1)
            S.v(f); gev = S.cV.bump()

            def f(s, Tc=Tc, gev=gev):
                s.wait_ge(S.cV.h, gev)
                s.activation(S.sgf[0:Tc, :], S.sgf[0:Tc, :], AF.Sigmoid).then_inc(S.cS.h, 1)
            S.s(f); gsg = S.cS.bump()

            def f(v, Tc=Tc, gsg=gsg):
                v.wait_ge(S.cS.h, gsg)
                v.tensor_tensor(S.sgf[0:Tc, :], S.sgf[0:Tc, :],
                                S.nwfull[0:Tc, :, :].rearrange("p a b -> p (a b)"),
                                ALU.mult).then_inc(S.cV.h, 1)
            S.v(f); S.cV.bump()

        def f(t, t0=t0, Tc=Tc, evs=S.cS.n, evv=S.cV.n, sinit=sinit, obw=S.qOBU.n):
            t.wait_ge(S.cS.h, evs)
            t.wait_ge(S.cV.h, evv)
            t.wait_ge(S.cV.h, sinit)
            t.wait_ge(S.qOBU.h, obw)
            for i in range(4):
                mm = t.matmul(S.PA[0:Tc, i * 512:(i + 1) * 512], S.f1_sb[:, t0:t0 + Tc],
                              S.wf2_sb[:, i * 512:(i + 1) * 512], start=True, stop=True)
            mm.then_inc(S.cT.h, 1)
        S.t(f); gpv = S.cT.bump()

        def f(v, Tc=Tc, gpv=gpv):
            v.wait_ge(S.cT.h, gpv)
            v.tensor_copy(S.gpre[0:Tc, :], S.PA[0:Tc, 0:2048]).then_inc(S.cV.h, 1)
        S.v(f); gpe = S.cV.bump()

        def f(s, Tc=Tc, gpe=gpe):
            s.wait_ge(S.cV.h, gpe)
            s.activation(S.tmpA[0:Tc, :], S.gpre[0:Tc, :], AF.Abs)
            s.drain()
            s.activation(S.tmpB[0:Tc, :], S.tmpA[0:Tc, :], AF.Exp, scale=-1.0)
            s.drain()
            s.activation(S.gtm[0:Tc, :], S.tmpB[0:Tc, :], AF.Ln, bias=1.0).then_inc(S.cS.h, 1)
        S.s(f); lgs = S.cS.bump()

        def f(v, Tc=Tc, lgs=lgs):
            v.wait_ge(S.cS.h, lgs)
            v.tensor_tensor(S.tmpB[0:Tc, :], S.gpre[0:Tc, :], S.tmpA[0:Tc, :], ALU.subtract)
            v.drain()
            v.scalar_tensor_tensor(S.gtm[0:Tc, :], S.tmpB[0:Tc, :], 0.5, S.gtm[0:Tc, :],
                                   ALU.mult, ALU.subtract).then_inc(S.cV.h, 1)
        S.v(f); gtv = S.cV.bump()

        def f(t, Tc=Tc, gtv=gtv, gpe=gpe):
            t.wait_ge(S.cV.h, gtv)
            for i in range(4):
                t.matmul(S.PA[0:Tc, i * 512:(i + 1) * 512], S.uinc[0:Tc, 0:Tc],
                         S.gtm[0:Tc, i * 512:(i + 1) * 512], start=True, stop=True)
            for h in range(H):
                mm = t.matmul(S.PB[0:128, h * 128:h * 128 + Tc], S.gtm[0:Tc, HF(h)],
                              S.uinc[0:Tc, 0:Tc], start=True, stop=True)
            mm.then_inc(S.cT.h, 1)
        S.t(f); lamv = S.cT.bump()

        def f(v, Tc=Tc, lamv=lamv):
            v.wait_ge(S.cT.h, lamv)
            v.tensor_copy(S.ltm[0:Tc, :], S.PA[0:Tc, 0:2048])
            v.tensor_copy(S.lfm.rearrange("p (a b) -> p a b", b=128)[:, :, 0:Tc],
                          S.PB[:, 0:2048].rearrange("p (a b) -> p a b", b=128)[:, :, 0:Tc]
                          ).then_inc(S.cV.h, 1)
        S.v(f); lev = S.cV.bump()

        def f(s, Tc=Tc, lev=lev):
            s.wait_ge(S.cV.h, lev)
            s.activation(S.p1f[0:Tc, :], S.ltm[0:Tc, :], AF.Exp, bias=45.0)
            s.activation(S.p2b[0:Tc, :], S.ltm[0:Tc, :], AF.Exp, scale=-1.0, bias=-45.0)
            s.activation(S.p2fb.rearrange("p (a b) -> p a b", b=128)[:, :, 0:Tc],
                         S.lfm.rearrange("p (a b) -> p a b", b=128)[:, :, 0:Tc],
                         AF.Exp, scale=-1.0, bias=-45.0)
            s.activation(S.fcol[:, :],
                         S.lfm.rearrange("p (a b) -> p a b", b=128)[:, :, Tc - 1:Tc]
                         .rearrange("p a b -> p (a b)"), AF.Exp, bias=45.0).then_inc(S.cS.h, 1)
        S.s(f); exv = S.cS.bump()

        def f(s, Tc=Tc, par=par, chld=chld, gtv2=S.cV.n):
            s.wait_ge(S.qCH[par].h, chld)
            s.wait_ge(S.cV.h, gtv2)
            s.activation(S.sq2[0:Tc, :], S.wtm_c[par][0:Tc, :], AF.Square).then_inc(S.cS.h, 1)
        S.s(f); wsq = S.cS.bump()

        def f(v, Tc=Tc, wsq=wsq):
            v.wait_ge(S.cS.h, wsq)
            v.tensor_reduce(S.wrow[0:Tc, :],
                            S.sq2[0:Tc, :].rearrange("p (a b) -> p a b", b=128),
                            axis=AX.X, op=ALU.add).then_inc(S.cV.h, 1)
        S.v(f); wsv = S.cV.bump()

        def f(s, Tc=Tc, wsv=wsv):
            s.wait_ge(S.cV.h, wsv)
            s.activation(S.wrow[0:Tc, :], S.wrow[0:Tc, :], AF.Ln, bias=1e-6)
            s.drain()
            s.activation(S.wrow[0:Tc, :], S.wrow[0:Tc, :], AF.Exp, scale=-0.5).then_inc(S.cS.h, 1)
        S.s(f); wrv = S.cS.bump()

        def f(v, ci=ci, Tc=Tc, wrv=wrv):
            v.wait_ge(S.cS.h, wrv)
            v.tensor_tensor(S.brs[0:Tc, :], S.beta[0:Tc, ci, :], S.wrow[0:Tc, :],
                            ALU.mult).then_inc(S.cV.h, 1)
        S.v(f); brsv = S.cV.bump()

        def f(v, ci=ci, Tc=Tc, exv=exv, chld=chld, par=par, state_only=state_only, brsv=brsv):
            v.wait_ge(S.cS.h, exv)
            v.wait_ge(S.cV.h, brsv)
            v.wait_ge(S.qCH[par].h, chld)
            for h in range(H):
                v.scalar_tensor_tensor(S.kpb[0:Tc, HF(h)], S.wtm_c[par][0:Tc, HF(h)],
                                       S.brs[0:Tc, h:h + 1], S.p2b[0:Tc, HF(h)],
                                       ALU.mult, ALU.mult)
            if not state_only:
                v.tensor_tensor(S.wtf[:, :],
                                S.wfm_c[par].rearrange("p a b -> p (a b)")[:, 0:H * 128],
                                S.p2fb[:, :], ALU.mult)
            v.engine_nop().then_inc(S.cV.h, 1)
        S.v(f); kpv = S.cV.bump()

        if not state_only:
            def f(t, Tc=Tc, par=par, chld=chld, evv=S.cV.n, lev=lev):
                t.wait_ge(S.qCH[par].h, chld)
                t.wait_ge(S.cV.h, lev)
                for h in range(H):
                    mm = t.matmul(S.PA[0:Tc, h * 128:h * 128 + Tc], S.kfm_c[par][:, h, 0:Tc],
                                  S.qfm_c[par][:, h, 0:Tc], start=True, stop=True)
                mm.then_inc(S.cT.h, 1)
            S.t(f); atv = S.cT.bump()

            def f(v, Tc=Tc, atv=atv):
                v.wait_ge(S.cT.h, atv)
                v.tensor_tensor(S.a_sb.rearrange("p (a b) -> p a b", b=128)[0:Tc, :, 0:Tc],
                                S.PA[0:Tc, 0:2048].rearrange("p (a b) -> p a b", b=128)[0:Tc, :, 0:Tc],
                                S.maskrep.rearrange("p (a b) -> p a b", b=128)[0:Tc, :, 0:Tc],
                                ALU.mult).then_inc(S.cV.h, 1)
            S.v(f); aev = S.cV.bump()

            def f(t, Tc=Tc, par=par, aev=aev, kpv=kpv, gev2=S.cV.n):
                t.wait_ge(S.cV.h, aev)
                t.wait_ge(S.cV.h, kpv)
                for h in range(H):
                    t.matmul(S.PB[0:Tc, h * 128:(h + 1) * 128], S.qfm_c[par][:, h, 0:Tc],
                             S.Skp[:, HF(h)], start=True, stop=False)
                    mm = t.matmul(S.PB[0:Tc, h * 128:(h + 1) * 128],
                                  S.a_sb[0:Tc, h * 128:h * 128 + Tc],
                                  S.kpb[0:Tc, HF(h)], start=False, stop=True)
                mm.then_inc(S.cT.h, 1)
            S.t(f); sv = S.cT.bump()

            def f(v, Tc=Tc, sv=sv):
                v.wait_ge(S.cT.h, sv)
                v.scalar_tensor_tensor(S.s_sb[0:Tc, :], S.PB[0:Tc, 0:2048], SCALE,
                                       S.p1f[0:Tc, :], ALU.mult, ALU.mult).then_inc(S.cV.h, 1)
            S.v(f); sev = S.cV.bump()

            def f(v, Tc=Tc, sev=sev):
                v.wait_ge(S.cV.h, sev)
                v.tensor_reduce(S.mrow[0:Tc, :],
                                S.s_sb[0:Tc, :].rearrange("p (a b) -> p a b", b=128),
                                axis=AX.X, op=ALU.max)
                v.drain()
                v.tensor_tensor(S.s_sb[0:Tc, :].rearrange("p (a b) -> p a b", b=128),
                                S.s_sb[0:Tc, :].rearrange("p (a b) -> p a b", b=128),
                                S.mrow[0:Tc, :].unsqueeze(2).broadcast_to([Tc, H, 128]),
                                ALU.subtract)
                v.engine_nop().then_inc(S.cV.h, 1)
            S.v(f); sm1 = S.cV.bump()

            def f(s, Tc=Tc, sm1=sm1):
                s.wait_ge(S.cV.h, sm1)
                s.activation(S.s_sb[0:Tc, :], S.s_sb[0:Tc, :], AF.Exp).then_inc(S.cS.h, 1)
            S.s(f); sm2 = S.cS.bump()

            def f(v, Tc=Tc, sm2=sm2):
                v.wait_ge(S.cS.h, sm2)
                v.tensor_reduce(S.srow[0:Tc, :],
                                S.s_sb[0:Tc, :].rearrange("p (a b) -> p a b", b=128),
                                axis=AX.X, op=ALU.add)
                v.drain()
                v.reciprocal(S.rrow[0:Tc, :], S.srow[0:Tc, :])
                v.drain()
                v.tensor_tensor(S.s_sb[0:Tc, :].rearrange("p (a b) -> p a b", b=128),
                                S.s_sb[0:Tc, :].rearrange("p (a b) -> p a b", b=128),
                                S.rrow[0:Tc, :].unsqueeze(2).broadcast_to([Tc, H, 128]),
                                ALU.mult)
                v.drain()
                v.tensor_tensor(S.a_sb[0:Tc, 0:2048], S.s_sb[0:Tc, :], S.p1f[0:Tc, :],
                                ALU.mult).then_inc(S.cV.h, 1)
            S.v(f); apv = S.cV.bump()

            PT = S.PA[:, 0:1024].bitcast(BF16)

            def f(t, Tc=Tc, apv=apv, PT=PT):
                t.wait_ge(S.cV.h, apv)
                for h in range(H):
                    mm = t.transpose(PT[:, h * 128:h * 128 + Tc], S.a_sb[0:Tc, HF(h)],
                                     S.identb[0:Tc, 0:Tc])
                mm.then_inc(S.cT.h, 1)
            S.t(f); aptv = S.cT.bump()

            def f(v, Tc=Tc, aptv=aptv, PT=PT):
                v.wait_ge(S.cT.h, aptv)
                v.tensor_copy(S.at_sb.rearrange("p (a b) -> p a b", b=128)[:, :, 0:Tc],
                              PT.rearrange("p (a b) -> p a b", b=128)[:, :, 0:Tc]
                              ).then_inc(S.cV.h, 1)
            S.v(f); apte = S.cV.bump()

            def f(t, Tc=Tc, apte=apte, kpv=kpv):
                t.wait_ge(S.cV.h, apte)
                for h in range(H):
                    mm = t.matmul(S.PA[0:Tc, h * 128:h * 128 + Tc], S.wtf[:, HS(h)],
                                  S.at_sb[:, h * 128:h * 128 + Tc], start=True, stop=True)
                mm.then_inc(S.cT.h, 1)
            S.t(f); mtv = S.cT.bump()

            def f(v, ci=ci, Tc=Tc, mtv=mtv):
                v.wait_ge(S.cT.h, mtv)
                for h in range(H):
                    v.scalar_tensor_tensor(S.m_sb[0:Tc, h * 128:h * 128 + Tc],
                                           S.PA[0:Tc, h * 128:h * 128 + Tc],
                                           S.brs[0:Tc, h:h + 1],
                                           S.maskrep[0:Tc, h * 128:h * 128 + Tc],
                                           ALU.mult, ALU.mult)
                v.engine_nop().then_inc(S.cV.h, 1)
            S.v(f); mev = S.cV.bump()

            def f(t, Tc=Tc, par=par, mev=mev, sev=sev):
                t.wait_ge(S.cV.h, mev)
                t.wait_ge(S.cV.h, sev)
                for h in range(H):
                    t.matmul(S.PB[0:Tc, HF(h)], S.m_sb[0:Tc, h * 128:h * 128 + Tc],
                             S.vtm_c[par][0:Tc, HF(h)], start=True, stop=False)
                    mm = t.matmul(S.PB[0:Tc, HF(h)], S.at_sb[:, h * 128:h * 128 + Tc],
                                  S.Svp[:, HF(h)], start=False, stop=True)
                mm.then_inc(S.cT.h, 1)
            S.t(f); ov = S.cT.bump()

            def f(s, Tc=Tc, ov=ov):
                s.wait_ge(S.cT.h, ov)
                s.activation(S.sq2[0:Tc, :], S.PB[0:Tc, 0:2048], AF.Square).then_inc(S.cS.h, 1)
            S.s(f); sqv = S.cS.bump()

            def f(v, Tc=Tc, sqv=sqv):
                v.wait_ge(S.cS.h, sqv)
                v.tensor_reduce(S.mrow[0:Tc, :],
                                S.sq2[0:Tc, :].rearrange("p (a b) -> p a b", b=128),
                                axis=AX.X, op=ALU.add).then_inc(S.cV.h, 1)
            S.v(f); msv = S.cV.bump()

            def f(s, Tc=Tc, msv=msv):
                s.wait_ge(S.cV.h, msv)
                s.activation(S.srow[0:Tc, :], S.mrow[0:Tc, :], AF.Ln, scale=1.0 / 128.0, bias=float(EPS))
                s.drain()
                s.activation(S.srow[0:Tc, :], S.srow[0:Tc, :], AF.Exp, scale=-0.5).then_inc(S.cS.h, 1)
            S.s(f); rsv = S.cS.bump()

            def f(v, ci=ci, t0=t0, Tc=Tc, rsv=rsv):
                v.wait_ge(S.cS.h, rsv)
                v.tensor_tensor(S.onrm[0:Tc, :].rearrange("p (a b) -> p a b", b=128),
                                S.PB[0:Tc, 0:2048].rearrange("p (a b) -> p a b", b=128),
                                S.srow[0:Tc, :].unsqueeze(2).broadcast_to([Tc, H, 128]),
                                ALU.mult)
                v.drain()
                v.tensor_tensor(S.ogat[0:Tc, :], S.onrm[0:Tc, :], S.sgf[0:Tc, :],
                                ALU.mult).then_inc(S.cV.h, 1)
            S.v(f); oev = S.cV.bump()

            def f(y, t0=t0, Tc=Tc, oev=oev):
                y.wait_ge(S.cV.h, oev)
                y.dma_start(out=S.obuf[t0 - WU:t0 - WU + Tc, :], in_=S.ogat[0:Tc, :]
                            ).then_inc(S.qOBU.h, 16)
            S.y(f); S.qOBU.bump(16)
        else:
            oev = S.cV.n

        evv = S.cV.n

        def f(t, Tc=Tc, par=par, kpv=kpv, evv=evv, chld=chld):
            t.wait_ge(S.cV.h, kpv)
            t.wait_ge(S.cV.h, evv)
            t.wait_ge(S.qCH[par].h, chld)
            for h in range(H):
                t.matmul(S.PA[:, HF(h)], S.kpb[0:Tc, HF(h)], S.ktm_c[par][0:Tc, HF(h)],
                         start=True, stop=True)
                mm = t.matmul(S.PB[:, HF(h)], S.kpb[0:Tc, HF(h)], S.vtm_c[par][0:Tc, HF(h)],
                              start=True, stop=True)
            mm.then_inc(S.cT.h, 1)
        S.t(f); dsv = S.cT.bump()
        ch_consumed[par] = dsv

        def f(v, Tc=Tc, dsv=dsv):
            v.wait_ge(S.cT.h, dsv)
            v.scalar_tensor_tensor(S.sfold[:, :], S.Sk[:, :], float(ECONST), S.PA[:, 0:2048],
                                   ALU.mult, ALU.add)
            v.drain()
            for h in range(H):
                v.tensor_scalar(S.Sk[:, HF(h)], S.sfold[:, HF(h)], S.fcol[:, h:h + 1], None, ALU.mult)
            v.drain()
            v.scalar_tensor_tensor(S.sfold[:, :], S.Sv[:, :], float(ECONST), S.PB[:, 0:2048],
                                   ALU.mult, ALU.add)
            v.drain()
            for h in range(H):
                v.tensor_scalar(S.Sv[:, HF(h)], S.sfold[:, HF(h)], S.fcol[:, h:h + 1], None, ALU.mult)
            v.drain()
            v.tensor_scalar(S.skb16[:, :], S.Sk[:, :], float(ECONST), None, ALU.mult)
            v.tensor_scalar(S.Svp[:, :], S.Sv[:, :], float(ECONST), None, ALU.mult)
            v.engine_nop().then_inc(S.cV.h, 1)
        S.v(f); fold = S.cV.bump()

        PT2 = S.PA[:, 0:1024].bitcast(BF16)

        def f(t, fold=fold, PT2=PT2):
            t.wait_ge(S.cV.h, fold)
            for h in range(H):
                mm = t.transpose(PT2[:, HF(h)], S.skb16[:, HF(h)], S.identb[:, :])
            mm.then_inc(S.cT.h, 1)
        S.t(f); skt = S.cT.bump()

        def f(v, skt=skt, PT2=PT2):
            v.wait_ge(S.cT.h, skt)
            v.tensor_copy(S.Skp[:, :], PT2[:, 0:2048]).then_inc(S.cV.h, 1)
        S.v(f); S.cV.bump()

    def all_sems(self):
        S = self
        out = [S.cT, S.cV, S.cS, S.cG, S.qS, S.qF, S.qG,
               S.qOBU, S.qOF]
        for lst in [S.qXT, S.qYT, S.qWS, S.qDS, S.qES, S.qED, S.qXG, S.qCH, S.qOS, S.qYS, S.qCW, S.qCE, S.qCDW, S.qCED, S.qXS, S.qOBs, S.qKV, S.qSCp]:
            out.extend(lst)
        return out

    def finalize(self):
        S = self
        nc = S.nc
        src_map = {"y_dram": S.y_dram, "x1_dram": S.x1_dram, "xbuf": S.xbuf,
                   "outbuf": S.outbuf, "x2_dram": S.x2_dram, "obuf": S.obuf,
                   "qfm_d": S.qfm_d, "kfm_d": S.kfm_d, "vfm_d": S.vfm_d, "wfm_d": S.wfm_d}
        sems = self.all_sems()
        allw = [(c.h, c.n) for c in sems]
        for name, dst in S.dbg_aps.items():

            def f(g, name=name, dst=dst, allw=allw):
                for h, n in allw:
                    if n:
                        g.wait_ge(h, n)
                g.dma_start(out=dst[:, :], in_=src_map[name][:, :]).then_inc(S.qS.h, 16)
            S.g(f); S.qS.bump(16)

        final = [(c.h, c.n) for c in sems]

        def f(y, final=final):
            for h, n in final:
                if n:
                    y.wait_ge(h, n)
        S.y(f)

        with nc.Block() as block:
            @block.tensor
            def _(eng):
                for fn in S.ops["t"]:
                    fn(eng)

            @block.vector
            def _(eng):
                for fn in S.ops["v"]:
                    fn(eng)

            @block.scalar
            def _(eng):
                for fn in S.ops["s"]:
                    fn(eng)

            @block.gpsimd
            def _(eng):
                for fn in S.ops["g"]:
                    fn(eng)

            @block.sync
            def _(eng):
                for fn in S.ops["y"]:
                    fn(eng)
        return nc


def shard_inputs(inputs):
    x = np.ascontiguousarray(inputs["L0_attn_hidden_states"][0])

    def slab4(w):
        return np.asarray(w).reshape(16, 128, 4, 512).transpose(2, 1, 0, 3)

    shared = {
        "Wq": slab4(inputs["L0_attn_Wq"]), "Wk": slab4(inputs["L0_attn_Wk"]),
        "Wv": slab4(inputs["L0_attn_Wv"]),
        "Ww": slab4(inputs["L0_attn_Ww"]), "cq": inputs["L0_attn_cq"], "ck": inputs["L0_attn_ck"],
        "cv": inputs["L0_attn_cv"], "Wf1": inputs["L0_attn_Wf1"], "Wf2": inputs["L0_attn_Wf2"],
        "Wb": inputs["L0_attn_Wb"], "Wg1": inputs["L0_attn_Wg1"], "Wg2": inputs["L0_attn_Wg2"],
        "bg2": inputs["L0_attn_bg2"], "norm_w": inputs["L0_attn_norm_w"],
        "Wo": slab4(inputs["L0_attn_Wo"]),
        "router": inputs["L0_ffn_router"],
        "w_up_gate": np.asarray(inputs["L0_ffn_w_up_gate"])
            .reshape(8, 16, 128, 11, 256).transpose(0, 3, 2, 1, 4),
        "w_down": np.asarray(inputs["L0_ffn_w_down"])
            .reshape(8, 11, 128, 4, 512).transpose(0, 3, 2, 1, 4),
        "shared_w_up_gate": np.asarray(inputs["L0_ffn_shared_w_up_gate"])
            .reshape(16, 128, 2, 2, 11, 256).transpose(2, 3, 4, 1, 0, 5),
        "shared_w_down": np.asarray(inputs["L0_ffn_shared_w_down"])
            .reshape(2, 22, 128, 4, 512).transpose(0, 3, 2, 1, 4),
    }
    shared = {k: np.ascontiguousarray(v) for k, v in shared.items()}
    in_maps = []
    for c in range(NCORE):
        lo = c * TOWN - WU
        if c == 0:
            ext = np.concatenate([np.zeros((WU, D), np.float32), x[0:TOWN]], axis=0)
        else:
            ext = x[lo:(c + 1) * TOWN]
        m = dict(shared)
        m["xin"] = np.ascontiguousarray(ext)
        m["wvalid"] = np.array([[0.0 if c == 0 else 1.0]], np.float32)
        in_maps.append(m)
    return in_maps


_CACHED = {}


def _build():
    if "nc" not in _CACHED:
        m = MK()
        m.emit_setup()
        w1 = m.emit_ffn(TEXT, m.xin, m.x1_dram, False, True)
        w2 = m.emit_attn(w1)
        bar = m.emit_barrier(m.all_sems())
        m.emit_ffn(TOWN, m.x2_dram, m.xoutA, False, False, src_wait=list(w2) + [(m.cG.h, bar)])
        _CACHED["nc"] = m.finalize()
    return _CACHED["nc"]


def kernel(**inputs):
    nc = _build()
    in_maps = shard_inputs(inputs)
    res = run_bass_kernel_spmd(nc, in_maps, core_ids=list(range(NCORE)))
    out = np.concatenate([res.results[c]["xout"] for c in range(NCORE)], axis=0)
    return out.reshape(1, NCORE * TOWN, D).astype(np.float32)


# version 07
# speedup vs baseline: 1.0377x; 1.0039x over previous
import numpy as np
import ml_dtypes
import concourse.bass as bass
import concourse.mybir as mybir
from concourse.bass_utils import run_bass_kernel_spmd

F32 = mybir.dt.float32
BF16 = mybir.dt.bfloat16
I32 = mybir.dt.int32
AF = mybir.ActivationFunctionType
ALU = mybir.AluOpType
AX = mybir.AxisListType
IOX = bass.IndirectOffsetOnAxis

D = 2048
NCORE = 8
TOWN = 1024
WU = 64
TEXT = TOWN + WU
H, DK, DV, M = 16, 128, 128, 128
SCALE = float(DK ** -0.5)
E = 8
CAP = 352
NER = E * CAP
TRASH = NER
FI = 1408
FJ = 5632
CSHIFT = -45.0
ECONST = float(np.exp(np.float64(CSHIFT)))
EPS = 1e-5
KB = 1024


class Ctr:
    def __init__(self, nc, name):
        self.h = nc.alloc_semaphore(name)
        self.n = 0

    def bump(self, k=1):
        self.n += k
        return self.n


class MK:
    def __init__(self, do_attn=True, do_ffn2=True, dbg=()):
        self.do_attn = do_attn
        self.do_ffn2 = do_ffn2
        nc = bass.Bass("TRN2", target_bir_lowering=False)
        self.nc = nc
        self.ops = {"t": [], "v": [], "s": [], "g": [], "y": []}

        def din(name, shape, dt=F32):
            return nc.dram_tensor(name, shape, dt, kind="ExternalInput").ap()

        self.xin = din("xin", [TEXT, D])
        self.WqA = din("Wq", [4, 128, 16, 512]); self.WkA = din("Wk", [4, 128, 16, 512])
        self.WvA = din("Wv", [4, 128, 16, 512]); self.WwA = din("Ww", [4, 128, 16, 512])
        self.cqA = din("cq", [D, 4]); self.ckA = din("ck", [D, 4]); self.cvA = din("cv", [D, 4])
        self.Wf1A = din("Wf1", [D, DV]); self.Wf2A = din("Wf2", [DV, H * M])
        self.WbA = din("Wb", [D, H])
        self.Wg1A = din("Wg1", [D, DV]); self.Wg2A = din("Wg2", [DV, H * DV])
        self.bg2A = din("bg2", [H * DV])
        self.normwA = din("norm_w", [DV])
        self.WoA = din("Wo", [4, 128, 16, 512])
        self.RtA = din("router", [D, E])
        self.WugA = din("w_up_gate", [E, 11, 128, 16, 256])
        self.WdnA = din("w_down", [E, 4, 128, 11, 512])
        self.SugA = din("shared_w_up_gate", [2, 2, 11, 128, 16, 256])
        self.SdnA = din("shared_w_down", [2, 4, 128, 22, 512])
        self.wvalidA = din("wvalid", [1, 1])
        self.xoutA = nc.dram_tensor("xout", [TOWN, D], F32, kind="ExternalOutput").ap()

        def scr(name, shape, dt):
            return nc.dram_tensor(name, shape, dt, kind="Internal").ap()

        self.y_dram = scr("y_dram", [TEXT, D], BF16)
        self.x1_dram = scr("x1_dram", [TEXT, D], F32)
        self.x2_dram = scr("x2_dram", [TOWN, D], F32)
        self.xbuf = scr("xbuf", [NER + 1, D], BF16)
        self.outbuf = scr("outbuf", [NER + 1, D], F32)
        self.qfm_d = scr("qfm_d", [D, TEXT], BF16)
        self.kfm_d = scr("kfm_d", [D, TEXT], BF16)
        self.vfm_d = scr("vfm_d", [D, TEXT], BF16)
        self.wfm_d = scr("wfm_d", [D, TEXT], BF16)
        self.obuf = scr("obuf", [TOWN, D], BF16)
        self.SugB = scr("SugB", [2, 2, 11, 128, 16, 256], BF16)
        self.SdnB = scr("SdnB", [2, 4, 128, 22, 512], BF16)
        self.WugB = scr("WugB", [E, 11, 128, 16, 256], BF16)
        self.WdnB = scr("WdnB", [E, 4, 128, 11, 512], BF16)

        it = nc.inline_tensor
        self.C_UINC = it(np.triu(np.ones((128, 128), np.float32)).copy(), "c_UINC").ap()
        self.C_USTR = it(np.triu(np.ones((128, 128), np.float32), 1).copy(), "c_USTR").ap()
        self.C_ONES1 = it(np.ones((1, 128), np.float32), "c_ONES1").ap()
        self.C_ONESC = it(np.ones((128, 1), np.float32), "c_ONESC").ap()
        self.C_IDENT = it(np.eye(128, dtype=ml_dtypes.bfloat16), "c_IDENT").ap()
        self.C_MASKR = it(np.tile(np.triu(np.ones((128, 128), np.float32)), (1, H))
                          .astype(ml_dtypes.bfloat16), "c_MASKR").ap()
        self.C_IOTAP = it((np.arange(8, dtype=np.float32)[None, :] + 1000.0).repeat(128, 0).copy(),
                          "c_IOTAP").ap()
        self.C_IOTA8 = it(np.arange(8, dtype=np.float32)[None, :].repeat(128, 0).copy(),
                          "c_IOTA8").ap()

        sb = lambda n, s, t: nc.alloc_sbuf_tensor(n, s, t)
        self.identb = sb("identb", [128, 128], BF16)
        self.uinc = sb("uinc", [128, 128], F32)
        self.ustrict = sb("ustrict", [128, 128], F32)
        self.ones1x = sb("ones1x", [1, 128], F32)
        self.onesc = sb("onesc", [128, 1], F32)
        self.maskrep = sb("maskrep", [128, H * 128], BF16)
        self.iota8p = sb("iota8p", [128, 8], F32)
        self.iota8 = sb("iota8", [128, 8], F32)
        self.rt_sb = sb("rt_sb", [128, 16, E], BF16)
        self.wv_sb = sb("wv_sb", [1, 1], F32)
        self.wv_col = sb("wv_col", [128, 1], F32)
        self.tcol = sb("tcol", [128, 1], F32)
        self.convw = sb("convw", [128, 3, 16, 4], F32)
        self.wf2_sb = sb("wf2_sb", [128, H * M], BF16)
        self.wg2_sb = sb("wg2_sb", [128, H * DV], BF16)
        self.wb_sb = sb("wb_sb", [128, 16, H], BF16)
        self.wf1_sb = sb("wf1_sb", [128, 16, 128], BF16)
        self.wg1_sb = sb("wg1_sb", [128, 16, 128], BF16)
        self.nwrow = sb("nwrow", [1, DV], F32)
        self.nw128 = sb("nw128", [128, DV], F32)
        self.lgt = sb("lgt", [128, 9, 8], F32)
        self.off_i = [sb(f"off{k}", [128, 9], I32) for k in range(2)]
        self.cwvt = [sb(f"cwv{k}", [128, 9], F32) for k in range(2)]
        self.r_top8 = sb("r_top8", [128, 8], F32)
        self.r_t8 = sb("r_t8", [128, 8], F32)
        self.r_ohA = sb("r_ohA", [128, 8], F32)
        self.r_ohB = sb("r_ohB", [128, 8], F32)
        self.r_ohs = sb("r_ohs", [128, 8], F32)
        self.t_sel0 = sb("t_sel0", [128, 1], F32)
        self.t_sel1 = sb("t_sel1", [128, 1], F32)
        self.t_pos0 = sb("t_pos0", [128, 1], F32)
        self.t_pos1 = sb("t_pos1", [128, 1], F32)
        self.t_v0 = sb("t_v0", [128, 1], F32)
        self.t_v1 = sb("t_v1", [128, 1], F32)
        self.t_cwA = sb("t_cwA", [128, 1], F32)
        self.t_cwB = sb("t_cwB", [128, 1], F32)
        self.t_a = sb("t_a", [128, 1], F32)
        self.t_b = sb("t_b", [128, 1], F32)
        self.t_mr = sb("t_mr", [128, 1], F32)
        self.t_rs = sb("t_rs", [128, 1], F32)
        self.lns = [[sb(f"ln{k}{p}", [128, 1], F32) for k in "abmr"] for p in range(2)]
        self.lnv = [0, 0]
        self.carry = sb("carry", [1, 8], F32)
        self.CVALS = [0.0, 1.0, -1.0, float(EPS), 45.0, -45.0, -0.5, 1.0 / 128.0,
                      float(SCALE), 0.5, float(ECONST), 1e-6]
        self.cpool = sb("cpool", [128, len(self.CVALS)], F32)
        for ci, val in enumerate(self.CVALS):
            nc.const_aps.register(F32, val, self.cpool[:, ci:ci + 1])

        ARENA = 180 * KB
        base, _ = nc.bump_sbuf(ARENA)
        self._ab = base

        def at(name, shape, dt, off):
            return nc.alloc_sbuf_tensor_at(name, shape, dt, offset=base + off)

        self.yfm = at("yfm", [128, 16, TEXT], BF16, 0)
        self.hsh = at("hsh", [128, 22, TEXT], BF16, 34816)
        self.xt = [at(f"xt{i}", [128, D], F32, 82688 + i * 8192) for i in range(2)]
        self.estg = [at("estg0", [128, 512], F32, 86784),
                     at("estg1", [128, 512], F32, 88832)]
        self.wslab = [at(f"wslab{i}", [128, 16, 256], BF16, 99072 + i * 8192) for i in range(2)]
        self.dslab = [at(f"dslab{i}", [128, 4, 512], BF16, 115456 + i * 4096) for i in range(2)]
        self.sqbuf = at("sqbuf", [128, D], F32, 123648)
        self.eslab = [at("eslabB", [128, 16, 256], BF16, 123648),
                      at("eslabA", [128, 16, 256], BF16, 131840)]
        self.xg = [at("xg0", [128, 16, CAP], BF16, 140032)]
        self.he = at("he", [128, 11, CAP], BF16, 151296)
        self.edslab = [at(f"edslab{i}", [128, 11, 512], BF16, 159040 + i * 11264) for i in range(2)]
        self.ytm_t = [at(f"ytm{i}", [128, D], BF16, 159040 + i * 4096) for i in range(2)]
        self.g0t = at("g0t", [128, D], F32, 131840)
        self.g1t = at("g1t", [128, D], F32, 140032)
        self.bg2row = at("bg2row", [1, H * DV], F32, 140032)
        self.conv_z = [at(f"conv_z{i}", [128, TEXT + 3], F32, 107264 + i * 4608) for i in range(2)]
        self.conv_a = at("conv_a", [128, TEXT], F32, 116480)
        self.conv_o = [at(f"conv_o{i}", [128, TEXT], BF16, 120832 + i * 2176) for i in range(2)]
        self.wslab4 = [at(f"wslab4_{i}", [128, 16, 512], BF16, 34816 + i * 16384) for i in range(2)]
        so = 0
        self.qfm_c = [at(f"qfmc{i}", [128, H, 128], BF16, so + i * 4096) for i in range(2)]
        self.kfm_c = [at(f"kfmc{i}", [128, H, 128], BF16, so + 8192 + i * 4096) for i in range(2)]
        self.wfm_c = [at(f"wfmc{i}", [128, H, 128], BF16, so + 16384 + i * 4096) for i in range(2)]
        self.ktm_c = [at(f"ktmc{i}", [128, H * DK], BF16, so + 24576 + i * 4096) for i in range(2)]
        self.vtm_c = [at(f"vtmc{i}", [128, H * DV], BF16, so + 32768 + i * 4096) for i in range(2)]
        self.wtm_c = [at(f"wtmc{i}", [128, H * M], BF16, so + 40960 + i * 4096) for i in range(2)]
        self.Sk = at("Sk", [128, H * DK], F32, 49152)
        self.Sv = at("Sv", [128, H * DV], F32, 57344)
        self.Skp = at("Skp", [128, H * M], BF16, 65536)
        self.Svp = at("Svp", [128, H * DV], BF16, 69632)
        self.f1_sb = at("f1_sb", [128, TEXT], BF16, 73728)
        self.g1_sb = at("g1_sb", [128, TEXT], BF16, 76032)
        self.beta = at("beta", [128, 9, H], F32, 78336)
        self.fcol = at("fcol", [128, H], F32, 78976)
        self.mrow = at("mrow", [128, H], F32, 79104)
        self.srow = at("srow", [128, H], F32, 79232)
        self.rrow = at("rrow", [128, H], F32, 79360)
        self.wrow = at("wrow", [128, H], F32, 79488)
        self.brs = at("brs", [128, H], F32, 79616)
        self.bg2f = at("bg2f", [128, H * DV], F32, 79872)
        self.nwfull = at("nwfull", [128, H, DV], F32, 88064)
        self.sgf = at("sgf", [128, H * DV], F32, 96256)
        self.gpre = at("gpre", [128, H * M], F32, 104448)
        self.s_sb = at("s_sb", [128, H * M], F32, 104448)
        self.sq2 = at("sq2", [128, H * DV], F32, 104448)
        self.tmpA = at("tmpA", [128, H * M], F32, 112640)
        self.lfm = at("lfm", [128, H * M], F32, 112640)
        self.at_sb = at("at_sb", [128, H * 128], BF16, 112640)
        self.m_sb = at("m_sb", [128, H * 128], BF16, 116736)
        self.onrm = at("onrm", [128, H * DV], F32, 112640)
        self.tmpB = at("tmpB", [128, H * M], F32, 120832)
        self.p1f = at("p1f", [128, H * M], F32, 120832)
        self.ogat = at("ogat", [128, H * DV], BF16, 120832)
        self.gtm = at("gtm", [128, H * M], F32, 129024)
        self.p2b = at("p2b", [128, H * M], BF16, 129024)
        self.p2fb = at("p2fb", [128, H * M], BF16, 133120)
        self.ltm = at("ltm", [128, H * M], F32, 137216)
        self.sfold = at("sfold", [128, H * DK], F32, 137216)
        self.kpb = at("kpb", [128, H * M], BF16, 145408)
        self.wtf = at("wtf", [128, H * M], BF16, 149504)
        self.a_sb = at("a_sb", [128, H * 128], BF16, 153600)
        self.skb16 = at("skb16", [128, H * DK], BF16, 157696)
        self.ofm = at("ofm", [128, 16, TOWN], BF16, 0)
        self.oslab = [at(f"oslab{i}", [128, 16, 512], BF16, 32768 + i * 16384) for i in range(2)]

        self.PA = nc.alloc_psum_tensor("PA", [128, 2048], F32)
        self.PB = nc.alloc_psum_tensor("PB", [128, 2048], F32)

        self.cT = Ctr(nc, "cT")
        self.cV = Ctr(nc, "cV")
        self.cS = Ctr(nc, "cS")
        self.qS = Ctr(nc, "qS")
        self.qXT = [Ctr(nc, f"qXT{i}") for i in range(2)]
        self.qYS = [Ctr(nc, f"qYS{i}") for i in range(2)]
        self.qF = Ctr(nc, "qF")
        self.qYT = [Ctr(nc, f"qYT{i}") for i in range(2)]
        self.qSCp = [Ctr(nc, f"qSCp{i}") for i in range(2)]
        self.qWS = [Ctr(nc, f"qWS{i}") for i in range(2)]
        self.qDS = [Ctr(nc, f"qDS{i}") for i in range(2)]
        self.qES = [Ctr(nc, f"qES{i}") for i in range(2)]
        self.qED = [Ctr(nc, f"qED{i}") for i in range(2)]
        self.qXG = [Ctr(nc, f"qXG{i}") for i in range(2)]
        self.qCW = [Ctr(nc, f"qCW{i}") for i in range(2)]
        self.qCE = [Ctr(nc, f"qCE{i}") for i in range(2)]
        self.qCDW = [Ctr(nc, f"qCDW{i}") for i in range(2)]
        self.qCED = [Ctr(nc, f"qCED{i}") for i in range(2)]
        self.cG = Ctr(nc, "cG")
        self.gdummy = nc.alloc_sbuf_tensor("gdummy", [1, 8], F32)
        self.qOBs = [Ctr(nc, f"qOBs{i}") for i in range(3)]
        self.qG = Ctr(nc, "qG")
        self.qXS = [Ctr(nc, f"qXS{i}") for i in range(2)]
        self.qKV = [Ctr(nc, f"qKV{i}") for i in range(2)]
        self.qCH = [Ctr(nc, f"qCH{i}") for i in range(2)]
        self.qOS = [Ctr(nc, f"qOS{i}") for i in range(2)]
        self.qOBU = Ctr(nc, "qOBU")
        self.qOF = Ctr(nc, "qOF")

        self.dbg_aps = {}
        shp_map = {"y_dram": ([TEXT, D], BF16), "x1_dram": ([TEXT, D], F32),
                   "xbuf": ([NER + 1, D], BF16), "outbuf": ([NER + 1, D], F32),
                   "x2_dram": ([TOWN, D], F32), "obuf": ([TOWN, D], BF16),
                   "qfm_d": ([D, TEXT], BF16), "kfm_d": ([D, TEXT], BF16),
                   "vfm_d": ([D, TEXT], BF16), "wfm_d": ([D, TEXT], BF16)}
        for name in dbg:
            shp, dt = shp_map[name]
            self.dbg_aps[name] = nc.dram_tensor("dbg_" + name, shp, dt, kind="ExternalOutput").ap()

    def t(self, fn): self.ops["t"].append(fn)
    def v(self, fn): self.ops["v"].append(fn)
    def s(self, fn): self.ops["s"].append(fn)
    def g(self, fn): self.ops["g"].append(fn)
    def y(self, fn): self.ops["y"].append(fn)

    def emit_setup(self):
        S = self
        for dst, src in [(S.identb, S.C_IDENT), (S.uinc, S.C_UINC), (S.ustrict, S.C_USTR),
                         (S.ones1x, S.C_ONES1), (S.onesc, S.C_ONESC), (S.maskrep, S.C_MASKR),
                         (S.iota8p, S.C_IOTAP), (S.iota8, S.C_IOTA8)]:
            def f(g, dst=dst, src=src):
                g.dma_start(out=dst[:], in_=src[:, :]).then_inc(S.qS.h, 16)
            S.g(f); S.qS.bump(16)

        def f(g):
            g.dma_start(out=S.rt_sb[:], in_=S.RtA.rearrange("(c p) e -> p c e", p=128)).then_inc(S.qS.h, 16)
            g.dma_start(out=S.convw[:, 0], in_=S.cqA.rearrange("(c p) k -> p c k", p=128)).then_inc(S.qS.h, 16)
            g.dma_start(out=S.convw[:, 1], in_=S.ckA.rearrange("(c p) k -> p c k", p=128)).then_inc(S.qS.h, 16)
            g.dma_start(out=S.convw[:, 2], in_=S.cvA.rearrange("(c p) k -> p c k", p=128)).then_inc(S.qS.h, 16)
            g.dma_start(out=S.wf2_sb[:], in_=S.Wf2A[:, :]).then_inc(S.qS.h, 16)
            g.dma_start(out=S.wg2_sb[:], in_=S.Wg2A[:, :]).then_inc(S.qS.h, 16)
            g.dma_start(out=S.wb_sb[:], in_=S.WbA.rearrange("(c p) e -> p c e", p=128)).then_inc(S.qS.h, 16)
            g.dma_start(out=S.wf1_sb[:], in_=S.Wf1A.rearrange("(c p) m -> p c m", p=128)).then_inc(S.qS.h, 16)
            g.dma_start(out=S.wg1_sb[:], in_=S.Wg1A.rearrange("(c p) m -> p c m", p=128)).then_inc(S.qS.h, 16)
            g.dma_start(out=S.nwrow[:], in_=S.normwA.rearrange("(o f) -> o f", o=1)).then_inc(S.qS.h, 16)
            g.dma_start(out=S.wv_sb[:], in_=S.wvalidA[:, :]).then_inc(S.qS.h, 16)
        S.g(f); S.qS.bump(16 * 11)

        def f(v):
            for ci, val in enumerate(S.CVALS):
                v.memset(S.cpool[:, ci:ci + 1], val)
            v.memset(S.sqbuf[0:1, :], 0.0).then_inc(S.cV.h, 1)
        S.v(f); zv = S.cV.bump()

        def f(g, zv=zv):
            g.wait_ge(S.cV.h, zv)
            g.dma_start(out=S.outbuf[TRASH:TRASH + 1, :], in_=S.sqbuf[0:1, :]).then_inc(S.qS.h, 16)
        S.g(f); S.qS.bump(16)
        self.setup_q = S.qS.n

        def f(t):
            t.wait_ge(S.qS.h, S.setup_q)
            t.matmul(S.PB[:, 0:128], S.ones1x[:, :], S.nwrow[:, :], start=True, stop=True)
            t.matmul(S.PB[:, 512:513], S.ones1x[:, :], S.wv_sb[:, :],
                     start=True, stop=True).then_inc(S.cT.h, 1)
        S.t(f); tv = S.cT.bump()

        def f(v, tv=tv):
            v.wait_ge(S.cT.h, tv)
            v.tensor_copy(S.nw128[:], S.PB[:, 0:128])
            v.tensor_copy(S.wv_col[:], S.PB[:, 512:513])
            v.drain()
            v.tensor_scalar(S.tcol[:], S.wv_col[:], -float(TRASH), float(TRASH), ALU.mult, ALU.add)
            v.engine_nop().then_inc(S.cV.h, 1)
        S.v(f); S.cV.bump()
        self.setup_v = S.cV.n
        self.setup_t = tv

    def emit_barrier(self, ctrs):
        S = self
        waits = [(c.h, c.n) for c in ctrs if c.n]
        gprev = S.cG.n

        def f(g, waits=waits, gprev=gprev):
            g.wait_ge(S.cG.h, gprev)
            for h, n in waits:
                g.wait_ge(h, n)
            g.memset(S.gdummy[:], 0.0).then_inc(S.cG.h, 1)
        S.g(f)
        return S.cG.bump()

    def emit_ln(self, x_src, T, src_wait=None, src_wait_tile=None):
        S = self
        nt = (T + 127) // 128
        ytm_store = [S.qYS[0].n, S.qYS[1].n]
        S.lnv = [S.cV.n, S.cV.n]
        half = None
        for i in range(nt):
            r0 = i * 128
            P = min(128, T - r0)
            b = i % 2
            xtile = S.xt[b]
            la, lb, lm, lr = S.lns[b]
            vprev = S.lnv[b]

            def f(y, xtile=xtile, r0=r0, P=P, b=b, vprev=vprev, src_wait=src_wait,
                  swt=(src_wait_tile(i) if src_wait_tile else []), xsw=S.qXS[b].n):
                y.wait_ge(S.cV.h, vprev)
                y.wait_ge(S.qXS[b].h, xsw)
                for hh, nn in (src_wait or []):
                    y.wait_ge(hh, nn)
                for hh, nn in swt:
                    y.wait_ge(hh, nn)
                y.dma_start(out=xtile[0:P, :], in_=x_src[r0:r0 + P, :]).then_inc(S.qXT[b].h, 16)
            S.y(f); xv = S.qXT[b].bump(16)

            def f(s, xtile=xtile, P=P, b=b, xv=xv, vprev=vprev, la=la, lb=lb):
                s.wait_ge(S.qXT[b].h, xv)
                s.wait_ge(S.cV.h, vprev)
                s.wait_ge(S.qS.h, S.setup_q)
                s.activation(S.sqbuf[0:P, :], xtile[0:P, :], AF.Square, accum_out=lb[0:P, :])
                s.activation(S.sqbuf[0:P, :], xtile[0:P, :], AF.Copy,
                             accum_out=la[0:P, :]).then_inc(S.cS.h, 1)
            S.s(f); sv = S.cS.bump()

            def f(v, P=P, sv=sv, la=la, lb=lb, lm=lm):
                v.wait_ge(S.cS.h, sv)
                v.tensor_scalar(la[0:P, :], la[0:P, :], 1.0 / D, None, ALU.mult)
                v.tensor_scalar(lb[0:P, :], lb[0:P, :], 1.0 / D, None, ALU.mult)
                v.drain()
                v.tensor_tensor(lm[0:P, :], la[0:P, :], la[0:P, :], ALU.mult)
                v.drain()
                v.tensor_tensor(lb[0:P, :], lb[0:P, :], lm[0:P, :], ALU.subtract)
                v.engine_nop().then_inc(S.cV.h, 1)
            S.v(f); vv = S.cV.bump()

            def f(s, P=P, vv=vv, lb=lb, lr=lr):
                s.wait_ge(S.cV.h, vv)
                s.activation(lr[0:P, :], lb[0:P, :], AF.Ln, bias=float(EPS))
                s.drain()
                s.activation(lr[0:P, :], lr[0:P, :], AF.Exp, scale=-0.5).then_inc(S.cS.h, 1)
            S.s(f); sv2 = S.cS.bump()

            def f(v, xtile=xtile, P=P, b=b, sv2=sv2, ytw=S.qSCp[b].n, ysw=ytm_store[b],
                  la=la, lm=lm, lr=lr):
                v.wait_ge(S.cS.h, sv2)
                v.wait_ge(S.qSCp[b].h, ytw)
                v.wait_ge(S.qYS[b].h, ysw)
                v.tensor_tensor(lm[0:P, :], la[0:P, :], lr[0:P, :], ALU.mult)
                v.drain()
                v.tensor_scalar(lm[0:P, :], lm[0:P, :], -1.0, None, ALU.mult)
                v.drain()
                v.tensor_scalar(S.ytm_t[b][0:P, :], xtile[0:P, :], lr[0:P, :], lm[0:P, :],
                                ALU.mult, ALU.add).then_inc(S.cV.h, 1)
            S.v(f); yv = S.cV.bump()
            S.lnv[b] = yv

            def f(y, b=b, r0=r0, P=P, yv=yv):
                y.wait_ge(S.cV.h, yv)
                y.dma_start(out=S.y_dram[r0:r0 + P, :], in_=S.ytm_t[b][0:P, :]).then_inc(S.qYS[b].h, 16)
            S.y(f); ytm_store[b] = S.qYS[b].bump(16)
            if i == 3:
                half = (S.qYS[0].n, S.qYS[1].n)
        return (half or (S.qYS[0].n, S.qYS[1].n)), (S.qYS[0].n, S.qYS[1].n)

    def emit_yfm(self, T, yhalf, ydone):
        S = self
        tprev = S.cT.n
        TA = min(512, T)
        for c in range(16):
            def f(y, c=c, yhalf=yhalf, tprev=tprev, TA=TA):
                y.wait_ge(S.qYS[0].h, yhalf[0])
                y.wait_ge(S.qYS[1].h, yhalf[1])
                y.wait_ge(S.cT.h, tprev)
                y.dma_start(out=S.yfm[:, c, 0:TA], in_=S.y_dram[0:TA, c * 128:(c + 1) * 128],
                            transpose=True).then_inc(S.qF.h, 16)
            S.y(f); S.qF.bump(16)
        fa = S.qF.n
        if T > TA:
            for c in range(16):
                def f(y, c=c, ydone=ydone, tprev=tprev, TA=TA, T=T):
                    y.wait_ge(S.qYS[0].h, ydone[0])
                    y.wait_ge(S.qYS[1].h, ydone[1])
                    y.wait_ge(S.cT.h, tprev)
                    y.dma_start(out=S.yfm[:, c, TA:T], in_=S.y_dram[TA:T, c * 128:(c + 1) * 128],
                                transpose=True).then_inc(S.qF.h, 16)
                S.y(f); S.qF.bump(16)
        return fa, S.qF.n

    def emit_routing(self, T, yfm_done, wv_apply):
        S = self
        nt = (T + 127) // 128
        for i in range(nt):
            r0 = i * 128
            P = min(128, T - r0)
            vprev = S.cV.n

            def f(t, i=i, r0=r0, P=P, yfm_done=yfm_done, vprev=vprev):
                t.wait_ge(S.qF.h, yfm_done)
                t.wait_ge(S.cV.h, vprev)
                for c in range(15):
                    t.matmul(S.PB[0:P, 1024:1032], S.yfm[:, c, r0:r0 + P], S.rt_sb[:, c, :],
                             start=(c == 0), stop=False)
                t.matmul(S.PB[0:P, 1024:1032], S.yfm[:, 15, r0:r0 + P], S.rt_sb[:, 15, :],
                         start=False, stop=True).then_inc(S.cT.h, 1)
            S.t(f); tv = S.cT.bump()

            def f(v, i=i, P=P, tv=tv):
                v.wait_ge(S.cT.h, tv)
                v.tensor_copy(S.lgt[0:P, i, :], S.PB[0:P, 1024:1032]).then_inc(S.cV.h, 1)
            S.v(f); S.cV.bump()

        def f(v):
            v.memset(S.carry[:], 0.0).then_inc(S.cV.h, 1)
        S.v(f); S.cV.bump()

        for i in range(nt):
            r0 = i * 128
            P = min(128, T - r0)
            lgdone = S.cV.n

            def f(v, i=i, P=P, lgdone=lgdone):
                L = S.lgt[0:P, i, :]
                v.wait_ge(S.cV.h, lgdone)
                v.max(S.r_top8[0:P, :], L)
                v.drain()
                v.tensor_tensor(S.r_t8[0:P, :], L, S.r_top8[0:P, 0:1].broadcast_to([P, 8]), ALU.is_equal)
                v.drain()
                v.scalar_tensor_tensor(S.r_t8[0:P, :], S.r_t8[0:P, :], -1000.0, S.iota8p[0:P, :],
                                       ALU.mult, ALU.add)
                v.drain()
                v.tensor_reduce(S.t_sel0[0:P, :], S.r_t8[0:P, :], axis=AX.X, op=ALU.min)
                v.drain()
                v.tensor_tensor(S.r_ohA[0:P, :], S.iota8[0:P, :],
                                S.t_sel0[0:P, :].broadcast_to([P, 8]), ALU.is_equal)
                v.drain()
                v.scalar_tensor_tensor(S.r_t8[0:P, :], S.r_ohA[0:P, :], -1e30, L, ALU.mult, ALU.add)
                v.drain()
                v.max(S.r_ohs[0:P, :], S.r_t8[0:P, :])
                v.drain()
                v.tensor_tensor(S.r_ohB[0:P, :], S.r_t8[0:P, :],
                                S.r_ohs[0:P, 0:1].broadcast_to([P, 8]), ALU.is_equal)
                v.drain()
                v.scalar_tensor_tensor(S.r_ohB[0:P, :], S.r_ohB[0:P, :], -1000.0, S.iota8p[0:P, :],
                                       ALU.mult, ALU.add)
                v.drain()
                v.tensor_reduce(S.t_sel1[0:P, :], S.r_ohB[0:P, :], axis=AX.X, op=ALU.min)
                v.drain()
                v.tensor_tensor(S.r_ohB[0:P, :], S.iota8[0:P, :],
                                S.t_sel1[0:P, :].broadcast_to([P, 8]), ALU.is_equal)
                v.tensor_tensor(S.t_a[0:P, :], S.r_ohs[0:P, 0:1], S.r_top8[0:P, 0:1], ALU.subtract)
                v.drain()
                v.tensor_tensor(S.r_ohs[0:P, :], S.r_ohA[0:P, :], S.r_ohB[0:P, :], ALU.add)
                v.engine_nop().then_inc(S.cV.h, 1)
            S.v(f); vv = S.cV.bump()

            def f(s, P=P, vv=vv):
                s.wait_ge(S.cV.h, vv)
                s.activation(S.t_a[0:P, :], S.t_a[0:P, :], AF.Exp).then_inc(S.cS.h, 1)
            S.s(f); sv = S.cS.bump()

            def f(v, P=P, sv=sv):
                v.wait_ge(S.cS.h, sv)
                v.tensor_scalar(S.t_a[0:P, :], S.t_a[0:P, :], 1.0, 1.0, ALU.mult, ALU.add)
                v.drain()
                v.reciprocal(S.t_cwA[0:P, :], S.t_a[0:P, :])
                v.drain()
                v.tensor_scalar(S.t_cwB[0:P, :], S.t_cwA[0:P, :], -1.0, 1.0, ALU.mult, ALU.add)
                v.engine_nop().then_inc(S.cV.h, 1)
            S.v(f); cwv_v = S.cV.bump()

            def f(t, i=i, P=P, cwv_v=cwv_v):
                t.wait_ge(S.cV.h, cwv_v)
                t.matmul(S.PB[0:P, 1536:1544], S.ustrict[0:P, 0:P], S.r_ohs[0:P, :], start=True, stop=False)
                t.matmul(S.PB[0:P, 1536:1544], S.ones1x[0:1, 0:P], S.carry[:, :], start=False, stop=True)
                t.matmul(S.PB[0:1, 1544:1552], S.onesc[0:P, 0:1], S.r_ohs[0:P, :],
                         start=True, stop=True).then_inc(S.cT.h, 1)
            S.t(f); tv2 = S.cT.bump()

            def f(v, i=i, P=P, tv2=tv2, wv_apply=wv_apply):
                v.wait_ge(S.cT.h, tv2)
                v.tensor_copy(S.r_t8[0:P, :], S.PB[0:P, 1536:1544])
                v.drain()
                v.tensor_tensor(S.carry[:, :], S.carry[:, :], S.PB[0:1, 1544:1552], ALU.add)
                v.drain()
                v.drain()
                v.tensor_tensor(S.r_top8[0:P, :], S.r_t8[0:P, :], S.r_ohA[0:P, :], ALU.mult)
                v.drain()
                v.tensor_reduce(S.t_pos0[0:P, :], S.r_top8[0:P, :], axis=AX.X, op=ALU.add)
                v.drain()
                v.tensor_tensor(S.r_top8[0:P, :], S.r_t8[0:P, :], S.r_ohB[0:P, :], ALU.mult)
                v.drain()
                v.tensor_reduce(S.t_pos1[0:P, :], S.r_top8[0:P, :], axis=AX.X, op=ALU.add)
                v.drain()
                for k, (sel, pos, vld, cw) in enumerate(
                        [(S.t_sel0, S.t_pos0, S.t_v0, S.t_cwA), (S.t_sel1, S.t_pos1, S.t_v1, S.t_cwB)]):
                    v.tensor_single_scalar(vld[0:P, :], pos[0:P, :], float(CAP), ALU.is_lt)
                    v.scalar_tensor_tensor(S.t_b[0:P, :], sel[0:P, :], float(CAP), pos[0:P, :],
                                           ALU.mult, ALU.add)
                    v.drain()
                    v.tensor_tensor(S.t_b[0:P, :], S.t_b[0:P, :], vld[0:P, :], ALU.mult)
                    v.tensor_scalar(S.t_mr[0:P, :], vld[0:P, :], -float(TRASH), float(TRASH),
                                    ALU.mult, ALU.add)
                    v.drain()
                    v.tensor_tensor(S.t_b[0:P, :], S.t_b[0:P, :], S.t_mr[0:P, :], ALU.add)
                    v.drain()
                    if wv_apply and i == 0:
                        v.tensor_tensor(S.t_b[0:WU, :], S.t_b[0:WU, :], S.wv_col[0:WU, :], ALU.mult)
                        v.drain()
                        v.tensor_tensor(S.t_b[0:WU, :], S.t_b[0:WU, :], S.tcol[0:WU, :], ALU.add)
                        v.drain()
                    v.tensor_copy(S.off_i[k][0:P, i:i + 1], S.t_b[0:P, :])
                    v.tensor_tensor(S.t_mr[0:P, :], cw[0:P, :], vld[0:P, :], ALU.mult)
                    v.drain()
                    v.tensor_scalar(S.cwvt[k][0:P, i:i + 1], S.t_mr[0:P, :], 0.5, None, ALU.mult)
                    v.drain()
                v.engine_nop().then_inc(S.cV.h, 1)
            S.v(f); S.cV.bump()
        return S.cV.n

    def emit_scatter(self, T, ln_done, rout_done):
        S = self
        nt = (T + 127) // 128
        for i in range(nt):
            r0 = i * 128
            P = min(128, T - r0)
            b = i % 2
            scprev = S.qSC.n

            def f(g, r0=r0, P=P, b=b, ln_done=ln_done, scprev=scprev):
                g.wait_ge(S.qYS[0].h, ln_done[0])
                g.wait_ge(S.qYS[1].h, ln_done[1])
                g.wait_ge(S.qSC.h, scprev)
                g.dma_start(out=S.ytm_t[b][0:P, :], in_=S.y_dram[r0:r0 + P, :]).then_inc(S.qYT[b].h, 16)
            S.g(f); ldv = S.qYT[b].bump(16)

            def f(g, i=i, P=P, b=b, ldv=ldv, rout_done=rout_done):
                g.wait_ge(S.qYT[b].h, ldv)
                g.wait_ge(S.cV.h, rout_done)
                g.indirect_dma_start(
                    out=S.xbuf[:, :], out_offset=IOX(S.off_i[0][0:P, i:i + 1], 0),
                    in_=S.ytm_t[b][0:P, :], in_offset=None).then_inc(S.qSC.h, 16)
                g.indirect_dma_start(
                    out=S.xbuf[:, :], out_offset=IOX(S.off_i[1][0:P, i:i + 1], 0),
                    in_=S.ytm_t[b][0:P, :], in_offset=None).then_inc(S.qSC.h, 16)
            S.g(f); S.qSC.bump(32)
        return S.qSC.n

    def emit_ffn(self, T, x_src, x_dst, use_cache, wv_apply, src_wait=None, src_wait_tile=None):
        S = self
        nt = (T + 127) // 128
        ntf = min(nt, 8)
        has64 = (nt == 9)

        ln_half, ln_done = self.emit_ln(x_src, T, src_wait, src_wait_tile)
        yfa, yfm_done = self.emit_yfm(T, ln_half, ln_done)

        cache_bar = 0
        if use_cache:
            cws = [(q.h, q.n) for q in (S.qCW + S.qCE + S.qCDW + S.qCED)]

            def f(g, cws=cws, cgp=S.cG.n):
                g.wait_ge(S.cG.h, cgp)
                for h, n in cws:
                    if n:
                        g.wait_ge(h, n)
                g.memset(S.gdummy[:], 0.0).then_inc(S.cG.h, 1)
            S.g(f); cache_bar = S.cG.bump()

        state = {}

        def gen_routing():
            for i in range(nt):
                r0 = i * 128
                P = min(128, T - r0)
                vprev = S.cV.n

                def f(t, i=i, r0=r0, P=P, yw=(yfa if i < 4 else yfm_done), vprev=vprev):
                    t.wait_ge(S.qF.h, yw)
                    t.wait_ge(S.cV.h, vprev)
                    for c in range(15):
                        t.matmul(S.PB[0:P, 1024:1032], S.yfm[:, c, r0:r0 + P], S.rt_sb[:, c, :],
                                 start=(c == 0), stop=False)
                    t.matmul(S.PB[0:P, 1024:1032], S.yfm[:, 15, r0:r0 + P], S.rt_sb[:, 15, :],
                             start=False, stop=True).then_inc(S.cT.h, 1)
                S.t(f); tv = S.cT.bump()

                def f(v, i=i, P=P, tv=tv):
                    v.wait_ge(S.cT.h, tv)
                    v.tensor_copy(S.lgt[0:P, i, :], S.PB[0:P, 1024:1032]).then_inc(S.cV.h, 1)
                S.v(f); S.cV.bump()
                yield

            def f(v):
                v.memset(S.carry[:], 0.0).then_inc(S.cV.h, 1)
            S.v(f); S.cV.bump()

            ytm_rel = [S.qSCp[0].n, S.qSCp[1].n]
            for i in range(nt):
                r0 = i * 128
                P = min(128, T - r0)
                lgdone = S.cV.n

                def f(v, i=i, P=P, lgdone=lgdone):
                    L = S.lgt[0:P, i, :]
                    v.wait_ge(S.cV.h, lgdone)
                    v.max(S.r_top8[0:P, :], L)
                    v.drain()
                    v.tensor_tensor(S.r_t8[0:P, :], L, S.r_top8[0:P, 0:1].broadcast_to([P, 8]), ALU.is_equal)
                    v.drain()
                    v.scalar_tensor_tensor(S.r_t8[0:P, :], S.r_t8[0:P, :], -1000.0, S.iota8p[0:P, :],
                                           ALU.mult, ALU.add)
                    v.drain()
                    v.tensor_reduce(S.t_sel0[0:P, :], S.r_t8[0:P, :], axis=AX.X, op=ALU.min)
                    v.drain()
                    v.tensor_tensor(S.r_ohA[0:P, :], S.iota8[0:P, :],
                                    S.t_sel0[0:P, :].broadcast_to([P, 8]), ALU.is_equal)
                    v.drain()
                    v.scalar_tensor_tensor(S.r_t8[0:P, :], S.r_ohA[0:P, :], -1e30, L, ALU.mult, ALU.add)
                    v.drain()
                    v.max(S.r_ohs[0:P, :], S.r_t8[0:P, :])
                    v.drain()
                    v.tensor_tensor(S.r_ohB[0:P, :], S.r_t8[0:P, :],
                                    S.r_ohs[0:P, 0:1].broadcast_to([P, 8]), ALU.is_equal)
                    v.drain()
                    v.scalar_tensor_tensor(S.r_ohB[0:P, :], S.r_ohB[0:P, :], -1000.0, S.iota8p[0:P, :],
                                           ALU.mult, ALU.add)
                    v.drain()
                    v.tensor_reduce(S.t_sel1[0:P, :], S.r_ohB[0:P, :], axis=AX.X, op=ALU.min)
                    v.drain()
                    v.tensor_tensor(S.r_ohB[0:P, :], S.iota8[0:P, :],
                                    S.t_sel1[0:P, :].broadcast_to([P, 8]), ALU.is_equal)
                    v.tensor_tensor(S.t_a[0:P, :], S.r_ohs[0:P, 0:1], S.r_top8[0:P, 0:1], ALU.subtract)
                    v.drain()
                    v.tensor_tensor(S.r_ohs[0:P, :], S.r_ohA[0:P, :], S.r_ohB[0:P, :], ALU.add)
                    v.engine_nop().then_inc(S.cV.h, 1)
                S.v(f); vv = S.cV.bump()

                def f(s, P=P, vv=vv):
                    s.wait_ge(S.cV.h, vv)
                    s.activation(S.t_a[0:P, :], S.t_a[0:P, :], AF.Exp).then_inc(S.cS.h, 1)
                S.s(f); sv = S.cS.bump()

                def f(v, P=P, sv=sv):
                    v.wait_ge(S.cS.h, sv)
                    v.tensor_scalar(S.t_a[0:P, :], S.t_a[0:P, :], 1.0, 1.0, ALU.mult, ALU.add)
                    v.drain()
                    v.reciprocal(S.t_cwA[0:P, :], S.t_a[0:P, :])
                    v.drain()
                    v.tensor_scalar(S.t_cwB[0:P, :], S.t_cwA[0:P, :], -1.0, 1.0, ALU.mult, ALU.add)
                    v.engine_nop().then_inc(S.cV.h, 1)
                S.v(f); cwv_v = S.cV.bump()

                def f(t, i=i, P=P, cwv_v=cwv_v):
                    t.wait_ge(S.cV.h, cwv_v)
                    t.matmul(S.PB[0:P, 1536:1544], S.ustrict[0:P, 0:P], S.r_ohs[0:P, :],
                             start=True, stop=False)
                    t.matmul(S.PB[0:P, 1536:1544], S.ones1x[0:1, 0:P], S.carry[:, :],
                             start=False, stop=True)
                    t.matmul(S.PB[0:1, 1544:1552], S.onesc[0:P, 0:1], S.r_ohs[0:P, :],
                             start=True, stop=True).then_inc(S.cT.h, 1)
                S.t(f); tv2 = S.cT.bump()

                def f(v, i=i, P=P, tv2=tv2, wv_apply=wv_apply):
                    v.wait_ge(S.cT.h, tv2)
                    v.tensor_copy(S.r_t8[0:P, :], S.PB[0:P, 1536:1544])
                    v.drain()
                    v.tensor_tensor(S.carry[:, :], S.carry[:, :], S.PB[0:1, 1544:1552], ALU.add)
                    v.drain()
                    v.tensor_tensor(S.r_top8[0:P, :], S.r_t8[0:P, :], S.r_ohA[0:P, :], ALU.mult)
                    v.drain()
                    v.tensor_reduce(S.t_pos0[0:P, :], S.r_top8[0:P, :], axis=AX.X, op=ALU.add)
                    v.drain()
                    v.tensor_tensor(S.r_top8[0:P, :], S.r_t8[0:P, :], S.r_ohB[0:P, :], ALU.mult)
                    v.drain()
                    v.tensor_reduce(S.t_pos1[0:P, :], S.r_top8[0:P, :], axis=AX.X, op=ALU.add)
                    v.drain()
                    for k, (sel, pos, vld, cw) in enumerate(
                            [(S.t_sel0, S.t_pos0, S.t_v0, S.t_cwA), (S.t_sel1, S.t_pos1, S.t_v1, S.t_cwB)]):
                        v.tensor_single_scalar(vld[0:P, :], pos[0:P, :], float(CAP), ALU.is_lt)
                        v.scalar_tensor_tensor(S.t_b[0:P, :], sel[0:P, :], float(CAP), pos[0:P, :],
                                               ALU.mult, ALU.add)
                        v.drain()
                        v.tensor_tensor(S.t_b[0:P, :], S.t_b[0:P, :], vld[0:P, :], ALU.mult)
                        v.tensor_scalar(S.t_mr[0:P, :], vld[0:P, :], -float(TRASH), float(TRASH),
                                        ALU.mult, ALU.add)
                        v.drain()
                        v.tensor_tensor(S.t_b[0:P, :], S.t_b[0:P, :], S.t_mr[0:P, :], ALU.add)
                        v.drain()
                        if wv_apply and i == 0:
                            v.tensor_tensor(S.t_b[0:WU, :], S.t_b[0:WU, :], S.wv_col[0:WU, :], ALU.mult)
                            v.drain()
                            v.tensor_tensor(S.t_b[0:WU, :], S.t_b[0:WU, :], S.tcol[0:WU, :], ALU.add)
                            v.drain()
                        v.tensor_copy(S.off_i[k][0:P, i:i + 1], S.t_b[0:P, :])
                        v.tensor_tensor(S.t_mr[0:P, :], cw[0:P, :], vld[0:P, :], ALU.mult)
                        v.drain()
                        v.tensor_scalar(S.cwvt[k][0:P, i:i + 1], S.t_mr[0:P, :], 0.5, None, ALU.mult)
                        v.drain()
                    v.engine_nop().then_inc(S.cV.h, 1)
                S.v(f); rd = S.cV.bump()

                b = i % 2

                def f(g, r0=r0, P=P, b=b, ln_done=ln_done, yrel=ytm_rel[b]):
                    g.wait_ge(S.qYS[0].h, ln_done[0])
                    g.wait_ge(S.qYS[1].h, ln_done[1])
                    g.wait_ge(S.qSCp[b].h, yrel)
                    g.dma_start(out=S.ytm_t[b][0:P, :], in_=S.y_dram[r0:r0 + P, :]).then_inc(S.qYT[b].h, 16)
                S.g(f); ldv = S.qYT[b].bump(16)

                def f(g, i=i, P=P, b=b, ldv=ldv, rd=rd):
                    g.wait_ge(S.qYT[b].h, ldv)
                    g.wait_ge(S.cV.h, rd)
                    g.indirect_dma_start(
                        out=S.xbuf[:, :], out_offset=IOX(S.off_i[0][0:P, i:i + 1], 0),
                        in_=S.ytm_t[b][0:P, :], in_offset=None).then_inc(S.qSCp[b].h, 16)
                    g.indirect_dma_start(
                        out=S.xbuf[:, :], out_offset=IOX(S.off_i[1][0:P, i:i + 1], 0),
                        in_=S.ytm_t[b][0:P, :], in_offset=None).then_inc(S.qSCp[b].h, 16)
                S.g(f); ytm_rel[b] = S.qSCp[b].bump(32)
                yield
            state["scat_done"] = [(S.qSCp[0].h, S.qSCp[0].n), (S.qSCp[1].h, S.qSCp[1].n)]

        def gen_shared():
            HB = FJ // 2
            xs_store = state["xs_store"]
            slab_mm = [S.cT.n, S.cT.n]
            slab_cw = [S.qCW[0].n, S.qCW[1].n]
            t8_ev = 0
            for hf in range(2):
                ev_par = [[(S.cV.h, S.cV.n), (S.cS.h, S.cS.n)] for _ in range(2)]
                ev64 = [(S.cV.h, t8_ev)] if t8_ev else []
                for phase in range(2):
                    for k2 in range(11):
                        b = k2 % 2

                        def f(g, b=b, phase=phase, hf=hf, k2=k2, sfm=slab_mm[b], sfc=slab_cw[b],
                              cb=cache_bar):
                            g.wait_ge(S.cT.h, sfm)
                            g.wait_ge(S.qCW[b].h, sfc)
                            if cb:
                                g.wait_ge(S.cG.h, cb)
                            g.dma_start(out=S.wslab[b][:],
                                        in_=S.SugA[phase, hf, k2]).then_inc(S.qWS[b].h, 16)
                        S.g(f); wld = S.qWS[b].bump(16)

                        for sub in range(2):
                            j = k2 * 2 + sub
                            par = j % 2

                            def f(t, b=b, sub=sub, par=par, wld=wld, epw=list(ev_par[par]),
                                  e64=list(ev64), has64=has64, yw=yfm_done):
                                t.wait_ge(S.qWS[b].h, wld)
                                t.wait_ge(S.qF.h, yw)
                                for hh, nn in epw:
                                    t.wait_ge(hh, nn)
                                for c in range(16):
                                    last = (c == 15)
                                    wsl = S.wslab[b][:, c, sub * 128:(sub + 1) * 128]
                                    t.matmul(S.PA[:, par * 1024: par * 1024 + 512], wsl,
                                             S.yfm[:, c, 0:512], start=(c == 0), stop=last)
                                    mm2 = t.matmul(S.PA[:, par * 1024 + 512: par * 1024 + 1024], wsl,
                                                   S.yfm[:, c, 512:1024], start=(c == 0), stop=last)
                                if has64:
                                    for hh, nn in e64:
                                        t.wait_ge(hh, nn)
                                    for c in range(16):
                                        mm2 = t.matmul(S.PB[:, 0:64], S.wslab[b][:, c, sub * 128:(sub + 1) * 128],
                                                       S.yfm[:, c, 1024:1088], start=(c == 0), stop=(c == 15))
                                mm2.then_inc(S.cT.h, 1)
                            S.t(f); mmv = S.cT.bump()
                            slab_mm[b] = mmv

                            if phase == 0:
                                def f(s, j=j, par=par, mmv=mmv, has64=has64):
                                    s.wait_ge(S.cT.h, mmv)
                                    s.activation(S.hsh[:, j, 0:512], S.PA[:, par * 1024: par * 1024 + 512], AF.Silu)
                                    act2 = s.activation(S.hsh[:, j, 512:1024],
                                                        S.PA[:, par * 1024 + 512: par * 1024 + 1024], AF.Silu)
                                    if has64:
                                        act2 = s.activation(S.hsh[:, j, 1024:1088], S.PB[:, 0:64], AF.Silu)
                                    act2.then_inc(S.cS.h, 1)
                                S.s(f); ev = (S.cS.h, S.cS.bump())
                            else:
                                def f(v, j=j, par=par, mmv=mmv, has64=has64):
                                    v.wait_ge(S.cT.h, mmv)
                                    v.tensor_tensor(S.hsh[:, j, 0:512], S.PA[:, par * 1024: par * 1024 + 512],
                                                    S.hsh[:, j, 0:512], ALU.mult)
                                    tt2 = v.tensor_tensor(S.hsh[:, j, 512:1024],
                                                          S.PA[:, par * 1024 + 512: par * 1024 + 1024],
                                                          S.hsh[:, j, 512:1024], ALU.mult)
                                    if has64:
                                        tt2 = v.tensor_tensor(S.hsh[:, j, 1024:1088], S.PB[:, 0:64],
                                                              S.hsh[:, j, 1024:1088], ALU.mult)
                                    tt2.then_inc(S.cV.h, 1)
                                S.v(f); ev = (S.cV.h, S.cV.bump())
                            ev_par[par] = [ev]
                            ev64 = [ev]
                        yield "up"
                up_done = S.cV.n
                up_done_s = S.cS.n

                dslab_mm = [S.cT.n, S.cT.n]
                dslab_cw = [S.qCDW[0].n, S.qCDW[1].n]
                ev_pa = [(S.cV.h, S.cV.n), (S.cS.h, S.cS.n)]
                batches = [list(range(0, min(4, ntf)))] + ([list(range(4, ntf))] if ntf > 4 else [])
                chunk_sets = [list(range(k, min(k + 4, 22))) for k in range(0, 22, 4)]
                for dg in range(4):
                    for bi, tts in enumerate(batches):
                        with_t8 = has64 and bi == len(batches) - 1
                        set_mm = []
                        for si, cs in enumerate(chunk_sets):
                            b = si % 2
                            use_blk = use_cache or bi > 0

                            def f(g, b=b, cs=cs, dg=dg, hf=hf, sfm=dslab_mm[b], sfc=dslab_cw[b],
                                  use_blk=use_blk, up_done=up_done, cb=cache_bar):
                                g.wait_ge(S.cT.h, sfm)
                                g.wait_ge(S.qCDW[b].h, sfc)
                                g.wait_ge(S.cV.h, up_done)
                                if cb:
                                    g.wait_ge(S.cG.h, cb)
                                g.dma_start(out=S.dslab[b][:, 0:len(cs), :],
                                            in_=S.SdnA[hf, dg][:, cs[0]:cs[0] + len(cs), :]
                                            ).then_inc(S.qDS[b].h, 16)
                            S.g(f); wld = S.qDS[b].bump(16)

                            def f(t, b=b, cs=cs, si=si, wld=wld, tts=tts, epa=list(ev_pa),
                                  ups=up_done_s, nsets=len(chunk_sets), with_t8=with_t8):
                                t.wait_ge(S.qDS[b].h, wld)
                                if si == 0:
                                    t.wait_ge(S.cS.h, ups)
                                    for hh, nn in epa:
                                        t.wait_ge(hh, nn)
                                for ci, fc in enumerate(cs):
                                    for ti, tt in enumerate(tts):
                                        ps = S.PA[:, ti * 512:(ti + 1) * 512]
                                        mm = t.matmul(ps, S.hsh[:, fc, tt * 128:(tt + 1) * 128],
                                                      S.dslab[b][:, ci, :],
                                                      start=(si == 0 and ci == 0),
                                                      stop=(si == nsets - 1 and ci == len(cs) - 1),
                                                      skip_group_check=True)
                                    if with_t8:
                                        mm = t.matmul(S.PB[0:64, 0:512], S.hsh[:, fc, 1024:1088],
                                                      S.dslab[b][:, ci, :],
                                                      start=(si == 0 and ci == 0),
                                                      stop=(si == nsets - 1 and ci == len(cs) - 1),
                                                      skip_group_check=True)
                                mm.then_inc(S.cT.h, 1)
                            S.t(f); setv = S.cT.bump()
                            set_mm.append(setv)
                            dslab_mm[b] = setv
                        mmv = set_mm[-1]

                        base_src = x_src if hf == 0 else x_dst
                        bsw = (src_wait or []) if hf == 0 else state["x_written"]
                        ev_pa = []
                        rtl = list(tts) + ([8] if with_t8 else [])
                        for ti, tt in enumerate(rtl):
                            is8 = tt == 8
                            P = 64 if is8 else 128
                            xb = tt % 2
                            vprev = S.cV.n

                            def f(y, tt=tt, P=P, xb=xb, dg=dg, base_src=base_src, vprev=vprev,
                                  bsw=bsw, xsw=xs_store[xb]):
                                y.wait_ge(S.cV.h, vprev)
                                y.wait_ge(S.qXS[xb].h, xsw)
                                for hh, nn in bsw:
                                    y.wait_ge(hh, nn)
                                y.dma_start(out=S.xt[xb][0:P, 0:512],
                                            in_=base_src[tt * 128:tt * 128 + P, dg * 512:(dg + 1) * 512]
                                            ).then_inc(S.qXT[xb].h, 16)
                            S.y(f); xld = S.qXT[xb].bump(16)

                            def f(v, ti=ti, P=P, is8=is8, xb=xb, mmv=mmv, xld=xld):
                                v.wait_ge(S.cT.h, mmv)
                                v.wait_ge(S.qXT[xb].h, xld)
                                ps = S.PB[0:64, 0:512] if is8 else S.PA[0:P, ti * 512:(ti + 1) * 512]
                                v.scalar_tensor_tensor(S.xt[xb][0:P, 0:512], ps,
                                                       0.5, S.xt[xb][0:P, 0:512],
                                                       ALU.mult, ALU.add).then_inc(S.cV.h, 1)
                            S.v(f); vv = S.cV.bump()
                            ev_pa.append((S.cV.h, vv))
                            if is8:
                                t8_ev = vv

                            def f(y, tt=tt, P=P, xb=xb, dg=dg, vv=vv):
                                y.wait_ge(S.cV.h, vv)
                                y.dma_start(out=x_dst[tt * 128:tt * 128 + P, dg * 512:(dg + 1) * 512],
                                            in_=S.xt[xb][0:P, 0:512]).then_inc(S.qXS[xb].h, 16)
                            S.y(f); xs_store[xb] = S.qXS[xb].bump(16)
                        ev_pa = [ev_pa[-1]]
                        yield "down"
                state["x_written"] = [(S.qXS[0].h, S.qXS[0].n), (S.qXS[1].h, S.qXS[1].n)]

        def gen_experts():
            xg_mm = S.cT.n
            eslab_mm = [S.cT.n, S.cT.n]
            edslab_mm = [S.cT.n, S.cT.n]
            estg_free = [S.qOBs[0].n, S.qOBs[1].n]
            estg_ix = 0
            dregions = [(1536, 2048)] if has64 else [(1536, 2048), (0, 512)]
            ev_reg = {0: [], 1: []}
            ev_sub = [[], []]
            scd = state["scat_done"]
            xgld = S.qXG[0].n
            for e in range(E):
                def f(y, e=e, scd=list(scd), xf=xg_mm):
                    for hh, nn in scd:
                        y.wait_ge(hh, nn)
                    y.wait_ge(S.cT.h, xf)
                    for c in range(16):
                        y.dma_start(out=S.xg[0][:, c, :],
                                    in_=S.xbuf[e * CAP:(e + 1) * CAP, c * 128:(c + 1) * 128],
                                    transpose=True).then_inc(S.qXG[0].h, 16)
                S.y(f); xgld = S.qXG[0].bump(16 * 16)
                yield

                for sl in range(11):
                    eb = sl % 2

                    def f(g, e=e, sl=sl, eb=eb, sfm=eslab_mm[eb],
                          scw=(list(scd) if e == 0 and sl < 2 else [])):
                        g.wait_ge(S.cT.h, sfm)
                        for hh, nn in scw:
                            g.wait_ge(hh, nn)
                        g.dma_start(out=S.eslab[eb][:],
                                    in_=S.WugA[e, sl]).then_inc(S.qES[eb].h, 16)
                    S.g(f); wld = S.qES[eb].bump(16)

                    for jj in range(2):
                        j = sl * 2 + jj
                        is_gate = j < 11
                        pcol = 512 if j % 2 == 0 else 1024

                        def f(t, eb=eb, jj=jj, pcol=pcol, wld=wld, xgld=xgld,
                              epw=list(ev_sub[j % 2])):
                            t.wait_ge(S.qES[eb].h, wld)
                            t.wait_ge(S.qXG[0].h, xgld)
                            for hh, nn in epw:
                                t.wait_ge(hh, nn)
                            for c in range(16):
                                mm = t.matmul(S.PB[:, pcol:pcol + CAP],
                                              S.eslab[eb][:, c, jj * 128:(jj + 1) * 128],
                                              S.xg[0][:, c, :], start=(c == 0), stop=(c == 15))
                            mm.then_inc(S.cT.h, 1)
                        S.t(f); mmv = S.cT.bump()
                        xg_mm = mmv
                        eslab_mm[eb] = mmv

                        if is_gate:
                            def f(s, j=j, pcol=pcol, mmv=mmv):
                                s.wait_ge(S.cT.h, mmv)
                                s.activation(S.he[:, j, :], S.PB[:, pcol:pcol + CAP],
                                             AF.Silu).then_inc(S.cS.h, 1)
                            S.s(f); ev_sub[j % 2] = [(S.cS.h, S.cS.bump())]
                        else:
                            def f(v, j=j, pcol=pcol, mmv=mmv):
                                v.wait_ge(S.cT.h, mmv)
                                v.tensor_tensor(S.he[:, j - 11, :], S.PB[:, pcol:pcol + CAP],
                                                S.he[:, j - 11, :], ALU.mult).then_inc(S.cV.h, 1)
                            S.v(f); ev_sub[j % 2] = [(S.cV.h, S.cV.bump())]
                    if sl % 2 == 1 or sl == 10:
                        yield
                he_done = S.cV.n
                he_done_s = S.cS.n

                for dg in range(4):
                    db = dg % 2

                    def f(g, e=e, dg=dg, db=db, sfm=edslab_mm[db],
                          scw=(list(scd) if e == 0 and dg < 2 else [])):
                        g.wait_ge(S.cT.h, sfm)
                        for hh, nn in scw:
                            g.wait_ge(hh, nn)
                        g.dma_start(out=S.edslab[db][:],
                                    in_=S.WdnA[e, dg]).then_inc(S.qED[db].h, 16)
                    S.g(f); wld = S.qED[db].bump(16)

                    for st in range(3):
                        SS = 128 if st < 2 else 96

                        eix = estg_ix % 2
                        estg_ix += 1
                        rix = eix % len(dregions)
                        pc0, pc1 = dregions[rix]

                        def f(t, st=st, SS=SS, db=db, wld=wld, epw=list(ev_reg[rix]), hs=he_done_s,
                              hv=he_done, pc0=pc0, pc1=pc1):
                            t.wait_ge(S.qED[db].h, wld)
                            t.wait_ge(S.cS.h, hs)
                            t.wait_ge(S.cV.h, hv)
                            for hh, nn in epw:
                                t.wait_ge(hh, nn)
                            for fc in range(11):
                                mm = t.matmul(S.PB[0:SS, pc0:pc1],
                                              S.he[:, fc, st * 128: st * 128 + SS],
                                              S.edslab[db][:, fc, :], start=(fc == 0), stop=(fc == 10))
                            mm.then_inc(S.cT.h, 1)
                        S.t(f); mmv = S.cT.bump()
                        edslab_mm[db] = mmv

                        if eix == 0:
                            def f(v, SS=SS, mmv=mmv, ef=estg_free[0], pc0=pc0, pc1=pc1):
                                v.wait_ge(S.cT.h, mmv)
                                v.wait_ge(S.qOBs[0].h, ef)
                                v.tensor_copy(S.estg[0][0:SS, :],
                                              S.PB[0:SS, pc0:pc1]).then_inc(S.cV.h, 1)
                            S.v(f); ev_reg[rix] = [(S.cV.h, S.cV.bump())]
                        else:
                            def f(s, SS=SS, mmv=mmv, ef=estg_free[1], pc0=pc0, pc1=pc1):
                                s.wait_ge(S.cT.h, mmv)
                                s.wait_ge(S.qOBs[1].h, ef)
                                s.activation(S.estg[1][0:SS, :],
                                             S.PB[0:SS, pc0:pc1], AF.Copy).then_inc(S.cS.h, 1)
                            S.s(f); ev_reg[rix] = [(S.cS.h, S.cS.bump())]
                        evh, evv = ev_reg[rix][0]

                        def f(y, e=e, st=st, SS=SS, dg=dg, eix=eix, evh=evh, evv=evv):
                            y.wait_ge(evh, evv)
                            y.dma_start(out=S.outbuf[e * CAP + st * 128: e * CAP + st * 128 + SS,
                                                     dg * 512:(dg + 1) * 512],
                                        in_=S.estg[eix][0:SS, :]).then_inc(S.qOBs[eix].h, 16)
                        S.y(f); estg_free[eix] = S.qOBs[eix].bump(16)
                    yield
            state["experts_done"] = [(S.qOBs[0].h, S.qOBs[0].n), (S.qOBs[1].h, S.qOBs[1].n)]

        state["xs_store"] = [S.qXS[0].n, S.qXS[1].n]
        gr = gen_routing()
        gs = gen_shared()
        ge = gen_experts()
        alive_r, alive_s, alive_e = True, True, False
        started_e = False
        while alive_r or alive_s or started_e:
            if alive_r:
                try:
                    next(gr); 
                    try:
                        next(gr)
                    except StopIteration:
                        alive_r = False
                except StopIteration:
                    alive_r = False
                if not alive_r:
                    started_e = True
            if alive_s:
                try:
                    next(gs)
                except StopIteration:
                    alive_s = False
            if started_e:
                try:
                    next(ge)
                except StopIteration:
                    started_e = False
            if not (alive_r or alive_s or started_e):
                break

        xs_store = state["xs_store"]
        x_written = state["x_written"]
        experts_done = state["experts_done"]
        for i in range(nt):
            r0 = i * 128
            P = min(128, T - r0)
            xb = i % 2
            gprev = S.cV.n

            def f(g, i=i, P=P, experts_done=experts_done, gprev=gprev,
                  cwf=[(q.h, q.n) for q in (S.qCE + S.qCED)]):
                for hh, nn in experts_done:
                    g.wait_ge(hh, nn)
                for hh, nn in cwf:
                    if nn:
                        g.wait_ge(hh, nn)
                g.wait_ge(S.cV.h, gprev)
                g.indirect_dma_start(
                    out=S.g0t[0:P, :], out_offset=None,
                    in_=S.outbuf[:, :], in_offset=IOX(S.off_i[0][0:P, i:i + 1], 0)).then_inc(S.qG.h, 16)
                g.indirect_dma_start(
                    out=S.g1t[0:P, :], out_offset=None,
                    in_=S.outbuf[:, :], in_offset=IOX(S.off_i[1][0:P, i:i + 1], 0)).then_inc(S.qG.h, 16)
            S.g(f); gld = S.qG.bump(32)

            def f(y, r0=r0, P=P, xb=xb, gprev=gprev, xww=list(x_written) + list(experts_done),
                  xsw=xs_store[xb]):
                y.wait_ge(S.cV.h, gprev)
                y.wait_ge(S.qXS[xb].h, xsw)
                for hh, nn in xww:
                    y.wait_ge(hh, nn)
                y.dma_start(out=S.xt[xb][0:P, :], in_=x_dst[r0:r0 + P, :]).then_inc(S.qXT[xb].h, 16)
            S.y(f); xld = S.qXT[xb].bump(16)

            def f(v, i=i, P=P, xb=xb, gld=gld, xld=xld):
                v.wait_ge(S.qG.h, gld)
                v.wait_ge(S.qXT[xb].h, xld)
                v.scalar_tensor_tensor(S.xt[xb][0:P, :], S.g0t[0:P, :], S.cwvt[0][0:P, i:i + 1],
                                       S.xt[xb][0:P, :], ALU.mult, ALU.add)
                v.drain()
                v.scalar_tensor_tensor(S.xt[xb][0:P, :], S.g1t[0:P, :], S.cwvt[1][0:P, i:i + 1],
                                       S.xt[xb][0:P, :], ALU.mult, ALU.add).then_inc(S.cV.h, 1)
            S.v(f); vv = S.cV.bump()

            def f(y, r0=r0, P=P, xb=xb, vv=vv):
                y.wait_ge(S.cV.h, vv)
                y.dma_start(out=x_dst[r0:r0 + P, :], in_=S.xt[xb][0:P, :]).then_inc(S.qXS[xb].h, 16)
            S.y(f); xs_store[xb] = S.qXS[xb].bump(16)
        return [(S.qXS[0].h, S.qXS[0].n), (S.qXS[1].h, S.qXS[1].n)]

    def emit_attn(self, x1_wait):
        S = self
        T = TEXT
        bar0 = self.emit_barrier(self.all_sems())
        x1_wait = list(x1_wait) + [(S.cG.h, bar0)]
        ln_half, ln_done = self.emit_ln(S.x1_dram, T, x1_wait)
        yfa, yfm_done = self.emit_yfm(T, ln_half, ln_done)

        b3bar = self.emit_barrier([S.qCW[0], S.qCW[1], S.qCDW[0], S.qCDW[1],
                                   S.qCE[0], S.qCE[1], S.qCED[0], S.qCED[1],
                                   S.qDS[0], S.qDS[1], S.qYT[0], S.qYT[1]])
        def f(v, b3bar=b3bar):
            v.wait_ge(S.cG.h, b3bar)
            v.memset(S.conv_z[0][:, 0:3], 0.0)
            v.memset(S.conv_z[1][:, 0:3], 0.0).then_inc(S.cV.h, 1)
        S.v(f); padz = S.cV.bump()

        projs = [(S.WqA, S.qfm_d, 0), (S.WkA, S.kfm_d, 1), (S.WvA, S.vfm_d, 2), (S.WwA, S.wfm_d, 2)]
        slab_mm = [S.cT.n, S.cT.n]
        ev_par = [[], []]
        for wi, (Wsrc, dst_dram, cslot) in enumerate(projs):
            for k4 in range(4):
                gi4 = wi * 4 + k4
                b = gi4 % 2

                def f(g, Wsrc=Wsrc, k4=k4, b=b, sfm=slab_mm[b]):
                    g.wait_ge(S.cT.h, sfm)
                    g.dma_start(out=S.wslab4[b][:],
                                in_=Wsrc[k4]).then_inc(S.qWS[b].h, 16)
                S.g(f); wld = S.qWS[b].bump(16)

                for sub in range(4):
                    mt = k4 * 4 + sub
                    gi = wi * 16 + mt
                    par = gi % 2
                    ob = gi % 2

                    def f(t, b=b, sub=sub, par=par, wld=wld, epw=list(ev_par[par]), yfm_done=yfm_done):
                        t.wait_ge(S.qWS[b].h, wld)
                        t.wait_ge(S.qF.h, yfm_done)
                        for hh, nn in epw:
                            t.wait_ge(hh, nn)
                        for c in range(16):
                            last = (c == 15)
                            wsl = S.wslab4[b][:, c, sub * 128:(sub + 1) * 128]
                            t.matmul(S.PA[:, par * 1024: par * 1024 + 512], wsl,
                                     S.yfm[:, c, 0:512], start=(c == 0), stop=last)
                            t.matmul(S.PA[:, par * 1024 + 512: par * 1024 + 1024], wsl,
                                     S.yfm[:, c, 512:1024], start=(c == 0), stop=last)
                            mm2 = t.matmul(S.PB[:, par * 512:par * 512 + 64], wsl,
                                           S.yfm[:, c, 1024:1088], start=(c == 0), stop=last)
                            if last:
                                mm2.then_inc(S.cT.h, 1)
                    S.t(f); mmv = S.cT.bump()
                    slab_mm[b] = mmv

                    def f(s, par=par, ob=ob, mmv=mmv, padz=padz, cvd=S.cV.n):
                        s.wait_ge(S.cT.h, mmv)
                        s.wait_ge(S.cV.h, cvd)
                        s.activation(S.conv_z[ob][:, 3:515], S.PA[:, par * 1024: par * 1024 + 512], AF.Copy)
                        s.activation(S.conv_z[ob][:, 515:1027], S.PA[:, par * 1024 + 512: par * 1024 + 1024],
                                     AF.Copy)
                        s.activation(S.conv_z[ob][:, 1027:1091], S.PB[:, par * 512:par * 512 + 64],
                                     AF.Copy).then_inc(S.cS.h, 1)
                    S.s(f); zev = S.cS.bump()
                    ev_par[par] = [(S.cS.h, zev)]

                    def f(v, mt=mt, cslot=cslot, ob=ob, zev=zev, padz=padz, kvw=S.qKV[ob].n):
                        v.wait_ge(S.cS.h, zev)
                        v.wait_ge(S.cV.h, padz)
                        z = S.conv_z[ob]
                        cw = S.convw
                        v.tensor_scalar(S.conv_a[:, 0:TEXT], z[:, 0:TEXT], cw[:, cslot, mt, 0:1], None, ALU.mult)
                        v.drain()
                        v.scalar_tensor_tensor(S.conv_a[:, 0:TEXT], z[:, 1:TEXT + 1], cw[:, cslot, mt, 1:2],
                                               S.conv_a[:, 0:TEXT], ALU.mult, ALU.add)
                        v.drain()
                        v.scalar_tensor_tensor(S.conv_a[:, 0:TEXT], z[:, 2:TEXT + 2], cw[:, cslot, mt, 2:3],
                                               S.conv_a[:, 0:TEXT], ALU.mult, ALU.add)
                        v.drain()
                        v.scalar_tensor_tensor(S.conv_a[:, 0:TEXT], z[:, 3:TEXT + 3], cw[:, cslot, mt, 3:4],
                                               S.conv_a[:, 0:TEXT], ALU.mult, ALU.add)
                        v.wait_ge(S.qKV[ob].h, kvw)
                        v.engine_nop().then_inc(S.cV.h, 1)
                    S.v(f); cvv = S.cV.bump()

                    def f(s, ob=ob, cvv=cvv):
                        s.wait_ge(S.cV.h, cvv)
                        s.activation(S.conv_o[ob][:, :], S.conv_a[:, 0:TEXT], AF.Silu).then_inc(S.cS.h, 1)
                    S.s(f); sil = S.cS.bump()

                    def f(y, dst_dram=dst_dram, mt=mt, ob=ob, sil=sil):
                        y.wait_ge(S.cS.h, sil)
                        y.dma_start(out=dst_dram[mt * 128:(mt + 1) * 128, :],
                                    in_=S.conv_o[ob][:, :]).then_inc(S.qKV[ob].h, 16)
                    S.y(f); S.qKV[ob].bump(16)
        kv_done = (S.qKV[0].n, S.qKV[1].n)

        evs, evv = S.cS.n, S.cV.n

        def f(t, evs=evs, evv=evv, yfm_done=yfm_done):
            t.wait_ge(S.cS.h, evs)
            t.wait_ge(S.cV.h, evv)
            t.wait_ge(S.qF.h, yfm_done)
            t.wait_ge(S.qS.h, S.setup_q)
            for c in range(16):
                last = (c == 15)
                t.matmul(S.PA[:, 0:512], S.wf1_sb[:, c, :], S.yfm[:, c, 0:512], start=(c == 0), stop=last)
                t.matmul(S.PA[:, 512:1024], S.wf1_sb[:, c, :], S.yfm[:, c, 512:1024], start=(c == 0), stop=last)
                t.matmul(S.PB[:, 0:64], S.wf1_sb[:, c, :], S.yfm[:, c, 1024:1088], start=(c == 0), stop=last)
                t.matmul(S.PA[:, 1024:1536], S.wg1_sb[:, c, :], S.yfm[:, c, 0:512], start=(c == 0), stop=last)
                t.matmul(S.PA[:, 1536:2048], S.wg1_sb[:, c, :], S.yfm[:, c, 512:1024], start=(c == 0), stop=last)
                mm = t.matmul(S.PB[:, 512:576], S.wg1_sb[:, c, :], S.yfm[:, c, 1024:1088],
                              start=(c == 0), stop=last)
            mm.then_inc(S.cT.h, 1)
        S.t(f); fgv = S.cT.bump()

        def f(v, fgv=fgv):
            v.wait_ge(S.cT.h, fgv)
            v.tensor_copy(S.f1_sb[:, 0:512], S.PA[:, 0:512])
            v.tensor_copy(S.f1_sb[:, 512:1024], S.PA[:, 512:1024])
            v.tensor_copy(S.f1_sb[:, 1024:1088], S.PB[:, 0:64])
            v.tensor_copy(S.g1_sb[:, 0:512], S.PA[:, 1024:1536])
            v.tensor_copy(S.g1_sb[:, 512:1024], S.PA[:, 1536:2048])
            v.tensor_copy(S.g1_sb[:, 1024:1088], S.PB[:, 512:576]).then_inc(S.cV.h, 1)
        S.v(f); fgev = S.cV.bump()

        for ci in range(9):
            t0 = 0 if ci == 0 else 64 + (ci - 1) * 128
            Tc = 64 if ci == 0 else 128

            def f(t, ci=ci, t0=t0, Tc=Tc, fgev=fgev, prev=S.cS.n, yfm_done=yfm_done):
                t.wait_ge(S.cV.h, fgev)
                t.wait_ge(S.cS.h, prev)
                t.wait_ge(S.qF.h, yfm_done)
                for c in range(15):
                    t.matmul(S.PB[0:Tc, 1024:1040], S.yfm[:, c, t0:t0 + Tc], S.wb_sb[:, c, :],
                             start=(c == 0), stop=False)
                t.matmul(S.PB[0:Tc, 1024:1040], S.yfm[:, 15, t0:t0 + Tc], S.wb_sb[:, 15, :],
                         start=False, stop=True).then_inc(S.cT.h, 1)
            S.t(f); bv = S.cT.bump()

            def f(s, ci=ci, Tc=Tc, bv=bv):
                s.wait_ge(S.cT.h, bv)
                s.activation(S.beta[0:Tc, ci, :], S.PB[0:Tc, 1024:1040], AF.Sigmoid).then_inc(S.cS.h, 1)
            S.s(f); S.cS.bump()

        sbar = self.emit_barrier(self.all_sems())

        def f(g, sbar=sbar):
            g.wait_ge(S.cG.h, sbar)
            g.dma_start(out=S.bg2row[:],
                        in_=S.bg2A.rearrange("(o f) -> o f", o=1)).then_inc(S.qS.h, 16)
        S.g(f); bg2ld = S.qS.bump(16)

        def f(v, sbar=sbar):
            v.wait_ge(S.cG.h, sbar)
            v.memset(S.Sk[:], 0.0)
            v.memset(S.Sv[:], 0.0)
            v.memset(S.Skp[:], 0.0)
            v.memset(S.Svp[:], 0.0)
            v.tensor_copy(S.nwfull[:], S.nw128[:, :].unsqueeze(1).broadcast_to([128, H, DV]))
            v.engine_nop().then_inc(S.cV.h, 1)
        S.v(f); sinit = S.cV.bump()

        def f(t, sinit=sinit, bg2ld=bg2ld):
            t.wait_ge(S.cV.h, sinit)
            t.wait_ge(S.qS.h, bg2ld)
            for i in range(4):
                mm = t.matmul(S.PA[:, i * 512:(i + 1) * 512], S.ones1x[:, :],
                              S.bg2row[:, i * 512:(i + 1) * 512], start=True, stop=True)
            mm.then_inc(S.cT.h, 1)
        S.t(f); bgt = S.cT.bump()

        def f(v, bgt=bgt):
            v.wait_ge(S.cT.h, bgt)
            v.tensor_copy(S.bg2f[:], S.PA[:, :]).then_inc(S.cV.h, 1)
        S.v(f); S.cV.bump()

        ch_consumed = [S.cT.n, S.cT.n]
        for ci in range(9):
            t0 = 0 if ci == 0 else 64 + (ci - 1) * 128
            Tc = 64 if ci == 0 else 128
            par = ci % 2
            state_only = (ci == 0)
            self._scan_chunk(ci, t0, Tc, par, state_only, kv_done, ch_consumed, sinit)
        obuf_done = S.qOBU.n

        b6bar = self.emit_barrier(self.all_sems())
        for c in range(16):
            def f(y, c=c, b6bar=b6bar, obuf_done=obuf_done):
                y.wait_ge(S.cG.h, b6bar)
                y.wait_ge(S.qOBU.h, obuf_done)
                y.dma_start(out=S.ofm[:, c, :], in_=S.obuf[0:TOWN, c * 128:(c + 1) * 128],
                            transpose=True).then_inc(S.qOF.h, 16)
            S.y(f); S.qOF.bump(16)
        ofm_done = S.qOF.n

        xs_store = [S.qXS[0].n, S.qXS[1].n]
        for dg in range(4):
            b = dg % 2

            def f(g, dg=dg, b=b, sf=S.cT.n, b6bar=b6bar):
                g.wait_ge(S.cG.h, b6bar)
                g.wait_ge(S.cT.h, sf)
                g.dma_start(out=S.oslab[b][:],
                            in_=S.WoA[dg]).then_inc(S.qOS[b].h, 16)
            S.g(f); wld = S.qOS[b].bump(16)

            for tt in range(8):
                xb = tt % 2
                pab = (tt % 4)
                ps = lambda: (S.PA[:, pab * 512:(pab + 1) * 512] if tt % 8 < 4 else
                              S.PB[:, pab * 512:(pab + 1) * 512])
                evv = S.cV.n

                def f(t, tt=tt, b=b, wld=wld, ofm_done=ofm_done, evv=evv, pab=pab):
                    t.wait_ge(S.qOS[b].h, wld)
                    t.wait_ge(S.qOF.h, ofm_done)
                    t.wait_ge(S.cV.h, evv)
                    pp = S.PA[:, pab * 512:(pab + 1) * 512] if tt < 4 else \
                         S.PB[:, pab * 512:(pab + 1) * 512]
                    for c in range(16):
                        mm = t.matmul(pp, S.ofm[:, c, tt * 128:(tt + 1) * 128],
                                      S.oslab[b][:, c, :], start=(c == 0), stop=(c == 15))
                    mm.then_inc(S.cT.h, 1)
                S.t(f); mmv = S.cT.bump()

                def f(y, tt=tt, xb=xb, dg=dg, vprev=S.cV.n, xsw=xs_store[xb], x1w=list(x1_wait or [])):
                    y.wait_ge(S.cV.h, vprev)
                    y.wait_ge(S.qXS[xb].h, xsw)
                    y.dma_start(out=S.xt[xb][:, 0:512],
                                in_=S.x1_dram[WU + tt * 128: WU + (tt + 1) * 128, dg * 512:(dg + 1) * 512]
                                ).then_inc(S.qXT[xb].h, 16)
                S.y(f); xld = S.qXT[xb].bump(16)

                def f(v, tt=tt, xb=xb, mmv=mmv, xld=xld, pab=pab):
                    v.wait_ge(S.cT.h, mmv)
                    v.wait_ge(S.qXT[xb].h, xld)
                    pp = S.PA[:, pab * 512:(pab + 1) * 512] if tt < 4 else \
                         S.PB[:, pab * 512:(pab + 1) * 512]
                    v.tensor_tensor(S.xt[xb][:, 0:512], pp, S.xt[xb][:, 0:512],
                                    ALU.add).then_inc(S.cV.h, 1)
                S.v(f); vv = S.cV.bump()

                def f(y, tt=tt, xb=xb, dg=dg, vv=vv):
                    y.wait_ge(S.cV.h, vv)
                    y.dma_start(out=S.x2_dram[tt * 128:(tt + 1) * 128, dg * 512:(dg + 1) * 512],
                                in_=S.xt[xb][:, 0:512]).then_inc(S.qXS[xb].h, 16)
                S.y(f); xs_store[xb] = S.qXS[xb].bump(16)
        return [(S.qXS[0].h, S.qXS[0].n), (S.qXS[1].h, S.qXS[1].n)]


    def _scan_chunk(self, ci, t0, Tc, par, state_only, kv_done, ch_consumed, sinit):
        S = self
        HS = lambda h: slice(h * 128, h * 128 + Tc)
        HF = lambda h: slice(h * 128, (h + 1) * 128)

        def f(y, ci=ci, t0=t0, Tc=Tc, par=par, state_only=state_only,
              kv_done=kv_done, chc=ch_consumed[par], sinit=sinit):
            y.wait_ge(S.cT.h, chc)
            y.wait_ge(S.cV.h, sinit)
            y.wait_ge(S.qKV[0].h, kv_done[0])
            y.wait_ge(S.qKV[1].h, kv_done[1])
            if not state_only:
                y.dma_start(out=S.qfm_c[par][:, :, 0:Tc],
                            in_=S.qfm_d.rearrange("(c p) t -> p c t", p=128)[:, :, t0:t0 + Tc]
                            ).then_inc(S.qCH[par].h, 16)
                y.dma_start(out=S.kfm_c[par][:, :, 0:Tc],
                            in_=S.kfm_d.rearrange("(c p) t -> p c t", p=128)[:, :, t0:t0 + Tc]
                            ).then_inc(S.qCH[par].h, 16)
            y.dma_start(out=S.wfm_c[par][:, :, 0:Tc],
                        in_=S.wfm_d.rearrange("(c p) t -> p c t", p=128)[:, :, t0:t0 + Tc]
                        ).then_inc(S.qCH[par].h, 16)
            y.dma_start(out=S.ktm_c[par][0:128, :], in_=S.kfm_d[:, t0:t0 + 128],
                        transpose=True).then_inc(S.qCH[par].h, 16)
            y.dma_start(out=S.vtm_c[par][0:128, :], in_=S.vfm_d[:, t0:t0 + 128],
                        transpose=True).then_inc(S.qCH[par].h, 16)
            y.dma_start(out=S.wtm_c[par][0:128, :], in_=S.wfm_d[:, t0:t0 + 128],
                        transpose=True).then_inc(S.qCH[par].h, 16)
        S.y(f)
        chld = S.qCH[par].bump(16 * (4 if state_only else 6))

        if not state_only:
            def f(t, t0=t0, Tc=Tc, evs=S.cS.n, evv=S.cV.n, sinit=sinit, obw=S.qOBU.n):
                t.wait_ge(S.cS.h, evs)
                t.wait_ge(S.cV.h, evv)
                t.wait_ge(S.cV.h, sinit)
                t.wait_ge(S.qOBU.h, obw)
                for i in range(4):
                    mm = t.matmul(S.PB[0:Tc, i * 512:(i + 1) * 512], S.g1_sb[:, t0:t0 + Tc],
                                  S.wg2_sb[:, i * 512:(i + 1) * 512], start=True, stop=True)
                mm.then_inc(S.cT.h, 1)
            S.t(f); gv = S.cT.bump()

            def f(v, Tc=Tc, gv=gv):
                v.wait_ge(S.cT.h, gv)
                v.tensor_tensor(S.sgf[0:Tc, :], S.PB[0:Tc, 0:2048], S.bg2f[0:Tc, :],
                                ALU.add).then_inc(S.cV.h, 1)
            S.v(f); gev = S.cV.bump()

            def f(s, Tc=Tc, gev=gev):
                s.wait_ge(S.cV.h, gev)
                s.activation(S.sgf[0:Tc, :], S.sgf[0:Tc, :], AF.Sigmoid).then_inc(S.cS.h, 1)
            S.s(f); gsg = S.cS.bump()

            def f(v, Tc=Tc, gsg=gsg):
                v.wait_ge(S.cS.h, gsg)
                v.tensor_tensor(S.sgf[0:Tc, :], S.sgf[0:Tc, :],
                                S.nwfull[0:Tc, :, :].rearrange("p a b -> p (a b)"),
                                ALU.mult).then_inc(S.cV.h, 1)
            S.v(f); S.cV.bump()

        def f(t, t0=t0, Tc=Tc, evs=S.cS.n, evv=S.cV.n, sinit=sinit, obw=S.qOBU.n):
            t.wait_ge(S.cS.h, evs)
            t.wait_ge(S.cV.h, evv)
            t.wait_ge(S.cV.h, sinit)
            t.wait_ge(S.qOBU.h, obw)
            for i in range(4):
                mm = t.matmul(S.PA[0:Tc, i * 512:(i + 1) * 512], S.f1_sb[:, t0:t0 + Tc],
                              S.wf2_sb[:, i * 512:(i + 1) * 512], start=True, stop=True)
            mm.then_inc(S.cT.h, 1)
        S.t(f); gpv = S.cT.bump()

        def f(v, Tc=Tc, gpv=gpv):
            v.wait_ge(S.cT.h, gpv)
            v.tensor_copy(S.gpre[0:Tc, :], S.PA[0:Tc, 0:2048]).then_inc(S.cV.h, 1)
        S.v(f); gpe = S.cV.bump()

        def f(s, Tc=Tc, gpe=gpe):
            s.wait_ge(S.cV.h, gpe)
            s.activation(S.tmpA[0:Tc, :], S.gpre[0:Tc, :], AF.Abs)
            s.drain()
            s.activation(S.tmpB[0:Tc, :], S.tmpA[0:Tc, :], AF.Exp, scale=-1.0)
            s.drain()
            s.activation(S.gtm[0:Tc, :], S.tmpB[0:Tc, :], AF.Ln, bias=1.0).then_inc(S.cS.h, 1)
        S.s(f); lgs = S.cS.bump()

        def f(v, Tc=Tc, lgs=lgs):
            v.wait_ge(S.cS.h, lgs)
            v.tensor_tensor(S.tmpB[0:Tc, :], S.gpre[0:Tc, :], S.tmpA[0:Tc, :], ALU.subtract)
            v.drain()
            v.scalar_tensor_tensor(S.gtm[0:Tc, :], S.tmpB[0:Tc, :], 0.5, S.gtm[0:Tc, :],
                                   ALU.mult, ALU.subtract).then_inc(S.cV.h, 1)
        S.v(f); gtv = S.cV.bump()

        def f(t, Tc=Tc, gtv=gtv, gpe=gpe):
            t.wait_ge(S.cV.h, gtv)
            for i in range(4):
                t.matmul(S.PA[0:Tc, i * 512:(i + 1) * 512], S.uinc[0:Tc, 0:Tc],
                         S.gtm[0:Tc, i * 512:(i + 1) * 512], start=True, stop=True)
            for h in range(H):
                mm = t.matmul(S.PB[0:128, h * 128:h * 128 + Tc], S.gtm[0:Tc, HF(h)],
                              S.uinc[0:Tc, 0:Tc], start=True, stop=True)
            mm.then_inc(S.cT.h, 1)
        S.t(f); lamv = S.cT.bump()

        def f(v, Tc=Tc, lamv=lamv):
            v.wait_ge(S.cT.h, lamv)
            v.tensor_copy(S.ltm[0:Tc, :], S.PA[0:Tc, 0:2048])
            v.tensor_copy(S.lfm.rearrange("p (a b) -> p a b", b=128)[:, :, 0:Tc],
                          S.PB[:, 0:2048].rearrange("p (a b) -> p a b", b=128)[:, :, 0:Tc]
                          ).then_inc(S.cV.h, 1)
        S.v(f); lev = S.cV.bump()

        def f(s, Tc=Tc, lev=lev):
            s.wait_ge(S.cV.h, lev)
            s.activation(S.p1f[0:Tc, :], S.ltm[0:Tc, :], AF.Exp, bias=45.0)
            s.activation(S.p2b[0:Tc, :], S.ltm[0:Tc, :], AF.Exp, scale=-1.0, bias=-45.0)
            s.activation(S.p2fb.rearrange("p (a b) -> p a b", b=128)[:, :, 0:Tc],
                         S.lfm.rearrange("p (a b) -> p a b", b=128)[:, :, 0:Tc],
                         AF.Exp, scale=-1.0, bias=-45.0)
            s.activation(S.fcol[:, :],
                         S.lfm.rearrange("p (a b) -> p a b", b=128)[:, :, Tc - 1:Tc]
                         .rearrange("p a b -> p (a b)"), AF.Exp, bias=45.0).then_inc(S.cS.h, 1)
        S.s(f); exv = S.cS.bump()

        def f(s, Tc=Tc, par=par, chld=chld, gtv2=S.cV.n):
            s.wait_ge(S.qCH[par].h, chld)
            s.wait_ge(S.cV.h, gtv2)
            s.activation(S.sq2[0:Tc, :], S.wtm_c[par][0:Tc, :], AF.Square).then_inc(S.cS.h, 1)
        S.s(f); wsq = S.cS.bump()

        def f(v, Tc=Tc, wsq=wsq):
            v.wait_ge(S.cS.h, wsq)
            v.tensor_reduce(S.wrow[0:Tc, :],
                            S.sq2[0:Tc, :].rearrange("p (a b) -> p a b", b=128),
                            axis=AX.X, op=ALU.add).then_inc(S.cV.h, 1)
        S.v(f); wsv = S.cV.bump()

        def f(s, Tc=Tc, wsv=wsv):
            s.wait_ge(S.cV.h, wsv)
            s.activation(S.wrow[0:Tc, :], S.wrow[0:Tc, :], AF.Ln, bias=1e-6)
            s.drain()
            s.activation(S.wrow[0:Tc, :], S.wrow[0:Tc, :], AF.Exp, scale=-0.5).then_inc(S.cS.h, 1)
        S.s(f); wrv = S.cS.bump()

        def f(v, ci=ci, Tc=Tc, wrv=wrv):
            v.wait_ge(S.cS.h, wrv)
            v.tensor_tensor(S.brs[0:Tc, :], S.beta[0:Tc, ci, :], S.wrow[0:Tc, :],
                            ALU.mult).then_inc(S.cV.h, 1)
        S.v(f); brsv = S.cV.bump()

        def f(v, ci=ci, Tc=Tc, exv=exv, chld=chld, par=par, state_only=state_only, brsv=brsv):
            v.wait_ge(S.cS.h, exv)
            v.wait_ge(S.cV.h, brsv)
            v.wait_ge(S.qCH[par].h, chld)
            for h in range(H):
                v.scalar_tensor_tensor(S.kpb[0:Tc, HF(h)], S.wtm_c[par][0:Tc, HF(h)],
                                       S.brs[0:Tc, h:h + 1], S.p2b[0:Tc, HF(h)],
                                       ALU.mult, ALU.mult)
            if not state_only:
                v.tensor_tensor(S.wtf[:, :],
                                S.wfm_c[par].rearrange("p a b -> p (a b)")[:, 0:H * 128],
                                S.p2fb[:, :], ALU.mult)
            v.engine_nop().then_inc(S.cV.h, 1)
        S.v(f); kpv = S.cV.bump()

        if not state_only:
            def f(t, Tc=Tc, par=par, chld=chld, evv=S.cV.n, lev=lev):
                t.wait_ge(S.qCH[par].h, chld)
                t.wait_ge(S.cV.h, lev)
                for h in range(H):
                    mm = t.matmul(S.PA[0:Tc, h * 128:h * 128 + Tc], S.kfm_c[par][:, h, 0:Tc],
                                  S.qfm_c[par][:, h, 0:Tc], start=True, stop=True)
                mm.then_inc(S.cT.h, 1)
            S.t(f); atv = S.cT.bump()

            def f(v, Tc=Tc, atv=atv):
                v.wait_ge(S.cT.h, atv)
                v.tensor_tensor(S.a_sb.rearrange("p (a b) -> p a b", b=128)[0:Tc, :, 0:Tc],
                                S.PA[0:Tc, 0:2048].rearrange("p (a b) -> p a b", b=128)[0:Tc, :, 0:Tc],
                                S.maskrep.rearrange("p (a b) -> p a b", b=128)[0:Tc, :, 0:Tc],
                                ALU.mult).then_inc(S.cV.h, 1)
            S.v(f); aev = S.cV.bump()

            def f(t, Tc=Tc, par=par, aev=aev, kpv=kpv, gev2=S.cV.n):
                t.wait_ge(S.cV.h, aev)
                t.wait_ge(S.cV.h, kpv)
                for h in range(H):
                    t.matmul(S.PB[0:Tc, h * 128:(h + 1) * 128], S.qfm_c[par][:, h, 0:Tc],
                             S.Skp[:, HF(h)], start=True, stop=False)
                    mm = t.matmul(S.PB[0:Tc, h * 128:(h + 1) * 128],
                                  S.a_sb[0:Tc, h * 128:h * 128 + Tc],
                                  S.kpb[0:Tc, HF(h)], start=False, stop=True)
                mm.then_inc(S.cT.h, 1)
            S.t(f); sv = S.cT.bump()

            def f(v, Tc=Tc, sv=sv):
                v.wait_ge(S.cT.h, sv)
                v.scalar_tensor_tensor(S.s_sb[0:Tc, :], S.PB[0:Tc, 0:2048], SCALE,
                                       S.p1f[0:Tc, :], ALU.mult, ALU.mult).then_inc(S.cV.h, 1)
            S.v(f); sev = S.cV.bump()

            def f(v, Tc=Tc, sev=sev):
                v.wait_ge(S.cV.h, sev)
                v.tensor_reduce(S.mrow[0:Tc, :],
                                S.s_sb[0:Tc, :].rearrange("p (a b) -> p a b", b=128),
                                axis=AX.X, op=ALU.max)
                v.drain()
                v.tensor_tensor(S.s_sb[0:Tc, :].rearrange("p (a b) -> p a b", b=128),
                                S.s_sb[0:Tc, :].rearrange("p (a b) -> p a b", b=128),
                                S.mrow[0:Tc, :].unsqueeze(2).broadcast_to([Tc, H, 128]),
                                ALU.subtract)
                v.engine_nop().then_inc(S.cV.h, 1)
            S.v(f); sm1 = S.cV.bump()

            def f(s, Tc=Tc, sm1=sm1):
                s.wait_ge(S.cV.h, sm1)
                s.activation(S.s_sb[0:Tc, :], S.s_sb[0:Tc, :], AF.Exp).then_inc(S.cS.h, 1)
            S.s(f); sm2 = S.cS.bump()

            def f(v, Tc=Tc, sm2=sm2):
                v.wait_ge(S.cS.h, sm2)
                v.tensor_reduce(S.srow[0:Tc, :],
                                S.s_sb[0:Tc, :].rearrange("p (a b) -> p a b", b=128),
                                axis=AX.X, op=ALU.add)
                v.drain()
                v.reciprocal(S.rrow[0:Tc, :], S.srow[0:Tc, :])
                v.drain()
                v.tensor_tensor(S.s_sb[0:Tc, :].rearrange("p (a b) -> p a b", b=128),
                                S.s_sb[0:Tc, :].rearrange("p (a b) -> p a b", b=128),
                                S.rrow[0:Tc, :].unsqueeze(2).broadcast_to([Tc, H, 128]),
                                ALU.mult)
                v.drain()
                v.tensor_tensor(S.a_sb[0:Tc, 0:2048], S.s_sb[0:Tc, :], S.p1f[0:Tc, :],
                                ALU.mult).then_inc(S.cV.h, 1)
            S.v(f); apv = S.cV.bump()

            PT = S.PA[:, 0:1024].bitcast(BF16)

            def f(t, Tc=Tc, apv=apv, PT=PT):
                t.wait_ge(S.cV.h, apv)
                for h in range(H):
                    mm = t.transpose(PT[:, h * 128:h * 128 + Tc], S.a_sb[0:Tc, HF(h)],
                                     S.identb[0:Tc, 0:Tc])
                mm.then_inc(S.cT.h, 1)
            S.t(f); aptv = S.cT.bump()

            def f(v, Tc=Tc, aptv=aptv, PT=PT):
                v.wait_ge(S.cT.h, aptv)
                v.tensor_copy(S.at_sb.rearrange("p (a b) -> p a b", b=128)[:, :, 0:Tc],
                              PT.rearrange("p (a b) -> p a b", b=128)[:, :, 0:Tc]
                              ).then_inc(S.cV.h, 1)
            S.v(f); apte = S.cV.bump()

            def f(t, Tc=Tc, apte=apte, kpv=kpv):
                t.wait_ge(S.cV.h, apte)
                for h in range(H):
                    mm = t.matmul(S.PA[0:Tc, h * 128:h * 128 + Tc], S.wtf[:, HS(h)],
                                  S.at_sb[:, h * 128:h * 128 + Tc], start=True, stop=True)
                mm.then_inc(S.cT.h, 1)
            S.t(f); mtv = S.cT.bump()

            def f(v, ci=ci, Tc=Tc, mtv=mtv):
                v.wait_ge(S.cT.h, mtv)
                for h in range(H):
                    v.scalar_tensor_tensor(S.m_sb[0:Tc, h * 128:h * 128 + Tc],
                                           S.PA[0:Tc, h * 128:h * 128 + Tc],
                                           S.brs[0:Tc, h:h + 1],
                                           S.maskrep[0:Tc, h * 128:h * 128 + Tc],
                                           ALU.mult, ALU.mult)
                v.engine_nop().then_inc(S.cV.h, 1)
            S.v(f); mev = S.cV.bump()

            def f(t, Tc=Tc, par=par, mev=mev, sev=sev):
                t.wait_ge(S.cV.h, mev)
                t.wait_ge(S.cV.h, sev)
                for h in range(H):
                    t.matmul(S.PB[0:Tc, HF(h)], S.m_sb[0:Tc, h * 128:h * 128 + Tc],
                             S.vtm_c[par][0:Tc, HF(h)], start=True, stop=False)
                    mm = t.matmul(S.PB[0:Tc, HF(h)], S.at_sb[:, h * 128:h * 128 + Tc],
                                  S.Svp[:, HF(h)], start=False, stop=True)
                mm.then_inc(S.cT.h, 1)
            S.t(f); ov = S.cT.bump()

            def f(s, Tc=Tc, ov=ov):
                s.wait_ge(S.cT.h, ov)
                s.activation(S.sq2[0:Tc, :], S.PB[0:Tc, 0:2048], AF.Square).then_inc(S.cS.h, 1)
            S.s(f); sqv = S.cS.bump()

            def f(v, Tc=Tc, sqv=sqv):
                v.wait_ge(S.cS.h, sqv)
                v.tensor_reduce(S.mrow[0:Tc, :],
                                S.sq2[0:Tc, :].rearrange("p (a b) -> p a b", b=128),
                                axis=AX.X, op=ALU.add).then_inc(S.cV.h, 1)
            S.v(f); msv = S.cV.bump()

            def f(s, Tc=Tc, msv=msv):
                s.wait_ge(S.cV.h, msv)
                s.activation(S.srow[0:Tc, :], S.mrow[0:Tc, :], AF.Ln, scale=1.0 / 128.0, bias=float(EPS))
                s.drain()
                s.activation(S.srow[0:Tc, :], S.srow[0:Tc, :], AF.Exp, scale=-0.5).then_inc(S.cS.h, 1)
            S.s(f); rsv = S.cS.bump()

            def f(v, ci=ci, t0=t0, Tc=Tc, rsv=rsv):
                v.wait_ge(S.cS.h, rsv)
                v.tensor_tensor(S.onrm[0:Tc, :].rearrange("p (a b) -> p a b", b=128),
                                S.PB[0:Tc, 0:2048].rearrange("p (a b) -> p a b", b=128),
                                S.srow[0:Tc, :].unsqueeze(2).broadcast_to([Tc, H, 128]),
                                ALU.mult)
                v.drain()
                v.tensor_tensor(S.ogat[0:Tc, :], S.onrm[0:Tc, :], S.sgf[0:Tc, :],
                                ALU.mult).then_inc(S.cV.h, 1)
            S.v(f); oev = S.cV.bump()

            def f(y, t0=t0, Tc=Tc, oev=oev):
                y.wait_ge(S.cV.h, oev)
                y.dma_start(out=S.obuf[t0 - WU:t0 - WU + Tc, :], in_=S.ogat[0:Tc, :]
                            ).then_inc(S.qOBU.h, 16)
            S.y(f); S.qOBU.bump(16)
        else:
            oev = S.cV.n

        evv = S.cV.n

        def f(t, Tc=Tc, par=par, kpv=kpv, evv=evv, chld=chld):
            t.wait_ge(S.cV.h, kpv)
            t.wait_ge(S.cV.h, evv)
            t.wait_ge(S.qCH[par].h, chld)
            for h in range(H):
                t.matmul(S.PA[:, HF(h)], S.kpb[0:Tc, HF(h)], S.ktm_c[par][0:Tc, HF(h)],
                         start=True, stop=True)
                mm = t.matmul(S.PB[:, HF(h)], S.kpb[0:Tc, HF(h)], S.vtm_c[par][0:Tc, HF(h)],
                              start=True, stop=True)
            mm.then_inc(S.cT.h, 1)
        S.t(f); dsv = S.cT.bump()
        ch_consumed[par] = dsv

        def f(v, Tc=Tc, dsv=dsv):
            v.wait_ge(S.cT.h, dsv)
            v.scalar_tensor_tensor(S.sfold[:, :], S.Sk[:, :], float(ECONST), S.PA[:, 0:2048],
                                   ALU.mult, ALU.add)
            v.drain()
            for h in range(H):
                v.tensor_scalar(S.Sk[:, HF(h)], S.sfold[:, HF(h)], S.fcol[:, h:h + 1], None, ALU.mult)
            v.drain()
            v.scalar_tensor_tensor(S.sfold[:, :], S.Sv[:, :], float(ECONST), S.PB[:, 0:2048],
                                   ALU.mult, ALU.add)
            v.drain()
            for h in range(H):
                v.tensor_scalar(S.Sv[:, HF(h)], S.sfold[:, HF(h)], S.fcol[:, h:h + 1], None, ALU.mult)
            v.drain()
            v.tensor_scalar(S.skb16[:, :], S.Sk[:, :], float(ECONST), None, ALU.mult)
            v.tensor_scalar(S.Svp[:, :], S.Sv[:, :], float(ECONST), None, ALU.mult)
            v.engine_nop().then_inc(S.cV.h, 1)
        S.v(f); fold = S.cV.bump()

        PT2 = S.PA[:, 0:1024].bitcast(BF16)

        def f(t, fold=fold, PT2=PT2):
            t.wait_ge(S.cV.h, fold)
            for h in range(H):
                mm = t.transpose(PT2[:, HF(h)], S.skb16[:, HF(h)], S.identb[:, :])
            mm.then_inc(S.cT.h, 1)
        S.t(f); skt = S.cT.bump()

        def f(v, skt=skt, PT2=PT2):
            v.wait_ge(S.cT.h, skt)
            v.tensor_copy(S.Skp[:, :], PT2[:, 0:2048]).then_inc(S.cV.h, 1)
        S.v(f); S.cV.bump()

    def all_sems(self):
        S = self
        out = [S.cT, S.cV, S.cS, S.cG, S.qS, S.qF, S.qG,
               S.qOBU, S.qOF]
        for lst in [S.qXT, S.qYT, S.qWS, S.qDS, S.qES, S.qED, S.qXG, S.qCH, S.qOS, S.qYS, S.qCW, S.qCE, S.qCDW, S.qCED, S.qXS, S.qOBs, S.qKV, S.qSCp]:
            out.extend(lst)
        return out

    def finalize(self):
        S = self
        nc = S.nc
        src_map = {"y_dram": S.y_dram, "x1_dram": S.x1_dram, "xbuf": S.xbuf,
                   "outbuf": S.outbuf, "x2_dram": S.x2_dram, "obuf": S.obuf,
                   "qfm_d": S.qfm_d, "kfm_d": S.kfm_d, "vfm_d": S.vfm_d, "wfm_d": S.wfm_d}
        sems = self.all_sems()
        allw = [(c.h, c.n) for c in sems]
        for name, dst in S.dbg_aps.items():

            def f(g, name=name, dst=dst, allw=allw):
                for h, n in allw:
                    if n:
                        g.wait_ge(h, n)
                g.dma_start(out=dst[:, :], in_=src_map[name][:, :]).then_inc(S.qS.h, 16)
            S.g(f); S.qS.bump(16)

        final = [(c.h, c.n) for c in sems]

        def f(y, final=final):
            for h, n in final:
                if n:
                    y.wait_ge(h, n)
        S.y(f)

        with nc.Block() as block:
            @block.tensor
            def _(eng):
                for fn in S.ops["t"]:
                    fn(eng)

            @block.vector
            def _(eng):
                for fn in S.ops["v"]:
                    fn(eng)

            @block.scalar
            def _(eng):
                for fn in S.ops["s"]:
                    fn(eng)

            @block.gpsimd
            def _(eng):
                for fn in S.ops["g"]:
                    fn(eng)

            @block.sync
            def _(eng):
                for fn in S.ops["y"]:
                    fn(eng)
        return nc


def shard_inputs(inputs):
    x = np.ascontiguousarray(inputs["L0_attn_hidden_states"][0])

    def slab4(w):
        return np.asarray(w).reshape(16, 128, 4, 512).transpose(2, 1, 0, 3)

    shared = {
        "Wq": slab4(inputs["L0_attn_Wq"]), "Wk": slab4(inputs["L0_attn_Wk"]),
        "Wv": slab4(inputs["L0_attn_Wv"]),
        "Ww": slab4(inputs["L0_attn_Ww"]), "cq": inputs["L0_attn_cq"], "ck": inputs["L0_attn_ck"],
        "cv": inputs["L0_attn_cv"], "Wf1": inputs["L0_attn_Wf1"], "Wf2": inputs["L0_attn_Wf2"],
        "Wb": inputs["L0_attn_Wb"], "Wg1": inputs["L0_attn_Wg1"], "Wg2": inputs["L0_attn_Wg2"],
        "bg2": inputs["L0_attn_bg2"], "norm_w": inputs["L0_attn_norm_w"],
        "Wo": slab4(inputs["L0_attn_Wo"]),
        "router": inputs["L0_ffn_router"],
        "w_up_gate": np.asarray(inputs["L0_ffn_w_up_gate"])
            .reshape(8, 16, 128, 11, 256).transpose(0, 3, 2, 1, 4),
        "w_down": np.asarray(inputs["L0_ffn_w_down"])
            .reshape(8, 11, 128, 4, 512).transpose(0, 3, 2, 1, 4),
        "shared_w_up_gate": np.asarray(inputs["L0_ffn_shared_w_up_gate"])
            .reshape(16, 128, 2, 2, 11, 256).transpose(2, 3, 4, 1, 0, 5),
        "shared_w_down": np.asarray(inputs["L0_ffn_shared_w_down"])
            .reshape(2, 22, 128, 4, 512).transpose(0, 3, 2, 1, 4),
    }
    shared = {k: np.ascontiguousarray(v) for k, v in shared.items()}
    in_maps = []
    for c in range(NCORE):
        lo = c * TOWN - WU
        if c == 0:
            ext = np.concatenate([np.zeros((WU, D), np.float32), x[0:TOWN]], axis=0)
        else:
            ext = x[lo:(c + 1) * TOWN]
        m = dict(shared)
        m["xin"] = np.ascontiguousarray(ext)
        m["wvalid"] = np.array([[0.0 if c == 0 else 1.0]], np.float32)
        in_maps.append(m)
    return in_maps


_CACHED = {}


def _build():
    if "nc" not in _CACHED:
        m = MK()
        m.emit_setup()
        w1 = m.emit_ffn(TEXT, m.xin, m.x1_dram, False, True)
        w2 = m.emit_attn(w1)
        bar = m.emit_barrier(m.all_sems())
        m.emit_ffn(TOWN, m.x2_dram, m.xoutA, False, False, src_wait=list(w2) + [(m.cG.h, bar)])
        _CACHED["nc"] = m.finalize()
    return _CACHED["nc"]


def kernel(**inputs):
    nc = _build()
    in_maps = shard_inputs(inputs)
    res = run_bass_kernel_spmd(nc, in_maps, core_ids=list(range(NCORE)))
    out = np.concatenate([res.results[c]["xout"] for c in range(NCORE)], axis=0)
    return out.reshape(1, NCORE * TOWN, D).astype(np.float32)


# version 08
# speedup vs baseline: 1.0808x; 1.0416x over previous
import numpy as np
import ml_dtypes
import concourse.bass as bass
import concourse.mybir as mybir
from concourse.bass_utils import run_bass_kernel_spmd

F32 = mybir.dt.float32
BF16 = mybir.dt.bfloat16
I32 = mybir.dt.int32
AF = mybir.ActivationFunctionType
ALU = mybir.AluOpType
AX = mybir.AxisListType
IOX = bass.IndirectOffsetOnAxis

D = 2048
NCORE = 8
TOWN = 1024
WU = 64
TEXT = TOWN + WU
H, DK, DV, M = 16, 128, 128, 128
SCALE = float(DK ** -0.5)
E = 8
CAP = 352
NER = E * CAP
TRASH = NER
FI = 1408
FJ = 5632
CSHIFT = -45.0
ECONST = float(np.exp(np.float64(CSHIFT)))
EPS = 1e-5
KB = 1024


class Ctr:
    def __init__(self, nc, name):
        self.h = nc.alloc_semaphore(name)
        self.n = 0

    def bump(self, k=1):
        self.n += k
        return self.n


class MK:
    def __init__(self, do_attn=True, do_ffn2=True, dbg=()):
        self.do_attn = do_attn
        self.do_ffn2 = do_ffn2
        nc = bass.Bass("TRN2", target_bir_lowering=False)
        self.nc = nc
        self.ops = {"t": [], "v": [], "s": [], "g": [], "y": []}

        def din(name, shape, dt=F32):
            return nc.dram_tensor(name, shape, dt, kind="ExternalInput").ap()

        self.xin = din("xin", [TEXT, D])
        self.WqA = din("Wq", [4, 128, 16, 512]); self.WkA = din("Wk", [4, 128, 16, 512])
        self.WvA = din("Wv", [4, 128, 16, 512]); self.WwA = din("Ww", [4, 128, 16, 512])
        self.cqA = din("cq", [D, 4]); self.ckA = din("ck", [D, 4]); self.cvA = din("cv", [D, 4])
        self.Wf1A = din("Wf1", [D, DV]); self.Wf2A = din("Wf2", [DV, H * M])
        self.WbA = din("Wb", [D, H])
        self.Wg1A = din("Wg1", [D, DV]); self.Wg2A = din("Wg2", [DV, H * DV])
        self.bg2A = din("bg2", [H * DV])
        self.normwA = din("norm_w", [DV])
        self.WoA = din("Wo", [4, 128, 16, 512])
        self.RtA = din("router", [D, E])
        self.WugA = din("w_up_gate", [E, 11, 128, 16, 256])
        self.WdnA = din("w_down", [E, 4, 128, 11, 512])
        self.SugA = din("shared_w_up_gate", [2, 2, 11, 128, 16, 256])
        self.SdnA = din("shared_w_down", [2, 4, 128, 22, 512])
        self.wvalidA = din("wvalid", [1, 1])
        self.xoutA = nc.dram_tensor("xout", [TOWN, D], F32, kind="ExternalOutput").ap()

        def scr(name, shape, dt):
            return nc.dram_tensor(name, shape, dt, kind="Internal").ap()

        self.y_dram = scr("y_dram", [TEXT, D], BF16)
        self.x1_dram = scr("x1_dram", [TEXT, D], F32)
        self.x2_dram = scr("x2_dram", [TOWN, D], F32)
        self.xbuf = scr("xbuf", [NER + 1, D], BF16)
        self.outbuf = scr("outbuf", [NER + 1, D], F32)
        self.qfm_d = scr("qfm_d", [D, TEXT], BF16)
        self.kfm_d = scr("kfm_d", [D, TEXT], BF16)
        self.vfm_d = scr("vfm_d", [D, TEXT], BF16)
        self.wfm_d = scr("wfm_d", [D, TEXT], BF16)
        self.obuf = scr("obuf", [TOWN, D], BF16)
        self.SugB = scr("SugB", [2, 2, 11, 128, 16, 256], BF16)
        self.SdnB = scr("SdnB", [2, 4, 128, 22, 512], BF16)
        self.WugB = scr("WugB", [E, 11, 128, 16, 256], BF16)
        self.WdnB = scr("WdnB", [E, 4, 128, 11, 512], BF16)

        it = nc.inline_tensor
        self.C_UINC = it(np.triu(np.ones((128, 128), np.float32)).copy(), "c_UINC").ap()
        self.C_USTR = it(np.triu(np.ones((128, 128), np.float32), 1).copy(), "c_USTR").ap()
        self.C_ONES1 = it(np.ones((1, 128), np.float32), "c_ONES1").ap()
        self.C_ONESC = it(np.ones((128, 1), np.float32), "c_ONESC").ap()
        self.C_IDENT = it(np.eye(128, dtype=ml_dtypes.bfloat16), "c_IDENT").ap()
        self.C_MASKR = it(np.tile(np.triu(np.ones((128, 128), np.float32)), (1, H))
                          .astype(ml_dtypes.bfloat16), "c_MASKR").ap()
        self.C_IOTAP = it((np.arange(8, dtype=np.float32)[None, :] + 1000.0).repeat(128, 0).copy(),
                          "c_IOTAP").ap()
        self.C_IOTA8 = it(np.arange(8, dtype=np.float32)[None, :].repeat(128, 0).copy(),
                          "c_IOTA8").ap()

        sb = lambda n, s, t: nc.alloc_sbuf_tensor(n, s, t)
        self.identb = sb("identb", [128, 128], BF16)
        self.uinc = sb("uinc", [128, 128], F32)
        self.ustrict = sb("ustrict", [128, 128], F32)
        self.ones1x = sb("ones1x", [1, 128], F32)
        self.onesc = sb("onesc", [128, 1], F32)
        self.maskrep = sb("maskrep", [128, H * 128], BF16)
        self.iota8p = sb("iota8p", [128, 8], F32)
        self.iota8 = sb("iota8", [128, 8], F32)
        self.rt_sb = sb("rt_sb", [128, 16, E], BF16)
        self.wv_sb = sb("wv_sb", [1, 1], F32)
        self.wv_col = sb("wv_col", [128, 1], F32)
        self.tcol = sb("tcol", [128, 1], F32)
        self.convw = sb("convw", [128, 3, 16, 4], F32)
        self.wf2_sb = sb("wf2_sb", [128, H * M], BF16)
        self.wg2_sb = sb("wg2_sb", [128, H * DV], BF16)
        self.wb_sb = sb("wb_sb", [128, 16, H], BF16)
        self.wf1_sb = sb("wf1_sb", [128, 16, 128], BF16)
        self.wg1_sb = sb("wg1_sb", [128, 16, 128], BF16)
        self.nwrow = sb("nwrow", [1, DV], F32)
        self.nw128 = sb("nw128", [128, DV], F32)
        self.lgt = sb("lgt", [128, 9, 8], F32)
        self.off_i = [sb(f"off{k}", [128, 9], I32) for k in range(2)]
        self.cwvt = [sb(f"cwv{k}", [128, 9], F32) for k in range(2)]
        self.r_top8 = sb("r_top8", [128, 8], F32)
        self.r_t8 = sb("r_t8", [128, 8], F32)
        self.r_ohA = sb("r_ohA", [128, 8], F32)
        self.r_ohB = sb("r_ohB", [128, 8], F32)
        self.r_ohs = sb("r_ohs", [128, 8], F32)
        self.t_sel0 = sb("t_sel0", [128, 1], F32)
        self.t_sel1 = sb("t_sel1", [128, 1], F32)
        self.t_pos0 = sb("t_pos0", [128, 1], F32)
        self.t_pos1 = sb("t_pos1", [128, 1], F32)
        self.t_v0 = sb("t_v0", [128, 1], F32)
        self.t_v1 = sb("t_v1", [128, 1], F32)
        self.t_cwA = sb("t_cwA", [128, 1], F32)
        self.t_cwB = sb("t_cwB", [128, 1], F32)
        self.t_a = sb("t_a", [128, 1], F32)
        self.t_b = sb("t_b", [128, 1], F32)
        self.t_mr = sb("t_mr", [128, 1], F32)
        self.t_rs = sb("t_rs", [128, 1], F32)
        self.lns = [[sb(f"ln{k}{p}", [128, 1], F32) for k in "abmr"] for p in range(2)]
        self.lnv = [0, 0]
        self.carry = sb("carry", [1, 8], F32)
        self.CVALS = [0.0, 1.0, -1.0, float(EPS), 45.0, -45.0, -0.5, 1.0 / 128.0,
                      float(SCALE), 0.5, float(ECONST), 1e-6]
        self.cpool = sb("cpool", [128, len(self.CVALS)], F32)
        for ci, val in enumerate(self.CVALS):
            nc.const_aps.register(F32, val, self.cpool[:, ci:ci + 1])

        ARENA = 180 * KB
        base, _ = nc.bump_sbuf(ARENA)
        self._ab = base

        def at(name, shape, dt, off):
            return nc.alloc_sbuf_tensor_at(name, shape, dt, offset=base + off)

        self.yfm = at("yfm", [128, 16, TEXT], BF16, 0)
        self.hsh = at("hsh", [128, 22, TEXT], BF16, 34816)
        self.xt = [at(f"xt{i}", [128, D], F32, 82688 + i * 8192) for i in range(2)]
        self.estg = [at("estg0", [128, 512], F32, 86784),
                     at("estg1", [128, 512], F32, 88832)]
        self.wslab = [at(f"wslab{i}", [128, 16, 256], BF16, 99072 + i * 8192) for i in range(2)]
        self.dslab = [at(f"dslab{i}", [128, 4, 512], BF16, 115456 + i * 4096) for i in range(2)]
        self.sqbuf = at("sqbuf", [128, D], F32, 123648)
        self.eslab = [at("eslabB", [128, 16, 256], BF16, 123648),
                      at("eslabA", [128, 16, 256], BF16, 131840)]
        self.xg = [at("xg0", [128, 16, CAP], BF16, 140032)]
        self.he = at("he", [128, 11, CAP], BF16, 151296)
        self.edslab = [at(f"edslab{i}", [128, 11, 512], BF16, 159040 + i * 11264) for i in range(2)]
        self.ytm_t = [at(f"ytm{i}", [128, D], BF16, 159040 + i * 4096) for i in range(2)]
        self.g0t = at("g0t", [128, D], F32, 131840)
        self.g1t = at("g1t", [128, D], F32, 140032)
        self.bg2row = at("bg2row", [1, H * DV], F32, 140032)
        self.conv_z = [at(f"conv_z{i}", [128, TEXT + 3], F32, 107264 + i * 4608) for i in range(2)]
        self.conv_a = at("conv_a", [128, TEXT], F32, 116480)
        self.conv_o = [at(f"conv_o{i}", [128, TEXT], BF16, 120832 + i * 2176) for i in range(2)]
        self.wslab4 = [at(f"wslab4_{i}", [128, 16, 512], BF16, 34816 + i * 16384) for i in range(2)]
        so = 0
        self.qfm_c = [at(f"qfmc{i}", [128, H, 128], BF16, so + i * 4096) for i in range(2)]
        self.kfm_c = [at(f"kfmc{i}", [128, H, 128], BF16, so + 8192 + i * 4096) for i in range(2)]
        self.wfm_c = [at(f"wfmc{i}", [128, H, 128], BF16, so + 16384 + i * 4096) for i in range(2)]
        self.ktm_c = [at(f"ktmc{i}", [128, H * DK], BF16, so + 24576 + i * 4096) for i in range(2)]
        self.vtm_c = [at(f"vtmc{i}", [128, H * DV], BF16, so + 32768 + i * 4096) for i in range(2)]
        self.wtm_c = [at(f"wtmc{i}", [128, H * M], BF16, so + 40960 + i * 4096) for i in range(2)]
        self.Sk = at("Sk", [128, H * DK], F32, 49152)
        self.Sv = at("Sv", [128, H * DV], F32, 57344)
        self.Skp = at("Skp", [128, H * M], BF16, 65536)
        self.Svp = at("Svp", [128, H * DV], BF16, 69632)
        self.f1_sb = at("f1_sb", [128, TEXT], BF16, 73728)
        self.g1_sb = at("g1_sb", [128, TEXT], BF16, 76032)
        self.beta = at("beta", [128, 9, H], F32, 78336)
        self.fcol = at("fcol", [128, H], F32, 78976)
        self.mrow = at("mrow", [128, H], F32, 79104)
        self.srow = at("srow", [128, H], F32, 79232)
        self.rrow = at("rrow", [128, H], F32, 79360)
        self.wrow = at("wrow", [128, H], F32, 79488)
        self.brs = at("brs", [128, H], F32, 79616)
        self.bg2f = at("bg2f", [128, H * DV], F32, 79872)
        self.nwfull = at("nwfull", [128, H, DV], F32, 88064)
        self.sgf = at("sgf", [128, H * DV], F32, 96256)
        self.gpre = at("gpre", [128, H * M], F32, 104448)
        self.s_sb = at("s_sb", [128, H * M], F32, 104448)
        self.sq2 = at("sq2", [128, H * DV], F32, 104448)
        self.tmpA = at("tmpA", [128, H * M], F32, 112640)
        self.lfm = at("lfm", [128, H * M], F32, 112640)
        self.at_sb = at("at_sb", [128, H * 128], BF16, 112640)
        self.m_sb = at("m_sb", [128, H * 128], BF16, 116736)
        self.onrm = at("onrm", [128, H * DV], F32, 112640)
        self.tmpB = at("tmpB", [128, H * M], F32, 120832)
        self.p1f = at("p1f", [128, H * M], F32, 120832)
        self.ogat = at("ogat", [128, H * DV], BF16, 120832)
        self.gtm = at("gtm", [128, H * M], F32, 129024)
        self.p2b = at("p2b", [128, H * M], BF16, 129024)
        self.p2fb = at("p2fb", [128, H * M], BF16, 133120)
        self.ltm = at("ltm", [128, H * M], F32, 137216)
        self.sfold = at("sfold", [128, H * DK], F32, 137216)
        self.kpb = at("kpb", [128, H * M], BF16, 145408)
        self.wtf = at("wtf", [128, H * M], BF16, 149504)
        self.a_sb = at("a_sb", [128, H * 128], BF16, 153600)
        self.skb16 = at("skb16", [128, H * DK], BF16, 157696)
        self.ofm = at("ofm", [128, 16, TOWN], BF16, 0)
        self.oslab = [at(f"oslab{i}", [128, 16, 512], BF16, 32768 + i * 16384) for i in range(2)]

        self.PA = nc.alloc_psum_tensor("PA", [128, 2048], F32)
        self.PB = nc.alloc_psum_tensor("PB", [128, 2048], F32)

        self.cT = Ctr(nc, "cT")
        self.cV = Ctr(nc, "cV")
        self.cS = Ctr(nc, "cS")
        self.qS = Ctr(nc, "qS")
        self.qXT = [Ctr(nc, f"qXT{i}") for i in range(2)]
        self.qYS = [Ctr(nc, f"qYS{i}") for i in range(2)]
        self.qF = Ctr(nc, "qF")
        self.qYT = [Ctr(nc, f"qYT{i}") for i in range(2)]
        self.qSCp = [Ctr(nc, f"qSCp{i}") for i in range(2)]
        self.qWS = [Ctr(nc, f"qWS{i}") for i in range(2)]
        self.qDS = [Ctr(nc, f"qDS{i}") for i in range(2)]
        self.qES = [Ctr(nc, f"qES{i}") for i in range(2)]
        self.qED = [Ctr(nc, f"qED{i}") for i in range(2)]
        self.qXG = [Ctr(nc, f"qXG{i}") for i in range(2)]
        self.qCW = [Ctr(nc, f"qCW{i}") for i in range(2)]
        self.qCE = [Ctr(nc, f"qCE{i}") for i in range(2)]
        self.qCDW = [Ctr(nc, f"qCDW{i}") for i in range(2)]
        self.qCED = [Ctr(nc, f"qCED{i}") for i in range(2)]
        self.cG = Ctr(nc, "cG")
        self.gdummy = nc.alloc_sbuf_tensor("gdummy", [1, 8], F32)
        self.qOBs = [Ctr(nc, f"qOBs{i}") for i in range(3)]
        self.qG = Ctr(nc, "qG")
        self.qXS = [Ctr(nc, f"qXS{i}") for i in range(2)]
        self.qKV = [Ctr(nc, f"qKV{i}") for i in range(2)]
        self.qCH = [Ctr(nc, f"qCH{i}") for i in range(2)]
        self.qOS = [Ctr(nc, f"qOS{i}") for i in range(2)]
        self.qOBU = Ctr(nc, "qOBU")
        self.qOF = Ctr(nc, "qOF")

        self.dbg_aps = {}
        shp_map = {"y_dram": ([TEXT, D], BF16), "x1_dram": ([TEXT, D], F32),
                   "xbuf": ([NER + 1, D], BF16), "outbuf": ([NER + 1, D], F32),
                   "x2_dram": ([TOWN, D], F32), "obuf": ([TOWN, D], BF16),
                   "qfm_d": ([D, TEXT], BF16), "kfm_d": ([D, TEXT], BF16),
                   "vfm_d": ([D, TEXT], BF16), "wfm_d": ([D, TEXT], BF16)}
        for name in dbg:
            shp, dt = shp_map[name]
            self.dbg_aps[name] = nc.dram_tensor("dbg_" + name, shp, dt, kind="ExternalOutput").ap()

    def t(self, fn): self.ops["t"].append(fn)
    def v(self, fn): self.ops["v"].append(fn)
    def s(self, fn): self.ops["s"].append(fn)
    def g(self, fn): self.ops["g"].append(fn)
    def y(self, fn): self.ops["y"].append(fn)

    def emit_setup(self):
        S = self
        for dst, src in [(S.identb, S.C_IDENT), (S.uinc, S.C_UINC), (S.ustrict, S.C_USTR),
                         (S.ones1x, S.C_ONES1), (S.onesc, S.C_ONESC), (S.maskrep, S.C_MASKR),
                         (S.iota8p, S.C_IOTAP), (S.iota8, S.C_IOTA8)]:
            def f(g, dst=dst, src=src):
                g.dma_start(out=dst[:], in_=src[:, :]).then_inc(S.qS.h, 16)
            S.g(f); S.qS.bump(16)

        def f(g):
            g.dma_start(out=S.rt_sb[:], in_=S.RtA.rearrange("(c p) e -> p c e", p=128)).then_inc(S.qS.h, 16)
            g.dma_start(out=S.convw[:, 0], in_=S.cqA.rearrange("(c p) k -> p c k", p=128)).then_inc(S.qS.h, 16)
            g.dma_start(out=S.convw[:, 1], in_=S.ckA.rearrange("(c p) k -> p c k", p=128)).then_inc(S.qS.h, 16)
            g.dma_start(out=S.convw[:, 2], in_=S.cvA.rearrange("(c p) k -> p c k", p=128)).then_inc(S.qS.h, 16)
            g.dma_start(out=S.wf2_sb[:], in_=S.Wf2A[:, :]).then_inc(S.qS.h, 16)
            g.dma_start(out=S.wg2_sb[:], in_=S.Wg2A[:, :]).then_inc(S.qS.h, 16)
            g.dma_start(out=S.wb_sb[:], in_=S.WbA.rearrange("(c p) e -> p c e", p=128)).then_inc(S.qS.h, 16)
            g.dma_start(out=S.wf1_sb[:], in_=S.Wf1A.rearrange("(c p) m -> p c m", p=128)).then_inc(S.qS.h, 16)
            g.dma_start(out=S.wg1_sb[:], in_=S.Wg1A.rearrange("(c p) m -> p c m", p=128)).then_inc(S.qS.h, 16)
            g.dma_start(out=S.nwrow[:], in_=S.normwA.rearrange("(o f) -> o f", o=1)).then_inc(S.qS.h, 16)
            g.dma_start(out=S.wv_sb[:], in_=S.wvalidA[:, :]).then_inc(S.qS.h, 16)
        S.g(f); S.qS.bump(16 * 11)

        def f(v):
            for ci, val in enumerate(S.CVALS):
                v.memset(S.cpool[:, ci:ci + 1], val)
            v.memset(S.sqbuf[0:1, :], 0.0).then_inc(S.cV.h, 1)
        S.v(f); zv = S.cV.bump()

        def f(g, zv=zv):
            g.wait_ge(S.cV.h, zv)
            g.dma_start(out=S.outbuf[TRASH:TRASH + 1, :], in_=S.sqbuf[0:1, :]).then_inc(S.qS.h, 16)
        S.g(f); S.qS.bump(16)
        self.setup_q = S.qS.n

        def f(t):
            t.wait_ge(S.qS.h, S.setup_q)
            t.matmul(S.PB[:, 0:128], S.ones1x[:, :], S.nwrow[:, :], start=True, stop=True)
            t.matmul(S.PB[:, 512:513], S.ones1x[:, :], S.wv_sb[:, :],
                     start=True, stop=True).then_inc(S.cT.h, 1)
        S.t(f); tv = S.cT.bump()

        def f(v, tv=tv):
            v.wait_ge(S.cT.h, tv)
            v.tensor_copy(S.nw128[:], S.PB[:, 0:128])
            v.tensor_copy(S.wv_col[:], S.PB[:, 512:513])
            v.drain()
            v.tensor_scalar(S.tcol[:], S.wv_col[:], -float(TRASH), float(TRASH), ALU.mult, ALU.add)
            v.engine_nop().then_inc(S.cV.h, 1)
        S.v(f); S.cV.bump()
        self.setup_v = S.cV.n
        self.setup_t = tv

    def emit_barrier(self, ctrs):
        S = self
        waits = [(c.h, c.n) for c in ctrs if c.n]
        gprev = S.cG.n

        def f(g, waits=waits, gprev=gprev):
            g.wait_ge(S.cG.h, gprev)
            for h, n in waits:
                g.wait_ge(h, n)
            g.memset(S.gdummy[:], 0.0).then_inc(S.cG.h, 1)
        S.g(f)
        return S.cG.bump()

    def emit_ln(self, x_src, T, src_wait=None, src_wait_tile=None):
        S = self
        nt = (T + 127) // 128
        ytm_store = [S.qYS[0].n, S.qYS[1].n]
        S.lnv = [S.cV.n, S.cV.n]
        half = None
        for i in range(nt):
            r0 = i * 128
            P = min(128, T - r0)
            b = i % 2
            xtile = S.xt[b]
            la, lb, lm, lr = S.lns[b]
            vprev = S.lnv[b]

            def f(y, xtile=xtile, r0=r0, P=P, b=b, vprev=vprev, src_wait=src_wait,
                  swt=(src_wait_tile(i) if src_wait_tile else []), xsw=S.qXS[b].n):
                y.wait_ge(S.cV.h, vprev)
                y.wait_ge(S.qXS[b].h, xsw)
                for hh, nn in (src_wait or []):
                    y.wait_ge(hh, nn)
                for hh, nn in swt:
                    y.wait_ge(hh, nn)
                y.dma_start(out=xtile[0:P, :], in_=x_src[r0:r0 + P, :]).then_inc(S.qXT[b].h, 16)
            S.y(f); xv = S.qXT[b].bump(16)

            def f(s, xtile=xtile, P=P, b=b, xv=xv, vprev=vprev, la=la, lb=lb):
                s.wait_ge(S.qXT[b].h, xv)
                s.wait_ge(S.cV.h, vprev)
                s.wait_ge(S.qS.h, S.setup_q)
                s.activation(S.sqbuf[0:P, :], xtile[0:P, :], AF.Square, accum_out=lb[0:P, :])
                s.activation(S.sqbuf[0:P, :], xtile[0:P, :], AF.Copy,
                             accum_out=la[0:P, :]).then_inc(S.cS.h, 1)
            S.s(f); sv = S.cS.bump()

            def f(v, P=P, sv=sv, la=la, lb=lb, lm=lm):
                v.wait_ge(S.cS.h, sv)
                v.tensor_scalar(la[0:P, :], la[0:P, :], 1.0 / D, None, ALU.mult)
                v.tensor_scalar(lb[0:P, :], lb[0:P, :], 1.0 / D, None, ALU.mult)
                v.drain()
                v.tensor_tensor(lm[0:P, :], la[0:P, :], la[0:P, :], ALU.mult)
                v.drain()
                v.tensor_tensor(lb[0:P, :], lb[0:P, :], lm[0:P, :], ALU.subtract)
                v.engine_nop().then_inc(S.cV.h, 1)
            S.v(f); vv = S.cV.bump()

            def f(s, P=P, vv=vv, lb=lb, lr=lr):
                s.wait_ge(S.cV.h, vv)
                s.activation(lr[0:P, :], lb[0:P, :], AF.Ln, bias=float(EPS))
                s.drain()
                s.activation(lr[0:P, :], lr[0:P, :], AF.Exp, scale=-0.5).then_inc(S.cS.h, 1)
            S.s(f); sv2 = S.cS.bump()

            def f(v, xtile=xtile, P=P, b=b, sv2=sv2, ytw=S.qSCp[b].n, ysw=ytm_store[b],
                  la=la, lm=lm, lr=lr):
                v.wait_ge(S.cS.h, sv2)
                v.wait_ge(S.qSCp[b].h, ytw)
                v.wait_ge(S.qYS[b].h, ysw)
                v.tensor_tensor(lm[0:P, :], la[0:P, :], lr[0:P, :], ALU.mult)
                v.drain()
                v.tensor_scalar(lm[0:P, :], lm[0:P, :], -1.0, None, ALU.mult)
                v.drain()
                v.tensor_scalar(S.ytm_t[b][0:P, :], xtile[0:P, :], lr[0:P, :], lm[0:P, :],
                                ALU.mult, ALU.add).then_inc(S.cV.h, 1)
            S.v(f); yv = S.cV.bump()
            S.lnv[b] = yv

            def f(y, b=b, r0=r0, P=P, yv=yv):
                y.wait_ge(S.cV.h, yv)
                y.dma_start(out=S.y_dram[r0:r0 + P, :], in_=S.ytm_t[b][0:P, :]).then_inc(S.qYS[b].h, 16)
            S.y(f); ytm_store[b] = S.qYS[b].bump(16)
            if i == 3:
                half = (S.qYS[0].n, S.qYS[1].n)
        return (half or (S.qYS[0].n, S.qYS[1].n)), (S.qYS[0].n, S.qYS[1].n)

    def emit_yfm(self, T, yhalf, ydone):
        S = self
        tprev = S.cT.n
        TA = min(512, T)
        for c in range(16):
            def f(y, c=c, yhalf=yhalf, tprev=tprev, TA=TA):
                y.wait_ge(S.qYS[0].h, yhalf[0])
                y.wait_ge(S.qYS[1].h, yhalf[1])
                y.wait_ge(S.cT.h, tprev)
                y.dma_start(out=S.yfm[:, c, 0:TA], in_=S.y_dram[0:TA, c * 128:(c + 1) * 128],
                            transpose=True).then_inc(S.qF.h, 16)
            S.y(f); S.qF.bump(16)
        fa = S.qF.n
        if T > TA:
            for c in range(16):
                def f(y, c=c, ydone=ydone, tprev=tprev, TA=TA, T=T):
                    y.wait_ge(S.qYS[0].h, ydone[0])
                    y.wait_ge(S.qYS[1].h, ydone[1])
                    y.wait_ge(S.cT.h, tprev)
                    y.dma_start(out=S.yfm[:, c, TA:T], in_=S.y_dram[TA:T, c * 128:(c + 1) * 128],
                                transpose=True).then_inc(S.qF.h, 16)
                S.y(f); S.qF.bump(16)
        return fa, S.qF.n

    def emit_routing(self, T, yfm_done, wv_apply):
        S = self
        nt = (T + 127) // 128
        for i in range(nt):
            r0 = i * 128
            P = min(128, T - r0)
            vprev = S.cV.n

            def f(t, i=i, r0=r0, P=P, yfm_done=yfm_done, vprev=vprev):
                t.wait_ge(S.qF.h, yfm_done)
                t.wait_ge(S.cV.h, vprev)
                for c in range(15):
                    t.matmul(S.PB[0:P, 1024:1032], S.yfm[:, c, r0:r0 + P], S.rt_sb[:, c, :],
                             start=(c == 0), stop=False)
                t.matmul(S.PB[0:P, 1024:1032], S.yfm[:, 15, r0:r0 + P], S.rt_sb[:, 15, :],
                         start=False, stop=True).then_inc(S.cT.h, 1)
            S.t(f); tv = S.cT.bump()

            def f(v, i=i, P=P, tv=tv):
                v.wait_ge(S.cT.h, tv)
                v.tensor_copy(S.lgt[0:P, i, :], S.PB[0:P, 1024:1032]).then_inc(S.cV.h, 1)
            S.v(f); S.cV.bump()

        def f(v):
            v.memset(S.carry[:], 0.0).then_inc(S.cV.h, 1)
        S.v(f); S.cV.bump()

        for i in range(nt):
            r0 = i * 128
            P = min(128, T - r0)
            lgdone = S.cV.n

            def f(v, i=i, P=P, lgdone=lgdone):
                L = S.lgt[0:P, i, :]
                v.wait_ge(S.cV.h, lgdone)
                v.max(S.r_top8[0:P, :], L)
                v.drain()
                v.tensor_tensor(S.r_t8[0:P, :], L, S.r_top8[0:P, 0:1].broadcast_to([P, 8]), ALU.is_equal)
                v.drain()
                v.scalar_tensor_tensor(S.r_t8[0:P, :], S.r_t8[0:P, :], -1000.0, S.iota8p[0:P, :],
                                       ALU.mult, ALU.add)
                v.drain()
                v.tensor_reduce(S.t_sel0[0:P, :], S.r_t8[0:P, :], axis=AX.X, op=ALU.min)
                v.drain()
                v.tensor_tensor(S.r_ohA[0:P, :], S.iota8[0:P, :],
                                S.t_sel0[0:P, :].broadcast_to([P, 8]), ALU.is_equal)
                v.drain()
                v.scalar_tensor_tensor(S.r_t8[0:P, :], S.r_ohA[0:P, :], -1e30, L, ALU.mult, ALU.add)
                v.drain()
                v.max(S.r_ohs[0:P, :], S.r_t8[0:P, :])
                v.drain()
                v.tensor_tensor(S.r_ohB[0:P, :], S.r_t8[0:P, :],
                                S.r_ohs[0:P, 0:1].broadcast_to([P, 8]), ALU.is_equal)
                v.drain()
                v.scalar_tensor_tensor(S.r_ohB[0:P, :], S.r_ohB[0:P, :], -1000.0, S.iota8p[0:P, :],
                                       ALU.mult, ALU.add)
                v.drain()
                v.tensor_reduce(S.t_sel1[0:P, :], S.r_ohB[0:P, :], axis=AX.X, op=ALU.min)
                v.drain()
                v.tensor_tensor(S.r_ohB[0:P, :], S.iota8[0:P, :],
                                S.t_sel1[0:P, :].broadcast_to([P, 8]), ALU.is_equal)
                v.tensor_tensor(S.t_a[0:P, :], S.r_ohs[0:P, 0:1], S.r_top8[0:P, 0:1], ALU.subtract)
                v.drain()
                v.tensor_tensor(S.r_ohs[0:P, :], S.r_ohA[0:P, :], S.r_ohB[0:P, :], ALU.add)
                v.engine_nop().then_inc(S.cV.h, 1)
            S.v(f); vv = S.cV.bump()

            def f(s, P=P, vv=vv):
                s.wait_ge(S.cV.h, vv)
                s.activation(S.t_a[0:P, :], S.t_a[0:P, :], AF.Exp).then_inc(S.cS.h, 1)
            S.s(f); sv = S.cS.bump()

            def f(v, P=P, sv=sv):
                v.wait_ge(S.cS.h, sv)
                v.tensor_scalar(S.t_a[0:P, :], S.t_a[0:P, :], 1.0, 1.0, ALU.mult, ALU.add)
                v.drain()
                v.reciprocal(S.t_cwA[0:P, :], S.t_a[0:P, :])
                v.drain()
                v.tensor_scalar(S.t_cwB[0:P, :], S.t_cwA[0:P, :], -1.0, 1.0, ALU.mult, ALU.add)
                v.engine_nop().then_inc(S.cV.h, 1)
            S.v(f); cwv_v = S.cV.bump()

            def f(t, i=i, P=P, cwv_v=cwv_v):
                t.wait_ge(S.cV.h, cwv_v)
                t.matmul(S.PB[0:P, 1536:1544], S.ustrict[0:P, 0:P], S.r_ohs[0:P, :], start=True, stop=False)
                t.matmul(S.PB[0:P, 1536:1544], S.ones1x[0:1, 0:P], S.carry[:, :], start=False, stop=True)
                t.matmul(S.PB[0:1, 1544:1552], S.onesc[0:P, 0:1], S.r_ohs[0:P, :],
                         start=True, stop=True).then_inc(S.cT.h, 1)
            S.t(f); tv2 = S.cT.bump()

            def f(v, i=i, P=P, tv2=tv2, wv_apply=wv_apply):
                v.wait_ge(S.cT.h, tv2)
                v.tensor_copy(S.r_t8[0:P, :], S.PB[0:P, 1536:1544])
                v.drain()
                v.tensor_tensor(S.carry[:, :], S.carry[:, :], S.PB[0:1, 1544:1552], ALU.add)
                v.drain()
                v.drain()
                v.tensor_tensor(S.r_top8[0:P, :], S.r_t8[0:P, :], S.r_ohA[0:P, :], ALU.mult)
                v.drain()
                v.tensor_reduce(S.t_pos0[0:P, :], S.r_top8[0:P, :], axis=AX.X, op=ALU.add)
                v.drain()
                v.tensor_tensor(S.r_top8[0:P, :], S.r_t8[0:P, :], S.r_ohB[0:P, :], ALU.mult)
                v.drain()
                v.tensor_reduce(S.t_pos1[0:P, :], S.r_top8[0:P, :], axis=AX.X, op=ALU.add)
                v.drain()
                for k, (sel, pos, vld, cw) in enumerate(
                        [(S.t_sel0, S.t_pos0, S.t_v0, S.t_cwA), (S.t_sel1, S.t_pos1, S.t_v1, S.t_cwB)]):
                    v.tensor_single_scalar(vld[0:P, :], pos[0:P, :], float(CAP), ALU.is_lt)
                    v.scalar_tensor_tensor(S.t_b[0:P, :], sel[0:P, :], float(CAP), pos[0:P, :],
                                           ALU.mult, ALU.add)
                    v.drain()
                    v.tensor_tensor(S.t_b[0:P, :], S.t_b[0:P, :], vld[0:P, :], ALU.mult)
                    v.tensor_scalar(S.t_mr[0:P, :], vld[0:P, :], -float(TRASH), float(TRASH),
                                    ALU.mult, ALU.add)
                    v.drain()
                    v.tensor_tensor(S.t_b[0:P, :], S.t_b[0:P, :], S.t_mr[0:P, :], ALU.add)
                    v.drain()
                    if wv_apply and i == 0:
                        v.tensor_tensor(S.t_b[0:WU, :], S.t_b[0:WU, :], S.wv_col[0:WU, :], ALU.mult)
                        v.drain()
                        v.tensor_tensor(S.t_b[0:WU, :], S.t_b[0:WU, :], S.tcol[0:WU, :], ALU.add)
                        v.drain()
                    v.tensor_copy(S.off_i[k][0:P, i:i + 1], S.t_b[0:P, :])
                    v.tensor_tensor(S.t_mr[0:P, :], cw[0:P, :], vld[0:P, :], ALU.mult)
                    v.drain()
                    v.tensor_scalar(S.cwvt[k][0:P, i:i + 1], S.t_mr[0:P, :], 0.5, None, ALU.mult)
                    v.drain()
                v.engine_nop().then_inc(S.cV.h, 1)
            S.v(f); S.cV.bump()
        return S.cV.n

    def emit_scatter(self, T, ln_done, rout_done):
        S = self
        nt = (T + 127) // 128
        for i in range(nt):
            r0 = i * 128
            P = min(128, T - r0)
            b = i % 2
            scprev = S.qSC.n

            def f(g, r0=r0, P=P, b=b, ln_done=ln_done, scprev=scprev):
                g.wait_ge(S.qYS[0].h, ln_done[0])
                g.wait_ge(S.qYS[1].h, ln_done[1])
                g.wait_ge(S.qSC.h, scprev)
                g.dma_start(out=S.ytm_t[b][0:P, :], in_=S.y_dram[r0:r0 + P, :]).then_inc(S.qYT[b].h, 16)
            S.g(f); ldv = S.qYT[b].bump(16)

            def f(g, i=i, P=P, b=b, ldv=ldv, rout_done=rout_done):
                g.wait_ge(S.qYT[b].h, ldv)
                g.wait_ge(S.cV.h, rout_done)
                g.indirect_dma_start(
                    out=S.xbuf[:, :], out_offset=IOX(S.off_i[0][0:P, i:i + 1], 0),
                    in_=S.ytm_t[b][0:P, :], in_offset=None).then_inc(S.qSC.h, 16)
                g.indirect_dma_start(
                    out=S.xbuf[:, :], out_offset=IOX(S.off_i[1][0:P, i:i + 1], 0),
                    in_=S.ytm_t[b][0:P, :], in_offset=None).then_inc(S.qSC.h, 16)
            S.g(f); S.qSC.bump(32)
        return S.qSC.n

    def emit_ffn(self, T, x_src, x_dst, use_cache, wv_apply, src_wait=None, src_wait_tile=None):
        S = self
        nt = (T + 127) // 128
        ntf = min(nt, 8)
        has64 = (nt == 9)

        ln_half, ln_done = self.emit_ln(x_src, T, src_wait, src_wait_tile)
        yfa, yfm_done = self.emit_yfm(T, ln_half, ln_done)

        cache_bar = 0
        if use_cache:
            cws = [(q.h, q.n) for q in (S.qCW + S.qCE + S.qCDW + S.qCED)]

            def f(g, cws=cws, cgp=S.cG.n):
                g.wait_ge(S.cG.h, cgp)
                for h, n in cws:
                    if n:
                        g.wait_ge(h, n)
                g.memset(S.gdummy[:], 0.0).then_inc(S.cG.h, 1)
            S.g(f); cache_bar = S.cG.bump()

        state = {}

        def gen_routing():
            for i in range(nt):
                r0 = i * 128
                P = min(128, T - r0)
                vprev = S.cV.n

                def f(t, i=i, r0=r0, P=P, yw=(yfa if i < 4 else yfm_done), vprev=vprev):
                    t.wait_ge(S.qF.h, yw)
                    t.wait_ge(S.cV.h, vprev)
                    for c in range(15):
                        t.matmul(S.PB[0:P, 1024:1032], S.yfm[:, c, r0:r0 + P], S.rt_sb[:, c, :],
                                 start=(c == 0), stop=False)
                    t.matmul(S.PB[0:P, 1024:1032], S.yfm[:, 15, r0:r0 + P], S.rt_sb[:, 15, :],
                             start=False, stop=True).then_inc(S.cT.h, 1)
                S.t(f); tv = S.cT.bump()

                def f(v, i=i, P=P, tv=tv):
                    v.wait_ge(S.cT.h, tv)
                    v.tensor_copy(S.lgt[0:P, i, :], S.PB[0:P, 1024:1032]).then_inc(S.cV.h, 1)
                S.v(f); S.cV.bump()
                yield

            def f(v):
                v.memset(S.carry[:], 0.0).then_inc(S.cV.h, 1)
            S.v(f); S.cV.bump()

            ytm_rel = [S.qSCp[0].n, S.qSCp[1].n]
            for i in range(nt):
                r0 = i * 128
                P = min(128, T - r0)
                lgdone = S.cV.n

                def f(v, i=i, P=P, lgdone=lgdone):
                    L = S.lgt[0:P, i, :]
                    v.wait_ge(S.cV.h, lgdone)
                    v.max(S.r_top8[0:P, :], L)
                    v.drain()
                    v.tensor_tensor(S.r_t8[0:P, :], L, S.r_top8[0:P, 0:1].broadcast_to([P, 8]), ALU.is_equal)
                    v.drain()
                    v.scalar_tensor_tensor(S.r_t8[0:P, :], S.r_t8[0:P, :], -1000.0, S.iota8p[0:P, :],
                                           ALU.mult, ALU.add)
                    v.drain()
                    v.tensor_reduce(S.t_sel0[0:P, :], S.r_t8[0:P, :], axis=AX.X, op=ALU.min)
                    v.drain()
                    v.tensor_tensor(S.r_ohA[0:P, :], S.iota8[0:P, :],
                                    S.t_sel0[0:P, :].broadcast_to([P, 8]), ALU.is_equal)
                    v.drain()
                    v.scalar_tensor_tensor(S.r_t8[0:P, :], S.r_ohA[0:P, :], -1e30, L, ALU.mult, ALU.add)
                    v.drain()
                    v.max(S.r_ohs[0:P, :], S.r_t8[0:P, :])
                    v.drain()
                    v.tensor_tensor(S.r_ohB[0:P, :], S.r_t8[0:P, :],
                                    S.r_ohs[0:P, 0:1].broadcast_to([P, 8]), ALU.is_equal)
                    v.drain()
                    v.scalar_tensor_tensor(S.r_ohB[0:P, :], S.r_ohB[0:P, :], -1000.0, S.iota8p[0:P, :],
                                           ALU.mult, ALU.add)
                    v.drain()
                    v.tensor_reduce(S.t_sel1[0:P, :], S.r_ohB[0:P, :], axis=AX.X, op=ALU.min)
                    v.drain()
                    v.tensor_tensor(S.r_ohB[0:P, :], S.iota8[0:P, :],
                                    S.t_sel1[0:P, :].broadcast_to([P, 8]), ALU.is_equal)
                    v.tensor_tensor(S.t_a[0:P, :], S.r_ohs[0:P, 0:1], S.r_top8[0:P, 0:1], ALU.subtract)
                    v.drain()
                    v.tensor_tensor(S.r_ohs[0:P, :], S.r_ohA[0:P, :], S.r_ohB[0:P, :], ALU.add)
                    v.engine_nop().then_inc(S.cV.h, 1)
                S.v(f); vv = S.cV.bump()

                def f(s, P=P, vv=vv):
                    s.wait_ge(S.cV.h, vv)
                    s.activation(S.t_a[0:P, :], S.t_a[0:P, :], AF.Exp).then_inc(S.cS.h, 1)
                S.s(f); sv = S.cS.bump()

                def f(v, P=P, sv=sv):
                    v.wait_ge(S.cS.h, sv)
                    v.tensor_scalar(S.t_a[0:P, :], S.t_a[0:P, :], 1.0, 1.0, ALU.mult, ALU.add)
                    v.drain()
                    v.reciprocal(S.t_cwA[0:P, :], S.t_a[0:P, :])
                    v.drain()
                    v.tensor_scalar(S.t_cwB[0:P, :], S.t_cwA[0:P, :], -1.0, 1.0, ALU.mult, ALU.add)
                    v.engine_nop().then_inc(S.cV.h, 1)
                S.v(f); cwv_v = S.cV.bump()

                def f(t, i=i, P=P, cwv_v=cwv_v):
                    t.wait_ge(S.cV.h, cwv_v)
                    t.matmul(S.PB[0:P, 1536:1544], S.ustrict[0:P, 0:P], S.r_ohs[0:P, :],
                             start=True, stop=False)
                    t.matmul(S.PB[0:P, 1536:1544], S.ones1x[0:1, 0:P], S.carry[:, :],
                             start=False, stop=True)
                    t.matmul(S.PB[0:1, 1544:1552], S.onesc[0:P, 0:1], S.r_ohs[0:P, :],
                             start=True, stop=True).then_inc(S.cT.h, 1)
                S.t(f); tv2 = S.cT.bump()

                def f(v, i=i, P=P, tv2=tv2, wv_apply=wv_apply):
                    v.wait_ge(S.cT.h, tv2)
                    v.tensor_copy(S.r_t8[0:P, :], S.PB[0:P, 1536:1544])
                    v.drain()
                    v.tensor_tensor(S.carry[:, :], S.carry[:, :], S.PB[0:1, 1544:1552], ALU.add)
                    v.drain()
                    v.tensor_tensor(S.r_top8[0:P, :], S.r_t8[0:P, :], S.r_ohA[0:P, :], ALU.mult)
                    v.drain()
                    v.tensor_reduce(S.t_pos0[0:P, :], S.r_top8[0:P, :], axis=AX.X, op=ALU.add)
                    v.drain()
                    v.tensor_tensor(S.r_top8[0:P, :], S.r_t8[0:P, :], S.r_ohB[0:P, :], ALU.mult)
                    v.drain()
                    v.tensor_reduce(S.t_pos1[0:P, :], S.r_top8[0:P, :], axis=AX.X, op=ALU.add)
                    v.drain()
                    for k, (sel, pos, vld, cw) in enumerate(
                            [(S.t_sel0, S.t_pos0, S.t_v0, S.t_cwA), (S.t_sel1, S.t_pos1, S.t_v1, S.t_cwB)]):
                        v.tensor_single_scalar(vld[0:P, :], pos[0:P, :], float(CAP), ALU.is_lt)
                        v.scalar_tensor_tensor(S.t_b[0:P, :], sel[0:P, :], float(CAP), pos[0:P, :],
                                               ALU.mult, ALU.add)
                        v.drain()
                        v.tensor_tensor(S.t_b[0:P, :], S.t_b[0:P, :], vld[0:P, :], ALU.mult)
                        v.tensor_scalar(S.t_mr[0:P, :], vld[0:P, :], -float(TRASH), float(TRASH),
                                        ALU.mult, ALU.add)
                        v.drain()
                        v.tensor_tensor(S.t_b[0:P, :], S.t_b[0:P, :], S.t_mr[0:P, :], ALU.add)
                        v.drain()
                        if wv_apply and i == 0:
                            v.tensor_tensor(S.t_b[0:WU, :], S.t_b[0:WU, :], S.wv_col[0:WU, :], ALU.mult)
                            v.drain()
                            v.tensor_tensor(S.t_b[0:WU, :], S.t_b[0:WU, :], S.tcol[0:WU, :], ALU.add)
                            v.drain()
                        v.tensor_copy(S.off_i[k][0:P, i:i + 1], S.t_b[0:P, :])
                        v.tensor_tensor(S.t_mr[0:P, :], cw[0:P, :], vld[0:P, :], ALU.mult)
                        v.drain()
                        v.tensor_scalar(S.cwvt[k][0:P, i:i + 1], S.t_mr[0:P, :], 0.5, None, ALU.mult)
                        v.drain()
                    v.engine_nop().then_inc(S.cV.h, 1)
                S.v(f); rd = S.cV.bump()

                b = i % 2

                def f(g, r0=r0, P=P, b=b, ln_done=ln_done, yrel=ytm_rel[b]):
                    g.wait_ge(S.qYS[0].h, ln_done[0])
                    g.wait_ge(S.qYS[1].h, ln_done[1])
                    g.wait_ge(S.qSCp[b].h, yrel)
                    g.dma_start(out=S.ytm_t[b][0:P, :], in_=S.y_dram[r0:r0 + P, :]).then_inc(S.qYT[b].h, 16)
                S.g(f); ldv = S.qYT[b].bump(16)

                def f(g, i=i, P=P, b=b, ldv=ldv, rd=rd):
                    g.wait_ge(S.qYT[b].h, ldv)
                    g.wait_ge(S.cV.h, rd)
                    g.indirect_dma_start(
                        out=S.xbuf[:, :], out_offset=IOX(S.off_i[0][0:P, i:i + 1], 0),
                        in_=S.ytm_t[b][0:P, :], in_offset=None).then_inc(S.qSCp[b].h, 16)
                    g.indirect_dma_start(
                        out=S.xbuf[:, :], out_offset=IOX(S.off_i[1][0:P, i:i + 1], 0),
                        in_=S.ytm_t[b][0:P, :], in_offset=None).then_inc(S.qSCp[b].h, 16)
                S.g(f); ytm_rel[b] = S.qSCp[b].bump(32)
                yield
            state["scat_done"] = [(S.qSCp[0].h, S.qSCp[0].n), (S.qSCp[1].h, S.qSCp[1].n)]

        def gen_shared():
            HB = FJ // 2
            xs_store = state["xs_store"]
            slab_mm = [S.cT.n, S.cT.n]
            slab_cw = [S.qCW[0].n, S.qCW[1].n]
            t8_ev = 0
            for hf in range(2):
                ev_par = [[(S.cV.h, S.cV.n), (S.cS.h, S.cS.n)] for _ in range(2)]
                ev64 = [(S.cV.h, t8_ev)] if t8_ev else []
                for phase in range(2):
                    for k2 in range(11):
                        b = k2 % 2

                        def f(g, b=b, phase=phase, hf=hf, k2=k2, sfm=slab_mm[b], sfc=slab_cw[b],
                              cb=cache_bar):
                            g.wait_ge(S.cT.h, sfm)
                            g.wait_ge(S.qCW[b].h, sfc)
                            if cb:
                                g.wait_ge(S.cG.h, cb)
                            g.dma_start(out=S.wslab[b][:],
                                        in_=S.SugA[phase, hf, k2]).then_inc(S.qWS[b].h, 16)
                        S.g(f); wld = S.qWS[b].bump(16)

                        for sub in range(2):
                            j = k2 * 2 + sub
                            par = j % 2

                            def f(t, b=b, sub=sub, par=par, wld=wld, epw=list(ev_par[par]),
                                  e64=list(ev64), has64=has64, yw=yfm_done):
                                t.wait_ge(S.qWS[b].h, wld)
                                t.wait_ge(S.qF.h, yw)
                                for hh, nn in epw:
                                    t.wait_ge(hh, nn)
                                for c in range(16):
                                    last = (c == 15)
                                    wsl = S.wslab[b][:, c, sub * 128:(sub + 1) * 128]
                                    t.matmul(S.PA[:, par * 1024: par * 1024 + 512], wsl,
                                             S.yfm[:, c, 0:512], start=(c == 0), stop=last)
                                    mm2 = t.matmul(S.PA[:, par * 1024 + 512: par * 1024 + 1024], wsl,
                                                   S.yfm[:, c, 512:1024], start=(c == 0), stop=last)
                                if has64:
                                    for hh, nn in e64:
                                        t.wait_ge(hh, nn)
                                    for c in range(16):
                                        mm2 = t.matmul(S.PB[:, 0:64], S.wslab[b][:, c, sub * 128:(sub + 1) * 128],
                                                       S.yfm[:, c, 1024:1088], start=(c == 0), stop=(c == 15))
                                mm2.then_inc(S.cT.h, 1)
                            S.t(f); mmv = S.cT.bump()
                            slab_mm[b] = mmv

                            if phase == 0:
                                def f(s, j=j, par=par, mmv=mmv, has64=has64):
                                    s.wait_ge(S.cT.h, mmv)
                                    s.activation(S.hsh[:, j, 0:512], S.PA[:, par * 1024: par * 1024 + 512], AF.Silu)
                                    act2 = s.activation(S.hsh[:, j, 512:1024],
                                                        S.PA[:, par * 1024 + 512: par * 1024 + 1024], AF.Silu)
                                    if has64:
                                        act2 = s.activation(S.hsh[:, j, 1024:1088], S.PB[:, 0:64], AF.Silu)
                                    act2.then_inc(S.cS.h, 1)
                                S.s(f); ev = (S.cS.h, S.cS.bump())
                            else:
                                def f(v, j=j, par=par, mmv=mmv, has64=has64):
                                    v.wait_ge(S.cT.h, mmv)
                                    v.tensor_tensor(S.hsh[:, j, 0:512], S.PA[:, par * 1024: par * 1024 + 512],
                                                    S.hsh[:, j, 0:512], ALU.mult)
                                    tt2 = v.tensor_tensor(S.hsh[:, j, 512:1024],
                                                          S.PA[:, par * 1024 + 512: par * 1024 + 1024],
                                                          S.hsh[:, j, 512:1024], ALU.mult)
                                    if has64:
                                        tt2 = v.tensor_tensor(S.hsh[:, j, 1024:1088], S.PB[:, 0:64],
                                                              S.hsh[:, j, 1024:1088], ALU.mult)
                                    tt2.then_inc(S.cV.h, 1)
                                S.v(f); ev = (S.cV.h, S.cV.bump())
                            ev_par[par] = [ev]
                            ev64 = [ev]
                        yield "up"
                up_done = S.cV.n
                up_done_s = S.cS.n

                dslab_mm = [S.cT.n, S.cT.n]
                dslab_cw = [S.qCDW[0].n, S.qCDW[1].n]
                ev_pa = [(S.cV.h, S.cV.n), (S.cS.h, S.cS.n)]
                batches = [list(range(0, min(4, ntf)))] + ([list(range(4, ntf))] if ntf > 4 else [])
                chunk_sets = [list(range(k, min(k + 4, 22))) for k in range(0, 22, 4)]
                for dg in range(4):
                    for bi, tts in enumerate(batches):
                        with_t8 = has64 and bi == len(batches) - 1
                        set_mm = []
                        for si, cs in enumerate(chunk_sets):
                            b = si % 2
                            use_blk = use_cache or bi > 0

                            def f(g, b=b, cs=cs, dg=dg, hf=hf, sfm=dslab_mm[b], sfc=dslab_cw[b],
                                  use_blk=use_blk, up_done=up_done, cb=cache_bar):
                                g.wait_ge(S.cT.h, sfm)
                                g.wait_ge(S.qCDW[b].h, sfc)
                                g.wait_ge(S.cV.h, up_done)
                                if cb:
                                    g.wait_ge(S.cG.h, cb)
                                g.dma_start(out=S.dslab[b][:, 0:len(cs), :],
                                            in_=S.SdnA[hf, dg][:, cs[0]:cs[0] + len(cs), :]
                                            ).then_inc(S.qDS[b].h, 16)
                            S.g(f); wld = S.qDS[b].bump(16)

                            def f(t, b=b, cs=cs, si=si, wld=wld, tts=tts, epa=list(ev_pa),
                                  ups=up_done_s, nsets=len(chunk_sets), with_t8=with_t8):
                                t.wait_ge(S.qDS[b].h, wld)
                                if si == 0:
                                    t.wait_ge(S.cS.h, ups)
                                    for hh, nn in epa:
                                        t.wait_ge(hh, nn)
                                for ci, fc in enumerate(cs):
                                    for ti, tt in enumerate(tts):
                                        ps = S.PA[:, ti * 512:(ti + 1) * 512]
                                        mm = t.matmul(ps, S.hsh[:, fc, tt * 128:(tt + 1) * 128],
                                                      S.dslab[b][:, ci, :],
                                                      start=(si == 0 and ci == 0),
                                                      stop=(si == nsets - 1 and ci == len(cs) - 1),
                                                      skip_group_check=True)
                                    if with_t8:
                                        mm = t.matmul(S.PB[0:64, 0:512], S.hsh[:, fc, 1024:1088],
                                                      S.dslab[b][:, ci, :],
                                                      start=(si == 0 and ci == 0),
                                                      stop=(si == nsets - 1 and ci == len(cs) - 1),
                                                      skip_group_check=True)
                                mm.then_inc(S.cT.h, 1)
                            S.t(f); setv = S.cT.bump()
                            set_mm.append(setv)
                            dslab_mm[b] = setv
                        mmv = set_mm[-1]

                        base_src = x_src if hf == 0 else x_dst
                        bsw = (src_wait or []) if hf == 0 else state["x_written"]
                        ev_pa = []
                        rtl = list(tts) + ([8] if with_t8 else [])
                        for ti, tt in enumerate(rtl):
                            is8 = tt == 8
                            P = 64 if is8 else 128
                            xb = tt % 2
                            vprev = S.cV.n

                            def f(y, tt=tt, P=P, xb=xb, dg=dg, base_src=base_src, vprev=vprev,
                                  bsw=bsw, xsw=xs_store[xb]):
                                y.wait_ge(S.cV.h, vprev)
                                y.wait_ge(S.qXS[xb].h, xsw)
                                for hh, nn in bsw:
                                    y.wait_ge(hh, nn)
                                y.dma_start(out=S.xt[xb][0:P, 0:512],
                                            in_=base_src[tt * 128:tt * 128 + P, dg * 512:(dg + 1) * 512]
                                            ).then_inc(S.qXT[xb].h, 16)
                            S.y(f); xld = S.qXT[xb].bump(16)

                            def f(v, ti=ti, P=P, is8=is8, xb=xb, mmv=mmv, xld=xld):
                                v.wait_ge(S.cT.h, mmv)
                                v.wait_ge(S.qXT[xb].h, xld)
                                ps = S.PB[0:64, 0:512] if is8 else S.PA[0:P, ti * 512:(ti + 1) * 512]
                                v.scalar_tensor_tensor(S.xt[xb][0:P, 0:512], ps,
                                                       0.5, S.xt[xb][0:P, 0:512],
                                                       ALU.mult, ALU.add).then_inc(S.cV.h, 1)
                            S.v(f); vv = S.cV.bump()
                            ev_pa.append((S.cV.h, vv))
                            if is8:
                                t8_ev = vv

                            def f(y, tt=tt, P=P, xb=xb, dg=dg, vv=vv):
                                y.wait_ge(S.cV.h, vv)
                                y.dma_start(out=x_dst[tt * 128:tt * 128 + P, dg * 512:(dg + 1) * 512],
                                            in_=S.xt[xb][0:P, 0:512]).then_inc(S.qXS[xb].h, 16)
                            S.y(f); xs_store[xb] = S.qXS[xb].bump(16)
                        ev_pa = [ev_pa[-1]]
                        yield "down"
                state["x_written"] = [(S.qXS[0].h, S.qXS[0].n), (S.qXS[1].h, S.qXS[1].n)]

        def gen_experts():
            xg_mm = S.cT.n
            eslab_mm = [S.cT.n, S.cT.n]
            edslab_mm = [S.cT.n, S.cT.n]
            estg_free = [S.qOBs[0].n, S.qOBs[1].n]
            estg_ix = 0
            dregions = [(1536, 2048)] if has64 else [(1536, 2048), (0, 512)]
            ev_reg = {0: [], 1: []}
            ev_sub = [[], []]
            scd = state["scat_done"]
            xgld = S.qXG[0].n
            for e in range(E):
                def f(y, e=e, scd=list(scd), xf=xg_mm):
                    for hh, nn in scd:
                        y.wait_ge(hh, nn)
                    y.wait_ge(S.cT.h, xf)
                    for c in range(16):
                        y.dma_start(out=S.xg[0][:, c, :],
                                    in_=S.xbuf[e * CAP:(e + 1) * CAP, c * 128:(c + 1) * 128],
                                    transpose=True).then_inc(S.qXG[0].h, 16)
                S.y(f); xgld = S.qXG[0].bump(16 * 16)
                yield

                for sl in range(11):
                    eb = sl % 2

                    def f(g, e=e, sl=sl, eb=eb, sfm=eslab_mm[eb],
                          scw=(list(scd) if e == 0 and sl < 2 else [])):
                        g.wait_ge(S.cT.h, sfm)
                        for hh, nn in scw:
                            g.wait_ge(hh, nn)
                        g.dma_start(out=S.eslab[eb][:],
                                    in_=S.WugA[e, sl]).then_inc(S.qES[eb].h, 16)
                    S.g(f); wld = S.qES[eb].bump(16)

                    for jj in range(2):
                        j = sl * 2 + jj
                        is_gate = j < 11
                        pcol = 512 if j % 2 == 0 else 1024

                        def f(t, eb=eb, jj=jj, pcol=pcol, wld=wld, xgld=xgld,
                              epw=list(ev_sub[j % 2])):
                            t.wait_ge(S.qES[eb].h, wld)
                            t.wait_ge(S.qXG[0].h, xgld)
                            for hh, nn in epw:
                                t.wait_ge(hh, nn)
                            for c in range(16):
                                mm = t.matmul(S.PB[:, pcol:pcol + CAP],
                                              S.eslab[eb][:, c, jj * 128:(jj + 1) * 128],
                                              S.xg[0][:, c, :], start=(c == 0), stop=(c == 15))
                            mm.then_inc(S.cT.h, 1)
                        S.t(f); mmv = S.cT.bump()
                        xg_mm = mmv
                        eslab_mm[eb] = mmv

                        if is_gate:
                            def f(s, j=j, pcol=pcol, mmv=mmv):
                                s.wait_ge(S.cT.h, mmv)
                                s.activation(S.he[:, j, :], S.PB[:, pcol:pcol + CAP],
                                             AF.Silu).then_inc(S.cS.h, 1)
                            S.s(f); ev_sub[j % 2] = [(S.cS.h, S.cS.bump())]
                        else:
                            def f(v, j=j, pcol=pcol, mmv=mmv):
                                v.wait_ge(S.cT.h, mmv)
                                v.tensor_tensor(S.he[:, j - 11, :], S.PB[:, pcol:pcol + CAP],
                                                S.he[:, j - 11, :], ALU.mult).then_inc(S.cV.h, 1)
                            S.v(f); ev_sub[j % 2] = [(S.cV.h, S.cV.bump())]
                    if sl % 2 == 1 or sl == 10:
                        yield
                he_done = S.cV.n
                he_done_s = S.cS.n

                for dg in range(4):
                    db = dg % 2

                    def f(g, e=e, dg=dg, db=db, sfm=edslab_mm[db],
                          scw=(list(scd) if e == 0 and dg < 2 else [])):
                        g.wait_ge(S.cT.h, sfm)
                        for hh, nn in scw:
                            g.wait_ge(hh, nn)
                        g.dma_start(out=S.edslab[db][:],
                                    in_=S.WdnA[e, dg]).then_inc(S.qED[db].h, 16)
                    S.g(f); wld = S.qED[db].bump(16)

                    for st in range(3):
                        SS = 128 if st < 2 else 96

                        eix = estg_ix % 2
                        estg_ix += 1
                        rix = eix % len(dregions)
                        pc0, pc1 = dregions[rix]

                        def f(t, st=st, SS=SS, db=db, wld=wld, epw=list(ev_reg[rix]), hs=he_done_s,
                              hv=he_done, pc0=pc0, pc1=pc1):
                            t.wait_ge(S.qED[db].h, wld)
                            t.wait_ge(S.cS.h, hs)
                            t.wait_ge(S.cV.h, hv)
                            for hh, nn in epw:
                                t.wait_ge(hh, nn)
                            for fc in range(11):
                                mm = t.matmul(S.PB[0:SS, pc0:pc1],
                                              S.he[:, fc, st * 128: st * 128 + SS],
                                              S.edslab[db][:, fc, :], start=(fc == 0), stop=(fc == 10))
                            mm.then_inc(S.cT.h, 1)
                        S.t(f); mmv = S.cT.bump()
                        edslab_mm[db] = mmv

                        if eix == 0:
                            def f(v, SS=SS, mmv=mmv, ef=estg_free[0], pc0=pc0, pc1=pc1):
                                v.wait_ge(S.cT.h, mmv)
                                v.wait_ge(S.qOBs[0].h, ef)
                                v.tensor_copy(S.estg[0][0:SS, :],
                                              S.PB[0:SS, pc0:pc1]).then_inc(S.cV.h, 1)
                            S.v(f); ev_reg[rix] = [(S.cV.h, S.cV.bump())]
                        else:
                            def f(s, SS=SS, mmv=mmv, ef=estg_free[1], pc0=pc0, pc1=pc1):
                                s.wait_ge(S.cT.h, mmv)
                                s.wait_ge(S.qOBs[1].h, ef)
                                s.activation(S.estg[1][0:SS, :],
                                             S.PB[0:SS, pc0:pc1], AF.Copy).then_inc(S.cS.h, 1)
                            S.s(f); ev_reg[rix] = [(S.cS.h, S.cS.bump())]
                        evh, evv = ev_reg[rix][0]

                        def f(y, e=e, st=st, SS=SS, dg=dg, eix=eix, evh=evh, evv=evv):
                            y.wait_ge(evh, evv)
                            y.dma_start(out=S.outbuf[e * CAP + st * 128: e * CAP + st * 128 + SS,
                                                     dg * 512:(dg + 1) * 512],
                                        in_=S.estg[eix][0:SS, :]).then_inc(S.qOBs[eix].h, 16)
                        S.y(f); estg_free[eix] = S.qOBs[eix].bump(16)
                    yield
            state["experts_done"] = [(S.qOBs[0].h, S.qOBs[0].n), (S.qOBs[1].h, S.qOBs[1].n)]

        state["xs_store"] = [S.qXS[0].n, S.qXS[1].n]
        gr = gen_routing()
        gs = gen_shared()
        ge = gen_experts()
        alive_r, alive_s, alive_e = True, True, False
        started_e = False
        while alive_r or alive_s or started_e:
            if alive_r:
                try:
                    next(gr); 
                    try:
                        next(gr)
                    except StopIteration:
                        alive_r = False
                except StopIteration:
                    alive_r = False
                if not alive_r:
                    started_e = True
            if alive_s:
                try:
                    next(gs)
                except StopIteration:
                    alive_s = False
            if started_e:
                try:
                    next(ge)
                except StopIteration:
                    started_e = False
            if not (alive_r or alive_s or started_e):
                break

        xs_store = state["xs_store"]
        x_written = state["x_written"]
        experts_done = state["experts_done"]
        for i in range(nt):
            r0 = i * 128
            P = min(128, T - r0)
            xb = i % 2
            gprev = S.cV.n

            def f(g, i=i, P=P, experts_done=experts_done, gprev=gprev,
                  cwf=[(q.h, q.n) for q in (S.qCE + S.qCED)]):
                for hh, nn in experts_done:
                    g.wait_ge(hh, nn)
                for hh, nn in cwf:
                    if nn:
                        g.wait_ge(hh, nn)
                g.wait_ge(S.cV.h, gprev)
                g.indirect_dma_start(
                    out=S.g0t[0:P, :], out_offset=None,
                    in_=S.outbuf[:, :], in_offset=IOX(S.off_i[0][0:P, i:i + 1], 0)).then_inc(S.qG.h, 16)
                g.indirect_dma_start(
                    out=S.g1t[0:P, :], out_offset=None,
                    in_=S.outbuf[:, :], in_offset=IOX(S.off_i[1][0:P, i:i + 1], 0)).then_inc(S.qG.h, 16)
            S.g(f); gld = S.qG.bump(32)

            def f(y, r0=r0, P=P, xb=xb, gprev=gprev, xww=list(x_written) + list(experts_done),
                  xsw=xs_store[xb]):
                y.wait_ge(S.cV.h, gprev)
                y.wait_ge(S.qXS[xb].h, xsw)
                for hh, nn in xww:
                    y.wait_ge(hh, nn)
                y.dma_start(out=S.xt[xb][0:P, :], in_=x_dst[r0:r0 + P, :]).then_inc(S.qXT[xb].h, 16)
            S.y(f); xld = S.qXT[xb].bump(16)

            def f(v, i=i, P=P, xb=xb, gld=gld, xld=xld):
                v.wait_ge(S.qG.h, gld)
                v.wait_ge(S.qXT[xb].h, xld)
                v.scalar_tensor_tensor(S.xt[xb][0:P, :], S.g0t[0:P, :], S.cwvt[0][0:P, i:i + 1],
                                       S.xt[xb][0:P, :], ALU.mult, ALU.add)
                v.drain()
                v.scalar_tensor_tensor(S.xt[xb][0:P, :], S.g1t[0:P, :], S.cwvt[1][0:P, i:i + 1],
                                       S.xt[xb][0:P, :], ALU.mult, ALU.add).then_inc(S.cV.h, 1)
            S.v(f); vv = S.cV.bump()

            def f(y, r0=r0, P=P, xb=xb, vv=vv):
                y.wait_ge(S.cV.h, vv)
                y.dma_start(out=x_dst[r0:r0 + P, :], in_=S.xt[xb][0:P, :]).then_inc(S.qXS[xb].h, 16)
            S.y(f); xs_store[xb] = S.qXS[xb].bump(16)
        return [(S.qXS[0].h, S.qXS[0].n), (S.qXS[1].h, S.qXS[1].n)]

    def emit_attn(self, x1_wait):
        S = self
        T = TEXT
        bar0 = self.emit_barrier(self.all_sems())
        x1_wait = list(x1_wait) + [(S.cG.h, bar0)]
        ln_half, ln_done = self.emit_ln(S.x1_dram, T, x1_wait)
        yfa, yfm_done = self.emit_yfm(T, ln_half, ln_done)

        b3bar = self.emit_barrier([S.qCW[0], S.qCW[1], S.qCDW[0], S.qCDW[1],
                                   S.qCE[0], S.qCE[1], S.qCED[0], S.qCED[1],
                                   S.qDS[0], S.qDS[1], S.qYT[0], S.qYT[1]])
        def f(v, b3bar=b3bar):
            v.wait_ge(S.cG.h, b3bar)
            v.memset(S.conv_z[0][:, 0:3], 0.0)
            v.memset(S.conv_z[1][:, 0:3], 0.0).then_inc(S.cV.h, 1)
        S.v(f); padz = S.cV.bump()

        projs = [(S.WqA, S.qfm_d, 0), (S.WkA, S.kfm_d, 1), (S.WvA, S.vfm_d, 2), (S.WwA, S.wfm_d, 2)]
        slab_mm = [S.cT.n, S.cT.n]
        ev_par = [[], []]
        for wi, (Wsrc, dst_dram, cslot) in enumerate(projs):
            for k4 in range(4):
                gi4 = wi * 4 + k4
                b = gi4 % 2

                def f(g, Wsrc=Wsrc, k4=k4, b=b, sfm=slab_mm[b]):
                    g.wait_ge(S.cT.h, sfm)
                    g.dma_start(out=S.wslab4[b][:],
                                in_=Wsrc[k4]).then_inc(S.qWS[b].h, 16)
                S.g(f); wld = S.qWS[b].bump(16)

                for sub in range(4):
                    mt = k4 * 4 + sub
                    gi = wi * 16 + mt
                    par = gi % 2
                    ob = gi % 2

                    def f(t, b=b, sub=sub, par=par, wld=wld, epw=list(ev_par[par]), yfm_done=yfm_done):
                        t.wait_ge(S.qWS[b].h, wld)
                        t.wait_ge(S.qF.h, yfm_done)
                        for hh, nn in epw:
                            t.wait_ge(hh, nn)
                        for c in range(16):
                            last = (c == 15)
                            wsl = S.wslab4[b][:, c, sub * 128:(sub + 1) * 128]
                            t.matmul(S.PA[:, par * 1024: par * 1024 + 512], wsl,
                                     S.yfm[:, c, 0:512], start=(c == 0), stop=last)
                            t.matmul(S.PA[:, par * 1024 + 512: par * 1024 + 1024], wsl,
                                     S.yfm[:, c, 512:1024], start=(c == 0), stop=last)
                            mm2 = t.matmul(S.PB[:, par * 512:par * 512 + 64], wsl,
                                           S.yfm[:, c, 1024:1088], start=(c == 0), stop=last)
                            if last:
                                mm2.then_inc(S.cT.h, 1)
                    S.t(f); mmv = S.cT.bump()
                    slab_mm[b] = mmv

                    def f(s, par=par, ob=ob, mmv=mmv, padz=padz, cvd=S.cV.n):
                        s.wait_ge(S.cT.h, mmv)
                        s.wait_ge(S.cV.h, cvd)
                        s.activation(S.conv_z[ob][:, 3:515], S.PA[:, par * 1024: par * 1024 + 512], AF.Copy)
                        s.activation(S.conv_z[ob][:, 515:1027], S.PA[:, par * 1024 + 512: par * 1024 + 1024],
                                     AF.Copy)
                        s.activation(S.conv_z[ob][:, 1027:1091], S.PB[:, par * 512:par * 512 + 64],
                                     AF.Copy).then_inc(S.cS.h, 1)
                    S.s(f); zev = S.cS.bump()
                    ev_par[par] = [(S.cS.h, zev)]

                    def f(v, mt=mt, cslot=cslot, ob=ob, zev=zev, padz=padz, kvw=S.qKV[ob].n):
                        v.wait_ge(S.cS.h, zev)
                        v.wait_ge(S.cV.h, padz)
                        z = S.conv_z[ob]
                        cw = S.convw
                        v.tensor_scalar(S.conv_a[:, 0:TEXT], z[:, 0:TEXT], cw[:, cslot, mt, 0:1], None, ALU.mult)
                        v.drain()
                        v.scalar_tensor_tensor(S.conv_a[:, 0:TEXT], z[:, 1:TEXT + 1], cw[:, cslot, mt, 1:2],
                                               S.conv_a[:, 0:TEXT], ALU.mult, ALU.add)
                        v.drain()
                        v.scalar_tensor_tensor(S.conv_a[:, 0:TEXT], z[:, 2:TEXT + 2], cw[:, cslot, mt, 2:3],
                                               S.conv_a[:, 0:TEXT], ALU.mult, ALU.add)
                        v.drain()
                        v.scalar_tensor_tensor(S.conv_a[:, 0:TEXT], z[:, 3:TEXT + 3], cw[:, cslot, mt, 3:4],
                                               S.conv_a[:, 0:TEXT], ALU.mult, ALU.add)
                        v.wait_ge(S.qKV[ob].h, kvw)
                        v.engine_nop().then_inc(S.cV.h, 1)
                    S.v(f); cvv = S.cV.bump()

                    def f(s, ob=ob, cvv=cvv):
                        s.wait_ge(S.cV.h, cvv)
                        s.activation(S.conv_o[ob][:, :], S.conv_a[:, 0:TEXT], AF.Silu).then_inc(S.cS.h, 1)
                    S.s(f); sil = S.cS.bump()

                    def f(y, dst_dram=dst_dram, mt=mt, ob=ob, sil=sil):
                        y.wait_ge(S.cS.h, sil)
                        y.dma_start(out=dst_dram[mt * 128:(mt + 1) * 128, :],
                                    in_=S.conv_o[ob][:, :]).then_inc(S.qKV[ob].h, 16)
                    S.y(f); S.qKV[ob].bump(16)
        kv_done = (S.qKV[0].n, S.qKV[1].n)

        evs, evv = S.cS.n, S.cV.n

        def f(t, evs=evs, evv=evv, yfm_done=yfm_done):
            t.wait_ge(S.cS.h, evs)
            t.wait_ge(S.cV.h, evv)
            t.wait_ge(S.qF.h, yfm_done)
            t.wait_ge(S.qS.h, S.setup_q)
            for c in range(16):
                last = (c == 15)
                t.matmul(S.PA[:, 0:512], S.wf1_sb[:, c, :], S.yfm[:, c, 0:512], start=(c == 0), stop=last)
                t.matmul(S.PA[:, 512:1024], S.wf1_sb[:, c, :], S.yfm[:, c, 512:1024], start=(c == 0), stop=last)
                t.matmul(S.PB[:, 0:64], S.wf1_sb[:, c, :], S.yfm[:, c, 1024:1088], start=(c == 0), stop=last)
                t.matmul(S.PA[:, 1024:1536], S.wg1_sb[:, c, :], S.yfm[:, c, 0:512], start=(c == 0), stop=last)
                t.matmul(S.PA[:, 1536:2048], S.wg1_sb[:, c, :], S.yfm[:, c, 512:1024], start=(c == 0), stop=last)
                mm = t.matmul(S.PB[:, 512:576], S.wg1_sb[:, c, :], S.yfm[:, c, 1024:1088],
                              start=(c == 0), stop=last)
            mm.then_inc(S.cT.h, 1)
        S.t(f); fgv = S.cT.bump()

        def f(v, fgv=fgv):
            v.wait_ge(S.cT.h, fgv)
            v.tensor_copy(S.f1_sb[:, 0:512], S.PA[:, 0:512])
            v.tensor_copy(S.f1_sb[:, 512:1024], S.PA[:, 512:1024])
            v.tensor_copy(S.f1_sb[:, 1024:1088], S.PB[:, 0:64])
            v.tensor_copy(S.g1_sb[:, 0:512], S.PA[:, 1024:1536])
            v.tensor_copy(S.g1_sb[:, 512:1024], S.PA[:, 1536:2048])
            v.tensor_copy(S.g1_sb[:, 1024:1088], S.PB[:, 512:576]).then_inc(S.cV.h, 1)
        S.v(f); fgev = S.cV.bump()

        for ci in range(9):
            t0 = 0 if ci == 0 else 64 + (ci - 1) * 128
            Tc = 64 if ci == 0 else 128

            def f(t, ci=ci, t0=t0, Tc=Tc, fgev=fgev, prev=S.cS.n, yfm_done=yfm_done):
                t.wait_ge(S.cV.h, fgev)
                t.wait_ge(S.cS.h, prev)
                t.wait_ge(S.qF.h, yfm_done)
                for c in range(15):
                    t.matmul(S.PB[0:Tc, 1024:1040], S.yfm[:, c, t0:t0 + Tc], S.wb_sb[:, c, :],
                             start=(c == 0), stop=False)
                t.matmul(S.PB[0:Tc, 1024:1040], S.yfm[:, 15, t0:t0 + Tc], S.wb_sb[:, 15, :],
                         start=False, stop=True).then_inc(S.cT.h, 1)
            S.t(f); bv = S.cT.bump()

            def f(s, ci=ci, Tc=Tc, bv=bv):
                s.wait_ge(S.cT.h, bv)
                s.activation(S.beta[0:Tc, ci, :], S.PB[0:Tc, 1024:1040], AF.Sigmoid).then_inc(S.cS.h, 1)
            S.s(f); S.cS.bump()

        sbar = self.emit_barrier(self.all_sems())

        def f(g, sbar=sbar):
            g.wait_ge(S.cG.h, sbar)
            g.dma_start(out=S.bg2row[:],
                        in_=S.bg2A.rearrange("(o f) -> o f", o=1)).then_inc(S.qS.h, 16)
        S.g(f); bg2ld = S.qS.bump(16)

        def f(v, sbar=sbar):
            v.wait_ge(S.cG.h, sbar)
            v.memset(S.Sk[:], 0.0)
            v.memset(S.Sv[:], 0.0)
            v.memset(S.Skp[:], 0.0)
            v.memset(S.Svp[:], 0.0)
            v.tensor_copy(S.nwfull[:], S.nw128[:, :].unsqueeze(1).broadcast_to([128, H, DV]))
            v.engine_nop().then_inc(S.cV.h, 1)
        S.v(f); sinit = S.cV.bump()

        def f(t, sinit=sinit, bg2ld=bg2ld):
            t.wait_ge(S.cV.h, sinit)
            t.wait_ge(S.qS.h, bg2ld)
            for i in range(4):
                mm = t.matmul(S.PA[:, i * 512:(i + 1) * 512], S.ones1x[:, :],
                              S.bg2row[:, i * 512:(i + 1) * 512], start=True, stop=True)
            mm.then_inc(S.cT.h, 1)
        S.t(f); bgt = S.cT.bump()

        def f(v, bgt=bgt):
            v.wait_ge(S.cT.h, bgt)
            v.tensor_copy(S.bg2f[:], S.PA[:, :]).then_inc(S.cV.h, 1)
        S.v(f); S.cV.bump()

        ch_consumed = [S.cT.n, S.cT.n]
        for ci in range(9):
            t0 = 0 if ci == 0 else 64 + (ci - 1) * 128
            Tc = 64 if ci == 0 else 128
            par = ci % 2
            state_only = (ci == 0)
            self._scan_chunk(ci, t0, Tc, par, state_only, kv_done, ch_consumed, sinit)
        obuf_done = S.qOBU.n

        b6bar = self.emit_barrier(self.all_sems())
        for c in range(16):
            def f(y, c=c, b6bar=b6bar, obuf_done=obuf_done):
                y.wait_ge(S.cG.h, b6bar)
                y.wait_ge(S.qOBU.h, obuf_done)
                y.dma_start(out=S.ofm[:, c, :], in_=S.obuf[0:TOWN, c * 128:(c + 1) * 128],
                            transpose=True).then_inc(S.qOF.h, 16)
            S.y(f); S.qOF.bump(16)
        ofm_done = S.qOF.n

        xs_store = [S.qXS[0].n, S.qXS[1].n]
        for dg in range(4):
            b = dg % 2

            def f(g, dg=dg, b=b, sf=S.cT.n, b6bar=b6bar):
                g.wait_ge(S.cG.h, b6bar)
                g.wait_ge(S.cT.h, sf)
                g.dma_start(out=S.oslab[b][:],
                            in_=S.WoA[dg]).then_inc(S.qOS[b].h, 16)
            S.g(f); wld = S.qOS[b].bump(16)

            for tt in range(8):
                xb = tt % 2
                pab = (tt % 4)
                ps = lambda: (S.PA[:, pab * 512:(pab + 1) * 512] if tt % 8 < 4 else
                              S.PB[:, pab * 512:(pab + 1) * 512])
                evv = S.cV.n

                def f(t, tt=tt, b=b, wld=wld, ofm_done=ofm_done, evv=evv, pab=pab):
                    t.wait_ge(S.qOS[b].h, wld)
                    t.wait_ge(S.qOF.h, ofm_done)
                    t.wait_ge(S.cV.h, evv)
                    pp = S.PA[:, pab * 512:(pab + 1) * 512] if tt < 4 else \
                         S.PB[:, pab * 512:(pab + 1) * 512]
                    for c in range(16):
                        mm = t.matmul(pp, S.ofm[:, c, tt * 128:(tt + 1) * 128],
                                      S.oslab[b][:, c, :], start=(c == 0), stop=(c == 15))
                    mm.then_inc(S.cT.h, 1)
                S.t(f); mmv = S.cT.bump()

                def f(y, tt=tt, xb=xb, dg=dg, vprev=S.cV.n, xsw=xs_store[xb], x1w=list(x1_wait or [])):
                    y.wait_ge(S.cV.h, vprev)
                    y.wait_ge(S.qXS[xb].h, xsw)
                    y.dma_start(out=S.xt[xb][:, 0:512],
                                in_=S.x1_dram[WU + tt * 128: WU + (tt + 1) * 128, dg * 512:(dg + 1) * 512]
                                ).then_inc(S.qXT[xb].h, 16)
                S.y(f); xld = S.qXT[xb].bump(16)

                def f(v, tt=tt, xb=xb, mmv=mmv, xld=xld, pab=pab):
                    v.wait_ge(S.cT.h, mmv)
                    v.wait_ge(S.qXT[xb].h, xld)
                    pp = S.PA[:, pab * 512:(pab + 1) * 512] if tt < 4 else \
                         S.PB[:, pab * 512:(pab + 1) * 512]
                    v.tensor_tensor(S.xt[xb][:, 0:512], pp, S.xt[xb][:, 0:512],
                                    ALU.add).then_inc(S.cV.h, 1)
                S.v(f); vv = S.cV.bump()

                def f(y, tt=tt, xb=xb, dg=dg, vv=vv):
                    y.wait_ge(S.cV.h, vv)
                    y.dma_start(out=S.x2_dram[tt * 128:(tt + 1) * 128, dg * 512:(dg + 1) * 512],
                                in_=S.xt[xb][:, 0:512]).then_inc(S.qXS[xb].h, 16)
                S.y(f); xs_store[xb] = S.qXS[xb].bump(16)
        return [(S.qXS[0].h, S.qXS[0].n), (S.qXS[1].h, S.qXS[1].n)]


    def _scan_chunk(self, ci, t0, Tc, par, state_only, kv_done, ch_consumed, sinit):
        S = self
        HS = lambda h: slice(h * 128, h * 128 + Tc)
        HF = lambda h: slice(h * 128, (h + 1) * 128)
        svp_prev = getattr(S, "_svp_ready", 0)

        def f(y, ci=ci, t0=t0, Tc=Tc, par=par, state_only=state_only,
              kv_done=kv_done, chc=ch_consumed[par], sinit=sinit):
            y.wait_ge(S.cT.h, chc)
            y.wait_ge(S.cV.h, sinit)
            y.wait_ge(S.qKV[0].h, kv_done[0])
            y.wait_ge(S.qKV[1].h, kv_done[1])
            if not state_only:
                y.dma_start(out=S.qfm_c[par][:, :, 0:Tc],
                            in_=S.qfm_d.rearrange("(c p) t -> p c t", p=128)[:, :, t0:t0 + Tc]
                            ).then_inc(S.qCH[par].h, 16)
                y.dma_start(out=S.kfm_c[par][:, :, 0:Tc],
                            in_=S.kfm_d.rearrange("(c p) t -> p c t", p=128)[:, :, t0:t0 + Tc]
                            ).then_inc(S.qCH[par].h, 16)
            y.dma_start(out=S.wfm_c[par][:, :, 0:Tc],
                        in_=S.wfm_d.rearrange("(c p) t -> p c t", p=128)[:, :, t0:t0 + Tc]
                        ).then_inc(S.qCH[par].h, 16)
            y.dma_start(out=S.ktm_c[par][0:128, :], in_=S.kfm_d[:, t0:t0 + 128],
                        transpose=True).then_inc(S.qCH[par].h, 16)
            y.dma_start(out=S.vtm_c[par][0:128, :], in_=S.vfm_d[:, t0:t0 + 128],
                        transpose=True).then_inc(S.qCH[par].h, 16)
            y.dma_start(out=S.wtm_c[par][0:128, :], in_=S.wfm_d[:, t0:t0 + 128],
                        transpose=True).then_inc(S.qCH[par].h, 16)
        S.y(f)
        chld = S.qCH[par].bump(16 * (4 if state_only else 6))

        if not state_only:
            def f(t, t0=t0, Tc=Tc, evs=S.cS.n, evv=S.cV.n, sinit=sinit, obw=S.qOBU.n):
                t.wait_ge(S.cS.h, evs)
                t.wait_ge(S.cV.h, evv)
                t.wait_ge(S.cV.h, sinit)
                t.wait_ge(S.qOBU.h, obw)
                for i in range(4):
                    mm = t.matmul(S.PB[0:Tc, i * 512:(i + 1) * 512], S.g1_sb[:, t0:t0 + Tc],
                                  S.wg2_sb[:, i * 512:(i + 1) * 512], start=True, stop=True)
                mm.then_inc(S.cT.h, 1)
            S.t(f); gv = S.cT.bump()

            def f(v, Tc=Tc, gv=gv):
                v.wait_ge(S.cT.h, gv)
                v.tensor_tensor(S.sgf[0:Tc, :], S.PB[0:Tc, 0:2048], S.bg2f[0:Tc, :],
                                ALU.add).then_inc(S.cV.h, 1)
            S.v(f); gev = S.cV.bump()

            def f(s, Tc=Tc, gev=gev):
                s.wait_ge(S.cV.h, gev)
                s.activation(S.sgf[0:Tc, :], S.sgf[0:Tc, :], AF.Sigmoid).then_inc(S.cS.h, 1)
            S.s(f); gsg = S.cS.bump()

            def f(v, Tc=Tc, gsg=gsg):
                v.wait_ge(S.cS.h, gsg)
                v.tensor_tensor(S.sgf[0:Tc, :], S.sgf[0:Tc, :],
                                S.nwfull[0:Tc, :, :].rearrange("p a b -> p (a b)"),
                                ALU.mult).then_inc(S.cV.h, 1)
            S.v(f); S.cV.bump()

        def f(t, t0=t0, Tc=Tc, evs=S.cS.n, evv=S.cV.n, sinit=sinit, obw=S.qOBU.n):
            t.wait_ge(S.cS.h, evs)
            t.wait_ge(S.cV.h, evv)
            t.wait_ge(S.cV.h, sinit)
            t.wait_ge(S.qOBU.h, obw)
            for i in range(4):
                mm = t.matmul(S.PA[0:Tc, i * 512:(i + 1) * 512], S.f1_sb[:, t0:t0 + Tc],
                              S.wf2_sb[:, i * 512:(i + 1) * 512], start=True, stop=True)
            mm.then_inc(S.cT.h, 1)
        S.t(f); gpv = S.cT.bump()

        def f(v, Tc=Tc, gpv=gpv):
            v.wait_ge(S.cT.h, gpv)
            v.tensor_copy(S.gpre[0:Tc, :], S.PA[0:Tc, 0:2048]).then_inc(S.cV.h, 1)
        S.v(f); gpe = S.cV.bump()

        def f(s, Tc=Tc, gpe=gpe):
            s.wait_ge(S.cV.h, gpe)
            s.activation(S.tmpA[0:Tc, :], S.gpre[0:Tc, :], AF.Abs)
            s.drain()
            s.activation(S.tmpB[0:Tc, :], S.tmpA[0:Tc, :], AF.Exp, scale=-1.0)
            s.drain()
            s.activation(S.gtm[0:Tc, :], S.tmpB[0:Tc, :], AF.Ln, bias=1.0).then_inc(S.cS.h, 1)
        S.s(f); lgs = S.cS.bump()

        def f(v, Tc=Tc, lgs=lgs):
            v.wait_ge(S.cS.h, lgs)
            v.tensor_tensor(S.tmpB[0:Tc, :], S.gpre[0:Tc, :], S.tmpA[0:Tc, :], ALU.subtract)
            v.drain()
            v.scalar_tensor_tensor(S.gtm[0:Tc, :], S.tmpB[0:Tc, :], 0.5, S.gtm[0:Tc, :],
                                   ALU.mult, ALU.subtract).then_inc(S.cV.h, 1)
        S.v(f); gtv = S.cV.bump()

        def f(t, Tc=Tc, gtv=gtv, gpe=gpe):
            t.wait_ge(S.cV.h, gtv)
            for i in range(4):
                t.matmul(S.PA[0:Tc, i * 512:(i + 1) * 512], S.uinc[0:Tc, 0:Tc],
                         S.gtm[0:Tc, i * 512:(i + 1) * 512], start=True, stop=True)
            for h in range(H):
                mm = t.matmul(S.PB[0:128, h * 128:h * 128 + Tc], S.gtm[0:Tc, HF(h)],
                              S.uinc[0:Tc, 0:Tc], start=True, stop=True)
            mm.then_inc(S.cT.h, 1)
        S.t(f); lamv = S.cT.bump()

        def f(v, Tc=Tc, lamv=lamv):
            v.wait_ge(S.cT.h, lamv)
            v.tensor_copy(S.ltm[0:Tc, :], S.PA[0:Tc, 0:2048])
            v.tensor_copy(S.lfm.rearrange("p (a b) -> p a b", b=128)[:, :, 0:Tc],
                          S.PB[:, 0:2048].rearrange("p (a b) -> p a b", b=128)[:, :, 0:Tc]
                          ).then_inc(S.cV.h, 1)
        S.v(f); lev = S.cV.bump()

        def f(s, Tc=Tc, lev=lev):
            s.wait_ge(S.cV.h, lev)
            s.activation(S.p1f[0:Tc, :], S.ltm[0:Tc, :], AF.Exp, bias=45.0)
            s.activation(S.p2b[0:Tc, :], S.ltm[0:Tc, :], AF.Exp, scale=-1.0, bias=-45.0)
            s.activation(S.p2fb.rearrange("p (a b) -> p a b", b=128)[:, :, 0:Tc],
                         S.lfm.rearrange("p (a b) -> p a b", b=128)[:, :, 0:Tc],
                         AF.Exp, scale=-1.0, bias=-45.0)
            s.activation(S.fcol[:, :],
                         S.lfm.rearrange("p (a b) -> p a b", b=128)[:, :, Tc - 1:Tc]
                         .rearrange("p a b -> p (a b)"), AF.Exp).then_inc(S.cS.h, 1)
        S.s(f); exv = S.cS.bump()

        def f(s, Tc=Tc, par=par, chld=chld, gtv2=S.cV.n):
            s.wait_ge(S.qCH[par].h, chld)
            s.wait_ge(S.cV.h, gtv2)
            s.activation(S.sq2[0:Tc, :], S.wtm_c[par][0:Tc, :], AF.Square).then_inc(S.cS.h, 1)
        S.s(f); wsq = S.cS.bump()

        def f(v, Tc=Tc, wsq=wsq):
            v.wait_ge(S.cS.h, wsq)
            v.tensor_reduce(S.wrow[0:Tc, :],
                            S.sq2[0:Tc, :].rearrange("p (a b) -> p a b", b=128),
                            axis=AX.X, op=ALU.add).then_inc(S.cV.h, 1)
        S.v(f); wsv = S.cV.bump()

        def f(s, Tc=Tc, wsv=wsv):
            s.wait_ge(S.cV.h, wsv)
            s.activation(S.wrow[0:Tc, :], S.wrow[0:Tc, :], AF.Ln, bias=1e-6)
            s.drain()
            s.activation(S.wrow[0:Tc, :], S.wrow[0:Tc, :], AF.Exp, scale=-0.5).then_inc(S.cS.h, 1)
        S.s(f); wrv = S.cS.bump()

        def f(v, ci=ci, Tc=Tc, wrv=wrv):
            v.wait_ge(S.cS.h, wrv)
            v.tensor_tensor(S.brs[0:Tc, :], S.beta[0:Tc, ci, :], S.wrow[0:Tc, :],
                            ALU.mult).then_inc(S.cV.h, 1)
        S.v(f); brsv = S.cV.bump()

        def f(v, ci=ci, Tc=Tc, exv=exv, chld=chld, par=par, state_only=state_only, brsv=brsv):
            v.wait_ge(S.cS.h, exv)
            v.wait_ge(S.cV.h, brsv)
            v.wait_ge(S.qCH[par].h, chld)
            v.tensor_tensor(S.lfm[0:Tc, :].rearrange("p (a b) -> p a b", b=128),
                            S.p2b[0:Tc, :].rearrange("p (a b) -> p a b", b=128),
                            S.brs[0:Tc, :].unsqueeze(2).broadcast_to([Tc, H, 128]), ALU.mult)
            v.drain()
            v.tensor_tensor(S.kpb[0:Tc, :], S.wtm_c[par][0:Tc, :], S.lfm[0:Tc, :], ALU.mult)
            if not state_only:
                v.tensor_tensor(S.wtf[:, :],
                                S.wfm_c[par].rearrange("p a b -> p (a b)")[:, 0:H * 128],
                                S.p2fb[:, :], ALU.mult)
            v.engine_nop().then_inc(S.cV.h, 1)
        S.v(f); kpv = S.cV.bump()

        if not state_only:
            def f(t, Tc=Tc, par=par, chld=chld, evv=S.cV.n, lev=lev):
                t.wait_ge(S.qCH[par].h, chld)
                t.wait_ge(S.cV.h, lev)
                for h in range(H):
                    mm = t.matmul(S.PA[0:Tc, h * 128:h * 128 + Tc], S.kfm_c[par][:, h, 0:Tc],
                                  S.qfm_c[par][:, h, 0:Tc], start=True, stop=True)
                mm.then_inc(S.cT.h, 1)
            S.t(f); atv = S.cT.bump()

            def f(v, Tc=Tc, atv=atv):
                v.wait_ge(S.cT.h, atv)
                v.tensor_tensor(S.a_sb.rearrange("p (a b) -> p a b", b=128)[0:Tc, :, 0:Tc],
                                S.PA[0:Tc, 0:2048].rearrange("p (a b) -> p a b", b=128)[0:Tc, :, 0:Tc],
                                S.maskrep.rearrange("p (a b) -> p a b", b=128)[0:Tc, :, 0:Tc],
                                ALU.mult).then_inc(S.cV.h, 1)
            S.v(f); aev = S.cV.bump()

            def f(t, Tc=Tc, par=par, aev=aev, kpv=kpv, gev2=S.cV.n):
                t.wait_ge(S.cV.h, aev)
                t.wait_ge(S.cV.h, kpv)
                for h in range(H):
                    t.matmul(S.PB[0:Tc, h * 128:(h + 1) * 128], S.qfm_c[par][:, h, 0:Tc],
                             S.Skp[:, HF(h)], start=True, stop=False)
                    mm = t.matmul(S.PB[0:Tc, h * 128:(h + 1) * 128],
                                  S.a_sb[0:Tc, h * 128:h * 128 + Tc],
                                  S.kpb[0:Tc, HF(h)], start=False, stop=True)
                mm.then_inc(S.cT.h, 1)
            S.t(f); sv = S.cT.bump()

            def f(v, Tc=Tc, sv=sv):
                v.wait_ge(S.cT.h, sv)
                v.scalar_tensor_tensor(S.s_sb[0:Tc, :], S.PB[0:Tc, 0:2048], SCALE,
                                       S.p1f[0:Tc, :], ALU.mult, ALU.mult)
                v.drain()
                v.tensor_reduce(S.mrow[0:Tc, :],
                                S.s_sb[0:Tc, :].rearrange("p (a b) -> p a b", b=128),
                                axis=AX.X, op=ALU.max)
                v.drain()
                v.tensor_tensor(S.s_sb[0:Tc, :].rearrange("p (a b) -> p a b", b=128),
                                S.s_sb[0:Tc, :].rearrange("p (a b) -> p a b", b=128),
                                S.mrow[0:Tc, :].unsqueeze(2).broadcast_to([Tc, H, 128]),
                                ALU.subtract)
                v.engine_nop().then_inc(S.cV.h, 1)
            S.v(f); sm1 = S.cV.bump()
            sev = sm1

            def f(s, Tc=Tc, sm1=sm1):
                s.wait_ge(S.cV.h, sm1)
                s.activation(S.s_sb[0:Tc, :], S.s_sb[0:Tc, :], AF.Exp).then_inc(S.cS.h, 1)
            S.s(f); sm2 = S.cS.bump()

            def f(v, Tc=Tc, sm2=sm2):
                v.wait_ge(S.cS.h, sm2)
                v.tensor_reduce(S.srow[0:Tc, :],
                                S.s_sb[0:Tc, :].rearrange("p (a b) -> p a b", b=128),
                                axis=AX.X, op=ALU.add)
                v.drain()
                v.reciprocal(S.rrow[0:Tc, :], S.srow[0:Tc, :])
                v.drain()
                v.tensor_tensor(S.s_sb[0:Tc, :].rearrange("p (a b) -> p a b", b=128),
                                S.s_sb[0:Tc, :].rearrange("p (a b) -> p a b", b=128),
                                S.rrow[0:Tc, :].unsqueeze(2).broadcast_to([Tc, H, 128]),
                                ALU.mult)
                v.drain()
                v.tensor_tensor(S.a_sb[0:Tc, 0:2048], S.s_sb[0:Tc, :], S.p1f[0:Tc, :],
                                ALU.mult).then_inc(S.cV.h, 1)
            S.v(f); apv = S.cV.bump()

            PT = S.PA[:, 0:1024].bitcast(BF16)

            def f(t, Tc=Tc, apv=apv, PT=PT):
                t.wait_ge(S.cV.h, apv)
                for h in range(H):
                    mm = t.transpose(PT[:, h * 128:h * 128 + Tc], S.a_sb[0:Tc, HF(h)],
                                     S.identb[0:Tc, 0:Tc])
                mm.then_inc(S.cT.h, 1)
            S.t(f); aptv = S.cT.bump()

            def f(v, Tc=Tc, aptv=aptv, PT=PT):
                v.wait_ge(S.cT.h, aptv)
                v.tensor_copy(S.at_sb.rearrange("p (a b) -> p a b", b=128)[:, :, 0:Tc],
                              PT.rearrange("p (a b) -> p a b", b=128)[:, :, 0:Tc]
                              ).then_inc(S.cV.h, 1)
            S.v(f); apte = S.cV.bump()

            def f(t, Tc=Tc, apte=apte, kpv=kpv):
                t.wait_ge(S.cV.h, apte)
                for h in range(H):
                    mm = t.matmul(S.PA[0:Tc, h * 128:h * 128 + Tc], S.wtf[:, HS(h)],
                                  S.at_sb[:, h * 128:h * 128 + Tc], start=True, stop=True)
                mm.then_inc(S.cT.h, 1)
            S.t(f); mtv = S.cT.bump()

            def f(v, ci=ci, Tc=Tc, mtv=mtv):
                v.wait_ge(S.cT.h, mtv)
                v.tensor_tensor(S.m_sb[0:Tc, :].rearrange("p (a b) -> p a b", b=128),
                                S.PA[0:Tc, 0:2048].rearrange("p (a b) -> p a b", b=128),
                                S.brs[0:Tc, :].unsqueeze(2).broadcast_to([Tc, H, 128]),
                                ALU.mult)
                v.drain()
                v.tensor_tensor(S.m_sb[0:Tc, :], S.m_sb[0:Tc, :], S.maskrep[0:Tc, :],
                                ALU.mult)
                v.engine_nop().then_inc(S.cV.h, 1)
            S.v(f); mev = S.cV.bump()

            def f(t, Tc=Tc, par=par, mev=mev, sev=sev, svp_prev=svp_prev):
                t.wait_ge(S.cV.h, mev)
                t.wait_ge(S.cV.h, sev)
                t.wait_ge(S.cS.h, svp_prev)
                for h in range(H):
                    t.matmul(S.PB[0:Tc, HF(h)], S.m_sb[0:Tc, h * 128:h * 128 + Tc],
                             S.vtm_c[par][0:Tc, HF(h)], start=True, stop=False)
                    mm = t.matmul(S.PB[0:Tc, HF(h)], S.at_sb[:, h * 128:h * 128 + Tc],
                                  S.Svp[:, HF(h)], start=False, stop=True)
                mm.then_inc(S.cT.h, 1)
            S.t(f); ov = S.cT.bump()

            def f(s, Tc=Tc, ov=ov):
                s.wait_ge(S.cT.h, ov)
                s.activation(S.sq2[0:Tc, :], S.PB[0:Tc, 0:2048], AF.Square).then_inc(S.cS.h, 1)
            S.s(f); sqv = S.cS.bump()

            def f(v, Tc=Tc, sqv=sqv):
                v.wait_ge(S.cS.h, sqv)
                v.tensor_reduce(S.mrow[0:Tc, :],
                                S.sq2[0:Tc, :].rearrange("p (a b) -> p a b", b=128),
                                axis=AX.X, op=ALU.add).then_inc(S.cV.h, 1)
            S.v(f); msv = S.cV.bump()

            def f(s, Tc=Tc, msv=msv):
                s.wait_ge(S.cV.h, msv)
                s.activation(S.srow[0:Tc, :], S.mrow[0:Tc, :], AF.Ln, scale=1.0 / 128.0, bias=float(EPS))
                s.drain()
                s.activation(S.srow[0:Tc, :], S.srow[0:Tc, :], AF.Exp, scale=-0.5).then_inc(S.cS.h, 1)
            S.s(f); rsv = S.cS.bump()

            def f(v, ci=ci, t0=t0, Tc=Tc, rsv=rsv):
                v.wait_ge(S.cS.h, rsv)
                v.tensor_tensor(S.onrm[0:Tc, :].rearrange("p (a b) -> p a b", b=128),
                                S.PB[0:Tc, 0:2048].rearrange("p (a b) -> p a b", b=128),
                                S.srow[0:Tc, :].unsqueeze(2).broadcast_to([Tc, H, 128]),
                                ALU.mult)
                v.drain()
                v.tensor_tensor(S.ogat[0:Tc, :], S.onrm[0:Tc, :], S.sgf[0:Tc, :],
                                ALU.mult).then_inc(S.cV.h, 1)
            S.v(f); oev = S.cV.bump()

            def f(y, t0=t0, Tc=Tc, oev=oev):
                y.wait_ge(S.cV.h, oev)
                y.dma_start(out=S.obuf[t0 - WU:t0 - WU + Tc, :], in_=S.ogat[0:Tc, :]
                            ).then_inc(S.qOBU.h, 16)
            S.y(f); S.qOBU.bump(16)
        else:
            oev = S.cV.n

        evv = S.cV.n

        def f(t, Tc=Tc, par=par, kpv=kpv, evv=evv, chld=chld):
            t.wait_ge(S.cV.h, kpv)
            t.wait_ge(S.cV.h, evv)
            t.wait_ge(S.qCH[par].h, chld)
            for h in range(H):
                t.matmul(S.PA[:, HF(h)], S.kpb[0:Tc, HF(h)], S.ktm_c[par][0:Tc, HF(h)],
                         start=True, stop=True)
                mm = t.matmul(S.PB[:, HF(h)], S.kpb[0:Tc, HF(h)], S.vtm_c[par][0:Tc, HF(h)],
                              start=True, stop=True)
            mm.then_inc(S.cT.h, 1)
        S.t(f); dsv = S.cT.bump()
        ch_consumed[par] = dsv

        def f(v, Tc=Tc, dsv=dsv):
            v.wait_ge(S.cT.h, dsv)
            v.tensor_tensor(S.sfold[:, :], S.Sk[:, :], S.PA[:, 0:2048], ALU.add)
            v.drain()
            v.tensor_tensor(S.Sk[:, :].rearrange("p (a b) -> p a b", b=128),
                            S.sfold[:, :].rearrange("p (a b) -> p a b", b=128),
                            S.fcol[:, :].unsqueeze(2).broadcast_to([128, H, 128]),
                            ALU.mult)
            v.engine_nop().then_inc(S.cV.h, 1)
        S.v(f); foldk = S.cV.bump()

        def f(s, foldk=foldk):
            s.wait_ge(S.cV.h, foldk)
            s.activation(S.skb16[:, :], S.Sk[:, :], AF.Copy).then_inc(S.cS.h, 1)
        S.s(f); castk = S.cS.bump()

        def f(v, Tc=Tc):
            v.tensor_tensor(S.sfold[:, :], S.Sv[:, :], S.PB[:, 0:2048], ALU.add)
            v.drain()
            v.tensor_tensor(S.Sv[:, :].rearrange("p (a b) -> p a b", b=128),
                            S.sfold[:, :].rearrange("p (a b) -> p a b", b=128),
                            S.fcol[:, :].unsqueeze(2).broadcast_to([128, H, 128]),
                            ALU.mult)
            v.engine_nop().then_inc(S.cV.h, 1)
        S.v(f); foldv = S.cV.bump()

        def f(s, foldv=foldv):
            s.wait_ge(S.cV.h, foldv)
            s.activation(S.Svp[:, :], S.Sv[:, :], AF.Copy).then_inc(S.cS.h, 1)
        S.s(f); S._svp_ready = S.cS.bump()
        fold = foldv

        PT2 = S.PA[:, 0:1024].bitcast(BF16)

        def f(t, castk=castk, PT2=PT2):
            t.wait_ge(S.cS.h, castk)
            for h in range(H):
                mm = t.transpose(PT2[:, HF(h)], S.skb16[:, HF(h)], S.identb[:, :])
            mm.then_inc(S.cT.h, 1)
        S.t(f); skt = S.cT.bump()

        def f(v, skt=skt, PT2=PT2):
            v.wait_ge(S.cT.h, skt)
            v.tensor_copy(S.Skp[:, :], PT2[:, 0:2048]).then_inc(S.cV.h, 1)
        S.v(f); S.cV.bump()

    def all_sems(self):
        S = self
        out = [S.cT, S.cV, S.cS, S.cG, S.qS, S.qF, S.qG,
               S.qOBU, S.qOF]
        for lst in [S.qXT, S.qYT, S.qWS, S.qDS, S.qES, S.qED, S.qXG, S.qCH, S.qOS, S.qYS, S.qCW, S.qCE, S.qCDW, S.qCED, S.qXS, S.qOBs, S.qKV, S.qSCp]:
            out.extend(lst)
        return out

    def finalize(self):
        S = self
        nc = S.nc
        src_map = {"y_dram": S.y_dram, "x1_dram": S.x1_dram, "xbuf": S.xbuf,
                   "outbuf": S.outbuf, "x2_dram": S.x2_dram, "obuf": S.obuf,
                   "qfm_d": S.qfm_d, "kfm_d": S.kfm_d, "vfm_d": S.vfm_d, "wfm_d": S.wfm_d}
        sems = self.all_sems()
        allw = [(c.h, c.n) for c in sems]
        for name, dst in S.dbg_aps.items():

            def f(g, name=name, dst=dst, allw=allw):
                for h, n in allw:
                    if n:
                        g.wait_ge(h, n)
                g.dma_start(out=dst[:, :], in_=src_map[name][:, :]).then_inc(S.qS.h, 16)
            S.g(f); S.qS.bump(16)

        final = [(c.h, c.n) for c in sems]

        def f(y, final=final):
            for h, n in final:
                if n:
                    y.wait_ge(h, n)
        S.y(f)

        with nc.Block() as block:
            @block.tensor
            def _(eng):
                for fn in S.ops["t"]:
                    fn(eng)

            @block.vector
            def _(eng):
                for fn in S.ops["v"]:
                    fn(eng)

            @block.scalar
            def _(eng):
                for fn in S.ops["s"]:
                    fn(eng)

            @block.gpsimd
            def _(eng):
                for fn in S.ops["g"]:
                    fn(eng)

            @block.sync
            def _(eng):
                for fn in S.ops["y"]:
                    fn(eng)
        return nc


def shard_inputs(inputs):
    x = np.ascontiguousarray(inputs["L0_attn_hidden_states"][0])

    def slab4(w):
        return np.asarray(w).reshape(16, 128, 4, 512).transpose(2, 1, 0, 3)

    shared = {
        "Wq": slab4(inputs["L0_attn_Wq"]), "Wk": slab4(inputs["L0_attn_Wk"]),
        "Wv": slab4(inputs["L0_attn_Wv"]),
        "Ww": slab4(inputs["L0_attn_Ww"]), "cq": inputs["L0_attn_cq"], "ck": inputs["L0_attn_ck"],
        "cv": inputs["L0_attn_cv"], "Wf1": inputs["L0_attn_Wf1"], "Wf2": inputs["L0_attn_Wf2"],
        "Wb": inputs["L0_attn_Wb"], "Wg1": inputs["L0_attn_Wg1"], "Wg2": inputs["L0_attn_Wg2"],
        "bg2": inputs["L0_attn_bg2"], "norm_w": inputs["L0_attn_norm_w"],
        "Wo": slab4(inputs["L0_attn_Wo"]),
        "router": inputs["L0_ffn_router"],
        "w_up_gate": np.asarray(inputs["L0_ffn_w_up_gate"])
            .reshape(8, 16, 128, 11, 256).transpose(0, 3, 2, 1, 4),
        "w_down": np.asarray(inputs["L0_ffn_w_down"])
            .reshape(8, 11, 128, 4, 512).transpose(0, 3, 2, 1, 4),
        "shared_w_up_gate": np.asarray(inputs["L0_ffn_shared_w_up_gate"])
            .reshape(16, 128, 2, 2, 11, 256).transpose(2, 3, 4, 1, 0, 5),
        "shared_w_down": np.asarray(inputs["L0_ffn_shared_w_down"])
            .reshape(2, 22, 128, 4, 512).transpose(0, 3, 2, 1, 4),
    }
    shared = {k: np.ascontiguousarray(v) for k, v in shared.items()}
    in_maps = []
    for c in range(NCORE):
        lo = c * TOWN - WU
        if c == 0:
            ext = np.concatenate([np.zeros((WU, D), np.float32), x[0:TOWN]], axis=0)
        else:
            ext = x[lo:(c + 1) * TOWN]
        m = dict(shared)
        m["xin"] = np.ascontiguousarray(ext)
        m["wvalid"] = np.array([[0.0 if c == 0 else 1.0]], np.float32)
        in_maps.append(m)
    return in_maps


_CACHED = {}


def _build():
    if "nc" not in _CACHED:
        m = MK()
        m.emit_setup()
        w1 = m.emit_ffn(TEXT, m.xin, m.x1_dram, False, True)
        w2 = m.emit_attn(w1)
        bar = m.emit_barrier(m.all_sems())
        m.emit_ffn(TOWN, m.x2_dram, m.xoutA, False, False, src_wait=list(w2) + [(m.cG.h, bar)])
        _CACHED["nc"] = m.finalize()
    return _CACHED["nc"]


def kernel(**inputs):
    nc = _build()
    in_maps = shard_inputs(inputs)
    res = run_bass_kernel_spmd(nc, in_maps, core_ids=list(range(NCORE)))
    out = np.concatenate([res.results[c]["xout"] for c in range(NCORE)], axis=0)
    return out.reshape(1, NCORE * TOWN, D).astype(np.float32)
